# version 02
# baseline (speedup 1.0000x reference)
import numpy as np
import ml_dtypes
import concourse.bass as bass
import concourse.mybir as mybir
from concourse.bass_utils import run_bass_kernel_spmd

F32 = mybir.dt.float32
BF16 = mybir.dt.bfloat16
AF = mybir.ActivationFunctionType
ALU = mybir.AluOpType
AX = mybir.AxisListType
bfdt = ml_dtypes.bfloat16

MAGIC = 12582912.0
LN_EPS = 1e-5
RMS_EPS = 1.1920929e-07
NCORE = 8
TLOC = 1024
D = 2048



class Chan:
    __slots__ = ("sem", "val", "step", "name")

    def __init__(self, nc, name, step=1):
        self.sem = nc.alloc_semaphore(name=name)
        self.val = 0
        self.step = step
        self.name = name

    def ev(self, inst, n=None):
        n = self.step if n is None else n
        inst.then_inc(self.sem, n)
        self.val += n
        return (self.sem, self.val)

    def cur(self):
        return (self.sem, self.val)


class SE:

    def __init__(self, eng, chan):
        self.eng = eng
        self.chan = chan
        self._waited = {}

    def wait(self, ev):
        if ev is None:
            return
        sem, val = ev
        if self._waited.get(sem.name, -1) >= val:
            return
        self.eng.wait_ge(sem, val)
        self._waited[sem.name] = val

    def op(self, name, *a, **kw):
        if self.chan.val > 0:
            self.wait(self.chan.cur())
        return self.chan.ev(getattr(self.eng, name)(*a, **kw))

    def raw(self, name, *a, **kw):
        return getattr(self.eng, name)(*a, **kw)

    def mark(self):
        return self.chan.ev(self.eng.nop())


class DmaSlot:

    def __init__(self, nc, name):
        self.chan = Chan(nc, name, 16)

    def dma(self, se, out, in_, **kw):
        self.chan.ev(se.eng.dma_start(out=out, in_=in_, **kw), 16)

    def dma_t(self, se, out, in_):
        self.chan.ev(se.eng.dma_start(out=out, in_=in_, transpose=True), 16)

    def done(self):
        return self.chan.cur()


class Prog:
    def __init__(self, name):
        self.nc = bass.Bass("TRN2", target_bir_lowering=False)
        self.name = name
        self.ev = {}

    def start(self):
        from contextlib import ExitStack
        self.es = ExitStack()
        self.es.__enter__()
        nc = self.nc
        self.block = self.es.enter_context(nc.Block())
        self.G = SE(nc.gpsimd, Chan(nc, "chG"))
        self.V = SE(nc.vector, Chan(nc, "chV"))
        self.S = SE(nc.scalar, Chan(nc, "chS"))
        self.T = SE(nc.tensor, Chan(nc, "chT"))
        self.Y = SE(nc.sync, Chan(nc, "chY"))
        return self

    def sec(self, eng, fn):
        getattr(self.block, eng)(lambda _e: fn())

    def sb(self, name, shape, dtype):
        return self.es.enter_context(self.nc.sbuf_tensor(name, shape, dtype))

    def ps(self, name, shape, dtype=F32):
        return self.es.enter_context(self.nc.psum_tensor(name, shape, dtype))

    def slot(self, name):
        return DmaSlot(self.nc, name)

    def fin(self):
        self.es.__exit__(None, None, None)
        return self.nc


def t2(x):
    return np.ascontiguousarray(np.asarray(x).swapaxes(-1, -2))



def build_quant():
    p = Prog("quant").start()
    nc = p.nc
    wgu = nc.dram_tensor("wgu", [2048, 2048], F32, kind="ExternalInput").ap()
    wpr = nc.dram_tensor("wpr", [256, 8192], F32, kind="ExternalInput").ap()
    qgu = nc.dram_tensor("qgu", [2048, 2048], BF16, kind="ExternalOutput").ap()
    qpr = nc.dram_tensor("qpr", [256, 8192], BF16, kind="ExternalOutput").ap()

    slI = p.slot("slI")
    slO = p.slot("slO")
    C = 8192
    w_t = p.sb("w_t", [128, C], F32)
    wbf = p.sb("wbf", [128, C], BF16)
    gs = p.sb("gs", [128, C // 64], F32)
    sc = p.sb("sc", [128, C // 64], F32)
    scb = p.sb("scb", [128, C // 64], BF16)
    rc = p.sb("rc", [128, C // 64], F32)
    qt = p.sb("qt", [128, C], F32)
    qb = p.sb("qb", [128, C], BF16)
    rm = p.sb("rm", [128, C], F32)
    wq = p.sb("wq", [128, C], BF16)

    tiles = [(wgu, qgu, i, 2048) for i in range(16)] + [(wpr, qpr, i, 8192) for i in range(2)]

    for t, (src, dst, i, cols) in enumerate(tiles):
        def load(src=src, i=i, cols=cols):
            Y = p.Y
            Y.wait(p.V.chan.cur())
            Y.wait(slO.done())
            slI.dma(Y, w_t[:, :cols], src[i * 128:(i + 1) * 128, :])

        def proc(cols=cols):
            V = p.V
            ng = cols // 64
            V.wait(slI.done())
            V.op("tensor_copy", wbf[:, :cols], w_t[:, :cols])
            V.op("tensor_reduce", gs[:, :ng],
                 wbf[:, :cols].rearrange("p (g e) -> p g e", e=64),
                 axis=AX.X, op=ALU.add, apply_absolute_value=True)
            V.op("tensor_scalar", sc[:, :ng], gs[:, :ng], 1.0 / 64.0, 1e-8, ALU.mult, ALU.max)
            V.op("tensor_copy", scb[:, :ng], sc[:, :ng])
            V.op("tensor_copy", sc[:, :ng], scb[:, :ng])
            V.op("reciprocal", rc[:, :ng], sc[:, :ng])
            V.op("tensor_tensor", qt[:, :cols].rearrange("p (g e) -> p g e", e=64),
                 wbf[:, :cols].rearrange("p (g e) -> p g e", e=64),
                 rc[:, :ng].unsqueeze(2).broadcast_to([128, ng, 64]), ALU.mult)
            V.op("tensor_copy", qb[:, :cols], qt[:, :cols])
            V.op("tensor_scalar", rm[:, :cols], qb[:, :cols], MAGIC, -MAGIC, ALU.add, ALU.add)
            V.op("tensor_scalar", rm[:, :cols], rm[:, :cols], 1.0, -1.0, ALU.min, ALU.max)
            V.op("tensor_tensor", wq[:, :cols].rearrange("p (g e) -> p g e", e=64),
                 rm[:, :cols].rearrange("p (g e) -> p g e", e=64),
                 sc[:, :ng].unsqueeze(2).broadcast_to([128, ng, 64]), ALU.mult)

        def store(dst=dst, i=i, cols=cols):
            Y = p.Y
            Y.wait(p.V.chan.cur())
            slO.dma(Y, dst[i * 128:(i + 1) * 128, :], wq[:, :cols])

        p.sec("sync", load)
        p.sec("vector", proc)
        p.sec("sync", store)

    p.sec("sync", lambda: p.Y.wait(slO.done()))
    return p.fin()



def build_ffn(add_partials):
    p = Prog("ffn").start()
    nc = p.nc
    x = nc.dram_tensor("x", [TLOC, D], F32, kind="ExternalInput").ap()
    if add_partials:
        p1 = nc.dram_tensor("p1", [TLOC, D], BF16, kind="ExternalInput").ap()
        p2 = nc.dram_tensor("p2", [TLOC, D], BF16, kind="ExternalInput").ap()
    wguT = nc.dram_tensor("wguT", [2048, 16384], BF16, kind="ExternalInput").ap()
    wprT = nc.dram_tensor("wprT", [8192, 2048], BF16, kind="ExternalInput").ap()
    xo = nc.dram_tensor("xo", [TLOC, D], F32, kind="ExternalOutput").ap()
    xln_d = nc.dram_tensor("xln_d", [TLOC, D], BF16).ap()
    y_d = nc.dram_tensor("y_d", [D, TLOC], BF16).ap()
    xr_d = nc.dram_tensor("xr_d", [TLOC, D], F32).ap()

    slX = p.slot("slX")
    slP = p.slot("slP")
    slLn = p.slot("slLn")
    slLt = p.slot("slLt")
    slW = [p.slot("slW0"), p.slot("slW1")]
    slWp = [p.slot("slWp0"), p.slot("slWp1")]
    slY = p.slot("slY")
    slYt = p.slot("slYt")
    slO = p.slot("slO")
    slXr = p.slot("slXr")
    slXf = p.slot("slXf")

    xr = p.sb("xr", [128, D], F32)
    xr2 = p.sb("xr2", [128, D], F32)
    pt = p.sb("pt", [128, 2, D], BF16)
    xlt = p.sb("xlt", [128, D], BF16)
    sq = p.sb("sqt", [128, D], F32)
    st = p.sb("st", [128, 8], F32)
    xlnf = p.sb("xlnf", [128, 16, TLOC], BF16)
    wbuf = p.sb("wbuf", [128, 2, 2, 16, 128], BF16)
    wpbuf = p.sb("wpbuf", [128, 2, 32, 128], BF16)
    t1 = p.sb("t1", [128, 512], F32)
    hf = p.sb("hf", [128, 512], F32)
    h_all = p.sb("h_all", [128, 64, 512], BF16)
    sqb = p.sb("sqb", [128, 512], BF16)
    rrow = p.sb("rrow", [1, 512], F32)
    rrowb = p.sb("rrowb", [1, 512], BF16)
    rb = p.sb("rb", [128, 512], F32)
    yt = p.sb("yt", [128, 512], BF16)
    ytm = p.sb("ytm", [128, D], BF16)
    xot = p.sb("xot", [128, D], F32)
    ones1 = p.sb("ones1", [1, 128], BF16)
    ones128 = p.sb("ones128", [128, 1], BF16)
    epsln = p.sb("epsln", [128, 1], F32)
    epsrms = p.sb("epsrms", [128, 1], F32)

    psG = [p.ps("psG0", [128, 512]), p.ps("psG1", [128, 512])]
    psU = [p.ps("psU0", [128, 512]), p.ps("psU1", [128, 512])]
    psH = p.ps("psH", [1, 512])
    psR = p.ps("psR", [128, 512])
    psP = [p.ps("psP0", [128, 512]), p.ps("psP1", [128, 512])]

    def init():
        G = p.G
        G.op("memset", ones1[:, :], 1.0)
        G.op("memset", ones128[:, :], 1.0)
        G.op("memset", epsln[:, :], LN_EPS)
        G.op("memset", epsrms[:, :], RMS_EPS)
        p.ev["init"] = p.G.chan.cur()

    p.sec("gpsimd", init)

    for i in range(8):
        def loadx(i=i):
            Y = p.Y
            Y.wait(p.ev.get(("lnv3", i - 1)))
            Y.wait(slXr.done())
            slX.dma(Y, xr[:, :], x[i * 128:(i + 1) * 128, :])
            if add_partials:
                slP.dma(Y, pt[:, 0, :], p1[i * 128:(i + 1) * 128, :])
                slP.dma(Y, pt[:, 1, :], p2[i * 128:(i + 1) * 128, :])

        def v1(i=i):
            V = p.V
            V.wait(slX.done())
            if add_partials:
                V.wait(slP.done())
                V.op("tensor_add", xr[:, :], xr[:, :], pt[:, 0, :])
                V.op("tensor_add", xr[:, :], xr[:, :], pt[:, 1, :])
            V.op("reduce_sum", st[:, 0:1], xr[:, :], axis=AX.X)
            p.ev[("lnv1", i)] = V.chan.cur()

        def stxr(i=i):
            Y = p.Y
            Y.wait(p.ev[("lnv1", i)])
            slXr.dma(Y, xr_d[i * 128:(i + 1) * 128, :], xr[:, :])

        def s1(i=i):
            S = p.S
            S.wait(p.ev[("lnv1", i)])
            S.op("activation", sq[:, :], xr[:, :], AF.Square)
            p.ev[("lns1", i)] = S.chan.cur()

        def v2(i=i):
            V = p.V
            V.wait(p.ev[("lns1", i)])
            V.op("reduce_sum", st[:, 1:2], sq[:, :], axis=AX.X)
            V.op("tensor_scalar", st[:, 0:1], st[:, 0:1], 1.0 / D, None, ALU.mult)
            V.op("tensor_scalar", st[:, 1:2], st[:, 1:2], 1.0 / D, None, ALU.mult)
            V.op("tensor_tensor", st[:, 2:3], st[:, 0:1], st[:, 0:1], ALU.mult)
            V.op("tensor_sub", st[:, 1:2], st[:, 1:2], st[:, 2:3])
            p.ev[("lnv2", i)] = V.chan.cur()

        def s2(i=i):
            S = p.S
            S.wait(p.ev[("lnv2", i)])
            S.wait(p.ev["init"])
            S.op("activation", st[:, 1:2], st[:, 1:2], AF.Ln, bias=epsln[:, 0:1])
            S.op("activation", st[:, 1:2], st[:, 1:2], AF.Exp, scale=-0.5)
            p.ev[("lns2", i)] = S.chan.cur()

        def v3(i=i):
            V = p.V
            V.wait(p.ev[("lns2", i)])
            V.wait(slLn.done())
            V.op("tensor_tensor", st[:, 3:4], st[:, 0:1], st[:, 1:2], ALU.mult)
            V.op("tensor_scalar", st[:, 3:4], st[:, 3:4], -1.0, None, ALU.mult)
            V.op("tensor_scalar", xlt[:, :], xr[:, :], st[:, 1:2], st[:, 3:4],
                 ALU.mult, ALU.add)
            p.ev[("lnv3", i)] = V.chan.cur()

        def stln(i=i):
            Y = p.Y
            Y.wait(p.ev[("lnv3", i)])
            slLn.dma(Y, xln_d[i * 128:(i + 1) * 128, :], xlt[:, :])
            p.ev[("xlt_st", i)] = slLn.done()

        p.sec("sync", loadx)
        p.sec("vector", v1)
        p.sec("sync", stxr)
        p.sec("scalar", s1)
        p.sec("vector", v2)
        p.sec("scalar", s2)
        p.sec("vector", v3)
        p.sec("sync", stln)

    def ldlnf():
        Y = p.Y
        Y.wait(slLn.done())
        for c in range(16):
            slLt.dma_t(Y, xlnf[:, c, :], xln_d[:, c * 128:(c + 1) * 128])

    p.sec("sync", ldlnf)
    xln_ready = slLt.done()

    NH = 64
    for tb in range(2):
        tsl = slice(tb * 512, (tb + 1) * 512)

        def w_pref(tb=tb):
            Y = p.Y
            if tb == 0:
                Y.wait(p.ev.get(("lnv1", 7)))
            if tb == 1:
                Y.wait(p.ev[(0, "gu_mm", NH - 1)])
            src_g = wguT[:, 0:128].rearrange("(kc p) c -> p kc c", p=128)
            src_u = wguT[:, 8192:8192 + 128].rearrange("(kc p) c -> p kc c", p=128)
            slW[0].dma(Y, wbuf[:, 0, 0, :, :], src_g)
            slW[0].dma(Y, wbuf[:, 0, 1, :, :], src_u)

        p.sec("sync", w_pref)

        for hc in range(NH):
            par = hc % 2

            def w_next(hc=hc, tb=tb):
                Y = p.Y
                if hc + 1 < NH:
                    npar = (hc + 1) % 2
                    Y.wait(p.ev.get((tb, "gu_mm", hc - 1)))
                    c0 = (hc + 1) * 128
                    src_g = wguT[:, c0:c0 + 128].rearrange("(kc p) c -> p kc c", p=128)
                    src_u = wguT[:, 8192 + c0:8192 + c0 + 128].rearrange(
                        "(kc p) c -> p kc c", p=128)
                    slW[npar].dma(Y, wbuf[:, npar, 0, :, :], src_g)
                    slW[npar].dma(Y, wbuf[:, npar, 1, :, :], src_u)

            def gu_mm(hc=hc, par=par, tb=tb):
                T = p.T
                T.wait(slW[par].done())
                T.wait(xln_ready)
                T.wait(p.ev.get((tb, "gu_v", hc - 2)))
                T.wait(p.ev.get((tb, "gu_s1", hc - 2)))
                if tb == 1 and hc <= 1:
                    T.wait(p.ev.get((0, "pj_mm", 15)))
                    T.wait(p.ev.get((0, "rms_read")))
                if hc > 0:
                    T.wait(p.ev[(tb, "gu_s2", hc - 1)])
                    T.raw("matmul", psH[0:1, :], ones128[:, 0:1], sqb[:, :],
                          start=(hc - 1 == 0), stop=(hc - 1 == NH - 1))
                for kc in range(16):
                    T.raw("matmul", psG[par][:, :], wbuf[:, par, 0, kc, :],
                          xlnf[:, kc, tsl], start=(kc == 0), stop=(kc == 15))
                for kc in range(16):
                    mm = T.raw("matmul", psU[par][:, :], wbuf[:, par, 1, kc, :],
                               xlnf[:, kc, tsl], start=(kc == 0), stop=(kc == 15))
                p.ev[(tb, "gu_mm", hc)] = p.T.chan.ev(mm)

            def gu_s1(hc=hc, par=par, tb=tb):
                S = p.S
                S.wait(p.ev[(tb, "gu_mm", hc)])
                S.op("activation", t1[:, :], psG[par][:, :], AF.Silu)
                p.ev[(tb, "gu_s1", hc)] = S.chan.cur()

            def gu_v(hc=hc, par=par, tb=tb):
                V = p.V
                V.wait(p.ev[(tb, "gu_s1", hc)])
                if tb == 1 and hc == 0:
                    V.wait(p.ev.get((0, "pj_mm", 15)))
                V.op("tensor_tensor", hf[:, :], t1[:, :], psU[par][:, :], ALU.mult)
                V.op("tensor_copy", h_all[:, hc, :], hf[:, :])
                p.ev[(tb, "gu_v", hc)] = V.chan.cur()

            def gu_s2(hc=hc, tb=tb):
                S = p.S
                S.wait(p.ev[(tb, "gu_v", hc)])
                if hc > 0:
                    S.wait(p.ev[(tb, "gu_mm", hc)])
                S.op("activation", sqb[:, :], hf[:, :], AF.Square)
                p.ev[(tb, "gu_s2", hc)] = S.chan.cur()

            p.sec("sync", w_next)
            p.sec("tensor", gu_mm)
            p.sec("scalar", gu_s1)
            p.sec("vector", gu_v)
            p.sec("scalar", gu_s2)

        def psh_last(tb=tb):
            T = p.T
            T.wait(p.ev[(tb, "gu_s2", NH - 1)])
            mm = T.raw("matmul", psH[0:1, :], ones128[:, 0:1], sqb[:, :],
                       start=False, stop=True)
            p.ev[(tb, "psh")] = p.T.chan.ev(mm)

        p.sec("tensor", psh_last)

        def rms_v1(tb=tb):
            V = p.V
            V.wait(p.ev[(tb, "psh")])
            V.op("tensor_scalar", rrow[0:1, :], psH[0:1, :], 1.0 / 8192.0, None, ALU.mult)
            p.ev[(tb, "rms_read")] = V.chan.cur()

        def rms_s(tb=tb):
            S = p.S
            S.wait(p.ev[(tb, "rms_read")])
            S.op("activation", rrow[0:1, :], rrow[0:1, :], AF.Ln, bias=epsrms[0:1, 0:1])
            S.op("activation", rrowb[0:1, :], rrow[0:1, :], AF.Exp, scale=-0.5)
            p.ev[(tb, "rms_s")] = S.chan.cur()

        def rms_t(tb=tb):
            T = p.T
            T.wait(p.ev[(tb, "rms_s")])
            T.wait(p.ev.get((tb - 1, "rb")))
            mm = T.raw("matmul", psR[:, :], ones1[0:1, :], rrowb[0:1, :], start=True, stop=True)
            p.ev[(tb, "rms_bc")] = p.T.chan.ev(mm)

        def rms_v2(tb=tb):
            V = p.V
            V.wait(p.ev[(tb, "rms_bc")])
            V.op("tensor_copy", rb[:, :], psR[:, :])
            p.ev[(tb, "rb")] = V.chan.cur()

        p.sec("vector", rms_v1)
        p.sec("scalar", rms_s)
        p.sec("tensor", rms_t)
        p.sec("vector", rms_v2)

        def wp_pref(tb=tb):
            Y = p.Y
            if tb == 1:
                Y.wait(p.ev[(0, "pj_mm", 15)])
            src = wprT[0:4096, 0:128].rearrange("(hc p) c -> p hc c", p=128)
            slWp[0].dma(Y, wpbuf[:, 0, :, :], src)

        p.sec("sync", wp_pref)

        for oc in range(16):
            par = oc % 2

            def wp_next(oc=oc, tb=tb):
                Y = p.Y
                if oc + 1 < 16:
                    npar = (oc + 1) % 2
                    Y.wait(p.ev.get((tb, "pj_mm", oc - 1)))
                    c0 = (oc + 1) * 128
                    src = wprT[0:4096, c0:c0 + 128].rearrange("(hc p) c -> p hc c", p=128)
                    slWp[npar].dma(Y, wpbuf[:, npar, :, :], src)

            def pj_h1(oc=oc, par=par, tb=tb):
                T = p.T
                T.wait(slWp[par].done())
                T.wait(p.ev[(tb, "gu_v", NH - 1)])
                T.wait(p.ev.get((tb, "pj_v", oc - 2)))
                for hc in range(32):
                    mm = T.raw("matmul", psP[par][:, :], wpbuf[:, par, hc, :],
                               h_all[:, hc, :], start=(hc == 0), stop=False)
                p.ev[(tb, "pj_h1", oc)] = p.T.chan.ev(mm)

            def wp_half2(oc=oc, par=par, tb=tb):
                Y = p.Y
                Y.wait(p.ev[(tb, "pj_h1", oc)])
                c0 = oc * 128
                src = wprT[4096:8192, c0:c0 + 128].rearrange("(hc p) c -> p hc c", p=128)
                slWp[par].dma(Y, wpbuf[:, par, :, :], src)

            def pj_mm(oc=oc, par=par, tb=tb):
                T = p.T
                T.wait(slWp[par].done())
                for hc in range(32):
                    mm = T.raw("matmul", psP[par][:, :], wpbuf[:, par, hc, :],
                               h_all[:, 32 + hc, :], start=False, stop=(hc == 31))
                p.ev[(tb, "pj_mm", oc)] = p.T.chan.ev(mm)

            def pj_v(oc=oc, par=par, tb=tb):
                V = p.V
                V.wait(p.ev[(tb, "pj_mm", oc)])
                V.wait(p.ev[(tb, "rb")])
                V.wait(slY.done())
                V.op("tensor_tensor", yt[:, :], psP[par][:, :], rb[:, :], ALU.mult)
                p.ev[(tb, "pj_v", oc)] = V.chan.cur()

            def pj_st(oc=oc, tb=tb):
                Y = p.Y
                Y.wait(p.ev[(tb, "pj_v", oc)])
                slY.dma(Y, y_d[oc * 128:(oc + 1) * 128, tb * 512:(tb + 1) * 512], yt[:, :])

            p.sec("tensor", pj_h1)
            p.sec("sync", wp_half2)
            p.sec("tensor", pj_mm)
            p.sec("sync", wp_next)
            p.sec("vector", pj_v)
            p.sec("sync", pj_st)

    for i in range(8):
        def ldy(i=i):
            Y = p.Y
            Y.wait(slY.done())
            Y.wait(slXr.done())
            Y.wait(p.ev.get(("res", i - 1)))
            slYt.dma_t(Y, ytm[:, :], y_d[:, i * 128:(i + 1) * 128])
            slXf.dma(Y, xr2[:, :], xr_d[i * 128:(i + 1) * 128, :])

        def addres(i=i):
            V = p.V
            V.wait(slYt.done())
            V.wait(slXf.done())
            V.wait(slO.done())
            V.op("tensor_add", xot[:, :], xr2[:, :], ytm[:, :])
            p.ev[("res", i)] = V.chan.cur()

        def sto(i=i):
            Y = p.Y
            Y.wait(p.ev[("res", i)])
            slO.dma(Y, xo[i * 128:(i + 1) * 128, :], xot[:, :])

        p.sec("sync", ldy)
        p.sec("vector", addres)
        p.sec("sync", sto)

    p.sec("sync", lambda: p.Y.wait(slO.done()))
    return p.fin()





def build_moe():
    from contextlib import ExitStack
    p = Prog("moe").start()
    nc = p.nc
    x = nc.dram_tensor("x", [TLOC, D], F32, kind="ExternalInput").ap()
    gwT = nc.dram_tensor("gwT", [D, 8], F32, kind="ExternalInput").ap()
    guT = nc.dram_tensor("guT", [8, D, 2816], F32, kind="ExternalInput").ap()
    dnT = nc.dram_tensor("dnT", [8, 1408, D], F32, kind="ExternalInput").ap()
    sguT = nc.dram_tensor("sguT", [D, 5632], F32, kind="ExternalInput").ap()
    sdnT = nc.dram_tensor("sdnT", [2816, D], F32, kind="ExternalInput").ap()
    sel8 = nc.dram_tensor("sel8", [8, 8 * 128], F32, kind="ExternalInput").ap()
    ident = nc.dram_tensor("ident", [128, 128], F32, kind="ExternalInput").ap()
    xo = nc.dram_tensor("xo", [TLOC, D], F32, kind="ExternalOutput").ap()
    xln_d = nc.dram_tensor("xln_d", [TLOC, D], BF16).ap()
    y_d = nc.dram_tensor("y_d", [D, TLOC], BF16).ap()
    xr_d = nc.dram_tensor("xr_d", [TLOC, D], F32).ap()

    slX = p.slot("slX")
    slLn = p.slot("slLn")
    slLt = p.slot("slLt")
    slXr = p.slot("slXr")
    slXf = p.slot("slXf")
    slC = p.slot("slC")
    slW = [p.slot("slW0"), p.slot("slW1")]
    slD = [p.slot("slD0"), p.slot("slD1")]
    slY = p.slot("slY")
    slYt = p.slot("slYt")
    slO = p.slot("slO")

    xlnf = p.sb("xlnf", [128, 16, TLOC], BF16)
    gwb = p.sb("gwb", [128, 16, 8], BF16)
    selb = p.sb("selb", [8, 8 * 128], BF16)
    idb = p.sb("idb", [128, 128], BF16)
    cmb8 = p.sb("cmb8", [8, TLOC], BF16)
    cb_all = p.sb("cb_all", [128, 8, TLOC], BF16)
    yac = p.sb("yac", [128, 16, TLOC], F32)
    epsln = p.sb("epsln2", [128, 1], F32)
    psC = p.ps("psC", [128, 512])

    def init():
        G = p.G
        G.op("memset", epsln[:, :], LN_EPS)
        G.op("memset", yac[:, :, :], 0.0)
        slC.dma(G, gwb[:, :, :], gwT.rearrange("(kc p) e -> p kc e", p=128))
        slC.dma(G, selb[:, :], sel8[:, :])
        slC.dma(G, idb[:, :], ident[:, :])
        p.ev["init"] = p.G.chan.cur()

    p.sec("gpsimd", init)

    esA = ExitStack()
    xr = esA.enter_context(nc.sbuf_tensor("xr", [128, D], F32))
    sq = esA.enter_context(nc.sbuf_tensor("sqt", [128, D], F32))
    xlt = esA.enter_context(nc.sbuf_tensor("xlt", [128, D], BF16))
    st = esA.enter_context(nc.sbuf_tensor("st", [128, 8], F32))

    for i in range(8):
        def loadx(i=i):
            Y = p.Y
            Y.wait(p.ev.get(("lnv3", i - 1)))
            Y.wait(slXr.done())
            slX.dma(Y, xr[:, :], x[i * 128:(i + 1) * 128, :])

        def v1(i=i):
            V = p.V
            V.wait(slX.done())
            V.op("reduce_sum", st[:, 0:1], xr[:, :], axis=AX.X)
            p.ev[("lnv1", i)] = V.chan.cur()

        def stxr(i=i):
            Y = p.Y
            Y.wait(p.ev[("lnv1", i)])
            slXr.dma(Y, xr_d[i * 128:(i + 1) * 128, :], xr[:, :])

        def s1(i=i):
            S = p.S
            S.wait(p.ev[("lnv1", i)])
            S.op("activation", sq[:, :], xr[:, :], AF.Square)
            p.ev[("lns1", i)] = S.chan.cur()

        def v2(i=i):
            V = p.V
            V.wait(p.ev[("lns1", i)])
            V.op("reduce_sum", st[:, 1:2], sq[:, :], axis=AX.X)
            V.op("tensor_scalar", st[:, 0:1], st[:, 0:1], 1.0 / D, None, ALU.mult)
            V.op("tensor_scalar", st[:, 1:2], st[:, 1:2], 1.0 / D, None, ALU.mult)
            V.op("tensor_tensor", st[:, 2:3], st[:, 0:1], st[:, 0:1], ALU.mult)
            V.op("tensor_sub", st[:, 1:2], st[:, 1:2], st[:, 2:3])
            p.ev[("lnv2", i)] = V.chan.cur()

        def s2(i=i):
            S = p.S
            S.wait(p.ev[("lnv2", i)])
            S.wait(p.ev["init"])
            S.op("activation", st[:, 1:2], st[:, 1:2], AF.Ln, bias=epsln[:, 0:1])
            S.op("activation", st[:, 1:2], st[:, 1:2], AF.Exp, scale=-0.5)
            p.ev[("lns2", i)] = S.chan.cur()

        def v3(i=i):
            V = p.V
            V.wait(p.ev[("lns2", i)])
            V.wait(slLn.done())
            V.op("tensor_tensor", st[:, 3:4], st[:, 0:1], st[:, 1:2], ALU.mult)
            V.op("tensor_scalar", st[:, 3:4], st[:, 3:4], -1.0, None, ALU.mult)
            V.op("tensor_scalar", xlt[:, :], xr[:, :], st[:, 1:2], st[:, 3:4],
                 ALU.mult, ALU.add)
            p.ev[("lnv3", i)] = V.chan.cur()

        def stln(i=i):
            Y = p.Y
            Y.wait(p.ev[("lnv3", i)])
            slLn.dma(Y, xln_d[i * 128:(i + 1) * 128, :], xlt[:, :])

        p.sec("sync", loadx)
        p.sec("vector", v1)
        p.sec("sync", stxr)
        p.sec("scalar", s1)
        p.sec("vector", v2)
        p.sec("scalar", s2)
        p.sec("vector", v3)
        p.sec("sync", stln)

    def ldlnf():
        Y = p.Y
        Y.wait(slLn.done())
        for c in range(16):
            slLt.dma_t(Y, xlnf[:, c, :], xln_d[:, c * 128:(c + 1) * 128])

    p.sec("sync", ldlnf)
    xln_ready = slLt.done()

    lg = esA.enter_context(nc.sbuf_tensor("lg", [128, 8], F32))
    m1 = esA.enter_context(nc.sbuf_tensor("m1", [128, 4], F32))
    eqm = esA.enter_context(nc.sbuf_tensor("eqm", [128, 8], F32))
    smx = esA.enter_context(nc.sbuf_tensor("smx", [128, 8], F32))
    cmbt = esA.enter_context(nc.sbuf_tensor("cmbt", [128, 8], BF16))
    esP = ExitStack()
    psL = esP.enter_context(nc.psum_tensor("psL", [128, 8], F32))
    psT8 = esP.enter_context(nc.psum_tensor("psT8", [128, 128], BF16))

    for tc in range(8):
        def r_mm(tc=tc):
            T = p.T
            T.wait(xln_ready)
            T.wait(slC.done())
            T.wait(p.ev.get(("r_v1", tc - 1)))
            mm = None
            for kc in range(16):
                mm = T.raw("matmul", psL[:, :], xlnf[:, kc, tc * 128:(tc + 1) * 128],
                           gwb[:, kc, :], start=(kc == 0), stop=(kc == 15))
            p.ev[("r_mm", tc)] = p.T.chan.ev(mm)

        def r_v(tc=tc):
            V = p.V
            V.wait(p.ev[("r_mm", tc)])
            V.op("tensor_copy", lg[:, :], psL[:, :])
            V.op("reduce_max", m1[:, 0:1], lg[:, :], axis=AX.X)
            V.op("tensor_scalar", eqm[:, :], lg[:, :], m1[:, 0:1], None, ALU.is_ge)
            V.op("tensor_scalar", eqm[:, :], eqm[:, :], -1e30, None, ALU.mult)
            V.op("tensor_add", eqm[:, :], eqm[:, :], lg[:, :])
            V.op("reduce_max", m1[:, 1:2], eqm[:, :], axis=AX.X)
            V.op("tensor_scalar", eqm[:, :], lg[:, :], m1[:, 1:2], None, ALU.is_ge)
            V.op("tensor_scalar", m1[:, 2:3], m1[:, 0:1], -1.0, None, ALU.mult)
            p.ev[("r_v1", tc)] = V.chan.cur()

        def r_s(tc=tc):
            S = p.S
            S.wait(p.ev[("r_v1", tc)])
            S.op("activation", smx[:, :], lg[:, :], AF.Exp, bias=m1[:, 2:3],
                 accum_out=m1[:, 3:4])
            p.ev[("r_s", tc)] = S.chan.cur()

        def r_v2(tc=tc):
            V = p.V
            V.wait(p.ev[("r_s", tc)])
            V.op("reciprocal", m1[:, 3:4], m1[:, 3:4])
            V.op("tensor_mul", smx[:, :], smx[:, :], eqm[:, :])
            V.op("tensor_scalar", cmbt[:, :], smx[:, :], m1[:, 3:4], None, ALU.mult)
            p.ev[("r_v2", tc)] = V.chan.cur()

        def r_t(tc=tc):
            T = p.T
            T.wait(p.ev[("r_v2", tc)])
            T.wait(p.ev.get(("r_v3", tc - 1)))
            tr = T.raw("transpose", psT8[0:8, :], cmbt[:, :], idb[:, :])
            p.ev[("r_t", tc)] = p.T.chan.ev(tr)

        def r_v3(tc=tc):
            V = p.V
            V.wait(p.ev[("r_t", tc)])
            V.op("tensor_copy", cmb8[:, tc * 128:(tc + 1) * 128], psT8[0:8, 0:128])
            p.ev[("r_v3", tc)] = V.chan.cur()

        p.sec("tensor", r_mm)
        p.sec("vector", r_v)
        p.sec("scalar", r_s)
        p.sec("vector", r_v2)
        p.sec("tensor", r_t)
        p.sec("vector", r_v3)

    for e in range(8):
        for tb in range(2):
            def cb_t(e=e, tb=tb):
                T = p.T
                T.wait(p.ev[("r_v3", 7)])
                T.wait(p.ev.get(("cb_ve", e, tb - 1)) if tb > 0
                       else p.ev.get(("cb_ve", e - 1, 1)))
                mm = T.raw("matmul", psC[:, :], selb[:, e * 128:(e + 1) * 128],
                           cmb8[:, tb * 512:(tb + 1) * 512], start=True, stop=True)
                p.ev[("cb_te", e, tb)] = p.T.chan.ev(mm)

            def cb_v(e=e, tb=tb):
                V = p.V
                V.wait(p.ev[("cb_te", e, tb)])
                V.op("tensor_copy", cb_all[:, e, tb * 512:(tb + 1) * 512], psC[:, :])
                p.ev[("cb_ve", e, tb)] = V.chan.cur()
                if tb == 1:
                    p.ev[("cb_v", e)] = V.chan.cur()

            p.sec("tensor", cb_t)
            p.sec("vector", cb_v)

    esA.close()
    esP.close()

    esW = ExitStack()
    wg = esW.enter_context(nc.sbuf_tensor("wg", [128, 2, 16, 128], BF16))
    wu = esW.enter_context(nc.sbuf_tensor("wu", [128, 2, 16, 128], BF16))
    wd = esW.enter_context(nc.sbuf_tensor("wd", [128, 2, 11, 128], BF16))
    t1 = esW.enter_context(nc.sbuf_tensor("t1", [128, 512], F32))
    hf = esW.enter_context(nc.sbuf_tensor("hf", [128, 512], F32))
    act = esW.enter_context(nc.sbuf_tensor("act", [128, 22, TLOC], BF16))

    psG = [p.ps("psG0", [128, 512]), p.ps("psG1", [128, 512])]
    psU = [p.ps("psU0", [128, 512]), p.ps("psU1", [128, 512])]
    psP = [p.ps("psP0", [128, 512]), p.ps("psP1", [128, 512])]

    def gu_src(e, fc, gate):
        if e < 8:
            off = (0 if gate else 1408) + fc * 128
            return guT[e, :, off:off + 128].rearrange("(kc p) c -> p kc c", p=128)
        off = (0 if gate else 2816) + fc * 128
        return sguT[:, off:off + 128].rearrange("(kc p) c -> p kc c", p=128)

    def dn_src(e, oc, half):
        if e < 8:
            return dnT[e, :, oc * 128:(oc + 1) * 128].rearrange("(fc p) c -> p fc c", p=128)
        r0 = half * 1408
        return sdnT[r0:r0 + 1408, oc * 128:(oc + 1) * 128].rearrange("(fc p) c -> p fc c", p=128)

    wpi = 0
    for e in range(9):
        NFC = 11 if e < 8 else 22
        for fc in range(NFC):
            par = wpi % 2
            wpi += 1

            def gu_load(e=e, fc=fc, par=par):
                G = p.G
                G.wait(p.ev.get(("gu_mm2", e, fc - 2)))
                if e > 0 and fc < 2:
                    pNFC = 11 if e - 1 < 8 else 22
                    G.wait(p.ev.get(("gu_mm2", e - 1, pNFC - 2 + fc)))
                if e == 0 and fc == 0:
                    G.wait(slXr.done())
                    G.wait(slLn.done())
                slW[par].dma(G, wg[:, par, :, :], gu_src(e, fc, True))
                slW[par].dma(G, wu[:, par, :, :], gu_src(e, fc, False))

            p.sec("gpsimd", gu_load)

            for tb in range(2):
                def gu_mm(e=e, fc=fc, par=par, tb=tb):
                    T = p.T
                    T.wait(slW[par].done())
                    T.wait(xln_ready)
                    T.wait(p.ev.get(("act_v", e, fc - 1, tb)))
                    mm = None
                    for kc in range(16):
                        T.raw("matmul", psG[tb][:, :], wg[:, par, kc, :],
                              xlnf[:, kc, tb * 512:(tb + 1) * 512],
                              start=(kc == 0), stop=(kc == 15))
                        mm = T.raw("matmul", psU[tb][:, :], wu[:, par, kc, :],
                                   xlnf[:, kc, tb * 512:(tb + 1) * 512],
                                   start=(kc == 0), stop=(kc == 15))
                    p.ev[("gu_mm", e, fc, tb)] = p.T.chan.ev(mm)
                    if tb == 1:
                        p.ev[("gu_mm2", e, fc)] = p.ev[("gu_mm", e, fc, tb)]

                def act_s(e=e, fc=fc, tb=tb):
                    S = p.S
                    S.wait(p.ev[("gu_mm", e, fc, tb)])
                    S.wait(p.V.chan.cur())
                    S.op("activation", t1[:, :], psG[tb][:, :], AF.Silu)
                    p.ev[("act_s", e, fc, tb)] = S.chan.cur()

                def act_v(e=e, fc=fc, tb=tb):
                    V = p.V
                    V.wait(p.ev[("act_s", e, fc, tb)])
                    V.op("tensor_tensor", hf[:, :], t1[:, :], psU[tb][:, :], ALU.mult)
                    if e < 8:
                        V.wait(p.ev[("cb_v", e)])
                        V.op("tensor_tensor", act[:, fc, tb * 512:(tb + 1) * 512],
                             hf[:, :], cb_all[:, e, tb * 512:(tb + 1) * 512], ALU.mult)
                    else:
                        V.op("tensor_copy", act[:, fc, tb * 512:(tb + 1) * 512], hf[:, :])
                    p.ev[("act_v", e, fc, tb)] = V.chan.cur()

                p.sec("tensor", gu_mm)
                p.sec("scalar", act_s)
                p.sec("vector", act_v)

        NDC = 11 if e < 8 else 22
        for oc in range(16):
            dpar = oc % 2

            def dn_load(e=e, oc=oc, dpar=dpar):
                G = p.G
                G.wait(p.ev.get(("dn_mm", e, oc - 2, 1)))
                if e > 0 and oc < 2:
                    G.wait(p.ev.get(("dn_mm", e - 1, 14 + oc, 1)))
                slD[dpar].dma(G, wd[:, dpar, :, :], dn_src(e, oc, 0))

            p.sec("gpsimd", dn_load)

            for tb in range(2):
                def dn_mm(e=e, oc=oc, dpar=dpar, tb=tb, NDC=NDC):
                    T = p.T
                    T.wait(slD[dpar].done())
                    T.wait(p.ev[("act_v", e, (11 if e < 8 else 22) - 1, 1)])
                    T.wait(p.ev.get(("dn_v", e, oc - 1, tb)))
                    mm = None
                    for fc in range(11):
                        mm = T.raw("matmul", psP[tb][:, :], wd[:, dpar, fc, :],
                                   act[:, fc, tb * 512:(tb + 1) * 512],
                                   start=(fc == 0), stop=(fc == 10 and NDC == 11))
                    p.ev[("dn_mm_h1", e, oc, tb)] = p.T.chan.ev(mm)
                    p.ev[("dn_mm", e, oc, tb)] = p.ev[("dn_mm_h1", e, oc, tb)]

                p.sec("tensor", dn_mm)

            if NDC == 22:
                def dn_load2(e=e, oc=oc, dpar=dpar):
                    G = p.G
                    G.wait(p.ev[("dn_mm_h1", e, oc, 1)])
                    slD[dpar].dma(G, wd[:, dpar, :, :], dn_src(e, oc, 1))

                p.sec("gpsimd", dn_load2)

                for tb in range(2):
                    def dn_mm2(e=e, oc=oc, dpar=dpar, tb=tb):
                        T = p.T
                        T.wait(slD[dpar].done())
                        mm = None
                        for fc in range(11):
                            mm = T.raw("matmul", psP[tb][:, :], wd[:, dpar, fc, :],
                                       act[:, 11 + fc, tb * 512:(tb + 1) * 512],
                                       start=False, stop=(fc == 10))
                        p.ev[("dn_mm", e, oc, tb)] = p.T.chan.ev(mm)

                    p.sec("tensor", dn_mm2)

            for tb in range(2):
                def dn_v(e=e, oc=oc, tb=tb):
                    V = p.V
                    V.wait(p.ev[("dn_mm", e, oc, tb)])
                    V.op("tensor_tensor", yac[:, oc, tb * 512:(tb + 1) * 512],
                         yac[:, oc, tb * 512:(tb + 1) * 512], psP[tb][:, :], ALU.add)
                    p.ev[("dn_v", e, oc, tb)] = V.chan.cur()

                p.sec("vector", dn_v)

    esW.close()
    esF = ExitStack()
    ybf = esF.enter_context(nc.sbuf_tensor("ybf", [128, TLOC], BF16))
    ytm = esF.enter_context(nc.sbuf_tensor("ytm", [128, D], BF16))
    xr2 = esF.enter_context(nc.sbuf_tensor("xr2", [128, D], F32))
    xot = esF.enter_context(nc.sbuf_tensor("xot", [128, D], F32))

    for oc in range(16):
        def ycast(oc=oc):
            V = p.V
            V.wait(p.ev[("dn_v", 8, oc, 1)])
            V.wait(slY.done())
            V.op("tensor_copy", ybf[:, :], yac[:, oc, :])
            p.ev[("yc", oc)] = V.chan.cur()

        def yst(oc=oc):
            Y = p.Y
            Y.wait(p.ev[("yc", oc)])
            slY.dma(Y, y_d[oc * 128:(oc + 1) * 128, :], ybf[:, :])

        p.sec("vector", ycast)
        p.sec("sync", yst)

    for i in range(8):
        def ldy(i=i):
            Y = p.Y
            Y.wait(slY.done())
            Y.wait(p.ev.get(("res", i - 1)))
            slYt.dma_t(Y, ytm[:, :], y_d[:, i * 128:(i + 1) * 128])
            slXf.dma(Y, xr2[:, :], xr_d[i * 128:(i + 1) * 128, :])

        def addres(i=i):
            V = p.V
            V.wait(slYt.done())
            V.wait(slXf.done())
            V.wait(slO.done())
            V.op("tensor_add", xot[:, :], xr2[:, :], ytm[:, :])
            p.ev[("res", i)] = V.chan.cur()

        def sto(i=i):
            Y = p.Y
            Y.wait(p.ev[("res", i)])
            slO.dma(Y, xo[i * 128:(i + 1) * 128, :], xot[:, :])

        p.sec("sync", ldy)
        p.sec("vector", addres)
        p.sec("sync", sto)

    p.sec("sync", lambda: p.Y.wait(slO.done()))
    esF.close()
    return p.fin()





def build_attn_b0(half):
    from contextlib import ExitStack
    p = Prog("attnb0").start()
    nc = p.nc
    SEQ = 2048
    xT = nc.dram_tensor("xT", [D, SEQ], F32, kind="ExternalInput").ap()
    xh = nc.dram_tensor("xh", [TLOC, D], F32, kind="ExternalInput").ap()
    wqT = nc.dram_tensor("wqT", [D, D], F32, kind="ExternalInput").ap()
    wkT = nc.dram_tensor("wkT", [D, D], F32, kind="ExternalInput").ap()
    wvT = nc.dram_tensor("wvT", [D, D], F32, kind="ExternalInput").ap()
    woT = nc.dram_tensor("woT", [D, D], F32, kind="ExternalInput").ap()
    bq = nc.dram_tensor("bq", [1, D], F32, kind="ExternalInput").ap()
    bk = nc.dram_tensor("bk", [1, D], F32, kind="ExternalInput").ap()
    bv = nc.dram_tensor("bv", [1, D], F32, kind="ExternalInput").ap()
    bo = nc.dram_tensor("bo", [1, D], F32, kind="ExternalInput").ap()
    cosT = nc.dram_tensor("cosT", [128, SEQ], F32, kind="ExternalInput").ap()
    sinT = nc.dram_tensor("sinT", [128, SEQ], F32, kind="ExternalInput").ap()
    bmask = nc.dram_tensor("bmask", [128, 1024], F32, kind="ExternalInput").ap()
    xo = nc.dram_tensor("xo", [TLOC, D], F32, kind="ExternalOutput").ap()
    v_d = nc.dram_tensor("v_d", [SEQ, D], BF16).ap()
    o_d = nc.dram_tensor("o_d", [D, TLOC], BF16).ap()

    NHP = 16
    Q0 = half * TLOC

    slX = p.slot("slX")
    slW = [p.slot("slW0"), p.slot("slW1")]
    slB = p.slot("slB")
    slR = p.slot("slR")
    slV = [p.slot("slV0"), p.slot("slV1")]
    slVs = p.slot("slVs")
    slOd = p.slot("slOd")
    slOb = [p.slot("slOb0"), p.slot("slOb1")]
    slWo = p.slot("slWo")
    slH = p.slot("slH")
    slO = p.slot("slO")

    kf = p.sb("kf", [128, 16, SEQ], BF16)
    qf = p.sb("qf", [128, 16, TLOC], BF16)
    xc = p.sb("xc", [128, SEQ], F32)
    ones1 = p.sb("ones1", [1, 512], BF16)
    ones128 = p.sb("ones128", [128, 1], BF16)
    ones128f = p.sb("ones128f", [128, 1], F32)
    oA = p.sb("oA", [128, 1], F32)
    oB = p.sb("oB", [128, 1], F32)
    epsln = p.sb("epsln", [128, 1], F32)
    brow = p.sb("brow", [1, 4, D], BF16)
    srwb = p.sb("srwb", [1, 2, SEQ], BF16)

    def init():
        G = p.G
        G.op("memset", ones1[:, :], 1.0)
        G.op("memset", ones128[:, :], 1.0)
        G.op("memset", ones128f[:, :], 1.0)
        G.op("memset", oA[:, :], 0.0)
        G.op("memset", oA[0:64, :], 1.0)
        G.op("memset", oB[:, :], 0.0)
        G.op("memset", oB[64:128, :], 1.0)
        G.op("memset", epsln[:, :], LN_EPS)
        slB.dma(G, brow[:, 0, :], bq[:, :])
        slB.dma(G, brow[:, 1, :], bk[:, :])
        slB.dma(G, brow[:, 2, :], bv[:, :])
        slB.dma(G, brow[:, 3, :], bo[:, :])
        p.ev["init"] = p.G.chan.cur()

    p.sec("gpsimd", init)

    esA = ExitStack()
    sqc = esA.enter_context(nc.sbuf_tensor("sqc", [128, SEQ], F32))
    srow = esA.enter_context(nc.sbuf_tensor("srow", [1, 3 * SEQ], F32))
    psSb = [esA.enter_context(nc.psum_tensor("psSb%d" % b, [1, 512], F32)) for b in range(4)]
    psS2b = [esA.enter_context(nc.psum_tensor("psS2b%d" % b, [1, 512], F32)) for b in range(4)]

    for c in range(16):
        def ldx(c=c):
            Y = p.Y
            Y.wait(p.ev.get(("st_s", c - 1)))
            Y.wait(p.ev.get(("st_t1", c - 1)))
            slX.dma(Y, xc[:, :], xT[c * 128:(c + 1) * 128, :])

        def st_t1(c=c):
            T = p.T
            T.wait(slX.done())
            T.wait(p.ev["init"])
            mm = None
            for b in range(4):
                mm = T.raw("matmul", psSb[b][0:1, :], ones128f[:, 0:1],
                           xc[:, b * 512:(b + 1) * 512], start=(c == 0), stop=(c == 15))
            p.ev[("st_t1", c)] = p.T.chan.ev(mm)

        def st_s(c=c):
            S = p.S
            S.wait(slX.done())
            S.wait(p.ev.get(("st_t2", c - 1)))
            S.op("activation", sqc[:, :], xc[:, :], AF.Square)
            p.ev[("st_s", c)] = S.chan.cur()

        def st_t2(c=c):
            T = p.T
            T.wait(p.ev[("st_s", c)])
            mm = None
            for b in range(4):
                mm = T.raw("matmul", psS2b[b][0:1, :], ones128f[:, 0:1],
                           sqc[:, b * 512:(b + 1) * 512], start=(c == 0), stop=(c == 15))
            p.ev[("st_t2", c)] = p.T.chan.ev(mm)

        p.sec("sync", ldx)
        p.sec("tensor", st_t1)
        p.sec("scalar", st_s)
        p.sec("tensor", st_t2)

    def fin_v():
        V = p.V
        V.wait(p.ev[("st_t1", 15)])
        V.wait(p.ev[("st_t2", 15)])
        for b in range(4):
            sl = slice(b * 512, (b + 1) * 512)
            V.op("tensor_scalar", srow[0:1, sl], psSb[b][0:1, :], 1.0 / D, None, ALU.mult)
            V.op("tensor_scalar", srow[0:1, 2048 + b * 512:2048 + (b + 1) * 512],
                 psS2b[b][0:1, :], 1.0 / D, None, ALU.mult)
        V.op("tensor_tensor", srow[0:1, 4096:6144], srow[0:1, 0:2048],
             srow[0:1, 0:2048], ALU.mult)
        V.op("tensor_sub", srow[0:1, 2048:4096], srow[0:1, 2048:4096],
             srow[0:1, 4096:6144])
        p.ev["fin_v"] = V.chan.cur()

    def fin_s():
        S = p.S
        S.wait(p.ev["fin_v"])
        S.wait(p.ev["init"])
        S.op("activation", srow[0:1, 2048:4096], srow[0:1, 2048:4096], AF.Ln,
             bias=epsln[0:1, 0:1])
        S.op("activation", srwb[0:1, 0, :], srow[0:1, 2048:4096], AF.Exp, scale=-0.5)
        S.op("activation", srow[0:1, 4096:6144], srow[0:1, 2048:4096], AF.Exp, scale=-0.5)
        p.ev["fin_s"] = S.chan.cur()

    def fin_v3():
        V = p.V
        V.wait(p.ev["fin_s"])
        V.op("tensor_tensor", srow[0:1, 2048:4096], srow[0:1, 0:2048],
             srow[0:1, 4096:6144], ALU.mult)
        V.op("tensor_scalar", srwb[0:1, 1, :], srow[0:1, 2048:4096], -1.0, None, ALU.mult)
        p.ev["stats"] = V.chan.cur()

    p.sec("vector", fin_v)
    p.sec("scalar", fin_s)
    p.sec("vector", fin_v3)
    esA.close()

    esB = ExitStack()
    xlnf = esB.enter_context(nc.sbuf_tensor("xlnf", [128, 16, SEQ], BF16))
    psBc = [esB.enter_context(nc.psum_tensor("psBc%d" % i, [128, 512], F32)) for i in range(2)]
    psPj = [esB.enter_context(nc.psum_tensor("psPj%d" % i, [128, 512], F32)) for i in range(4)]
    esB1 = ExitStack()
    rnb = esB1.enter_context(nc.sbuf_tensor("rnb", [128, 2, SEQ], BF16))
    wbuf = esB1.enter_context(nc.sbuf_tensor("wbuf", [128, 1, 16, 128], BF16))

    for b in range(4):
        def bc_t(b=b):
            T = p.T
            T.wait(p.ev["stats"])
            T.wait(p.ev.get(("bc_v", b - 1)))
            T.raw("matmul", psBc[0][:, :], ones1[0:1, 0:128],
                  srwb[0:1, 0, b * 512:(b + 1) * 512], start=True, stop=True)
            m2 = T.raw("matmul", psBc[1][:, :], ones1[0:1, 0:128],
                       srwb[0:1, 1, b * 512:(b + 1) * 512], start=True, stop=True)
            p.ev[("bc_t", b)] = p.T.chan.ev(m2)

        def bc_v(b=b):
            V = p.V
            V.wait(p.ev[("bc_t", b)])
            V.op("tensor_copy", rnb[:, 0, b * 512:(b + 1) * 512], psBc[0][:, :])
            V.op("tensor_copy", rnb[:, 1, b * 512:(b + 1) * 512], psBc[1][:, :])
            p.ev[("bc_v", b)] = V.chan.cur()

        p.sec("tensor", bc_t)
        p.sec("vector", bc_v)

    for c in range(16):
        def ldx2(c=c):
            Y = p.Y
            Y.wait(p.ev["stats"])
            Y.wait(p.ev.get(("ap_v", c - 1)))
            slX.dma(Y, xc[:, :], xT[c * 128:(c + 1) * 128, :])

        def ap_v(c=c):
            V = p.V
            V.wait(slX.done())
            V.wait(p.ev[("bc_v", 3)])
            V.op("tensor_tensor", xc[:, :], xc[:, :], rnb[:, 0, :], ALU.mult)
            V.op("tensor_tensor", xlnf[:, c, :], xc[:, :], rnb[:, 1, :], ALU.add)
            p.ev[("ap_v", c)] = V.chan.cur()

        p.sec("sync", ldx2)
        p.sec("vector", ap_v)

    xln_ev = lambda: p.ev[("ap_v", 15)]

    def emit_fm_proj(tag, wsrc, bias_idx, dstf, ncols, c0, prev=None):
        nb = ncols // 512
        for oc in range(16):
            par = 0

            def wload(oc=oc, par=par):
                G = p.G
                if oc == 0 and prev is not None:
                    G.wait(p.ev[(prev, "mm", 15)])
                G.wait(p.ev.get((tag, "mm", oc - 1)))
                slW[par].dma(G, wbuf[:, par, :, :],
                             wsrc[:, oc * 128:(oc + 1) * 128]
                             .rearrange("(kc p) c -> p kc c", p=128))

            def mmf(oc=oc, par=0):
                T = p.T
                T.wait(slW[0].done())
                T.wait(slB.done())
                T.wait(xln_ev())
                if oc == 0 and prev is not None:
                    T.wait(p.ev[(prev, "ev", 15)])
                T.wait(p.ev.get((tag, "ev", oc - 1)))
                mm = None
                for b in range(nb):
                    for kc in range(16):
                        T.raw("matmul", psPj[b][:, :], wbuf[:, par, kc, :],
                              xlnf[:, kc, c0 + b * 512:c0 + (b + 1) * 512],
                              start=(kc == 0), stop=False)
                    mm = T.raw("matmul", psPj[b][:, :],
                               brow[0:1, bias_idx, oc * 128:(oc + 1) * 128],
                               ones1[0:1, :], start=False, stop=True)
                p.ev[(tag, "mm", oc)] = p.T.chan.ev(mm)

            def evf(oc=oc):
                V = p.V
                V.wait(p.ev[(tag, "mm", oc)])
                for b in range(nb):
                    V.op("tensor_copy", dstf[:, oc, b * 512:(b + 1) * 512], psPj[b][:, :])
                p.ev[(tag, "ev", oc)] = V.chan.cur()

            p.sec("gpsimd", wload)
            p.sec("tensor", mmf)
            p.sec("vector", evf)

    emit_fm_proj("q", wqT, 0, qf, TLOC, Q0)
    emit_fm_proj("k", wkT, 1, kf, SEQ, 0, prev="q")

    esB1.close()
    esB2 = ExitStack()
    wvb = esB2.enter_context(nc.sbuf_tensor("wvb", [128, 16, 256], BF16))
    ev_t = esB2.enter_context(nc.sbuf_tensor("ev_t", [128, 256], BF16))

    for oblk in range(8):
        def wvload(oblk=oblk):
            G = p.G
            if oblk == 0:
                G.wait(p.ev[("k", "mm", 15)])
            G.wait(p.ev.get(("v_mm", oblk - 1, 15)))
            slW[0].dma(G, wvb[:, :, :],
                       wvT[:, oblk * 256:(oblk + 1) * 256]
                       .rearrange("(kc p) c -> p kc c", p=128))

        p.sec("gpsimd", wvload)
        for tc in range(16):
            def v_mm(oblk=oblk, tc=tc):
                T = p.T
                T.wait(slW[0].done())
                T.wait(slB.done())
                if oblk == 0 and tc < 2:
                    T.wait(p.ev[("k", "ev", 15)])
                    T.wait(p.ev[("q", "ev", 15)])
                if tc < 2:
                    T.wait(p.ev.get(("v_ev", oblk - 1, 14 + tc)))
                T.wait(p.ev.get(("v_ev", oblk, tc - 2)))
                pj = psPj[tc % 2]
                for kc in range(16):
                    T.raw("matmul", pj[:, 0:256], xlnf[:, kc, tc * 128:(tc + 1) * 128],
                          wvb[:, kc, :], start=(kc == 0), stop=False)
                mm = T.raw("matmul", pj[:, 0:256], ones1[0:1, 0:128],
                           brow[0:1, 2, oblk * 256:(oblk + 1) * 256],
                           start=False, stop=True)
                p.ev[("v_mm", oblk, tc)] = p.T.chan.ev(mm)

            def v_ev(oblk=oblk, tc=tc):
                S = p.S
                S.wait(p.ev[("v_mm", oblk, tc)])
                S.wait(slVs.done())
                S.op("activation", ev_t[:, :], psPj[tc % 2][:, 0:256], AF.Sigmoid)
                p.ev[("v_ev", oblk, tc)] = S.chan.cur()

            def v_st(oblk=oblk, tc=tc):
                Y = p.Y
                Y.wait(p.ev[("v_ev", oblk, tc)])
                slVs.dma(Y, v_d[tc * 128:(tc + 1) * 128, oblk * 256:(oblk + 1) * 256],
                         ev_t[:, :])

            p.sec("tensor", v_mm)
            p.sec("scalar", v_ev)
            p.sec("sync", v_st)

    p.ev["proj_done"] = p.ev[("v_ev", 7, 15)]
    esB2.close()
    esB.close()

    esC = ExitStack()
    cost = esC.enter_context(nc.sbuf_tensor("cost", [128, SEQ], BF16))
    sint = esC.enter_context(nc.sbuf_tensor("sint", [128, SEQ], BF16))
    bmsk = esC.enter_context(nc.sbuf_tensor("bmsk", [128, 1024], BF16))
    rtmp = esC.enter_context(nc.sbuf_tensor("rtmp", [128, SEQ], BF16))
    rtm2 = esC.enter_context(nc.sbuf_tensor("rtm2", [128, SEQ], BF16))
    sq2 = esC.enter_context(nc.sbuf_tensor("sq2", [128, SEQ], F32))
    acc = esC.enter_context(nc.sbuf_tensor("acc", [128, 2], F32))
    crow = esC.enter_context(nc.sbuf_tensor("crow", [1, 3, 64], F32))
    cb128 = esC.enter_context(nc.sbuf_tensor("cb128", [128, 32], F32))
    vbuf = [esC.enter_context(nc.sbuf_tensor("vbuf%d" % i, [128, 16, 128], BF16))
            for i in range(2)]
    atA = esC.enter_context(nc.sbuf_tensor("atA", [128, 2, 512], BF16))
    atB = esC.enter_context(nc.sbuf_tensor("atB", [128, 2, 512], BF16))
    rsb = esC.enter_context(nc.sbuf_tensor("rsb", [1, 2, 512], BF16))
    nrm = esC.enter_context(nc.sbuf_tensor("nrm", [128, 512], F32))
    obuf2 = esC.enter_context(nc.sbuf_tensor("obuf2", [64, 1024], BF16))
    esCa = ExitStack()
    psC1 = esCa.enter_context(nc.psum_tensor("psC1", [128, 64], F32))

    def ldrope():
        G = p.G
        G.wait(p.ev["proj_done"])
        slR.dma(G, cost[:, :], cosT[:, :])
        slR.dma(G, sint[:, :], sinT[:, :])
        slR.dma(G, bmsk[:, :], bmask[:, :])

    p.sec("gpsimd", ldrope)

    def rope_chunk(E, buf, c, n, c0):
        E.op("tensor_copy", rtmp[0:32, 0:n], buf[32:64, c, 0:n])
        E.op("tensor_copy", rtmp[32:64, 0:n], buf[0:32, c, 0:n])
        E.op("tensor_copy", rtmp[64:96, 0:n], buf[96:128, c, 0:n])
        E.op("tensor_copy", rtmp[96:128, 0:n], buf[64:96, c, 0:n])
        E.op("tensor_tensor", rtm2[:, 0:n], buf[:, c, 0:n], cost[:, c0:c0 + n], ALU.mult)
        E.op("tensor_tensor", rtmp[:, 0:n], rtmp[:, 0:n], sint[:, c0:c0 + n], ALU.mult)
        E.op("tensor_tensor", buf[:, c, 0:n], rtm2[:, 0:n], rtmp[:, 0:n], ALU.add)

    for c in range(16):
        def rq(c=c):
            V = p.V
            V.wait(slR.done())
            rope_chunk(V, qf, c, TLOC, Q0)
            p.ev[("rq", c)] = V.chan.cur()

        p.sec("vector", rq)
    for c in range(16):
        def rk(c=c):
            V = p.V
            V.wait(slR.done())
            rope_chunk(V, kf, c, SEQ, 0)
            p.ev[("rk", c)] = V.chan.cur()

        p.sec("vector", rk)

    for hp in range(16):
        def cb_s(hp=hp):
            S = p.S
            S.wait(p.ev[("rq", hp)])
            S.wait(p.ev[("rk", hp)])
            S.wait(p.ev.get(("cb_t", hp - 1)))
            S.op("activation", sq2[:, 0:TLOC], qf[:, hp, :], AF.Square,
                 accum_out=acc[:, 0:1])
            S.op("activation", sq2[:, 0:SEQ], kf[:, hp, :], AF.Square,
                 accum_out=acc[:, 1:2])
            p.ev[("cb_s", hp)] = S.chan.cur()

        def cb_t(hp=hp):
            T = p.T
            T.wait(p.ev[("cb_s", hp)])
            T.raw("matmul", psC1[0:1, hp * 2:hp * 2 + 1], oA[:, 0:1], acc[:, 0:1],
                  start=True, stop=True)
            T.raw("matmul", psC1[0:1, hp * 2 + 1:hp * 2 + 2], oB[:, 0:1], acc[:, 0:1],
                  start=True, stop=True)
            T.raw("matmul", psC1[0:1, 32 + hp * 2:32 + hp * 2 + 1], oA[:, 0:1],
                  acc[:, 1:2], start=True, stop=True)
            mm = T.raw("matmul", psC1[0:1, 32 + hp * 2 + 1:32 + hp * 2 + 2], oB[:, 0:1],
                       acc[:, 1:2], start=True, stop=True)
            p.ev[("cb_t", hp)] = p.T.chan.ev(mm)

        p.sec("scalar", cb_s)
        p.sec("tensor", cb_t)

    def cfin_v():
        V = p.V
        V.wait(p.ev[("cb_t", 15)])
        V.op("tensor_scalar", crow[0:1, 0, 0:32], psC1[0:1, 0:32], 1.0 / TLOC, None, ALU.mult)
        V.op("tensor_scalar", crow[0:1, 1, 0:32], psC1[0:1, 32:64], 1.0 / SEQ, None, ALU.mult)
        V.op("tensor_tensor", crow[0:1, 0, 0:32], crow[0:1, 0, 0:32],
             crow[0:1, 1, 0:32], ALU.mult)
        p.ev["cfin_v"] = V.chan.cur()

    def cfin_s():
        S = p.S
        S.wait(p.ev["cfin_v"])
        S.op("activation", crow[0:1, 1, 0:32], crow[0:1, 0, 0:32], AF.Ln)
        S.op("activation", crow[0:1, 2, 0:32], crow[0:1, 1, 0:32], AF.Exp, scale=0.5)
        p.ev["cfin_s"] = S.chan.cur()

    def cfin_v2():
        V = p.V
        V.wait(p.ev["cfin_s"])
        V.op("tensor_scalar", crow[0:1, 2, 0:32], crow[0:1, 2, 0:32], -0.5, None, ALU.mult)
        V.op("tensor_copy", rsb[0:1, 0, 0:32], crow[0:1, 2, 0:32])
        p.ev["cfin_v2"] = V.chan.cur()

    def cfin_t():
        T = p.T
        T.wait(p.ev["cfin_v2"])
        mm = T.raw("matmul", psC1[:, 0:32], ones1[0:1, 0:128], rsb[0:1, 0, 0:32],
                   start=True, stop=True)
        p.ev["cfin_t"] = p.T.chan.ev(mm)

    def cfin_v3():
        V = p.V
        V.wait(p.ev["cfin_t"])
        V.op("tensor_copy", cb128[:, :], psC1[:, 0:32])
        p.ev["cbias"] = V.chan.cur()

    p.sec("vector", cfin_v)
    p.sec("scalar", cfin_s)
    p.sec("vector", cfin_v2)
    p.sec("tensor", cfin_t)
    p.sec("vector", cfin_v3)
    esCa.close()
    psSc = [esC.enter_context(nc.psum_tensor("psSc%d" % i, [128, 512], F32))
            for i in range(4)]
    psOa = esC.enter_context(nc.psum_tensor("psOa", [64, 512], F32))
    psOb = esC.enter_context(nc.psum_tensor("psOb", [64, 512], F32))
    psRa = esC.enter_context(nc.psum_tensor("psRa", [1, 512], F32))
    psRb = esC.enter_context(nc.psum_tensor("psRb", [1, 512], F32))

    for hp in range(16):
        def vload(hp=hp):
            Y = p.Y
            Y.wait(slVs.done())
            Y.wait(p.ev.get(("av_done", hp - 2)))
            slV[hp % 2].dma(Y, vbuf[hp % 2][:, :, :],
                            v_d[:, hp * 128:(hp + 1) * 128]
                            .rearrange("(kc p) c -> p kc c", p=128))

        p.sec("sync", vload)

        for qb in range(2):
            qlo = Q0 + qb * 512
            kc_hi = (qlo + 512) // 128
            band_lo = qlo // 128
            last_kc = kc_hi - 1

            for kc in range(kc_hi):
                sp = kc % 2

                def sc_t(hp=hp, qb=qb, kc=kc, sp=sp):
                    T = p.T
                    T.wait(p.ev["cbias"])
                    T.wait(p.ev[("rq", hp)])
                    T.wait(p.ev[("rk", hp)])
                    if kc < 2:
                        T.wait(p.ev.get(("onrm", hp, qb - 1)) or
                               p.ev.get(("onrm", hp - 1, 1)))
                    T.wait(p.ev.get(("ex", hp, qb, kc - 2)))
                    m1 = T.raw("matmul", psSc[sp * 2][:, :],
                               kf[0:64, hp, kc * 128:(kc + 1) * 128],
                               qf[0:64, hp, qb * 512:(qb + 1) * 512],
                               start=True, stop=True, tile_position=(0, 0))
                    m2 = T.raw("matmul", psSc[sp * 2 + 1][:, :],
                               kf[64:128, hp, kc * 128:(kc + 1) * 128],
                               qf[64:128, hp, qb * 512:(qb + 1) * 512],
                               start=True, stop=True, tile_position=(64, 0))
                    p.ev[("sc", hp, qb, kc)] = p.T.chan.ev(m2)

                def ex_s(hp=hp, qb=qb, kc=kc, sp=sp):
                    S = p.S
                    S.wait(p.ev[("sc", hp, qb, kc)])
                    S.wait(p.ev["cbias"])
                    S.wait(p.ev.get(("rsav", hp, qb, kc - 2)))
                    S.op("activation", atA[:, sp, :], psSc[sp * 2][:, :], AF.Exp,
                         scale=0.125, bias=cb128[:, hp * 2:hp * 2 + 1])
                    S.op("activation", atB[:, sp, :], psSc[sp * 2 + 1][:, :], AF.Exp,
                         scale=0.125, bias=cb128[:, hp * 2 + 1:hp * 2 + 2])
                    p.ev[("ex", hp, qb, kc)] = S.chan.cur()

                p.sec("tensor", sc_t)
                p.sec("scalar", ex_s)

                if kc >= band_lo:
                    off = 384 - (kc * 128 - qlo)

                    def msk_g(hp=hp, qb=qb, kc=kc, sp=sp, off=off):
                        G = p.G
                        G.wait(p.ev[("ex", hp, qb, kc)])
                        G.wait(slR.done())
                        G.op("tensor_tensor", atA[:, sp, :], atA[:, sp, :],
                             bmsk[:, off:off + 512], ALU.mult)
                        G.op("tensor_tensor", atB[:, sp, :], atB[:, sp, :],
                             bmsk[:, off:off + 512], ALU.mult)
                        p.ev[("mk", hp, qb, kc)] = p.G.chan.cur()

                    p.sec("gpsimd", msk_g)

                def rs_av(hp=hp, qb=qb, kc=kc, sp=sp, band=(kc >= band_lo),
                          first=(kc == 0), last=(kc == last_kc)):
                    T = p.T
                    if band:
                        T.wait(p.ev[("mk", hp, qb, kc)])
                    else:
                        T.wait(p.ev[("ex", hp, qb, kc)])
                    T.wait(slV[hp % 2].done())
                    if first:
                        T.wait(p.ev.get(("onrm", hp, qb - 1)) or
                               p.ev.get(("onrm", hp - 1, 1)))
                    T.raw("matmul", psRa[0:1, :], ones128[:, 0:1], atA[:, sp, :],
                          start=first, stop=last)
                    T.raw("matmul", psRb[0:1, :], ones128[:, 0:1], atB[:, sp, :],
                          start=first, stop=last)
                    T.raw("matmul", psOa[:, :], vbuf[hp % 2][:, kc, 0:64],
                          atA[:, sp, :], start=first, stop=last)
                    mm = T.raw("matmul", psOb[:, :], vbuf[hp % 2][:, kc, 64:128],
                               atB[:, sp, :], start=first, stop=last)
                    p.ev[("rsav", hp, qb, kc)] = p.T.chan.ev(mm)

                p.sec("tensor", rs_av)

            def onrm_v(hp=hp, qb=qb):
                V = p.V
                V.wait(p.ev[("rsav", hp, qb, last_kc)])
                V.op("reciprocal", nrm[0:1, :], psRa[0:1, :])
                V.op("tensor_copy", rsb[0:1, 0, :], nrm[0:1, :])
                V.op("reciprocal", nrm[0:1, :], psRb[0:1, :])
                V.op("tensor_copy", rsb[0:1, 1, :], nrm[0:1, :])
                p.ev[("onrm_v", hp, qb)] = V.chan.cur()

            def onrm_t(hp=hp, qb=qb):
                T = p.T
                T.wait(p.ev[("onrm_v", hp, qb)])
                T.raw("matmul", psSc[0][0:64, :], ones1[0:1, 0:64], rsb[0:1, 0, :],
                      start=True, stop=True)
                mm = T.raw("matmul", psSc[1][0:64, :], ones1[0:1, 0:64], rsb[0:1, 1, :],
                           start=True, stop=True)
                p.ev[("onrm_t", hp, qb)] = p.T.chan.ev(mm)

            def onrm_v2(hp=hp, qb=qb):
                V = p.V
                V.wait(p.ev[("onrm_t", hp, qb)])
                V.wait(slOd.done())
                V.op("tensor_copy", nrm[0:64, :], psSc[0][0:64, :])
                V.op("tensor_tensor", obuf2[0:64, 0:512], psOa[:, :], nrm[0:64, :], ALU.mult)
                V.op("tensor_copy", nrm[0:64, :], psSc[1][0:64, :])
                V.op("tensor_tensor", obuf2[0:64, 512:1024], psOb[:, :], nrm[0:64, :],
                     ALU.mult)
                p.ev[("onrm", hp, qb)] = V.chan.cur()

            def ost(hp=hp, qb=qb):
                Y = p.Y
                Y.wait(p.ev[("onrm", hp, qb)])
                slOd.dma(Y, o_d[hp * 128:hp * 128 + 64, qb * 512:(qb + 1) * 512],
                         obuf2[0:64, 0:512])
                slOd.dma(Y, o_d[hp * 128 + 64:hp * 128 + 128, qb * 512:(qb + 1) * 512],
                         obuf2[0:64, 512:1024])
                if qb == 1:
                    p.ev[("av_done", hp)] = slOd.done()

            p.sec("vector", onrm_v)
            p.sec("tensor", onrm_t)
            p.sec("vector", onrm_v2)
            p.sec("sync", ost)

    esC.close()

    esD = ExitStack()
    obuf = [esD.enter_context(nc.sbuf_tensor("obuf%d" % i, [128, TLOC], BF16))
            for i in range(2)]
    wob = esD.enter_context(nc.sbuf_tensor("wob", [128, 16, 512], BF16))
    xhb = esD.enter_context(nc.sbuf_tensor("xhb", [128, 512], F32))
    xot = esD.enter_context(nc.sbuf_tensor("xot", [128, 512], F32))
    psJ = [esD.enter_context(nc.psum_tensor("psJ%d" % i, [128, 512], F32))
           for i in range(8)]

    for oblk in range(4):
        def woload(oblk=oblk):
            G = p.G
            G.wait(p.ev[("onrm", 15, 1)])
            G.wait(p.ev.get(("oj_mm", oblk - 1, 15)))
            slWo.dma(G, wob[:, :, :],
                     woT[:, oblk * 512:(oblk + 1) * 512]
                     .rearrange("(ec p) c -> p ec c", p=128))

        p.sec("gpsimd", woload)

        for ec in range(16):
            def oload(oblk=oblk, ec=ec):
                Y = p.Y
                Y.wait(p.ev[("onrm", 15, 1)])
                Y.wait(slOd.done())
                Y.wait(p.ev.get(("oj_mm", oblk, ec - 2)))
                slOb[ec % 2].dma(Y, obuf[ec % 2][:, :], o_d[ec * 128:(ec + 1) * 128, :])

            def oj_mm(oblk=oblk, ec=ec):
                T = p.T
                T.wait(p.ev[("onrm", 15, 1)])
                T.wait(slOb[ec % 2].done())
                T.wait(slWo.done())
                T.wait(slB.done())
                if ec == 0:
                    T.wait(p.ev.get(("oj_v", oblk - 1, 7)))
                mm = None
                for tc in range(8):
                    mm = T.raw("matmul", psJ[tc][:, :], obuf[ec % 2][:, tc * 128:(tc + 1) * 128],
                               wob[:, ec, :], start=(ec == 0), stop=False)
                if ec == 15:
                    for tc in range(8):
                        mm = T.raw("matmul", psJ[tc][:, :], ones1[0:1, 0:128],
                                   brow[0:1, 3, oblk * 512:(oblk + 1) * 512],
                                   start=False, stop=True)
                p.ev[("oj_mm", oblk, ec)] = p.T.chan.ev(mm)

            p.sec("sync", oload)
            p.sec("tensor", oj_mm)

        for tc in range(8):
            def xld(oblk=oblk, tc=tc):
                Y = p.Y
                Y.wait(p.ev[("onrm", 15, 1)])
                Y.wait(p.ev.get(("oj_st", oblk, tc - 1)))
                slH.dma(Y, xhb[:, :], xh[tc * 128:(tc + 1) * 128,
                                        oblk * 512:(oblk + 1) * 512])

            def oj_v(oblk=oblk, tc=tc):
                V = p.V
                V.wait(p.ev[("oj_mm", oblk, 15)])
                V.wait(slH.done())
                V.wait(slO.done())
                V.op("tensor_tensor", xot[:, :], psJ[tc][:, :], xhb[:, :], ALU.add)
                p.ev[("oj_v", oblk, tc)] = V.chan.cur()

            def oj_st(oblk=oblk, tc=tc):
                Y = p.Y
                Y.wait(p.ev[("oj_v", oblk, tc)])
                slO.dma(Y, xo[tc * 128:(tc + 1) * 128, oblk * 512:(oblk + 1) * 512],
                        xot[:, :])
                p.ev[("oj_st", oblk, tc)] = slO.done()

            p.sec("sync", xld)
            p.sec("vector", oj_v)
            p.sec("sync", oj_st)

    p.sec("sync", lambda: p.Y.wait(slO.done()))
    esD.close()
    return p.fin()


def build_attn_b1(half):
    from contextlib import ExitStack
    p = Prog("attnb1").start()
    nc = p.nc
    SEQ = 2048
    xT = nc.dram_tensor("xT", [D, SEQ], F32, kind="ExternalInput").ap()
    xh = nc.dram_tensor("xh", [TLOC, D], F32, kind="ExternalInput").ap()
    wqT = nc.dram_tensor("wqT", [D, D], F32, kind="ExternalInput").ap()
    wsmT = nc.dram_tensor("wsmT", [D, 320], F32, kind="ExternalInput").ap()
    woT = nc.dram_tensor("woT", [D, D], F32, kind="ExternalInput").ap()
    cosT = nc.dram_tensor("cosT", [128, SEQ], F32, kind="ExternalInput").ap()
    sinT = nc.dram_tensor("sinT", [128, SEQ], F32, kind="ExternalInput").ap()
    bmask = nc.dram_tensor("bmask", [128, 1024], F32, kind="ExternalInput").ap()
    xo = nc.dram_tensor("xo", [TLOC, D], F32, kind="ExternalOutput").ap()
    sm_d = nc.dram_tensor("sm_d", [32, SEQ], F32).ap()
    v_d = nc.dram_tensor("v_d", [D, SEQ], BF16).ap()
    o_d = nc.dram_tensor("o_d", [D, TLOC], BF16).ap()

    Q0 = half * TLOC
    SCL = 0.08838834764831845
    CMUL = -0.04419417382415922

    slX = p.slot("slX")
    slW = [p.slot("slW0"), p.slot("slW1")]
    slR = p.slot("slR")
    slSm = p.slot("slSm")
    slKr = p.slot("slKr")
    slVs = p.slot("slVs")
    slV = [p.slot("slV0"), p.slot("slV1")]
    slOd = p.slot("slOd")
    slOb = [p.slot("slOb0"), p.slot("slOb1")]
    slWo = p.slot("slWo")
    slH = p.slot("slH")
    slO = p.slot("slO")

    qf = p.sb("qf", [128, 16, TLOC], BF16)
    kpf = p.sb("kpf", [128, SEQ], BF16)
    vpf = p.sb("vpf", [128, SEQ], BF16)
    hgf = p.sb("hgf", [64, SEQ], F32)
    xc = p.sb("xc", [128, SEQ], F32)
    ones1 = p.sb("ones1", [1, 512], BF16)
    ones128 = p.sb("ones128", [128, 1], BF16)
    ones128f = p.sb("ones128f", [128, 1], F32)
    epsln = p.sb("epsln", [128, 1], F32)
    onef = p.sb("onef", [128, 1], F32)
    srwb = p.sb("srwb", [1, 2, SEQ], BF16)

    def init():
        G = p.G
        G.op("memset", ones1[:, :], 1.0)
        G.op("memset", ones128[:, :], 1.0)
        G.op("memset", ones128f[:, :], 1.0)
        G.op("memset", epsln[:, :], LN_EPS)
        G.op("memset", onef[:, :], 1.0)
        p.ev["init"] = p.G.chan.cur()

    p.sec("gpsimd", init)

    esA = ExitStack()
    sqc = esA.enter_context(nc.sbuf_tensor("sqc", [128, SEQ], F32))
    srow = esA.enter_context(nc.sbuf_tensor("srow", [1, 3 * SEQ], F32))
    psSb = [esA.enter_context(nc.psum_tensor("psSb%d" % b, [1, 512], F32)) for b in range(4)]
    psS2b = [esA.enter_context(nc.psum_tensor("psS2b%d" % b, [1, 512], F32)) for b in range(4)]

    for c in range(16):
        def ldx(c=c):
            Y = p.Y
            Y.wait(p.ev.get(("st_s", c - 1)))
            Y.wait(p.ev.get(("st_t1", c - 1)))
            slX.dma(Y, xc[:, :], xT[c * 128:(c + 1) * 128, :])

        def st_t1(c=c):
            T = p.T
            T.wait(slX.done())
            T.wait(p.ev["init"])
            mm = None
            for b in range(4):
                mm = T.raw("matmul", psSb[b][0:1, :], ones128f[:, 0:1],
                           xc[:, b * 512:(b + 1) * 512], start=(c == 0), stop=(c == 15))
            p.ev[("st_t1", c)] = p.T.chan.ev(mm)

        def st_s(c=c):
            S = p.S
            S.wait(slX.done())
            S.wait(p.ev.get(("st_t2", c - 1)))
            S.op("activation", sqc[:, :], xc[:, :], AF.Square)
            p.ev[("st_s", c)] = S.chan.cur()

        def st_t2(c=c):
            T = p.T
            T.wait(p.ev[("st_s", c)])
            mm = None
            for b in range(4):
                mm = T.raw("matmul", psS2b[b][0:1, :], ones128f[:, 0:1],
                           sqc[:, b * 512:(b + 1) * 512], start=(c == 0), stop=(c == 15))
            p.ev[("st_t2", c)] = p.T.chan.ev(mm)

        p.sec("sync", ldx)
        p.sec("tensor", st_t1)
        p.sec("scalar", st_s)
        p.sec("tensor", st_t2)

    def fin_v():
        V = p.V
        V.wait(p.ev[("st_t1", 15)])
        V.wait(p.ev[("st_t2", 15)])
        for b in range(4):
            sl = slice(b * 512, (b + 1) * 512)
            V.op("tensor_scalar", srow[0:1, sl], psSb[b][0:1, :], 1.0 / D, None, ALU.mult)
            V.op("tensor_scalar", srow[0:1, 2048 + b * 512:2048 + (b + 1) * 512],
                 psS2b[b][0:1, :], 1.0 / D, None, ALU.mult)
        V.op("tensor_tensor", srow[0:1, 4096:6144], srow[0:1, 0:2048],
             srow[0:1, 0:2048], ALU.mult)
        V.op("tensor_sub", srow[0:1, 2048:4096], srow[0:1, 2048:4096],
             srow[0:1, 4096:6144])
        p.ev["fin_v"] = V.chan.cur()

    def fin_s():
        S = p.S
        S.wait(p.ev["fin_v"])
        S.wait(p.ev["init"])
        S.op("activation", srow[0:1, 2048:4096], srow[0:1, 2048:4096], AF.Ln,
             bias=epsln[0:1, 0:1])
        S.op("activation", srwb[0:1, 0, :], srow[0:1, 2048:4096], AF.Exp, scale=-0.5)
        S.op("activation", srow[0:1, 4096:6144], srow[0:1, 2048:4096], AF.Exp, scale=-0.5)
        p.ev["fin_s"] = S.chan.cur()

    def fin_v3():
        V = p.V
        V.wait(p.ev["fin_s"])
        V.op("tensor_tensor", srow[0:1, 2048:4096], srow[0:1, 0:2048],
             srow[0:1, 4096:6144], ALU.mult)
        V.op("tensor_scalar", srwb[0:1, 1, :], srow[0:1, 2048:4096], -1.0, None, ALU.mult)
        p.ev["stats"] = V.chan.cur()

    p.sec("vector", fin_v)
    p.sec("scalar", fin_s)
    p.sec("vector", fin_v3)
    esA.close()

    esB = ExitStack()
    xlnf = esB.enter_context(nc.sbuf_tensor("xlnf", [128, 16, SEQ], BF16))
    psBc = [esB.enter_context(nc.psum_tensor("psBc%d" % i, [128, 512], F32)) for i in range(2)]
    psPj = [esB.enter_context(nc.psum_tensor("psPj%d" % i, [128, 512], F32)) for i in range(4)]
    rnb = esB.enter_context(nc.sbuf_tensor("rnb", [128, 2, SEQ], BF16))
    wbuf = esB.enter_context(nc.sbuf_tensor("wbuf", [128, 1, 16, 128], BF16))

    for b in range(4):
        def bc_t(b=b):
            T = p.T
            T.wait(p.ev["stats"])
            T.wait(p.ev.get(("bc_v", b - 1)))
            T.raw("matmul", psBc[0][:, :], ones1[0:1, 0:128],
                  srwb[0:1, 0, b * 512:(b + 1) * 512], start=True, stop=True)
            m2 = T.raw("matmul", psBc[1][:, :], ones1[0:1, 0:128],
                       srwb[0:1, 1, b * 512:(b + 1) * 512], start=True, stop=True)
            p.ev[("bc_t", b)] = p.T.chan.ev(m2)

        def bc_v(b=b):
            V = p.V
            V.wait(p.ev[("bc_t", b)])
            V.op("tensor_copy", rnb[:, 0, b * 512:(b + 1) * 512], psBc[0][:, :])
            V.op("tensor_copy", rnb[:, 1, b * 512:(b + 1) * 512], psBc[1][:, :])
            p.ev[("bc_v", b)] = V.chan.cur()

        p.sec("tensor", bc_t)
        p.sec("vector", bc_v)

    for c in range(16):
        def ldx2(c=c):
            Y = p.Y
            Y.wait(p.ev["stats"])
            Y.wait(p.ev.get(("ap_v", c - 1)))
            slX.dma(Y, xc[:, :], xT[c * 128:(c + 1) * 128, :])

        def ap_v(c=c):
            V = p.V
            V.wait(slX.done())
            V.wait(p.ev[("bc_v", 3)])
            V.op("tensor_tensor", xc[:, :], xc[:, :], rnb[:, 0, :], ALU.mult)
            V.op("tensor_tensor", xlnf[:, c, :], xc[:, :], rnb[:, 1, :], ALU.add)
            p.ev[("ap_v", c)] = V.chan.cur()

        p.sec("sync", ldx2)
        p.sec("vector", ap_v)

    xln_ev = lambda: p.ev[("ap_v", 15)]

    for oc in range(16):
        def wload(oc=oc):
            G = p.G
            G.wait(p.ev.get(("q", "mm", oc - 1)))
            slW[0].dma(G, wbuf[:, 0, :, :],
                       wqT[:, oc * 128:(oc + 1) * 128]
                       .rearrange("(kc p) c -> p kc c", p=128))

        def mmf(oc=oc):
            T = p.T
            T.wait(slW[0].done())
            T.wait(xln_ev())
            T.wait(p.ev.get(("q", "ev", oc - 1)))
            mm = None
            for b in range(2):
                for kc in range(16):
                    mm = T.raw("matmul", psPj[b][:, :], wbuf[:, 0, kc, :],
                               xlnf[:, kc, Q0 + b * 512:Q0 + (b + 1) * 512],
                               start=(kc == 0), stop=(kc == 15))
            p.ev[("q", "mm", oc)] = p.T.chan.ev(mm)

        def evf(oc=oc):
            V = p.V
            V.wait(p.ev[("q", "mm", oc)])
            for b in range(2):
                V.op("tensor_copy", qf[:, oc, b * 512:(b + 1) * 512], psPj[b][:, :])
            p.ev[("q", "ev", oc)] = V.chan.cur()

        p.sec("gpsimd", wload)
        p.sec("tensor", mmf)
        p.sec("vector", evf)

    for rc, (c0w, Mc) in enumerate([(0, 128), (128, 128), (256, 64)]):
        def wsload(rc=rc, c0w=c0w, Mc=Mc):
            G = p.G
            if rc == 0:
                G.wait(p.ev[("q", "mm", 15)])
            G.wait(p.ev.get(("sm", "mm", rc - 1, 3)))
            slW[1].dma(G, wbuf[:, 0, :, 0:Mc],
                       wsmT[:, c0w:c0w + Mc].rearrange("(kc p) c -> p kc c", p=128))

        p.sec("gpsimd", wsload)
        for tb in range(4):
            def sm_mm(rc=rc, tb=tb, Mc=Mc):
                T = p.T
                T.wait(slW[1].done())
                if rc == 0 and tb < 2:
                    T.wait(p.ev[("q", "ev", 15)])
                if tb < 2:
                    T.wait(p.ev.get(("sm", "ev", rc - 1, 2 + tb)))
                T.wait(p.ev.get(("sm", "ev", rc, tb - 2)))
                mm = None
                for kc in range(16):
                    mm = T.raw("matmul", psPj[tb % 2][0:Mc, :], wbuf[:, 0, kc, 0:Mc],
                               xlnf[:, kc, tb * 512:(tb + 1) * 512],
                               start=(kc == 0), stop=(kc == 15))
                p.ev[("sm", "mm", rc, tb)] = p.T.chan.ev(mm)

            def sm_ev(rc=rc, tb=tb):
                V = p.V
                V.wait(p.ev[("sm", "mm", rc, tb)])
                sl = slice(tb * 512, (tb + 1) * 512)
                if rc == 0:
                    V.op("tensor_copy", kpf[:, sl], psPj[tb % 2][:, :])
                elif rc == 1:
                    V.op("tensor_copy", vpf[:, sl], psPj[tb % 2][:, :])
                else:
                    V.op("tensor_copy", hgf[0:64, sl], psPj[tb % 2][0:64, :])
                p.ev[("sm", "ev", rc, tb)] = V.chan.cur()

            p.sec("tensor", sm_mm)
            p.sec("vector", sm_ev)

    p.ev["smallp"] = p.ev[("sm", "ev", 2, 3)]
    esB.close()

    esC = ExitStack()
    kf = esC.enter_context(nc.sbuf_tensor("kf", [128, 16, SEQ], BF16))

    esC1 = ExitStack()
    gt1 = esC1.enter_context(nc.sbuf_tensor("gt1", [32, SEQ], F32))
    gt2 = esC1.enter_context(nc.sbuf_tensor("gt2", [32, SEQ], F32))
    gt3 = esC1.enter_context(nc.sbuf_tensor("gt3", [32, SEQ], F32))
    zer = esC1.enter_context(nc.sbuf_tensor("zer", [32, SEQ], F32))

    def scan_g():
        G = p.G
        G.wait(p.ev["smallp"])
        G.op("memset", zer[:, :], 0.0)
        G.op("memset", gt2[:, 0:1], 0.0)
        p.ev["scz"] = G.chan.cur()

    def scan_s1():
        S = p.S
        S.wait(p.ev["smallp"])
        S.op("activation", gt1[0:32, :], hgf[32:64, :], AF.Abs)
        S.op("activation", gt1[0:32, :], gt1[0:32, :], AF.Exp, scale=-1.0)
        S.op("activation", gt1[0:32, :], gt1[0:32, :], AF.Ln, bias=onef[0:32, 0:1])
        p.ev["sc1"] = S.chan.cur()

    def scan_v1():
        V = p.V
        V.wait(p.ev["sc1"])
        V.wait(p.ev["scz"])
        V.op("tensor_scalar", gt3[0:32, :], hgf[32:64, :], 0.0, None, ALU.min)
        V.op("tensor_sub", gt3[0:32, :], gt3[0:32, :], gt1[0:32, :])
        V.op("tensor_tensor_scan", gt2[0:32, 1:SEQ], gt3[0:32, 0:SEQ - 1],
             zer[0:32, 0:SEQ - 1], 0.0, ALU.add, ALU.add)
        p.ev["sc2"] = V.chan.cur()

    def scan_s2():
        S = p.S
        S.wait(p.ev["sc2"])
        S.op("activation", gt1[0:32, :], gt2[0:32, :], AF.Exp)
        p.ev["sc3"] = S.chan.cur()

    def scan_v2():
        V = p.V
        V.wait(p.ev["sc3"])
        V.op("tensor_tensor", gt1[0:32, :], hgf[0:32, :], gt1[0:32, :], ALU.mult)
        V.op("tensor_tensor_scan", gt3[0:32, :], gt1[0:32, :], zer[0:32, :],
             0.0, ALU.add, ALU.add)
        p.ev["sc4"] = V.chan.cur()

    def scan_st():
        Y = p.Y
        Y.wait(p.ev["sc4"])
        slSm.dma(Y, sm_d[:, :], gt3[0:32, :])

    p.sec("gpsimd", scan_g)
    p.sec("scalar", scan_s1)
    p.sec("vector", scan_v1)
    p.sec("scalar", scan_s2)
    p.sec("vector", scan_v2)
    p.sec("sync", scan_st)
    esC1.close()

    esC2 = ExitStack()
    cost = esC2.enter_context(nc.sbuf_tensor("cost", [128, SEQ], BF16))
    sint = esC2.enter_context(nc.sbuf_tensor("sint", [128, SEQ], BF16))
    bmsk = esC2.enter_context(nc.sbuf_tensor("bmsk", [128, 1024], BF16))
    khrb = esC2.enter_context(nc.sbuf_tensor("khrb", [1, SEQ], BF16))
    vhrb = esC2.enter_context(nc.sbuf_tensor("vhrb", [1, SEQ], BF16))
    khb = esC2.enter_context(nc.sbuf_tensor("khb", [128, SEQ], BF16))
    vhb = esC2.enter_context(nc.sbuf_tensor("vhb", [128, SEQ], BF16))
    vtmp = esC2.enter_context(nc.sbuf_tensor("vtmp", [128, SEQ], BF16))
    rtmp = esC2.enter_context(nc.sbuf_tensor("rtmp", [128, SEQ], BF16))
    rtm2 = esC2.enter_context(nc.sbuf_tensor("rtm2", [128, SEQ], BF16))
    sq2 = esC2.enter_context(nc.sbuf_tensor("sq2", [128, SEQ], F32))
    acc = esC2.enter_context(nc.sbuf_tensor("acc", [128, 2], F32))
    crow = esC2.enter_context(nc.sbuf_tensor("crow", [1, 3, 16], F32))
    cb128 = esC2.enter_context(nc.sbuf_tensor("cb128", [128, 16], F32))
    rsb = esC2.enter_context(nc.sbuf_tensor("rsb", [1, 2, 512], BF16))
    nrm = esC2.enter_context(nc.sbuf_tensor("nrm", [128, 512], F32))
    obuf = esC2.enter_context(nc.sbuf_tensor("obuf", [128, 512], BF16))
    vbuf = [esC2.enter_context(nc.sbuf_tensor("vbuf%d" % i, [128, 16, 128], BF16))
            for i in range(2)]
    atb = esC2.enter_context(nc.sbuf_tensor("atb", [128, 2, 512], BF16))
    esCa = ExitStack()
    psB1 = esCa.enter_context(nc.psum_tensor("psB1", [128, 512], F32))
    psB2 = esCa.enter_context(nc.psum_tensor("psB2", [128, 512], F32))
    psC1 = esCa.enter_context(nc.psum_tensor("psC1", [128, 32], F32))

    def ldrope():
        G = p.G
        G.wait(p.ev["sc4"])
        slR.dma(G, cost[:, :], cosT[:, :])
        slR.dma(G, sint[:, :], sinT[:, :])
        slR.dma(G, bmsk[:, :], bmask[:, :])

    p.sec("gpsimd", ldrope)

    for h in range(16):
        def krow(h=h):
            G = p.G
            G.wait(slSm.done())
            G.wait(p.ev.get(("bev", h - 1)))
            slKr.dma(G, khrb[0:1, :], sm_d[h:h + 1, :])
            slKr.dma(G, vhrb[0:1, :], sm_d[16 + h:17 + h, :])

        def bct(h=h, tb=0):
            T = p.T
            T.wait(slKr.done())
            T.wait(p.ev.get(("bcv", h, tb - 1)) or p.ev.get(("bcv", h - 1, 3)))
            T.raw("matmul", psB1[:, :], ones1[0:1, 0:128],
                  khrb[0:1, tb * 512:(tb + 1) * 512], start=True, stop=True)
            mm = T.raw("matmul", psB2[:, :], ones1[0:1, 0:128],
                       vhrb[0:1, tb * 512:(tb + 1) * 512], start=True, stop=True)
            p.ev[("bct", h, tb)] = p.T.chan.ev(mm)

        def bev(h=h, tb=0):
            V = p.V
            if tb == 0:
                V.wait(p.ev.get(("kv", h - 1)))
            V.wait(p.ev[("bct", h, tb)])
            sl = slice(tb * 512, (tb + 1) * 512)
            V.op("tensor_copy", khb[:, sl], psB1[:, :])
            V.op("tensor_copy", vhb[:, sl], psB2[:, :])
            p.ev[("bcv", h, tb)] = V.chan.cur()
            if tb == 3:
                p.ev[("bev", h)] = V.chan.cur()

        def kvbuild(h=h):
            G = p.G
            G.wait(p.ev[("bev", h)])
            G.wait(slVs.done())
            G.op("tensor_tensor", kf[:, h, :], kpf[:, :], khb[:, :], ALU.mult)
            G.op("tensor_tensor", vtmp[:, :], vpf[:, :], vhb[:, :], ALU.mult)
            p.ev[("kv", h)] = G.chan.cur()

        def vst(h=h):
            Y = p.Y
            Y.wait(p.ev[("kv", h)])
            slVs.dma(Y, v_d[h * 128:(h + 1) * 128, :], vtmp[:, :])

        def rk(h=h):
            V = p.V
            V.wait(slR.done())
            V.wait(p.ev[("kv", h)])
            V.op("tensor_copy", rtmp[0:64, :], kf[64:128, h, :])
            V.op("tensor_copy", rtmp[64:128, :], kf[0:64, h, :])
            V.op("tensor_tensor", rtm2[:, :], kf[:, h, :], cost[:, :], ALU.mult)
            V.op("tensor_tensor", rtmp[:, :], rtmp[:, :], sint[:, :], ALU.mult)
            V.op("tensor_tensor", kf[:, h, :], rtm2[:, :], rtmp[:, :], ALU.add)
            p.ev[("rk", h)] = V.chan.cur()

        p.sec("gpsimd", krow)
        for tb in range(4):
            p.sec("tensor", lambda h=h, tb=tb: bct(h, tb))
            p.sec("vector", lambda h=h, tb=tb: bev(h, tb))
        p.sec("gpsimd", kvbuild)
        p.sec("sync", vst)
        p.sec("vector", rk)

    for h in range(16):
        def rq(h=h):
            V = p.V
            V.wait(slR.done())
            V.wait(p.ev[("q", "ev", h)])
            V.op("tensor_copy", rtmp[0:64, 0:TLOC], qf[64:128, h, :])
            V.op("tensor_copy", rtmp[64:128, 0:TLOC], qf[0:64, h, :])
            V.op("tensor_tensor", rtm2[:, 0:TLOC], qf[:, h, :],
                 cost[:, Q0:Q0 + TLOC], ALU.mult)
            V.op("tensor_tensor", rtmp[:, 0:TLOC], rtmp[:, 0:TLOC],
                 sint[:, Q0:Q0 + TLOC], ALU.mult)
            V.op("tensor_tensor", qf[:, h, :], rtm2[:, 0:TLOC], rtmp[:, 0:TLOC], ALU.add)
            p.ev[("rq", h)] = V.chan.cur()

        p.sec("vector", rq)

    for h in range(16):
        def cb_s(h=h):
            S = p.S
            S.wait(p.ev[("rq", h)])
            S.wait(p.ev[("rk", h)])
            S.wait(p.ev.get(("cb_t", h - 1)))
            S.op("activation", sq2[:, 0:TLOC], qf[:, h, :], AF.Square,
                 accum_out=acc[:, 0:1])
            S.op("activation", sq2[:, 0:SEQ], kf[:, h, :], AF.Square,
                 accum_out=acc[:, 1:2])
            p.ev[("cb_s", h)] = S.chan.cur()

        def cb_t(h=h):
            T = p.T
            T.wait(p.ev[("cb_s", h)])
            T.raw("matmul", psC1[0:1, h:h + 1], ones128f[:, 0:1], acc[:, 0:1],
                  start=True, stop=True)
            mm = T.raw("matmul", psC1[0:1, 16 + h:17 + h], ones128f[:, 0:1],
                       acc[:, 1:2], start=True, stop=True)
            p.ev[("cb_t", h)] = p.T.chan.ev(mm)

        p.sec("scalar", cb_s)
        p.sec("tensor", cb_t)

    def cfin_v():
        V = p.V
        V.wait(p.ev[("cb_t", 15)])
        V.op("tensor_scalar", crow[0:1, 0, :], psC1[0:1, 0:16], 1.0 / TLOC, None, ALU.mult)
        V.op("tensor_scalar", crow[0:1, 1, :], psC1[0:1, 16:32], 1.0 / SEQ, None, ALU.mult)
        V.op("tensor_tensor", crow[0:1, 0, :], crow[0:1, 0, :], crow[0:1, 1, :], ALU.mult)
        p.ev["cfin_v"] = V.chan.cur()

    def cfin_s():
        S = p.S
        S.wait(p.ev["cfin_v"])
        S.op("activation", crow[0:1, 1, :], crow[0:1, 0, :], AF.Ln)
        S.op("activation", crow[0:1, 2, :], crow[0:1, 1, :], AF.Exp, scale=0.5)
        p.ev["cfin_s"] = S.chan.cur()

    def cfin_v2():
        V = p.V
        V.wait(p.ev["cfin_s"])
        V.op("tensor_scalar", crow[0:1, 2, :], crow[0:1, 2, :], CMUL, None, ALU.mult)
        V.op("tensor_copy", rsb[0:1, 0, 0:16], crow[0:1, 2, :])
        p.ev["cfin_v2"] = V.chan.cur()

    def cfin_t():
        T = p.T
        T.wait(p.ev["cfin_v2"])
        mm = T.raw("matmul", psC1[:, 0:16], ones1[0:1, 0:128], rsb[0:1, 0, 0:16],
                   start=True, stop=True)
        p.ev["cfin_t"] = p.T.chan.ev(mm)

    def cfin_v3():
        V = p.V
        V.wait(p.ev["cfin_t"])
        V.op("tensor_copy", cb128[:, :], psC1[:, 0:16])
        p.ev["cbias"] = V.chan.cur()

    p.sec("vector", cfin_v)
    p.sec("scalar", cfin_s)
    p.sec("vector", cfin_v2)
    p.sec("tensor", cfin_t)
    p.sec("vector", cfin_v3)
    esCa.close()
    psSc = [esC2.enter_context(nc.psum_tensor("psSc%d" % i, [128, 512], F32))
            for i in range(2)]
    psN = esC2.enter_context(nc.psum_tensor("psN", [128, 512], F32))
    psO = esC2.enter_context(nc.psum_tensor("psO", [128, 512], F32))
    psRa = esC2.enter_context(nc.psum_tensor("psRa", [1, 512], F32))

    for h in range(16):
        def vload(h=h):
            Y = p.Y
            Y.wait(slVs.done())
            Y.wait(p.ev.get(("av_done", h - 2)))
            for tc in range(16):
                slV[h % 2].dma_t(Y, vbuf[h % 2][:, tc, :],
                                 v_d[h * 128:(h + 1) * 128, tc * 128:(tc + 1) * 128])

        p.sec("sync", vload)

        for qb in range(2):
            qlo = Q0 + qb * 512
            kc_hi = (qlo + 512) // 128
            band_lo = qlo // 128
            last_kc = kc_hi - 1

            for kc in range(kc_hi):
                sp = kc % 2

                def sc_t(h=h, qb=qb, kc=kc, sp=sp):
                    T = p.T
                    T.wait(p.ev["cbias"])
                    T.wait(p.ev[("rq", h)])
                    T.wait(p.ev[("rk", h)])
                    if kc < 2:
                        T.wait(p.ev.get(("onrm", h, qb - 1)) or
                               p.ev.get(("onrm", h - 1, 1)))
                    T.wait(p.ev.get(("ex", h, qb, kc - 2)))
                    mm = T.raw("matmul", psSc[sp][:, :],
                               kf[:, h, kc * 128:(kc + 1) * 128],
                               qf[:, h, qb * 512:(qb + 1) * 512],
                               start=True, stop=True)
                    p.ev[("sc", h, qb, kc)] = p.T.chan.ev(mm)

                def ex_s(h=h, qb=qb, kc=kc, sp=sp):
                    S = p.S
                    S.wait(p.ev[("sc", h, qb, kc)])
                    S.wait(p.ev["cbias"])
                    S.wait(p.ev.get(("rsav", h, qb, kc - 2)))
                    S.op("activation", atb[:, sp, :], psSc[sp][:, :], AF.Exp,
                         scale=SCL, bias=cb128[:, h:h + 1])
                    p.ev[("ex", h, qb, kc)] = S.chan.cur()

                p.sec("tensor", sc_t)
                p.sec("scalar", ex_s)

                if kc >= band_lo:
                    off = 384 - (kc * 128 - qlo)

                    def msk_g(h=h, qb=qb, kc=kc, sp=sp, off=off):
                        G = p.G
                        G.wait(p.ev[("ex", h, qb, kc)])
                        G.wait(slR.done())
                        G.op("tensor_tensor", atb[:, sp, :], atb[:, sp, :],
                             bmsk[:, off:off + 512], ALU.mult)
                        p.ev[("mk", h, qb, kc)] = p.G.chan.cur()

                    p.sec("gpsimd", msk_g)

                def rs_av(h=h, qb=qb, kc=kc, sp=sp, band=(kc >= band_lo),
                          first=(kc == 0), last=(kc == last_kc)):
                    T = p.T
                    if band:
                        T.wait(p.ev[("mk", h, qb, kc)])
                    else:
                        T.wait(p.ev[("ex", h, qb, kc)])
                    T.wait(slV[h % 2].done())
                    if first:
                        T.wait(p.ev.get(("onrm", h, qb - 1)) or
                               p.ev.get(("onrm", h - 1, 1)))
                    T.raw("matmul", psRa[0:1, :], ones128[:, 0:1], atb[:, sp, :],
                          start=first, stop=last)
                    mm = T.raw("matmul", psO[:, :], vbuf[h % 2][:, kc, :],
                               atb[:, sp, :], start=first, stop=last)
                    p.ev[("rsav", h, qb, kc)] = p.T.chan.ev(mm)

                p.sec("tensor", rs_av)

            def onrm_v(h=h, qb=qb):
                V = p.V
                V.wait(p.ev[("rsav", h, qb, last_kc)])
                V.op("reciprocal", nrm[0:1, :], psRa[0:1, :])
                V.op("tensor_copy", rsb[0:1, 1, :], nrm[0:1, :])
                p.ev[("onrm_v", h, qb)] = V.chan.cur()

            def onrm_t(h=h, qb=qb):
                T = p.T
                T.wait(p.ev[("onrm_v", h, qb)])
                mm = T.raw("matmul", psN[:, :], ones1[0:1, 0:128], rsb[0:1, 1, :],
                           start=True, stop=True)
                p.ev[("onrm_t", h, qb)] = p.T.chan.ev(mm)

            def onrm_v2(h=h, qb=qb):
                V = p.V
                V.wait(p.ev[("onrm_t", h, qb)])
                V.wait(slOd.done())
                V.op("tensor_copy", nrm[:, :], psN[:, :])
                V.op("tensor_tensor", obuf[:, :], psO[:, :], nrm[:, :], ALU.mult)
                p.ev[("onrm", h, qb)] = V.chan.cur()

            def ost(h=h, qb=qb):
                Y = p.Y
                Y.wait(p.ev[("onrm", h, qb)])
                slOd.dma(Y, o_d[h * 128:(h + 1) * 128, qb * 512:(qb + 1) * 512],
                         obuf[:, :])
                if qb == 1:
                    p.ev[("av_done", h)] = slOd.done()

            p.sec("vector", onrm_v)
            p.sec("tensor", onrm_t)
            p.sec("vector", onrm_v2)
            p.sec("sync", ost)

    esC2.close()
    esC.close()

    esD = ExitStack()
    obf = [esD.enter_context(nc.sbuf_tensor("obf%d" % i, [128, TLOC], BF16))
           for i in range(2)]
    wob = esD.enter_context(nc.sbuf_tensor("wob", [128, 16, 512], BF16))
    xhb = esD.enter_context(nc.sbuf_tensor("xhb", [128, 512], F32))
    xot = esD.enter_context(nc.sbuf_tensor("xot", [128, 512], F32))
    psJ = [esD.enter_context(nc.psum_tensor("psJ%d" % i, [128, 512], F32))
           for i in range(8)]

    for oblk in range(4):
        def woload(oblk=oblk):
            G = p.G
            G.wait(p.ev[("onrm", 15, 1)])
            G.wait(p.ev.get(("oj_mm", oblk - 1, 15)))
            slWo.dma(G, wob[:, :, :],
                     woT[:, oblk * 512:(oblk + 1) * 512]
                     .rearrange("(ec p) c -> p ec c", p=128))

        p.sec("gpsimd", woload)

        for ec in range(16):
            def oload(oblk=oblk, ec=ec):
                Y = p.Y
                Y.wait(p.ev[("onrm", 15, 1)])
                Y.wait(slOd.done())
                Y.wait(p.ev.get(("oj_mm", oblk, ec - 2)))
                slOb[ec % 2].dma(Y, obf[ec % 2][:, :], o_d[ec * 128:(ec + 1) * 128, :])

            def oj_mm(oblk=oblk, ec=ec):
                T = p.T
                T.wait(p.ev[("onrm", 15, 1)])
                T.wait(slOb[ec % 2].done())
                T.wait(slWo.done())
                if ec == 0:
                    T.wait(p.ev.get(("oj_v", oblk - 1, 7)))
                mm = None
                for tc in range(8):
                    mm = T.raw("matmul", psJ[tc][:, :], obf[ec % 2][:, tc * 128:(tc + 1) * 128],
                               wob[:, ec, :], start=(ec == 0), stop=(ec == 15))
                p.ev[("oj_mm", oblk, ec)] = p.T.chan.ev(mm)

            p.sec("sync", oload)
            p.sec("tensor", oj_mm)

        for tc in range(8):
            def xld(oblk=oblk, tc=tc):
                Y = p.Y
                Y.wait(p.ev[("onrm", 15, 1)])
                Y.wait(p.ev.get(("oj_st", oblk, tc - 1)))
                slH.dma(Y, xhb[:, :], xh[tc * 128:(tc + 1) * 128,
                                        oblk * 512:(oblk + 1) * 512])

            def oj_v(oblk=oblk, tc=tc):
                V = p.V
                V.wait(p.ev[("oj_mm", oblk, 15)])
                V.wait(slH.done())
                V.wait(slO.done())
                V.op("tensor_tensor", xot[:, :], psJ[tc][:, :], xhb[:, :], ALU.add)
                p.ev[("oj_v", oblk, tc)] = V.chan.cur()

            def oj_st(oblk=oblk, tc=tc):
                Y = p.Y
                Y.wait(p.ev[("oj_v", oblk, tc)])
                slO.dma(Y, xo[tc * 128:(tc + 1) * 128, oblk * 512:(oblk + 1) * 512],
                        xot[:, :])
                p.ev[("oj_st", oblk, tc)] = slO.done()

            p.sec("sync", xld)
            p.sec("vector", oj_v)
            p.sec("sync", oj_st)

    p.sec("sync", lambda: p.Y.wait(slO.done()))
    esD.close()
    return p.fin()


def build_attn_a0():
    from contextlib import ExitStack
    p = Prog("attna0").start()
    nc = p.nc
    SEQ = 2048
    EL = 1024
    xT = nc.dram_tensor("xT", [D, SEQ], F32, kind="ExternalInput").ap()
    wq = nc.dram_tensor("wq", [D, EL], F32, kind="ExternalInput").ap()
    wk = nc.dram_tensor("wk", [D, D], F32, kind="ExternalInput").ap()
    wv = nc.dram_tensor("wv", [D, D], F32, kind="ExternalInput").ap()
    wo = nc.dram_tensor("wo", [EL, D], F32, kind="ExternalInput").ap()
    py = nc.dram_tensor("py", [SEQ, D], BF16, kind="ExternalOutput").ap()
    q_d = nc.dram_tensor("q_d", [SEQ, EL], BF16).ap()
    k_d = nc.dram_tensor("k_d", [SEQ, D], BF16).ap()
    v_d = nc.dram_tensor("v_d", [D, SEQ], BF16).ap()
    a_d = nc.dram_tensor("a_d", [EL, SEQ], BF16).ap()

    slX = p.slot("slX")
    slW = p.slot("slW")
    slQs = p.slot("slQs")
    slKs = p.slot("slKs")
    slVs = p.slot("slVs")
    slA = p.slot("slA")
    slKr = p.slot("slKr")
    slVr = p.slot("slVr")
    slQc = p.slot("slQc")
    slT = p.slot("slT")
    slWo = p.slot("slWo")
    slP = p.slot("slP")

    xc = p.sb("xc", [128, SEQ], F32)
    ones1 = p.sb("ones1", [1, 512], BF16)
    ones128f = p.sb("ones128f", [128, 1], F32)
    epsln = p.sb("epsln", [128, 1], F32)
    srwb = p.sb("srwb", [1, 2, SEQ], BF16)

    def init():
        G = p.G
        G.op("memset", ones1[:, :], 1.0)
        G.op("memset", ones128f[:, :], 1.0)
        G.op("memset", epsln[:, :], LN_EPS)
        p.ev["init"] = p.G.chan.cur()

    p.sec("gpsimd", init)

    esA = ExitStack()
    sqc = esA.enter_context(nc.sbuf_tensor("sqc", [128, SEQ], F32))
    srow = esA.enter_context(nc.sbuf_tensor("srow", [1, 3 * SEQ], F32))
    psSb = [esA.enter_context(nc.psum_tensor("psSb%d" % b, [1, 512], F32)) for b in range(4)]
    psS2b = [esA.enter_context(nc.psum_tensor("psS2b%d" % b, [1, 512], F32)) for b in range(4)]

    for c in range(16):
        def ldx(c=c):
            Y = p.Y
            Y.wait(p.ev.get(("st_s", c - 1)))
            Y.wait(p.ev.get(("st_t1", c - 1)))
            slX.dma(Y, xc[:, :], xT[c * 128:(c + 1) * 128, :])

        def st_t1(c=c):
            T = p.T
            T.wait(slX.done())
            T.wait(p.ev["init"])
            mm = None
            for b in range(4):
                mm = T.raw("matmul", psSb[b][0:1, :], ones128f[:, 0:1],
                           xc[:, b * 512:(b + 1) * 512], start=(c == 0), stop=(c == 15))
            p.ev[("st_t1", c)] = p.T.chan.ev(mm)

        def st_s(c=c):
            S = p.S
            S.wait(slX.done())
            S.wait(p.ev.get(("st_t2", c - 1)))
            S.op("activation", sqc[:, :], xc[:, :], AF.Square)
            p.ev[("st_s", c)] = S.chan.cur()

        def st_t2(c=c):
            T = p.T
            T.wait(p.ev[("st_s", c)])
            mm = None
            for b in range(4):
                mm = T.raw("matmul", psS2b[b][0:1, :], ones128f[:, 0:1],
                           sqc[:, b * 512:(b + 1) * 512], start=(c == 0), stop=(c == 15))
            p.ev[("st_t2", c)] = p.T.chan.ev(mm)

        p.sec("sync", ldx)
        p.sec("tensor", st_t1)
        p.sec("scalar", st_s)
        p.sec("tensor", st_t2)

    def fin_v():
        V = p.V
        V.wait(p.ev[("st_t1", 15)])
        V.wait(p.ev[("st_t2", 15)])
        for b in range(4):
            sl = slice(b * 512, (b + 1) * 512)
            V.op("tensor_scalar", srow[0:1, sl], psSb[b][0:1, :], 1.0 / D, None, ALU.mult)
            V.op("tensor_scalar", srow[0:1, 2048 + b * 512:2048 + (b + 1) * 512],
                 psS2b[b][0:1, :], 1.0 / D, None, ALU.mult)
        V.op("tensor_tensor", srow[0:1, 4096:6144], srow[0:1, 0:2048],
             srow[0:1, 0:2048], ALU.mult)
        V.op("tensor_sub", srow[0:1, 2048:4096], srow[0:1, 2048:4096],
             srow[0:1, 4096:6144])
        p.ev["fin_v"] = V.chan.cur()

    def fin_s():
        S = p.S
        S.wait(p.ev["fin_v"])
        S.wait(p.ev["init"])
        S.op("activation", srow[0:1, 2048:4096], srow[0:1, 2048:4096], AF.Ln,
             bias=epsln[0:1, 0:1])
        S.op("activation", srwb[0:1, 0, :], srow[0:1, 2048:4096], AF.Exp, scale=-0.5)
        S.op("activation", srow[0:1, 4096:6144], srow[0:1, 2048:4096], AF.Exp, scale=-0.5)
        p.ev["fin_s"] = S.chan.cur()

    def fin_v3():
        V = p.V
        V.wait(p.ev["fin_s"])
        V.op("tensor_tensor", srow[0:1, 2048:4096], srow[0:1, 0:2048],
             srow[0:1, 4096:6144], ALU.mult)
        V.op("tensor_scalar", srwb[0:1, 1, :], srow[0:1, 2048:4096], -1.0, None, ALU.mult)
        p.ev["stats"] = V.chan.cur()

    p.sec("vector", fin_v)
    p.sec("scalar", fin_s)
    p.sec("vector", fin_v3)
    esA.close()

    esB = ExitStack()
    xlnf = esB.enter_context(nc.sbuf_tensor("xlnf", [128, 16, SEQ], BF16))
    psBc = [esB.enter_context(nc.psum_tensor("psBc%d" % i, [128, 512], F32)) for i in range(2)]
    psPj = [esB.enter_context(nc.psum_tensor("psPj%d" % i, [128, 512], F32)) for i in range(2)]
    rnb = esB.enter_context(nc.sbuf_tensor("rnb", [128, 2, SEQ], BF16))
    wpb = esB.enter_context(nc.sbuf_tensor("wpb", [128, 16, 512], BF16))
    wvb = esB.enter_context(nc.sbuf_tensor("wvb", [128, 16, 128], BF16))
    etb = esB.enter_context(nc.sbuf_tensor("etb", [128, 2, 512], BF16))

    for b in range(4):
        def bc_t(b=b):
            T = p.T
            T.wait(p.ev["stats"])
            T.wait(p.ev.get(("bc_v", b - 1)))
            T.raw("matmul", psBc[0][:, :], ones1[0:1, 0:128],
                  srwb[0:1, 0, b * 512:(b + 1) * 512], start=True, stop=True)
            m2 = T.raw("matmul", psBc[1][:, :], ones1[0:1, 0:128],
                       srwb[0:1, 1, b * 512:(b + 1) * 512], start=True, stop=True)
            p.ev[("bc_t", b)] = p.T.chan.ev(m2)

        def bc_v(b=b):
            V = p.V
            V.wait(p.ev[("bc_t", b)])
            V.op("tensor_copy", rnb[:, 0, b * 512:(b + 1) * 512], psBc[0][:, :])
            V.op("tensor_copy", rnb[:, 1, b * 512:(b + 1) * 512], psBc[1][:, :])
            p.ev[("bc_v", b)] = V.chan.cur()

        p.sec("tensor", bc_t)
        p.sec("vector", bc_v)

    for c in range(16):
        def ldx2(c=c):
            Y = p.Y
            Y.wait(p.ev["stats"])
            Y.wait(p.ev.get(("ap_v", c - 1)))
            slX.dma(Y, xc[:, :], xT[c * 128:(c + 1) * 128, :])

        def ap_v(c=c):
            V = p.V
            V.wait(slX.done())
            V.wait(p.ev[("bc_v", 3)])
            V.op("tensor_tensor", xc[:, :], xc[:, :], rnb[:, 0, :], ALU.mult)
            V.op("tensor_tensor", xlnf[:, c, :], xc[:, :], rnb[:, 1, :], ALU.add)
            p.ev[("ap_v", c)] = V.chan.cur()

        p.sec("sync", ldx2)
        p.sec("vector", ap_v)

    xln_ev = lambda: p.ev[("ap_v", 15)]

    def emit_tm_proj(tag, wsrc, dst, nblk, slSt, prevtag, prevslot=None):
        for eb in range(nblk):
            def wload(eb=eb):
                G = p.G
                if eb == 0 and prevtag is not None:
                    G.wait(p.ev[(prevtag + "mm", None, 15)])
                G.wait(p.ev.get((tag + "mm", eb - 1, 15)))
                slW.dma(G, wpb[:, :, :],
                        wsrc[:, eb * 512:(eb + 1) * 512]
                        .rearrange("(kc p) c -> p kc c", p=128))

            p.sec("gpsimd", wload)
            for tc in range(16):
                def mmf(tag=tag, eb=eb, tc=tc):
                    T = p.T
                    T.wait(slW.done())
                    T.wait(xln_ev())
                    T.wait(p.ev.get((tag + "ev", eb, tc - 2)) or
                           (p.ev.get((tag + "ev", eb - 1, 14 + tc)) if tc < 2 else None))
                    if eb == 0 and tc < 2 and prevtag is not None:
                        T.wait(p.ev[(prevtag + "ev", None, 14 + tc)])
                    mm = None
                    for kc in range(16):
                        mm = T.raw("matmul", psPj[tc % 2][:, :],
                                   xlnf[:, kc, tc * 128:(tc + 1) * 128],
                                   wpb[:, kc, :], start=(kc == 0), stop=(kc == 15))
                    p.ev[(tag + "mm", eb, tc)] = p.T.chan.ev(mm)
                    if tc == 15:
                        p.ev[(tag + "mm", None, 15)] = p.ev[(tag + "mm", eb, tc)]

                def evf(tag=tag, eb=eb, tc=tc):
                    V = p.V
                    V.wait(p.ev[(tag + "mm", eb, tc)])
                    V.wait(slSt.done())
                    if eb == 0 and tc < 2 and prevslot is not None:
                        V.wait(prevslot.done())
                    V.op("tensor_copy", etb[:, tc % 2, :], psPj[tc % 2][:, :])
                    p.ev[(tag + "ev", eb, tc)] = V.chan.cur()
                    if eb == nblk - 1 and tc >= 14:
                        p.ev[(tag + "ev", None, tc)] = p.ev[(tag + "ev", eb, tc)]

                def stf(tag=tag, eb=eb, tc=tc):
                    Y = p.Y
                    Y.wait(p.ev[(tag + "ev", eb, tc)])
                    slSt.dma(Y, dst[tc * 128:(tc + 1) * 128, eb * 512:(eb + 1) * 512],
                             etb[:, tc % 2, :])

                p.sec("tensor", mmf)
                p.sec("vector", evf)
                p.sec("sync", stf)

    emit_tm_proj("q", wq, q_d, 2, slQs, None)
    emit_tm_proj("k", wk, k_d, 4, slKs, "q", slQs)

    for fc in range(16):
        def wvload(fc=fc):
            G = p.G
            if fc == 0:
                G.wait(p.ev[("kmm", None, 15)])
            G.wait(p.ev.get(("vmm", fc - 1, 3)))
            slW.dma(G, wvb[:, :, :],
                    wv[:, fc * 128:(fc + 1) * 128].rearrange("(kc p) c -> p kc c", p=128))

        p.sec("gpsimd", wvload)
        for sb in range(4):
            def vmm(fc=fc, sb=sb):
                T = p.T
                T.wait(slW.done())
                T.wait(xln_ev())
                if fc == 0 and sb < 2:
                    T.wait(p.ev[("kev", None, 14 + sb)])
                T.wait(p.ev.get(("vev", fc, sb - 2)) or
                       (p.ev.get(("vev", fc - 1, 2 + sb)) if sb < 2 else None))
                mm = None
                for kc in range(16):
                    mm = T.raw("matmul", psPj[sb % 2][:, :], wvb[:, kc, :],
                               xlnf[:, kc, sb * 512:(sb + 1) * 512],
                               start=(kc == 0), stop=(kc == 15))
                p.ev[("vmm", fc, sb)] = p.T.chan.ev(mm)

            def vev(fc=fc, sb=sb):
                V = p.V
                V.wait(p.ev[("vmm", fc, sb)])
                V.wait(slVs.done())
                if fc == 0 and sb < 2:
                    V.wait(slKs.done())
                V.op("tensor_copy", etb[:, sb % 2, :], psPj[sb % 2][:, :])
                p.ev[("vev", fc, sb)] = V.chan.cur()

            def vst(fc=fc, sb=sb):
                Y = p.Y
                Y.wait(p.ev[("vev", fc, sb)])
                slVs.dma(Y, v_d[fc * 128:(fc + 1) * 128, sb * 512:(sb + 1) * 512],
                         etb[:, sb % 2, :])

            p.sec("tensor", vmm)
            p.sec("vector", vev)
            p.sec("sync", vst)

    p.ev["proj_mm_last"] = p.ev[("vmm", 15, 3)]
    esB.close()

    esC = ExitStack()
    vres = esC.enter_context(nc.sbuf_tensor("vres", [128, 16, SEQ], BF16))
    esC1 = ExitStack()
    kres = esC1.enter_context(nc.sbuf_tensor("kres", [128, 16, SEQ], BF16))
    qcb = esC1.enter_context(nc.sbuf_tensor("qcb", [128, 16, 128], BF16))
    af = esC1.enter_context(nc.sbuf_tensor("af", [128, SEQ], F32))
    ab = esC1.enter_context(nc.sbuf_tensor("ab", [128, SEQ], BF16))
    mx4 = esC1.enter_context(nc.sbuf_tensor("mx4", [128, 8], F32))
    racc = esC1.enter_context(nc.sbuf_tensor("racc", [128, 6], F32))
    psS = [esC1.enter_context(nc.psum_tensor("psS%d" % i, [128, 512], F32))
           for i in range(4)]

    def ldkv():
        Y = p.Y
        Y.wait(p.ev["proj_mm_last"])
        Y.wait(p.ev[("vev", 15, 3)])
        Y.wait(slQs.done())
        Y.wait(slKs.done())
        Y.wait(slVs.done())
        slKr.dma(Y, kres[:, :, :], k_d.rearrange("(sc p) f -> p sc f", p=128))
        slVr.dma(Y, vres[:, :, :], v_d.rearrange("(fc p) s -> p fc s", p=128))

    p.sec("sync", ldkv)

    for ec in range(8):
        def qld(ec=ec):
            Y = p.Y
            Y.wait(slQs.done())
            Y.wait(p.ev.get(("samm", ec - 1, 3)))
            slQc.dma(Y, qcb[:, :, :],
                     q_d[:, ec * 128:(ec + 1) * 128].rearrange("(sc p) e -> p sc e", p=128))

        p.sec("sync", qld)

        for fb in range(4):
            def samm(ec=ec, fb=fb):
                T = p.T
                T.wait(slQc.done())
                T.wait(slKr.done())
                T.wait(p.ev.get(("ex", ec - 1, fb)))
                mm = None
                for sc in range(16):
                    mm = T.raw("matmul", psS[fb][:, :], qcb[:, sc, :],
                               kres[:, sc, fb * 512:(fb + 1) * 512],
                               start=(sc == 0), stop=(sc == 15))
                p.ev[("samm", ec, fb)] = p.T.chan.ev(mm)

            p.sec("tensor", samm)

        def stat_v(ec=ec):
            V = p.V
            V.wait(p.ev[("samm", ec, 3)])
            for fb in range(4):
                V.op("tensor_reduce", mx4[:, fb:fb + 1], psS[fb][:, :],
                     axis=AX.X, op=ALU.max)
            V.op("tensor_tensor", mx4[:, 4:5], mx4[:, 0:1], mx4[:, 1:2], ALU.max)
            V.op("tensor_tensor", mx4[:, 5:6], mx4[:, 2:3], mx4[:, 3:4], ALU.max)
            V.op("tensor_tensor", mx4[:, 6:7], mx4[:, 4:5], mx4[:, 5:6], ALU.max)
            V.op("tensor_scalar", mx4[:, 7:8], mx4[:, 6:7], -0.125, None, ALU.mult)
            p.ev[("mx", ec)] = V.chan.cur()

        def ex_s(ec=ec):
            S = p.S
            S.wait(p.ev[("mx", ec)])
            for fb in range(4):
                S.op("activation", af[:, fb * 512:(fb + 1) * 512], psS[fb][:, :],
                     AF.Exp, scale=0.125, bias=mx4[:, 7:8],
                     accum_out=racc[:, fb:fb + 1])
                p.ev[("ex", ec, fb)] = S.chan.cur()
            p.ev[("exd", ec)] = S.chan.cur()

        def ab_v(ec=ec):
            V = p.V
            V.wait(p.ev[("exd", ec)])
            V.wait(slA.done())
            V.op("tensor_tensor", racc[:, 4:5], racc[:, 0:1], racc[:, 1:2], ALU.add)
            V.op("tensor_tensor", racc[:, 5:6], racc[:, 2:3], racc[:, 3:4], ALU.add)
            V.op("tensor_tensor", racc[:, 4:5], racc[:, 4:5], racc[:, 5:6], ALU.add)
            V.op("reciprocal", racc[:, 5:6], racc[:, 4:5])
            V.op("tensor_scalar", ab[:, :], af[:, :], racc[:, 5:6], None, ALU.mult)
            p.ev[("ab", ec)] = V.chan.cur()

        def ast(ec=ec):
            Y = p.Y
            Y.wait(p.ev[("ab", ec)])
            slA.dma(Y, a_d[ec * 128:(ec + 1) * 128, :], ab[:, :])

        p.sec("vector", stat_v)
        p.sec("scalar", ex_s)
        p.sec("vector", ab_v)
        p.sec("sync", ast)

    esC1.close()

    esC2 = ExitStack()
    atb = esC2.enter_context(nc.sbuf_tensor("atb", [128, 16, EL], BF16))
    ofm = esC2.enter_context(nc.sbuf_tensor("ofm", [128, 8, SEQ], BF16))
    wob = esC2.enter_context(nc.sbuf_tensor("wob", [128, 8, 512], BF16))
    pyt = esC2.enter_context(nc.sbuf_tensor("pyt", [128, 2, 512], BF16))
    psO = [esC2.enter_context(nc.psum_tensor("psO%d" % i, [128, 512], F32))
           for i in range(2)]
    psJ = [esC2.enter_context(nc.psum_tensor("psJ%d" % i, [128, 512], F32))
           for i in range(2)]

    def atl():
        Y = p.Y
        Y.wait(slA.done())
        for fc in range(16):
            slT.dma_t(Y, atb[:, fc, :], a_d[:, fc * 128:(fc + 1) * 128])

    p.sec("sync", atl)

    for ec in range(8):
        for sb in range(4):
            def avmm(ec=ec, sb=sb):
                T = p.T
                T.wait(slT.done())
                T.wait(slVr.done())
                T.wait(p.ev.get(("ofme", ec, sb - 2)) or
                       (p.ev.get(("ofme", ec - 1, 2 + sb)) if sb < 2 else None))
                mm = None
                for fc in range(16):
                    mm = T.raw("matmul", psO[sb % 2][:, :],
                               atb[:, fc, ec * 128:(ec + 1) * 128],
                               vres[:, fc, sb * 512:(sb + 1) * 512],
                               start=(fc == 0), stop=(fc == 15))
                p.ev[("avmm", ec, sb)] = p.T.chan.ev(mm)

            def ofme(ec=ec, sb=sb):
                V = p.V
                V.wait(p.ev[("avmm", ec, sb)])
                V.op("tensor_copy", ofm[:, ec, sb * 512:(sb + 1) * 512], psO[sb % 2][:, :])
                p.ev[("ofme", ec, sb)] = V.chan.cur()

            p.sec("tensor", avmm)
            p.sec("vector", ofme)

    for oc in range(4):
        def pwload(oc=oc):
            G = p.G
            G.wait(slA.done())
            G.wait(p.ev.get(("pjmm", oc - 1, 15)))
            slWo.dma(G, wob[:, :, :],
                     wo[:, oc * 512:(oc + 1) * 512].rearrange("(ec p) c -> p ec c", p=128))

        p.sec("gpsimd", pwload)
        for tc in range(16):
            def pjmm(oc=oc, tc=tc):
                T = p.T
                T.wait(slWo.done())
                T.wait(p.ev[("ofme", 7, 3)])
                T.wait(p.ev.get(("pjev", oc, tc - 2)) or
                       (p.ev.get(("pjev", oc - 1, 14 + tc)) if tc < 2 else None))
                mm = None
                for ec in range(8):
                    mm = T.raw("matmul", psJ[tc % 2][:, :],
                               ofm[:, ec, tc * 128:(tc + 1) * 128],
                               wob[:, ec, :], start=(ec == 0), stop=(ec == 7))
                p.ev[("pjmm", oc, tc)] = p.T.chan.ev(mm)

            def pjev(oc=oc, tc=tc):
                V = p.V
                V.wait(p.ev[("pjmm", oc, tc)])
                V.wait(slP.done())
                V.op("tensor_copy", pyt[:, tc % 2, :], psJ[tc % 2][:, :])
                p.ev[("pjev", oc, tc)] = V.chan.cur()

            def pjst(oc=oc, tc=tc):
                Y = p.Y
                Y.wait(p.ev[("pjev", oc, tc)])
                slP.dma(Y, py[tc * 128:(tc + 1) * 128, oc * 512:(oc + 1) * 512],
                        pyt[:, tc % 2, :])

            p.sec("tensor", pjmm)
            p.sec("vector", pjev)
            p.sec("sync", pjst)

    p.sec("sync", lambda: p.Y.wait(slP.done()))
    esC2.close()
    esC.close()
    return p.fin()


def build_attn_a1():
    from contextlib import ExitStack
    p = Prog("attna1").start()
    nc = p.nc
    SEQ = 2048
    SCL2 = 0.08838834764831845
    xT = nc.dram_tensor("xT", [D, SEQ], F32, kind="ExternalInput").ap()
    xh = nc.dram_tensor("xh", [TLOC, D], F32, kind="ExternalInput").ap()
    wq = nc.dram_tensor("wq", [D, 1024], F32, kind="ExternalInput").ap()
    wk = nc.dram_tensor("wk", [D, D], F32, kind="ExternalInput").ap()
    wv = nc.dram_tensor("wv", [D, D], F32, kind="ExternalInput").ap()
    wp = nc.dram_tensor("wp", [D, D], F32, kind="ExternalInput").ap()
    bq = nc.dram_tensor("bq", [1, 1024], F32, kind="ExternalInput").ap()
    bk = nc.dram_tensor("bk", [1, D], F32, kind="ExternalInput").ap()
    bv = nc.dram_tensor("bv", [1, D], F32, kind="ExternalInput").ap()
    bp = nc.dram_tensor("bp", [1, D], F32, kind="ExternalInput").ap()
    xo = nc.dram_tensor("xo", [TLOC, D], F32, kind="ExternalOutput").ap()
    q_d = nc.dram_tensor("q_d", [SEQ, 1024], BF16).ap()
    k_d = nc.dram_tensor("k_d", [SEQ, D], BF16).ap()
    v_d = nc.dram_tensor("v_d", [SEQ, D], BF16).ap()
    ctx_d = nc.dram_tensor("ctx_d", [SEQ, 1024], BF16).ap()

    slX = p.slot("slX")
    slW = p.slot("slW")
    slB = p.slot("slB")
    slQs = p.slot("slQs")
    slKs = p.slot("slKs")
    slVs = p.slot("slVs")
    slQc = p.slot("slQc")
    slKc = p.slot("slKc")
    slVc = p.slot("slVc")
    slCs = p.slot("slCs")
    slT = p.slot("slT")
    slWo = p.slot("slWo")
    slH = p.slot("slH")
    slO = p.slot("slO")

    xc = p.sb("xc", [128, SEQ], F32)
    ones1 = p.sb("ones1", [1, 512], BF16)
    ones128f = p.sb("ones128f", [128, 1], F32)
    epsln = p.sb("epsln", [128, 1], F32)
    srwb = p.sb("srwb", [1, 2, SEQ], BF16)
    brow = p.sb("brow", [1, 4, D], BF16)

    def init():
        G = p.G
        G.op("memset", ones1[:, :], 1.0)
        G.op("memset", ones128f[:, :], 1.0)
        G.op("memset", epsln[:, :], LN_EPS)
        slB.dma(G, brow[:, 0, 0:1024], bq[:, :])
        slB.dma(G, brow[:, 1, :], bk[:, :])
        slB.dma(G, brow[:, 2, :], bv[:, :])
        slB.dma(G, brow[:, 3, :], bp[:, :])
        p.ev["init"] = p.G.chan.cur()

    p.sec("gpsimd", init)

    esA = ExitStack()
    sqc = esA.enter_context(nc.sbuf_tensor("sqc", [128, SEQ], F32))
    srow = esA.enter_context(nc.sbuf_tensor("srow", [1, 3 * SEQ], F32))
    psSb = [esA.enter_context(nc.psum_tensor("psSb%d" % b, [1, 512], F32)) for b in range(4)]
    psS2b = [esA.enter_context(nc.psum_tensor("psS2b%d" % b, [1, 512], F32)) for b in range(4)]

    for c in range(16):
        def ldx(c=c):
            Y = p.Y
            Y.wait(p.ev.get(("st_s", c - 1)))
            Y.wait(p.ev.get(("st_t1", c - 1)))
            slX.dma(Y, xc[:, :], xT[c * 128:(c + 1) * 128, :])

        def st_t1(c=c):
            T = p.T
            T.wait(slX.done())
            T.wait(p.ev["init"])
            mm = None
            for b in range(4):
                mm = T.raw("matmul", psSb[b][0:1, :], ones128f[:, 0:1],
                           xc[:, b * 512:(b + 1) * 512], start=(c == 0), stop=(c == 15))
            p.ev[("st_t1", c)] = p.T.chan.ev(mm)

        def st_s(c=c):
            S = p.S
            S.wait(slX.done())
            S.wait(p.ev.get(("st_t2", c - 1)))
            S.op("activation", sqc[:, :], xc[:, :], AF.Square)
            p.ev[("st_s", c)] = S.chan.cur()

        def st_t2(c=c):
            T = p.T
            T.wait(p.ev[("st_s", c)])
            mm = None
            for b in range(4):
                mm = T.raw("matmul", psS2b[b][0:1, :], ones128f[:, 0:1],
                           sqc[:, b * 512:(b + 1) * 512], start=(c == 0), stop=(c == 15))
            p.ev[("st_t2", c)] = p.T.chan.ev(mm)

        p.sec("sync", ldx)
        p.sec("tensor", st_t1)
        p.sec("scalar", st_s)
        p.sec("tensor", st_t2)

    def fin_v():
        V = p.V
        V.wait(p.ev[("st_t1", 15)])
        V.wait(p.ev[("st_t2", 15)])
        for b in range(4):
            sl = slice(b * 512, (b + 1) * 512)
            V.op("tensor_scalar", srow[0:1, sl], psSb[b][0:1, :], 1.0 / D, None, ALU.mult)
            V.op("tensor_scalar", srow[0:1, 2048 + b * 512:2048 + (b + 1) * 512],
                 psS2b[b][0:1, :], 1.0 / D, None, ALU.mult)
        V.op("tensor_tensor", srow[0:1, 4096:6144], srow[0:1, 0:2048],
             srow[0:1, 0:2048], ALU.mult)
        V.op("tensor_sub", srow[0:1, 2048:4096], srow[0:1, 2048:4096],
             srow[0:1, 4096:6144])
        p.ev["fin_v"] = V.chan.cur()

    def fin_s():
        S = p.S
        S.wait(p.ev["fin_v"])
        S.wait(p.ev["init"])
        S.op("activation", srow[0:1, 2048:4096], srow[0:1, 2048:4096], AF.Ln,
             bias=epsln[0:1, 0:1])
        S.op("activation", srwb[0:1, 0, :], srow[0:1, 2048:4096], AF.Exp, scale=-0.5)
        S.op("activation", srow[0:1, 4096:6144], srow[0:1, 2048:4096], AF.Exp, scale=-0.5)
        p.ev["fin_s"] = S.chan.cur()

    def fin_v3():
        V = p.V
        V.wait(p.ev["fin_s"])
        V.op("tensor_tensor", srow[0:1, 2048:4096], srow[0:1, 0:2048],
             srow[0:1, 4096:6144], ALU.mult)
        V.op("tensor_scalar", srwb[0:1, 1, :], srow[0:1, 2048:4096], -1.0, None, ALU.mult)
        p.ev["stats"] = V.chan.cur()

    p.sec("vector", fin_v)
    p.sec("scalar", fin_s)
    p.sec("vector", fin_v3)
    esA.close()

    esB = ExitStack()
    xlnf = esB.enter_context(nc.sbuf_tensor("xlnf", [128, 16, SEQ], BF16))
    psBc = [esB.enter_context(nc.psum_tensor("psBc%d" % i, [128, 512], F32)) for i in range(2)]
    psPj = [esB.enter_context(nc.psum_tensor("psPj%d" % i, [128, 512], F32)) for i in range(2)]
    rnb = esB.enter_context(nc.sbuf_tensor("rnb", [128, 2, SEQ], BF16))
    wpb = esB.enter_context(nc.sbuf_tensor("wpb", [128, 16, 512], BF16))
    etb = esB.enter_context(nc.sbuf_tensor("etb", [128, 2, 512], BF16))

    for b in range(4):
        def bc_t(b=b):
            T = p.T
            T.wait(p.ev["stats"])
            T.wait(p.ev.get(("bc_v", b - 1)))
            T.raw("matmul", psBc[0][:, :], ones1[0:1, 0:128],
                  srwb[0:1, 0, b * 512:(b + 1) * 512], start=True, stop=True)
            m2 = T.raw("matmul", psBc[1][:, :], ones1[0:1, 0:128],
                       srwb[0:1, 1, b * 512:(b + 1) * 512], start=True, stop=True)
            p.ev[("bc_t", b)] = p.T.chan.ev(m2)

        def bc_v(b=b):
            V = p.V
            V.wait(p.ev[("bc_t", b)])
            V.op("tensor_copy", rnb[:, 0, b * 512:(b + 1) * 512], psBc[0][:, :])
            V.op("tensor_copy", rnb[:, 1, b * 512:(b + 1) * 512], psBc[1][:, :])
            p.ev[("bc_v", b)] = V.chan.cur()

        p.sec("tensor", bc_t)
        p.sec("vector", bc_v)

    for c in range(16):
        def ldx2(c=c):
            Y = p.Y
            Y.wait(p.ev["stats"])
            Y.wait(p.ev.get(("ap_v", c - 1)))
            slX.dma(Y, xc[:, :], xT[c * 128:(c + 1) * 128, :])

        def ap_v(c=c):
            V = p.V
            V.wait(slX.done())
            V.wait(p.ev[("bc_v", 3)])
            V.op("tensor_tensor", xc[:, :], xc[:, :], rnb[:, 0, :], ALU.mult)
            V.op("tensor_tensor", xlnf[:, c, :], xc[:, :], rnb[:, 1, :], ALU.add)
            p.ev[("ap_v", c)] = V.chan.cur()

        p.sec("sync", ldx2)
        p.sec("vector", ap_v)

    xln_ev = lambda: p.ev[("ap_v", 15)]

    projs = [("q", wq, q_d, 2, slQs, 0, None, None),
             ("k", wk, k_d, 4, slKs, 1, "q", slQs),
             ("v", wv, v_d, 4, slVs, 2, "k", slKs)]
    for tag, wsrc, dst, nblk, slSt, bidx, prevtag, prevslot in projs:
        for eb in range(nblk):
            def wload(tag=tag, wsrc=wsrc, eb=eb, prevtag=prevtag):
                G = p.G
                if eb == 0 and prevtag is not None:
                    G.wait(p.ev[(prevtag + "mm", None, 15)])
                G.wait(p.ev.get((tag + "mm", eb - 1, 15)))
                slW.dma(G, wpb[:, :, :],
                        wsrc[:, eb * 512:(eb + 1) * 512]
                        .rearrange("(kc p) c -> p kc c", p=128))

            p.sec("gpsimd", wload)
            for tc in range(16):
                def mmf(tag=tag, eb=eb, tc=tc, bidx=bidx, prevtag=prevtag, nblk=nblk):
                    T = p.T
                    T.wait(slW.done())
                    T.wait(slB.done())
                    T.wait(xln_ev())
                    T.wait(p.ev.get((tag + "ev", eb, tc - 2)) or
                           (p.ev.get((tag + "ev", eb - 1, 14 + tc)) if tc < 2 else None))
                    if eb == 0 and tc < 2 and prevtag is not None:
                        T.wait(p.ev[(prevtag + "ev", None, 14 + tc)])
                    for kc in range(16):
                        T.raw("matmul", psPj[tc % 2][:, :],
                              xlnf[:, kc, tc * 128:(tc + 1) * 128],
                              wpb[:, kc, :], start=(kc == 0), stop=False)
                    mm = T.raw("matmul", psPj[tc % 2][:, :], ones1[0:1, 0:128],
                               brow[0:1, bidx, eb * 512:(eb + 1) * 512],
                               start=False, stop=True)
                    p.ev[(tag + "mm", eb, tc)] = p.T.chan.ev(mm)
                    if tc == 15:
                        p.ev[(tag + "mm", None, 15)] = p.ev[(tag + "mm", eb, tc)]

                def evf(tag=tag, eb=eb, tc=tc, slSt=slSt, prevslot=prevslot, nblk=nblk):
                    V = p.V
                    V.wait(p.ev[(tag + "mm", eb, tc)])
                    V.wait(slSt.done())
                    if eb == 0 and tc < 2 and prevslot is not None:
                        V.wait(prevslot.done())
                    V.op("tensor_copy", etb[:, tc % 2, :], psPj[tc % 2][:, :])
                    p.ev[(tag + "ev", eb, tc)] = V.chan.cur()
                    if eb == nblk - 1 and tc >= 14:
                        p.ev[(tag + "ev", None, tc)] = p.ev[(tag + "ev", eb, tc)]

                def stf(tag=tag, eb=eb, tc=tc, dst=dst, slSt=slSt):
                    Y = p.Y
                    Y.wait(p.ev[(tag + "ev", eb, tc)])
                    slSt.dma(Y, dst[tc * 128:(tc + 1) * 128, eb * 512:(eb + 1) * 512],
                             etb[:, tc % 2, :])

                p.sec("tensor", mmf)
                p.sec("vector", evf)
                p.sec("sync", stf)

    p.ev["proj_mm_last"] = p.ev[("vmm", None, 15)]
    esB.close()

    esC = ExitStack()
    qc = esC.enter_context(nc.sbuf_tensor("qc", [128, 1024], BF16))
    kcb = esC.enter_context(nc.sbuf_tensor("kcb", [128, SEQ], BF16))
    vcb = esC.enter_context(nc.sbuf_tensor("vcb", [128, 2, SEQ], BF16))
    s16 = esC.enter_context(nc.sbuf_tensor("s16", [128, 128], F32))
    e16 = esC.enter_context(nc.sbuf_tensor("e16", [128, 128], F32))
    w16 = esC.enter_context(nc.sbuf_tensor("w16", [128, 2, 128], F32))
    prW = esC.enter_context(nc.sbuf_tensor("prW", [128, 16, 128], F32))
    prT = esC.enter_context(nc.sbuf_tensor("prT", [128, 2, 128, 16], F32))
    mx8 = esC.enter_context(nc.sbuf_tensor("mx8", [128, 8], F32))
    rs8 = esC.enter_context(nc.sbuf_tensor("rs8", [128, 8], F32))
    ctx_t = esC.enter_context(nc.sbuf_tensor("ctx_t", [128, 2, 8, 128], F32))
    ctxb = esC.enter_context(nc.sbuf_tensor("ctxb", [128, 2, 1024], BF16))

    s16g = s16.rearrange("p (h t) -> p h t", t=16)
    e16g = e16.rearrange("p (h t) -> p h t", t=16)

    for tc in range(16):
        pr = tc % 2

        def hld(tc=tc, pr=pr):
            Y = p.Y
            if tc == 0:
                Y.wait(p.ev["proj_mm_last"])
                Y.wait(p.ev[("vev", None, 15)])
                Y.wait(slQs.done())
                Y.wait(slKs.done())
                Y.wait(slVs.done())
            Y.wait(p.ev.get(("scv", tc - 1)))
            Y.wait(p.ev.get(("ctg0", tc - 2)))
            slQc.dma(Y, qc[:, :], q_d[tc * 128:(tc + 1) * 128, :])
            slKc.dma(Y, kcb[:, :], k_d[tc * 128:(tc + 1) * 128, :])
            slVc.dma(Y, vcb[:, pr, :], v_d[tc * 128:(tc + 1) * 128, :])

        def sc_v(tc=tc):
            V = p.V
            V.wait(slQc.done())
            V.wait(slKc.done())
            V.wait(p.ev.get(("exp", tc - 1)))
            kg = kcb[:, :].rearrange("p (t d) -> p t d", d=128)
            for hh in range(8):
                V.op("tensor_tensor", prW[:, :, :],
                     qc[:, hh * 128:(hh + 1) * 128].unsqueeze(1)
                     .broadcast_to([128, 16, 128]), kg, ALU.mult)
                V.op("tensor_reduce", s16[:, hh * 16:(hh + 1) * 16],
                     prW[:, :, :], axis=AX.X, op=ALU.add)
            p.ev[("scv", tc)] = V.chan.cur()

        def sm_v(tc=tc):
            V = p.V
            V.op("tensor_reduce", mx8[:, :], s16g, axis=AX.X, op=ALU.max)
            V.op("tensor_tensor", s16g, s16g,
                 mx8[:, :].unsqueeze(2).broadcast_to([128, 8, 16]), ALU.subtract)
            p.ev[("smv", tc)] = V.chan.cur()

        def sm_s(tc=tc):
            S = p.S
            S.wait(p.ev[("smv", tc)])
            S.op("activation", e16[:, :], s16[:, :], AF.Exp, scale=SCL2)
            p.ev[("exp", tc)] = S.chan.cur()

        def sm_v2(tc=tc, pr=pr):
            V = p.V
            V.wait(p.ev[("exp", tc)])
            V.wait(p.ev.get(("ctg0", tc - 2)))
            V.op("tensor_reduce", rs8[:, :], e16g, axis=AX.X, op=ALU.add)
            V.op("reciprocal", rs8[:, :], rs8[:, :])
            V.op("tensor_tensor", w16[:, pr, :].rearrange("p (h t) -> p h t", t=16),
                 e16g, rs8[:, :].unsqueeze(2).broadcast_to([128, 8, 16]), ALU.mult)
            p.ev[("wv", tc)] = V.chan.cur()

        def ct_g(tc=tc, pr=pr, hh=0):
            G = p.G
            if hh == 0:
                G.wait(slVc.done())
                G.wait(p.ev[("wv", tc)])
            vg = vcb[:, pr, :].rearrange("p (t d) -> p t d", d=128)
            G.wait(p.ev.get(("ctr", tc, hh - 2)) or
                   (p.ev.get(("ctr", tc - 1, 6 + hh)) if hh < 2 else None))
            G.op("tensor_tensor",
                 prT[:, hh % 2, :, :].rearrange("p d t -> p t d"),
                 w16[:, pr, hh * 16:(hh + 1) * 16].unsqueeze(2)
                 .broadcast_to([128, 16, 128]), vg, ALU.mult)
            p.ev[("ctp", tc, hh)] = G.chan.cur()
            if hh == 7:
                p.ev[("ctg0", tc)] = G.chan.cur()

        def ct_r(tc=tc, pr=pr, hh=0):
            V = p.V
            V.wait(p.ev[("ctp", tc, hh)])
            V.op("tensor_reduce", ctx_t[:, pr, hh, :], prT[:, hh % 2, :, :],
                 axis=AX.X, op=ALU.add)
            p.ev[("ctr", tc, hh)] = V.chan.cur()
            if hh == 7:
                p.ev[("ctg", tc)] = V.chan.cur()

        def ct_e(tc=tc, pr=pr):
            V = p.V
            V.wait(p.ev[("ctg", tc)])
            V.wait(slCs.done())
            V.op("tensor_copy", ctxb[:, pr, :], ctx_t[:, pr, :, :])
            p.ev[("cte", tc)] = V.chan.cur()

        def ct_s(tc=tc, pr=pr):
            Y = p.Y
            Y.wait(p.ev[("cte", tc)])
            slCs.dma(Y, ctx_d[tc * 128:(tc + 1) * 128, :], ctxb[:, pr, :])

        p.sec("sync", hld)
        p.sec("vector", sc_v)
        p.sec("vector", sm_v)
        p.sec("scalar", sm_s)
        p.sec("vector", sm_v2)
        for hh in range(8):
            p.sec("gpsimd", lambda tc=tc, pr=pr, hh=hh: ct_g(tc, pr, hh))
            p.sec("vector", lambda tc=tc, pr=pr, hh=hh: ct_r(tc, pr, hh))
        p.sec("vector", ct_e)
        p.sec("sync", ct_s)

    esC.close()

    esD = ExitStack()
    ctxT = esD.enter_context(nc.sbuf_tensor("ctxT", [128, 8, SEQ], BF16))
    ofb = esD.enter_context(nc.sbuf_tensor("ofb", [128, 8, 16, 128], BF16))
    wob = esD.enter_context(nc.sbuf_tensor("wob", [128, 16, 512], BF16))
    xhb = esD.enter_context(nc.sbuf_tensor("xhb", [128, 512], F32))
    xot = esD.enter_context(nc.sbuf_tensor("xot", [128, 512], F32))
    psJ = [esD.enter_context(nc.psum_tensor("psJ%d" % i, [128, 512], F32))
           for i in range(2)]

    def ctl():
        Y = p.Y
        Y.wait(slCs.done())
        for hh in range(8):
            slT.dma_t(Y, ctxT[:, hh, :], ctx_d[:, hh * 128:(hh + 1) * 128])

    p.sec("sync", ctl)

    def scr_v():
        V = p.V
        V.wait(slT.done())
        for hh in range(8):
            V.op("tensor_copy", ofb[:, hh, :, :],
                 ctxT[:, hh, :].rearrange("p (u r) -> p r u", r=16))
        p.ev["scr"] = V.chan.cur()

    p.sec("vector", scr_v)

    for oc in range(4):
        def pwload(oc=oc):
            G = p.G
            G.wait(slCs.done())
            G.wait(p.ev.get(("pjmm", oc - 1, 7)))
            slWo.dma(G, wob[:, :, :],
                     wp[:, oc * 512:(oc + 1) * 512].rearrange("(kc p) c -> p kc c", p=128))

        p.sec("gpsimd", pwload)
        for sc in range(8):
            def pjmm(oc=oc, sc=sc):
                T = p.T
                T.wait(slWo.done())
                T.wait(slB.done())
                T.wait(p.ev["scr"])
                T.wait(p.ev.get(("pje", oc, sc - 2)) or
                       (p.ev.get(("pje", oc - 1, 6 + sc)) if sc < 2 else None))
                for kc in range(16):
                    T.raw("matmul", psJ[sc % 2][:, :], ofb[:, sc, kc, :],
                          wob[:, kc, :], start=(kc == 0), stop=False)
                mm = T.raw("matmul", psJ[sc % 2][:, :], ones1[0:1, 0:128],
                           brow[0:1, 3, oc * 512:(oc + 1) * 512],
                           start=False, stop=True)
                p.ev[("pjmm", oc, sc)] = p.T.chan.ev(mm)

            def xld(oc=oc, sc=sc):
                Y = p.Y
                Y.wait(p.ev.get(("pje", oc, sc - 1)))
                slH.dma(Y, xhb[:, :], xh[sc * 128:(sc + 1) * 128,
                                        oc * 512:(oc + 1) * 512])

            def pje(oc=oc, sc=sc):
                V = p.V
                V.wait(p.ev[("pjmm", oc, sc)])
                V.wait(slH.done())
                V.wait(slO.done())
                V.op("tensor_tensor", xot[:, :], psJ[sc % 2][:, :], xhb[:, :], ALU.add)
                p.ev[("pje", oc, sc)] = V.chan.cur()

            def pst(oc=oc, sc=sc):
                Y = p.Y
                Y.wait(p.ev[("pje", oc, sc)])
                slO.dma(Y, xo[sc * 128:(sc + 1) * 128, oc * 512:(oc + 1) * 512],
                        xot[:, :])

            p.sec("tensor", pjmm)
            p.sec("sync", xld)
            p.sec("vector", pje)
            p.sec("sync", pst)

    p.sec("sync", lambda: p.Y.wait(slO.done()))
    esD.close()
    return p.fin()



_NC_CACHE = {}


def _get(name, builder):
    if name not in _NC_CACHE:
        _NC_CACHE[name] = builder()
    return _NC_CACHE[name]


def run_quant(wgu_full, wpr_full):
    nc = _get("quant", build_quant)
    ins = []
    for c in range(NCORE):
        ins.append({
            "wgu": np.ascontiguousarray(wgu_full[c * 2048:(c + 1) * 2048, :]),
            "wpr": np.ascontiguousarray(wpr_full[c * 256:(c + 1) * 256, :]),
        })
    res = run_bass_kernel_spmd(nc, ins, core_ids=list(range(NCORE)))
    qgu = np.concatenate([r["qgu"] for r in res.results], 0)
    qpr = np.concatenate([r["qpr"] for r in res.results], 0)
    return qgu, qpr


def run_ffn(x_full, wguT, wprT, parts=None):
    nc = _get("ffn%d" % (parts is not None), lambda: build_ffn(parts is not None))
    xf = x_full.reshape(-1, D)
    ins = []
    for c in range(NCORE):
        d = {"x": np.ascontiguousarray(xf[c * TLOC:(c + 1) * TLOC]),
             "wguT": wguT, "wprT": wprT}
        if parts is not None:
            p1f, p2f = parts
            d["p1"] = np.ascontiguousarray(p1f.reshape(-1, D)[c * TLOC:(c + 1) * TLOC])
            d["p2"] = np.ascontiguousarray(p2f.reshape(-1, D)[c * TLOC:(c + 1) * TLOC])
        ins.append(d)
    res = run_bass_kernel_spmd(nc, ins, core_ids=list(range(NCORE)))
    return np.concatenate([r["xo"] for r in res.results], 0).reshape(x_full.shape)


def _asc(a):
    return np.ascontiguousarray(a)


def _rope_tables_l0():
    half = 32
    inv = 1.0 / 10000.0 ** (np.arange(0, half, dtype=np.float32) / half)
    pos = np.arange(2048, dtype=np.float32)
    ang = np.einsum("i,j->ij", pos, inv)
    cos64 = np.cos(np.concatenate([ang, ang], -1))
    sin64 = np.sin(np.concatenate([ang, ang], -1))
    cosT = np.tile(cos64.T, (2, 1)).astype(np.float32)
    sgn = np.concatenate([-sin64.T[:32], sin64.T[32:]], 0)
    sinT = np.tile(sgn, (2, 1)).astype(np.float32)
    return cosT, sinT


def _rope_tables_l1():
    half = 64
    inv = 1.0 / 10000.0 ** (np.arange(0, half, dtype=np.float32) / half)
    pos = np.arange(2048, dtype=np.float32)
    ang = np.einsum("i,j->ij", pos, inv)
    cosT = np.tile(np.cos(ang).T, (2, 1)).astype(np.float32)
    sinT = np.concatenate([-np.sin(ang).T, np.sin(ang).T], 0).astype(np.float32)
    return cosT, sinT


def prep_weights(ins):
    W = {}
    W["bmask"] = (np.arange(1024)[None, :] >=
                  (np.arange(128)[:, None] + 384)).astype(np.float32)
    W["cosT0"], W["sinT0"] = _rope_tables_l0()
    W["cosT1"], W["sinT1"] = _rope_tables_l1()
    W["b0_wqT"] = t2(ins["L0_attn_b_Wq"])
    W["b0_wkT"] = t2(ins["L0_attn_b_Wk"])
    W["b0_wvT"] = t2(ins["L0_attn_b_Wv"])
    W["b0_woT"] = t2(ins["L0_attn_b_Wo"])
    W["b0_bq"] = _asc(ins["L0_attn_b_bq"][None])
    W["b0_bk"] = _asc(ins["L0_attn_b_bk"][None])
    W["b0_bv"] = _asc(ins["L0_attn_b_bv"][None])
    W["b0_bo"] = _asc(ins["L0_attn_b_bo"][None])
    W["b1_wqT"] = t2(ins["L1_attn_b_Wq"])
    W["b1_wsmT"] = _asc(np.concatenate(
        [t2(ins["L1_attn_b_Wkv"]), t2(ins["L1_attn_b_Wkvh"]),
         t2(ins["L1_attn_b_Wg"])], axis=1))
    W["b1_woT"] = t2(ins["L1_attn_b_Wout"])
    Wqkv = ins["L1_attn_a_Wqkv"]
    bqkv = ins["L1_attn_a_bqkv"]
    W["a1_wq"] = [_asc(Wqkv[:, th * 1024:(th + 1) * 1024]) for th in range(2)]
    W["a1_wk"] = _asc(Wqkv[:, 2048:4096])
    W["a1_wv"] = _asc(Wqkv[:, 4096:6144])
    W["a1_wp"] = ins["L1_attn_a_Wproj"]
    W["a1_bq"] = [_asc(bqkv[None, th * 1024:(th + 1) * 1024]) for th in range(2)]
    W["a1_bk"] = _asc(bqkv[None, 2048:4096])
    W["a1_bv"] = _asc(bqkv[None, 4096:6144])
    W["a1_bp"] = _asc(ins["L1_attn_a_bproj"][None])
    W["a0_wq"] = [_asc(ins["L0_attn_a_Wq"][:, eh * 1024:(eh + 1) * 1024])
                  for eh in range(2)]
    W["a0_wk"] = ins["L0_attn_a_Wk"]
    W["a0_wv"] = ins["L0_attn_a_Wv"]
    W["a0_wo"] = [_asc(ins["L0_attn_a_Wo"][eh * 1024:(eh + 1) * 1024])
                  for eh in range(2)]
    W["gwT"] = t2(ins["L1_ffn_gate_weight"])
    W["guT"] = t2(ins["L1_ffn_gate_up_weights"])
    W["dnT"] = t2(ins["L1_ffn_down_weights"])
    W["sguT"] = t2(ins["L1_ffn_shared_gate_up"])
    W["sdnT"] = t2(ins["L1_ffn_shared_down"])
    sel8 = np.zeros((8, 8 * 128), np.float32)
    for e in range(8):
        sel8[e, e * 128:(e + 1) * 128] = 1.0
    W["sel8"] = sel8
    W["ident"] = np.eye(128, dtype=np.float32)
    return W


def im_quant(ins):
    wgu, wpr = ins["L0_ffn_w_gate_up"], ins["L0_ffn_w_proj"]
    return [{"wgu": _asc(wgu[c * 2048:(c + 1) * 2048]),
             "wpr": _asc(wpr[c * 256:(c + 1) * 256])} for c in range(NCORE)]


def im_attn_a0(W, x):
    im = []
    for c in range(NCORE):
        b, eh = c // 2, c % 2
        im.append({"xT": t2(x[b]), "wq": W["a0_wq"][eh], "wk": W["a0_wk"],
                   "wv": W["a0_wv"], "wo": W["a0_wo"][eh]})
    return im


def im_ffn(x, wguT, wprT, parts=None):
    xf = x.reshape(-1, D)
    im = []
    for c in range(NCORE):
        d = {"x": _asc(xf[c * TLOC:(c + 1) * TLOC]), "wguT": wguT, "wprT": wprT}
        if parts is not None:
            p1f, p2f = parts
            d["p1"] = _asc(p1f.reshape(-1, D)[c * TLOC:(c + 1) * TLOC])
            d["p2"] = _asc(p2f.reshape(-1, D)[c * TLOC:(c + 1) * TLOC])
        im.append(d)
    return im


def im_attn_b0(W, x, half):
    im = []
    for c in range(NCORE):
        b = c % 4
        im.append({"xT": t2(x[b]), "xh": _asc(x[b, half * 1024:(half + 1) * 1024]),
                   "wqT": W["b0_wqT"], "wkT": W["b0_wkT"], "wvT": W["b0_wvT"],
                   "woT": W["b0_woT"], "bq": W["b0_bq"], "bk": W["b0_bk"],
                   "bv": W["b0_bv"], "bo": W["b0_bo"],
                   "cosT": W["cosT0"], "sinT": W["sinT0"], "bmask": W["bmask"]})
    return im


def im_attn_a1(W, x):
    im = []
    for c in range(NCORE):
        b, th = c // 2, c % 2
        im.append({"xT": t2(x[b]), "xh": _asc(x[b, th * 1024:(th + 1) * 1024]),
                   "wq": W["a1_wq"][th], "wk": W["a1_wk"], "wv": W["a1_wv"],
                   "wp": W["a1_wp"], "bq": W["a1_bq"][th], "bk": W["a1_bk"],
                   "bv": W["a1_bv"], "bp": W["a1_bp"]})
    return im


def im_moe(W, x):
    xf = x.reshape(-1, D)
    return [{"x": _asc(xf[c * TLOC:(c + 1) * TLOC]), "gwT": W["gwT"],
             "guT": W["guT"], "dnT": W["dnT"], "sguT": W["sguT"],
             "sdnT": W["sdnT"], "sel8": W["sel8"], "ident": W["ident"]}
            for c in range(NCORE)]


def im_attn_b1(W, x, half):
    im = []
    for c in range(NCORE):
        b = c % 4
        im.append({"xT": t2(x[b]), "xh": _asc(x[b, half * 1024:(half + 1) * 1024]),
                   "wqT": W["b1_wqT"], "wsmT": W["b1_wsmT"], "woT": W["b1_woT"],
                   "cosT": W["cosT1"], "sinT": W["sinT1"], "bmask": W["bmask"]})
    return im


def _launch(name, builder, im, keys):
    nc = _get(name, builder)
    res = run_bass_kernel_spmd(nc, im, core_ids=list(range(NCORE)))
    return [{k: np.asarray(r[k]) for k in keys} for r in res.results]


def _halves_to_x(outs0, outs1, shape):
    x = np.empty(shape, np.float32)
    for b in range(4):
        x[b, 0:1024] = outs0[b]["xo"]
        x[b, 1024:2048] = outs1[b]["xo"]
    return x


def kernel(**inputs):
    ins = inputs
    x0 = np.asarray(ins["L0_attn_a_x"], np.float32)
    shape = x0.shape
    W = prep_weights(ins)

    qres = _launch("quant", build_quant, im_quant(ins), ["qgu", "qpr"])
    qgu = np.concatenate([r["qgu"] for r in qres], 0)
    qpr = np.concatenate([r["qpr"] for r in qres], 0)
    wguT, wprT = t2(qgu), t2(qpr)

    a0 = _launch("attna0", build_attn_a0, im_attn_a0(W, x0), ["py"])
    p1f = np.stack([a0[2 * b]["py"] for b in range(4)])
    p2f = np.stack([a0[2 * b + 1]["py"] for b in range(4)])

    f1 = _launch("ffn1", lambda: build_ffn(True),
                 im_ffn(x0, wguT, wprT, (p1f, p2f)), ["xo"])
    x2 = np.concatenate([r["xo"] for r in f1], 0).reshape(shape)

    b0h0 = _launch("ab0h0", lambda: build_attn_b0(0), im_attn_b0(W, x2, 0), ["xo"])
    b0h1 = _launch("ab0h1", lambda: build_attn_b0(1), im_attn_b0(W, x2, 1), ["xo"])
    x3 = _halves_to_x(b0h0, b0h1, shape)

    f2 = _launch("ffn0", lambda: build_ffn(False), im_ffn(x3, wguT, wprT), ["xo"])
    x4 = np.concatenate([r["xo"] for r in f2], 0).reshape(shape)

    a1 = _launch("attna1", build_attn_a1, im_attn_a1(W, x4), ["xo"])
    x5 = np.empty(shape, np.float32)
    for c in range(NCORE):
        b, th = c // 2, c % 2
        x5[b, th * 1024:(th + 1) * 1024] = a1[c]["xo"]

    m1 = _launch("moe", build_moe, im_moe(W, x5), ["xo"])
    x6 = np.concatenate([r["xo"] for r in m1], 0).reshape(shape)

    b1h0 = _launch("ab1h0", lambda: build_attn_b1(0), im_attn_b1(W, x6, 0), ["xo"])
    b1h1 = _launch("ab1h1", lambda: build_attn_b1(1), im_attn_b1(W, x6, 1), ["xo"])
    x7 = _halves_to_x(b1h0, b1h1, shape)

    m2 = _launch("moe", build_moe, im_moe(W, x7), ["xo"])
    x8 = np.concatenate([r["xo"] for r in m2], 0).reshape(shape)
    return x8.astype(np.float32)


# version 03
# speedup vs baseline: 1.3794x; 1.3794x over previous
import numpy as np
import ml_dtypes
import concourse.bass as bass
import concourse.mybir as mybir
from concourse.bass_utils import run_bass_kernel_spmd

F32 = mybir.dt.float32
BF16 = mybir.dt.bfloat16
AF = mybir.ActivationFunctionType
ALU = mybir.AluOpType
AX = mybir.AxisListType
bfdt = ml_dtypes.bfloat16

MAGIC = 12582912.0
LN_EPS = 1e-5
RMS_EPS = 1.1920929e-07
NCORE = 8
TLOC = 1024
D = 2048



class Chan:
    __slots__ = ("sem", "val", "step", "name")

    def __init__(self, nc, name, step=1):
        self.sem = nc.alloc_semaphore(name=name)
        self.val = 0
        self.step = step
        self.name = name

    def ev(self, inst, n=None):
        n = self.step if n is None else n
        inst.then_inc(self.sem, n)
        self.val += n
        return (self.sem, self.val)

    def cur(self):
        return (self.sem, self.val)


class SE:

    def __init__(self, eng, chan):
        self.eng = eng
        self.chan = chan
        self._waited = {}

    def wait(self, ev):
        if ev is None:
            return
        sem, val = ev
        if self._waited.get(sem.name, -1) >= val:
            return
        self.eng.wait_ge(sem, val)
        self._waited[sem.name] = val

    def op(self, name, *a, **kw):
        if self.chan.val > 0:
            self.wait(self.chan.cur())
        return self.chan.ev(getattr(self.eng, name)(*a, **kw))

    def raw(self, name, *a, **kw):
        return getattr(self.eng, name)(*a, **kw)

    def mark(self):
        return self.chan.ev(self.eng.nop())


class DmaSlot:

    def __init__(self, nc, name):
        self.chan = Chan(nc, name, 16)

    def dma(self, se, out, in_, **kw):
        self.chan.ev(se.eng.dma_start(out=out, in_=in_, **kw), 16)

    def dma_t(self, se, out, in_):
        self.chan.ev(se.eng.dma_start(out=out, in_=in_, transpose=True), 16)

    def done(self):
        return self.chan.cur()


class Prog:
    def __init__(self, name):
        self.nc = bass.Bass("TRN2", target_bir_lowering=False)
        self.name = name
        self.ev = {}

    def start(self):
        from contextlib import ExitStack
        self.es = ExitStack()
        self.es.__enter__()
        nc = self.nc
        self.block = self.es.enter_context(nc.Block())
        self.G = SE(nc.gpsimd, Chan(nc, "chG"))
        self.V = SE(nc.vector, Chan(nc, "chV"))
        self.S = SE(nc.scalar, Chan(nc, "chS"))
        self.T = SE(nc.tensor, Chan(nc, "chT"))
        self.Y = SE(nc.sync, Chan(nc, "chY"))
        return self

    def sec(self, eng, fn):
        getattr(self.block, eng)(lambda _e: fn())

    def sb(self, name, shape, dtype):
        return self.es.enter_context(self.nc.sbuf_tensor(name, shape, dtype))

    def ps(self, name, shape, dtype=F32):
        return self.es.enter_context(self.nc.psum_tensor(name, shape, dtype))

    def slot(self, name):
        return DmaSlot(self.nc, name)

    def fin(self):
        self.es.__exit__(None, None, None)
        return self.nc


def t2(x):
    return np.ascontiguousarray(np.asarray(x).swapaxes(-1, -2))



def build_quant():
    p = Prog("quant").start()
    nc = p.nc
    wgu = nc.dram_tensor("wgu", [2048, 2048], F32, kind="ExternalInput").ap()
    wpr = nc.dram_tensor("wpr", [256, 8192], F32, kind="ExternalInput").ap()
    qgu = nc.dram_tensor("qgu", [2048, 2048], BF16, kind="ExternalOutput").ap()
    qpr = nc.dram_tensor("qpr", [256, 8192], BF16, kind="ExternalOutput").ap()

    slI = p.slot("slI")
    slO = p.slot("slO")
    C = 8192
    w_t = p.sb("w_t", [128, C], F32)
    wbf = p.sb("wbf", [128, C], BF16)
    gs = p.sb("gs", [128, C // 64], F32)
    sc = p.sb("sc", [128, C // 64], F32)
    scb = p.sb("scb", [128, C // 64], BF16)
    rc = p.sb("rc", [128, C // 64], F32)
    qt = p.sb("qt", [128, C], F32)
    qb = p.sb("qb", [128, C], BF16)
    rm = p.sb("rm", [128, C], F32)
    wq = p.sb("wq", [128, C], BF16)

    tiles = [(wgu, qgu, i, 2048) for i in range(16)] + [(wpr, qpr, i, 8192) for i in range(2)]

    for t, (src, dst, i, cols) in enumerate(tiles):
        def load(src=src, i=i, cols=cols):
            Y = p.Y
            Y.wait(p.V.chan.cur())
            Y.wait(slO.done())
            slI.dma(Y, w_t[:, :cols], src[i * 128:(i + 1) * 128, :])

        def proc(cols=cols):
            V = p.V
            ng = cols // 64
            V.wait(slI.done())
            V.op("tensor_copy", wbf[:, :cols], w_t[:, :cols])
            V.op("tensor_reduce", gs[:, :ng],
                 wbf[:, :cols].rearrange("p (g e) -> p g e", e=64),
                 axis=AX.X, op=ALU.add, apply_absolute_value=True)
            V.op("tensor_scalar", sc[:, :ng], gs[:, :ng], 1.0 / 64.0, 1e-8, ALU.mult, ALU.max)
            V.op("tensor_copy", scb[:, :ng], sc[:, :ng])
            V.op("tensor_copy", sc[:, :ng], scb[:, :ng])
            V.op("reciprocal", rc[:, :ng], sc[:, :ng])
            V.op("tensor_tensor", qt[:, :cols].rearrange("p (g e) -> p g e", e=64),
                 wbf[:, :cols].rearrange("p (g e) -> p g e", e=64),
                 rc[:, :ng].unsqueeze(2).broadcast_to([128, ng, 64]), ALU.mult)
            V.op("tensor_copy", qb[:, :cols], qt[:, :cols])
            V.op("tensor_scalar", rm[:, :cols], qb[:, :cols], MAGIC, -MAGIC, ALU.add, ALU.add)
            V.op("tensor_scalar", rm[:, :cols], rm[:, :cols], 1.0, -1.0, ALU.min, ALU.max)
            V.op("tensor_tensor", wq[:, :cols].rearrange("p (g e) -> p g e", e=64),
                 rm[:, :cols].rearrange("p (g e) -> p g e", e=64),
                 sc[:, :ng].unsqueeze(2).broadcast_to([128, ng, 64]), ALU.mult)

        def store(dst=dst, i=i, cols=cols):
            Y = p.Y
            Y.wait(p.V.chan.cur())
            slO.dma(Y, dst[i * 128:(i + 1) * 128, :], wq[:, :cols])

        p.sec("sync", load)
        p.sec("vector", proc)
        p.sec("sync", store)

    p.sec("sync", lambda: p.Y.wait(slO.done()))
    return p.fin()



def build_ffn(add_partials):
    p = Prog("ffn").start()
    nc = p.nc
    x = nc.dram_tensor("x", [TLOC, D], F32, kind="ExternalInput").ap()
    if add_partials:
        p1 = nc.dram_tensor("p1", [TLOC, D], BF16, kind="ExternalInput").ap()
        p2 = nc.dram_tensor("p2", [TLOC, D], BF16, kind="ExternalInput").ap()
    wguT = nc.dram_tensor("wguT", [2048, 16384], BF16, kind="ExternalInput").ap()
    wprT = nc.dram_tensor("wprT", [8192, 2048], BF16, kind="ExternalInput").ap()
    xo = nc.dram_tensor("xo", [TLOC, D], F32, kind="ExternalOutput").ap()
    xln_d = nc.dram_tensor("xln_d", [TLOC, D], BF16).ap()
    y_d = nc.dram_tensor("y_d", [D, TLOC], BF16).ap()
    xr_d = nc.dram_tensor("xr_d", [TLOC, D], F32).ap()

    slX = p.slot("slX")
    slP = p.slot("slP")
    slLn = p.slot("slLn")
    slLt = p.slot("slLt")
    slW = [p.slot("slW0"), p.slot("slW1")]
    slWp = [p.slot("slWp0"), p.slot("slWp1")]
    slY = p.slot("slY")
    slYt = p.slot("slYt")
    slO = p.slot("slO")
    slXr = p.slot("slXr")
    slXf = p.slot("slXf")

    xr = p.sb("xr", [128, D], F32)
    xr2 = p.sb("xr2", [128, D], F32)
    pt = p.sb("pt", [128, 2, D], BF16)
    xlt = p.sb("xlt", [128, D], BF16)
    sq = p.sb("sqt", [128, D], F32)
    st = p.sb("st", [128, 8], F32)
    xlnf = p.sb("xlnf", [128, 16, TLOC], BF16)
    wbuf = p.sb("wbuf", [128, 2, 2, 16, 128], BF16)
    wpbuf = p.sb("wpbuf", [128, 2, 32, 128], BF16)
    t1 = p.sb("t1", [128, 512], F32)
    hf = p.sb("hf", [128, 512], F32)
    h_all = p.sb("h_all", [128, 64, 512], BF16)
    sqb = p.sb("sqb", [128, 512], BF16)
    rrow = p.sb("rrow", [1, 512], F32)
    rrowb = p.sb("rrowb", [1, 512], BF16)
    rb = p.sb("rb", [128, 512], F32)
    yt = p.sb("yt", [128, 512], BF16)
    ytm = p.sb("ytm", [128, D], BF16)
    xot = p.sb("xot", [128, D], F32)
    ones1 = p.sb("ones1", [1, 128], BF16)
    ones128 = p.sb("ones128", [128, 1], BF16)
    epsln = p.sb("epsln", [128, 1], F32)
    epsrms = p.sb("epsrms", [128, 1], F32)

    psG = [p.ps("psG0", [128, 512]), p.ps("psG1", [128, 512])]
    psU = [p.ps("psU0", [128, 512]), p.ps("psU1", [128, 512])]
    psH = p.ps("psH", [1, 512])
    psR = p.ps("psR", [128, 512])
    psP = [p.ps("psP0", [128, 512]), p.ps("psP1", [128, 512])]

    def init():
        G = p.G
        G.op("memset", ones1[:, :], 1.0)
        G.op("memset", ones128[:, :], 1.0)
        G.op("memset", epsln[:, :], LN_EPS)
        G.op("memset", epsrms[:, :], RMS_EPS)
        p.ev["init"] = p.G.chan.cur()

    p.sec("gpsimd", init)

    for i in range(8):
        def loadx(i=i):
            Y = p.Y
            Y.wait(p.ev.get(("lnv3", i - 1)))
            Y.wait(slXr.done())
            slX.dma(Y, xr[:, :], x[i * 128:(i + 1) * 128, :])
            if add_partials:
                slP.dma(Y, pt[:, 0, :], p1[i * 128:(i + 1) * 128, :])
                slP.dma(Y, pt[:, 1, :], p2[i * 128:(i + 1) * 128, :])

        def v1(i=i):
            V = p.V
            V.wait(slX.done())
            if add_partials:
                V.wait(slP.done())
                V.op("tensor_add", xr[:, :], xr[:, :], pt[:, 0, :])
                V.op("tensor_add", xr[:, :], xr[:, :], pt[:, 1, :])
            V.op("reduce_sum", st[:, 0:1], xr[:, :], axis=AX.X)
            p.ev[("lnv1", i)] = V.chan.cur()

        def stxr(i=i):
            Y = p.Y
            Y.wait(p.ev[("lnv1", i)])
            slXr.dma(Y, xr_d[i * 128:(i + 1) * 128, :], xr[:, :])

        def s1(i=i):
            S = p.S
            S.wait(p.ev[("lnv1", i)])
            S.op("activation", sq[:, :], xr[:, :], AF.Square)
            p.ev[("lns1", i)] = S.chan.cur()

        def v2(i=i):
            V = p.V
            V.wait(p.ev[("lns1", i)])
            V.op("reduce_sum", st[:, 1:2], sq[:, :], axis=AX.X)
            V.op("tensor_scalar", st[:, 0:1], st[:, 0:1], 1.0 / D, None, ALU.mult)
            V.op("tensor_scalar", st[:, 1:2], st[:, 1:2], 1.0 / D, None, ALU.mult)
            V.op("tensor_tensor", st[:, 2:3], st[:, 0:1], st[:, 0:1], ALU.mult)
            V.op("tensor_sub", st[:, 1:2], st[:, 1:2], st[:, 2:3])
            p.ev[("lnv2", i)] = V.chan.cur()

        def s2(i=i):
            S = p.S
            S.wait(p.ev[("lnv2", i)])
            S.wait(p.ev["init"])
            S.op("activation", st[:, 1:2], st[:, 1:2], AF.Ln, bias=epsln[:, 0:1])
            S.op("activation", st[:, 1:2], st[:, 1:2], AF.Exp, scale=-0.5)
            p.ev[("lns2", i)] = S.chan.cur()

        def v3(i=i):
            V = p.V
            V.wait(p.ev[("lns2", i)])
            V.wait(slLn.done())
            V.op("tensor_tensor", st[:, 3:4], st[:, 0:1], st[:, 1:2], ALU.mult)
            V.op("tensor_scalar", st[:, 3:4], st[:, 3:4], -1.0, None, ALU.mult)
            V.op("tensor_scalar", xlt[:, :], xr[:, :], st[:, 1:2], st[:, 3:4],
                 ALU.mult, ALU.add)
            p.ev[("lnv3", i)] = V.chan.cur()

        def stln(i=i):
            Y = p.Y
            Y.wait(p.ev[("lnv3", i)])
            slLn.dma(Y, xln_d[i * 128:(i + 1) * 128, :], xlt[:, :])
            p.ev[("xlt_st", i)] = slLn.done()

        p.sec("sync", loadx)
        p.sec("vector", v1)
        p.sec("sync", stxr)
        p.sec("scalar", s1)
        p.sec("vector", v2)
        p.sec("scalar", s2)
        p.sec("vector", v3)
        p.sec("sync", stln)

    def ldlnf():
        Y = p.Y
        Y.wait(slLn.done())
        for c in range(16):
            slLt.dma_t(Y, xlnf[:, c, :], xln_d[:, c * 128:(c + 1) * 128])

    p.sec("sync", ldlnf)
    xln_ready = slLt.done()

    NH = 64
    for tb in range(2):
        tsl = slice(tb * 512, (tb + 1) * 512)

        def w_pref(tb=tb):
            Y = p.Y
            if tb == 0:
                Y.wait(p.ev.get(("lnv1", 7)))
            if tb == 1:
                Y.wait(p.ev[(0, "gu_mm", NH - 1)])
            src_g = wguT[:, 0:128].rearrange("(kc p) c -> p kc c", p=128)
            src_u = wguT[:, 8192:8192 + 128].rearrange("(kc p) c -> p kc c", p=128)
            slW[0].dma(Y, wbuf[:, 0, 0, :, :], src_g)
            slW[0].dma(Y, wbuf[:, 0, 1, :, :], src_u)

        p.sec("sync", w_pref)

        for hc in range(NH):
            par = hc % 2

            def w_next(hc=hc, tb=tb):
                Y = p.Y
                if hc + 1 < NH:
                    npar = (hc + 1) % 2
                    Y.wait(p.ev.get((tb, "gu_mm", hc - 1)))
                    c0 = (hc + 1) * 128
                    src_g = wguT[:, c0:c0 + 128].rearrange("(kc p) c -> p kc c", p=128)
                    src_u = wguT[:, 8192 + c0:8192 + c0 + 128].rearrange(
                        "(kc p) c -> p kc c", p=128)
                    slW[npar].dma(Y, wbuf[:, npar, 0, :, :], src_g)
                    slW[npar].dma(Y, wbuf[:, npar, 1, :, :], src_u)

            def gu_mm(hc=hc, par=par, tb=tb):
                T = p.T
                T.wait(slW[par].done())
                T.wait(xln_ready)
                T.wait(p.ev.get((tb, "gu_v", hc - 2)))
                T.wait(p.ev.get((tb, "gu_s1", hc - 2)))
                if tb == 1 and hc <= 1:
                    T.wait(p.ev.get((0, "pj_mm", 15)))
                    T.wait(p.ev.get((0, "rms_read")))
                if hc > 0:
                    T.wait(p.ev[(tb, "gu_s2", hc - 1)])
                    T.raw("matmul", psH[0:1, :], ones128[:, 0:1], sqb[:, :],
                          start=(hc - 1 == 0), stop=(hc - 1 == NH - 1))
                for kc in range(16):
                    T.raw("matmul", psG[par][:, :], wbuf[:, par, 0, kc, :],
                          xlnf[:, kc, tsl], start=(kc == 0), stop=(kc == 15))
                for kc in range(16):
                    mm = T.raw("matmul", psU[par][:, :], wbuf[:, par, 1, kc, :],
                               xlnf[:, kc, tsl], start=(kc == 0), stop=(kc == 15))
                p.ev[(tb, "gu_mm", hc)] = p.T.chan.ev(mm)

            def gu_s1(hc=hc, par=par, tb=tb):
                S = p.S
                S.wait(p.ev[(tb, "gu_mm", hc)])
                S.op("activation", t1[:, :], psG[par][:, :], AF.Silu)
                p.ev[(tb, "gu_s1", hc)] = S.chan.cur()

            def gu_v(hc=hc, par=par, tb=tb):
                V = p.V
                V.wait(p.ev[(tb, "gu_s1", hc)])
                if tb == 1 and hc == 0:
                    V.wait(p.ev.get((0, "pj_mm", 15)))
                V.op("tensor_tensor", hf[:, :], t1[:, :], psU[par][:, :], ALU.mult)
                V.op("tensor_copy", h_all[:, hc, :], hf[:, :])
                p.ev[(tb, "gu_v", hc)] = V.chan.cur()

            def gu_s2(hc=hc, tb=tb):
                S = p.S
                S.wait(p.ev[(tb, "gu_v", hc)])
                if hc > 0:
                    S.wait(p.ev[(tb, "gu_mm", hc)])
                S.op("activation", sqb[:, :], hf[:, :], AF.Square)
                p.ev[(tb, "gu_s2", hc)] = S.chan.cur()

            p.sec("sync", w_next)
            p.sec("tensor", gu_mm)
            p.sec("scalar", gu_s1)
            p.sec("vector", gu_v)
            p.sec("scalar", gu_s2)

        def psh_last(tb=tb):
            T = p.T
            T.wait(p.ev[(tb, "gu_s2", NH - 1)])
            mm = T.raw("matmul", psH[0:1, :], ones128[:, 0:1], sqb[:, :],
                       start=False, stop=True)
            p.ev[(tb, "psh")] = p.T.chan.ev(mm)

        p.sec("tensor", psh_last)

        def rms_v1(tb=tb):
            V = p.V
            V.wait(p.ev[(tb, "psh")])
            V.op("tensor_scalar", rrow[0:1, :], psH[0:1, :], 1.0 / 8192.0, None, ALU.mult)
            p.ev[(tb, "rms_read")] = V.chan.cur()

        def rms_s(tb=tb):
            S = p.S
            S.wait(p.ev[(tb, "rms_read")])
            S.op("activation", rrow[0:1, :], rrow[0:1, :], AF.Ln, bias=epsrms[0:1, 0:1])
            S.op("activation", rrowb[0:1, :], rrow[0:1, :], AF.Exp, scale=-0.5)
            p.ev[(tb, "rms_s")] = S.chan.cur()

        def rms_t(tb=tb):
            T = p.T
            T.wait(p.ev[(tb, "rms_s")])
            T.wait(p.ev.get((tb - 1, "rb")))
            mm = T.raw("matmul", psR[:, :], ones1[0:1, :], rrowb[0:1, :], start=True, stop=True)
            p.ev[(tb, "rms_bc")] = p.T.chan.ev(mm)

        def rms_v2(tb=tb):
            V = p.V
            V.wait(p.ev[(tb, "rms_bc")])
            V.op("tensor_copy", rb[:, :], psR[:, :])
            p.ev[(tb, "rb")] = V.chan.cur()

        p.sec("vector", rms_v1)
        p.sec("scalar", rms_s)
        p.sec("tensor", rms_t)
        p.sec("vector", rms_v2)

        def wp_pref(tb=tb):
            Y = p.Y
            if tb == 1:
                Y.wait(p.ev[(0, "pj_mm", 15)])
            src = wprT[0:4096, 0:128].rearrange("(hc p) c -> p hc c", p=128)
            slWp[0].dma(Y, wpbuf[:, 0, :, :], src)

        p.sec("sync", wp_pref)

        for oc in range(16):
            par = oc % 2

            def wp_next(oc=oc, tb=tb):
                Y = p.Y
                if oc + 1 < 16:
                    npar = (oc + 1) % 2
                    Y.wait(p.ev.get((tb, "pj_mm", oc - 1)))
                    c0 = (oc + 1) * 128
                    src = wprT[0:4096, c0:c0 + 128].rearrange("(hc p) c -> p hc c", p=128)
                    slWp[npar].dma(Y, wpbuf[:, npar, :, :], src)

            def pj_h1(oc=oc, par=par, tb=tb):
                T = p.T
                T.wait(slWp[par].done())
                T.wait(p.ev[(tb, "gu_v", NH - 1)])
                T.wait(p.ev.get((tb, "pj_v", oc - 2)))
                for hc in range(32):
                    mm = T.raw("matmul", psP[par][:, :], wpbuf[:, par, hc, :],
                               h_all[:, hc, :], start=(hc == 0), stop=False)
                p.ev[(tb, "pj_h1", oc)] = p.T.chan.ev(mm)

            def wp_half2(oc=oc, par=par, tb=tb):
                Y = p.Y
                Y.wait(p.ev[(tb, "pj_h1", oc)])
                c0 = oc * 128
                src = wprT[4096:8192, c0:c0 + 128].rearrange("(hc p) c -> p hc c", p=128)
                slWp[par].dma(Y, wpbuf[:, par, :, :], src)

            def pj_mm(oc=oc, par=par, tb=tb):
                T = p.T
                T.wait(slWp[par].done())
                for hc in range(32):
                    mm = T.raw("matmul", psP[par][:, :], wpbuf[:, par, hc, :],
                               h_all[:, 32 + hc, :], start=False, stop=(hc == 31))
                p.ev[(tb, "pj_mm", oc)] = p.T.chan.ev(mm)

            def pj_v(oc=oc, par=par, tb=tb):
                V = p.V
                V.wait(p.ev[(tb, "pj_mm", oc)])
                V.wait(p.ev[(tb, "rb")])
                V.wait(slY.done())
                V.op("tensor_tensor", yt[:, :], psP[par][:, :], rb[:, :], ALU.mult)
                p.ev[(tb, "pj_v", oc)] = V.chan.cur()

            def pj_st(oc=oc, tb=tb):
                Y = p.Y
                Y.wait(p.ev[(tb, "pj_v", oc)])
                slY.dma(Y, y_d[oc * 128:(oc + 1) * 128, tb * 512:(tb + 1) * 512], yt[:, :])

            p.sec("tensor", pj_h1)
            p.sec("sync", wp_half2)
            p.sec("tensor", pj_mm)
            p.sec("sync", wp_next)
            p.sec("vector", pj_v)
            p.sec("sync", pj_st)

    for i in range(8):
        def ldy(i=i):
            Y = p.Y
            Y.wait(slY.done())
            Y.wait(slXr.done())
            Y.wait(p.ev.get(("res", i - 1)))
            slYt.dma_t(Y, ytm[:, :], y_d[:, i * 128:(i + 1) * 128])
            slXf.dma(Y, xr2[:, :], xr_d[i * 128:(i + 1) * 128, :])

        def addres(i=i):
            V = p.V
            V.wait(slYt.done())
            V.wait(slXf.done())
            V.wait(slO.done())
            V.op("tensor_add", xot[:, :], xr2[:, :], ytm[:, :])
            p.ev[("res", i)] = V.chan.cur()

        def sto(i=i):
            Y = p.Y
            Y.wait(p.ev[("res", i)])
            slO.dma(Y, xo[i * 128:(i + 1) * 128, :], xot[:, :])

        p.sec("sync", ldy)
        p.sec("vector", addres)
        p.sec("sync", sto)

    p.sec("sync", lambda: p.Y.wait(slO.done()))
    return p.fin()





def build_moe(add_partials=False):
    from contextlib import ExitStack
    p = Prog("moe").start()
    nc = p.nc
    x = nc.dram_tensor("x", [TLOC, D], F32, kind="ExternalInput").ap()
    if add_partials:
        p1 = nc.dram_tensor("p1", [TLOC, D], BF16, kind="ExternalInput").ap()
        p2 = nc.dram_tensor("p2", [TLOC, D], BF16, kind="ExternalInput").ap()
    gwT = nc.dram_tensor("gwT", [D, 8], F32, kind="ExternalInput").ap()
    guT = nc.dram_tensor("guT", [8, D, 2816], F32, kind="ExternalInput").ap()
    dnT = nc.dram_tensor("dnT", [8, 1408, D], F32, kind="ExternalInput").ap()
    sguT = nc.dram_tensor("sguT", [D, 5632], F32, kind="ExternalInput").ap()
    sdnT = nc.dram_tensor("sdnT", [2816, D], F32, kind="ExternalInput").ap()
    sel8 = nc.dram_tensor("sel8", [8, 8 * 128], F32, kind="ExternalInput").ap()
    ident = nc.dram_tensor("ident", [128, 128], F32, kind="ExternalInput").ap()
    xo = nc.dram_tensor("xo", [TLOC, D], F32, kind="ExternalOutput").ap()
    xln_d = nc.dram_tensor("xln_d", [TLOC, D], BF16).ap()
    y_d = nc.dram_tensor("y_d", [D, TLOC], BF16).ap()
    xr_d = nc.dram_tensor("xr_d", [TLOC, D], F32).ap()

    slX = p.slot("slX")
    slLn = p.slot("slLn")
    slLt = p.slot("slLt")
    slXr = p.slot("slXr")
    slXf = p.slot("slXf")
    slC = p.slot("slC")
    slW = [p.slot("slW0"), p.slot("slW1")]
    slD = [p.slot("slD0"), p.slot("slD1")]
    slY = p.slot("slY")
    slYt = p.slot("slYt")
    slO = p.slot("slO")

    xlnf = p.sb("xlnf", [128, 16, TLOC], BF16)
    gwb = p.sb("gwb", [128, 16, 8], BF16)
    selb = p.sb("selb", [8, 8 * 128], BF16)
    idb = p.sb("idb", [128, 128], BF16)
    cmb8 = p.sb("cmb8", [8, TLOC], BF16)
    cb_all = p.sb("cb_all", [128, 8, TLOC], BF16)
    yac = p.sb("yac", [128, 16, TLOC], F32)
    epsln = p.sb("epsln2", [128, 1], F32)
    psC = p.ps("psC", [128, 512])

    def init():
        G = p.G
        G.op("memset", epsln[:, :], LN_EPS)
        G.op("memset", yac[:, :, :], 0.0)
        slC.dma(G, gwb[:, :, :], gwT.rearrange("(kc p) e -> p kc e", p=128))
        slC.dma(G, selb[:, :], sel8[:, :])
        slC.dma(G, idb[:, :], ident[:, :])
        p.ev["init"] = p.G.chan.cur()

    p.sec("gpsimd", init)

    esA = ExitStack()
    xr = esA.enter_context(nc.sbuf_tensor("xr", [128, D], F32))
    sq = esA.enter_context(nc.sbuf_tensor("sqt", [128, D], F32))
    xlt = esA.enter_context(nc.sbuf_tensor("xlt", [128, D], BF16))
    st = esA.enter_context(nc.sbuf_tensor("st", [128, 8], F32))
    if add_partials:
        pb1 = esA.enter_context(nc.sbuf_tensor("pb1", [128, D], BF16))
        pb2 = esA.enter_context(nc.sbuf_tensor("pb2", [128, D], BF16))

    for i in range(8):
        def loadx(i=i):
            Y = p.Y
            Y.wait(p.ev.get(("lnv3", i - 1)))
            Y.wait(slXr.done())
            slX.dma(Y, xr[:, :], x[i * 128:(i + 1) * 128, :])
            if add_partials:
                slX.dma(Y, pb1[:, :], p1[i * 128:(i + 1) * 128, :])
                slX.dma(Y, pb2[:, :], p2[i * 128:(i + 1) * 128, :])

        def v1(i=i):
            V = p.V
            V.wait(slX.done())
            if add_partials:
                V.op("tensor_tensor", xr[:, :], xr[:, :], pb1[:, :], ALU.add)
                V.op("tensor_tensor", xr[:, :], xr[:, :], pb2[:, :], ALU.add)
            V.op("reduce_sum", st[:, 0:1], xr[:, :], axis=AX.X)
            p.ev[("lnv1", i)] = V.chan.cur()

        def stxr(i=i):
            Y = p.Y
            Y.wait(p.ev[("lnv1", i)])
            slXr.dma(Y, xr_d[i * 128:(i + 1) * 128, :], xr[:, :])

        def s1(i=i):
            S = p.S
            S.wait(p.ev[("lnv1", i)])
            S.op("activation", sq[:, :], xr[:, :], AF.Square)
            p.ev[("lns1", i)] = S.chan.cur()

        def v2(i=i):
            V = p.V
            V.wait(p.ev[("lns1", i)])
            V.op("reduce_sum", st[:, 1:2], sq[:, :], axis=AX.X)
            V.op("tensor_scalar", st[:, 0:1], st[:, 0:1], 1.0 / D, None, ALU.mult)
            V.op("tensor_scalar", st[:, 1:2], st[:, 1:2], 1.0 / D, None, ALU.mult)
            V.op("tensor_tensor", st[:, 2:3], st[:, 0:1], st[:, 0:1], ALU.mult)
            V.op("tensor_sub", st[:, 1:2], st[:, 1:2], st[:, 2:3])
            p.ev[("lnv2", i)] = V.chan.cur()

        def s2(i=i):
            S = p.S
            S.wait(p.ev[("lnv2", i)])
            S.wait(p.ev["init"])
            S.op("activation", st[:, 1:2], st[:, 1:2], AF.Ln, bias=epsln[:, 0:1])
            S.op("activation", st[:, 1:2], st[:, 1:2], AF.Exp, scale=-0.5)
            p.ev[("lns2", i)] = S.chan.cur()

        def v3(i=i):
            V = p.V
            V.wait(p.ev[("lns2", i)])
            V.wait(slLn.done())
            V.op("tensor_tensor", st[:, 3:4], st[:, 0:1], st[:, 1:2], ALU.mult)
            V.op("tensor_scalar", st[:, 3:4], st[:, 3:4], -1.0, None, ALU.mult)
            V.op("tensor_scalar", xlt[:, :], xr[:, :], st[:, 1:2], st[:, 3:4],
                 ALU.mult, ALU.add)
            p.ev[("lnv3", i)] = V.chan.cur()

        def stln(i=i):
            Y = p.Y
            Y.wait(p.ev[("lnv3", i)])
            slLn.dma(Y, xln_d[i * 128:(i + 1) * 128, :], xlt[:, :])

        p.sec("sync", loadx)
        p.sec("vector", v1)
        p.sec("sync", stxr)
        p.sec("scalar", s1)
        p.sec("vector", v2)
        p.sec("scalar", s2)
        p.sec("vector", v3)
        p.sec("sync", stln)

    def ldlnf():
        Y = p.Y
        Y.wait(slLn.done())
        for c in range(16):
            slLt.dma_t(Y, xlnf[:, c, :], xln_d[:, c * 128:(c + 1) * 128])

    p.sec("sync", ldlnf)
    xln_ready = slLt.done()

    lg = esA.enter_context(nc.sbuf_tensor("lg", [128, 8], F32))
    m1 = esA.enter_context(nc.sbuf_tensor("m1", [128, 4], F32))
    eqm = esA.enter_context(nc.sbuf_tensor("eqm", [128, 8], F32))
    smx = esA.enter_context(nc.sbuf_tensor("smx", [128, 8], F32))
    cmbt = esA.enter_context(nc.sbuf_tensor("cmbt", [128, 8], BF16))
    esP = ExitStack()
    psL = esP.enter_context(nc.psum_tensor("psL", [128, 8], F32))
    psT8 = esP.enter_context(nc.psum_tensor("psT8", [128, 128], BF16))

    for tc in range(8):
        def r_mm(tc=tc):
            T = p.T
            T.wait(xln_ready)
            T.wait(slC.done())
            T.wait(p.ev.get(("r_v1", tc - 1)))
            mm = None
            for kc in range(16):
                mm = T.raw("matmul", psL[:, :], xlnf[:, kc, tc * 128:(tc + 1) * 128],
                           gwb[:, kc, :], start=(kc == 0), stop=(kc == 15))
            p.ev[("r_mm", tc)] = p.T.chan.ev(mm)

        def r_v(tc=tc):
            V = p.V
            V.wait(p.ev[("r_mm", tc)])
            V.op("tensor_copy", lg[:, :], psL[:, :])
            V.op("reduce_max", m1[:, 0:1], lg[:, :], axis=AX.X)
            V.op("tensor_scalar", eqm[:, :], lg[:, :], m1[:, 0:1], None, ALU.is_ge)
            V.op("tensor_scalar", eqm[:, :], eqm[:, :], -1e30, None, ALU.mult)
            V.op("tensor_add", eqm[:, :], eqm[:, :], lg[:, :])
            V.op("reduce_max", m1[:, 1:2], eqm[:, :], axis=AX.X)
            V.op("tensor_scalar", eqm[:, :], lg[:, :], m1[:, 1:2], None, ALU.is_ge)
            V.op("tensor_scalar", m1[:, 2:3], m1[:, 0:1], -1.0, None, ALU.mult)
            p.ev[("r_v1", tc)] = V.chan.cur()

        def r_s(tc=tc):
            S = p.S
            S.wait(p.ev[("r_v1", tc)])
            S.op("activation", smx[:, :], lg[:, :], AF.Exp, bias=m1[:, 2:3],
                 accum_out=m1[:, 3:4])
            p.ev[("r_s", tc)] = S.chan.cur()

        def r_v2(tc=tc):
            V = p.V
            V.wait(p.ev[("r_s", tc)])
            V.op("reciprocal", m1[:, 3:4], m1[:, 3:4])
            V.op("tensor_mul", smx[:, :], smx[:, :], eqm[:, :])
            V.op("tensor_scalar", cmbt[:, :], smx[:, :], m1[:, 3:4], None, ALU.mult)
            p.ev[("r_v2", tc)] = V.chan.cur()

        def r_t(tc=tc):
            T = p.T
            T.wait(p.ev[("r_v2", tc)])
            T.wait(p.ev.get(("r_v3", tc - 1)))
            tr = T.raw("transpose", psT8[0:8, :], cmbt[:, :], idb[:, :])
            p.ev[("r_t", tc)] = p.T.chan.ev(tr)

        def r_v3(tc=tc):
            V = p.V
            V.wait(p.ev[("r_t", tc)])
            V.op("tensor_copy", cmb8[:, tc * 128:(tc + 1) * 128], psT8[0:8, 0:128])
            p.ev[("r_v3", tc)] = V.chan.cur()

        p.sec("tensor", r_mm)
        p.sec("vector", r_v)
        p.sec("scalar", r_s)
        p.sec("vector", r_v2)
        p.sec("tensor", r_t)
        p.sec("vector", r_v3)

    for e in range(8):
        for tb in range(2):
            def cb_t(e=e, tb=tb):
                T = p.T
                T.wait(p.ev[("r_v3", 7)])
                T.wait(p.ev.get(("cb_ve", e, tb - 1)) if tb > 0
                       else p.ev.get(("cb_ve", e - 1, 1)))
                mm = T.raw("matmul", psC[:, :], selb[:, e * 128:(e + 1) * 128],
                           cmb8[:, tb * 512:(tb + 1) * 512], start=True, stop=True)
                p.ev[("cb_te", e, tb)] = p.T.chan.ev(mm)

            def cb_v(e=e, tb=tb):
                V = p.V
                V.wait(p.ev[("cb_te", e, tb)])
                V.op("tensor_copy", cb_all[:, e, tb * 512:(tb + 1) * 512], psC[:, :])
                p.ev[("cb_ve", e, tb)] = V.chan.cur()
                if tb == 1:
                    p.ev[("cb_v", e)] = V.chan.cur()

            p.sec("tensor", cb_t)
            p.sec("vector", cb_v)

    esA.close()
    esP.close()

    esW = ExitStack()
    wg = esW.enter_context(nc.sbuf_tensor("wg", [128, 2, 16, 128], BF16))
    wu = esW.enter_context(nc.sbuf_tensor("wu", [128, 2, 16, 128], BF16))
    wd = esW.enter_context(nc.sbuf_tensor("wd", [128, 2, 11, 128], BF16))
    t1 = esW.enter_context(nc.sbuf_tensor("t1", [128, 512], F32))
    hf = esW.enter_context(nc.sbuf_tensor("hf", [128, 512], F32))
    act = esW.enter_context(nc.sbuf_tensor("act", [128, 22, TLOC], BF16))

    psG = [p.ps("psG0", [128, 512]), p.ps("psG1", [128, 512])]
    psU = [p.ps("psU0", [128, 512]), p.ps("psU1", [128, 512])]
    psP = [p.ps("psP0", [128, 512]), p.ps("psP1", [128, 512])]

    def gu_src(e, fc, gate):
        if e < 8:
            off = (0 if gate else 1408) + fc * 128
            return guT[e, :, off:off + 128].rearrange("(kc p) c -> p kc c", p=128)
        off = (0 if gate else 2816) + fc * 128
        return sguT[:, off:off + 128].rearrange("(kc p) c -> p kc c", p=128)

    def dn_src(e, oc, half):
        if e < 8:
            return dnT[e, :, oc * 128:(oc + 1) * 128].rearrange("(fc p) c -> p fc c", p=128)
        r0 = half * 1408
        return sdnT[r0:r0 + 1408, oc * 128:(oc + 1) * 128].rearrange("(fc p) c -> p fc c", p=128)

    wpi = 0
    for e in range(9):
        NFC = 11 if e < 8 else 22
        for fc in range(NFC):
            par = wpi % 2
            wpi += 1

            def gu_load(e=e, fc=fc, par=par):
                G = p.G
                G.wait(p.ev.get(("gu_mm2", e, fc - 2)))
                if e > 0 and fc < 2:
                    pNFC = 11 if e - 1 < 8 else 22
                    G.wait(p.ev.get(("gu_mm2", e - 1, pNFC - 2 + fc)))
                if e == 0 and fc == 0:
                    G.wait(slXr.done())
                    G.wait(slLn.done())
                slW[par].dma(G, wg[:, par, :, :], gu_src(e, fc, True))
                slW[par].dma(G, wu[:, par, :, :], gu_src(e, fc, False))

            p.sec("gpsimd", gu_load)

            for tb in range(2):
                def gu_mm(e=e, fc=fc, par=par, tb=tb):
                    T = p.T
                    T.wait(slW[par].done())
                    T.wait(xln_ready)
                    T.wait(p.ev.get(("act_v", e, fc - 1, tb)))
                    mm = None
                    for kc in range(16):
                        T.raw("matmul", psG[tb][:, :], wg[:, par, kc, :],
                              xlnf[:, kc, tb * 512:(tb + 1) * 512],
                              start=(kc == 0), stop=(kc == 15))
                        mm = T.raw("matmul", psU[tb][:, :], wu[:, par, kc, :],
                                   xlnf[:, kc, tb * 512:(tb + 1) * 512],
                                   start=(kc == 0), stop=(kc == 15))
                    p.ev[("gu_mm", e, fc, tb)] = p.T.chan.ev(mm)
                    if tb == 1:
                        p.ev[("gu_mm2", e, fc)] = p.ev[("gu_mm", e, fc, tb)]

                def act_s(e=e, fc=fc, tb=tb):
                    S = p.S
                    S.wait(p.ev[("gu_mm", e, fc, tb)])
                    S.wait(p.V.chan.cur())
                    S.op("activation", t1[:, :], psG[tb][:, :], AF.Silu)
                    p.ev[("act_s", e, fc, tb)] = S.chan.cur()

                def act_v(e=e, fc=fc, tb=tb):
                    V = p.V
                    V.wait(p.ev[("act_s", e, fc, tb)])
                    V.op("tensor_tensor", hf[:, :], t1[:, :], psU[tb][:, :], ALU.mult)
                    if e < 8:
                        V.wait(p.ev[("cb_v", e)])
                        V.op("tensor_tensor", act[:, fc, tb * 512:(tb + 1) * 512],
                             hf[:, :], cb_all[:, e, tb * 512:(tb + 1) * 512], ALU.mult)
                    else:
                        V.op("tensor_copy", act[:, fc, tb * 512:(tb + 1) * 512], hf[:, :])
                    p.ev[("act_v", e, fc, tb)] = V.chan.cur()

                p.sec("tensor", gu_mm)
                p.sec("scalar", act_s)
                p.sec("vector", act_v)

        NDC = 11 if e < 8 else 22
        for oc in range(16):
            dpar = oc % 2

            def dn_load(e=e, oc=oc, dpar=dpar):
                G = p.G
                G.wait(p.ev.get(("dn_mm", e, oc - 2, 1)))
                if e > 0 and oc < 2:
                    G.wait(p.ev.get(("dn_mm", e - 1, 14 + oc, 1)))
                slD[dpar].dma(G, wd[:, dpar, :, :], dn_src(e, oc, 0))

            p.sec("gpsimd", dn_load)

            for tb in range(2):
                def dn_mm(e=e, oc=oc, dpar=dpar, tb=tb, NDC=NDC):
                    T = p.T
                    T.wait(slD[dpar].done())
                    T.wait(p.ev[("act_v", e, (11 if e < 8 else 22) - 1, 1)])
                    T.wait(p.ev.get(("dn_v", e, oc - 1, tb)))
                    mm = None
                    for fc in range(11):
                        mm = T.raw("matmul", psP[tb][:, :], wd[:, dpar, fc, :],
                                   act[:, fc, tb * 512:(tb + 1) * 512],
                                   start=(fc == 0), stop=(fc == 10 and NDC == 11))
                    p.ev[("dn_mm_h1", e, oc, tb)] = p.T.chan.ev(mm)
                    p.ev[("dn_mm", e, oc, tb)] = p.ev[("dn_mm_h1", e, oc, tb)]

                p.sec("tensor", dn_mm)

            if NDC == 22:
                def dn_load2(e=e, oc=oc, dpar=dpar):
                    G = p.G
                    G.wait(p.ev[("dn_mm_h1", e, oc, 1)])
                    slD[dpar].dma(G, wd[:, dpar, :, :], dn_src(e, oc, 1))

                p.sec("gpsimd", dn_load2)

                for tb in range(2):
                    def dn_mm2(e=e, oc=oc, dpar=dpar, tb=tb):
                        T = p.T
                        T.wait(slD[dpar].done())
                        mm = None
                        for fc in range(11):
                            mm = T.raw("matmul", psP[tb][:, :], wd[:, dpar, fc, :],
                                       act[:, 11 + fc, tb * 512:(tb + 1) * 512],
                                       start=False, stop=(fc == 10))
                        p.ev[("dn_mm", e, oc, tb)] = p.T.chan.ev(mm)

                    p.sec("tensor", dn_mm2)

            for tb in range(2):
                def dn_v(e=e, oc=oc, tb=tb):
                    V = p.V
                    V.wait(p.ev[("dn_mm", e, oc, tb)])
                    V.op("tensor_tensor", yac[:, oc, tb * 512:(tb + 1) * 512],
                         yac[:, oc, tb * 512:(tb + 1) * 512], psP[tb][:, :], ALU.add)
                    p.ev[("dn_v", e, oc, tb)] = V.chan.cur()

                p.sec("vector", dn_v)

    esW.close()
    esF = ExitStack()
    ybf = esF.enter_context(nc.sbuf_tensor("ybf", [128, TLOC], BF16))
    ytm = esF.enter_context(nc.sbuf_tensor("ytm", [128, D], BF16))
    xr2 = esF.enter_context(nc.sbuf_tensor("xr2", [128, D], F32))
    xot = esF.enter_context(nc.sbuf_tensor("xot", [128, D], F32))

    for oc in range(16):
        def ycast(oc=oc):
            V = p.V
            V.wait(p.ev[("dn_v", 8, oc, 1)])
            V.wait(slY.done())
            V.op("tensor_copy", ybf[:, :], yac[:, oc, :])
            p.ev[("yc", oc)] = V.chan.cur()

        def yst(oc=oc):
            Y = p.Y
            Y.wait(p.ev[("yc", oc)])
            slY.dma(Y, y_d[oc * 128:(oc + 1) * 128, :], ybf[:, :])

        p.sec("vector", ycast)
        p.sec("sync", yst)

    for i in range(8):
        def ldy(i=i):
            Y = p.Y
            Y.wait(slY.done())
            Y.wait(p.ev.get(("res", i - 1)))
            slYt.dma_t(Y, ytm[:, :], y_d[:, i * 128:(i + 1) * 128])
            slXf.dma(Y, xr2[:, :], xr_d[i * 128:(i + 1) * 128, :])

        def addres(i=i):
            V = p.V
            V.wait(slYt.done())
            V.wait(slXf.done())
            V.wait(slO.done())
            V.op("tensor_add", xot[:, :], xr2[:, :], ytm[:, :])
            p.ev[("res", i)] = V.chan.cur()

        def sto(i=i):
            Y = p.Y
            Y.wait(p.ev[("res", i)])
            slO.dma(Y, xo[i * 128:(i + 1) * 128, :], xot[:, :])

        p.sec("sync", ldy)
        p.sec("vector", addres)
        p.sec("sync", sto)

    p.sec("sync", lambda: p.Y.wait(slO.done()))
    esF.close()
    return p.fin()





def build_attn_b0(half):
    from contextlib import ExitStack
    p = Prog("attnb0").start()
    nc = p.nc
    SEQ = 2048
    xT = nc.dram_tensor("xT", [D, SEQ], F32, kind="ExternalInput").ap()
    xh = nc.dram_tensor("xh", [TLOC, D], F32, kind="ExternalInput").ap()
    wqT = nc.dram_tensor("wqT", [D, D], F32, kind="ExternalInput").ap()
    wkT = nc.dram_tensor("wkT", [D, D], F32, kind="ExternalInput").ap()
    wvT = nc.dram_tensor("wvT", [D, D], F32, kind="ExternalInput").ap()
    woT = nc.dram_tensor("woT", [D, D], F32, kind="ExternalInput").ap()
    bq = nc.dram_tensor("bq", [1, D], F32, kind="ExternalInput").ap()
    bk = nc.dram_tensor("bk", [1, D], F32, kind="ExternalInput").ap()
    bv = nc.dram_tensor("bv", [1, D], F32, kind="ExternalInput").ap()
    bo = nc.dram_tensor("bo", [1, D], F32, kind="ExternalInput").ap()
    cosT = nc.dram_tensor("cosT", [128, SEQ], F32, kind="ExternalInput").ap()
    sinT = nc.dram_tensor("sinT", [128, SEQ], F32, kind="ExternalInput").ap()
    bmask = nc.dram_tensor("bmask", [128, 1024], F32, kind="ExternalInput").ap()
    xo = nc.dram_tensor("xo", [TLOC, D], F32, kind="ExternalOutput").ap()
    v_d = nc.dram_tensor("v_d", [SEQ, D], BF16).ap()
    o_d = nc.dram_tensor("o_d", [D, TLOC], BF16).ap()

    NHP = 16
    Q0 = half * TLOC

    slX = p.slot("slX")
    slW = [p.slot("slW0"), p.slot("slW1")]
    slB = p.slot("slB")
    slR = p.slot("slR")
    slV = [p.slot("slV0"), p.slot("slV1")]
    slVs = p.slot("slVs")
    slOd = p.slot("slOd")
    slOb = [p.slot("slOb0"), p.slot("slOb1")]
    slWo = p.slot("slWo")
    slH = p.slot("slH")
    slO = p.slot("slO")

    kf = p.sb("kf", [128, 16, SEQ], BF16)
    qf = p.sb("qf", [128, 16, TLOC], BF16)
    xc = p.sb("xc", [128, SEQ], F32)
    ones1 = p.sb("ones1", [1, 512], BF16)
    ones128 = p.sb("ones128", [128, 1], BF16)
    ones128f = p.sb("ones128f", [128, 1], F32)
    oA = p.sb("oA", [128, 1], F32)
    oB = p.sb("oB", [128, 1], F32)
    epsln = p.sb("epsln", [128, 1], F32)
    brow = p.sb("brow", [1, 4, D], BF16)
    srwb = p.sb("srwb", [1, 2, SEQ], BF16)

    def init():
        G = p.G
        G.op("memset", ones1[:, :], 1.0)
        G.op("memset", ones128[:, :], 1.0)
        G.op("memset", ones128f[:, :], 1.0)
        G.op("memset", oA[:, :], 0.0)
        G.op("memset", oA[0:64, :], 1.0)
        G.op("memset", oB[:, :], 0.0)
        G.op("memset", oB[64:128, :], 1.0)
        G.op("memset", epsln[:, :], LN_EPS)
        slB.dma(G, brow[:, 0, :], bq[:, :])
        slB.dma(G, brow[:, 1, :], bk[:, :])
        slB.dma(G, brow[:, 2, :], bv[:, :])
        slB.dma(G, brow[:, 3, :], bo[:, :])
        p.ev["init"] = p.G.chan.cur()

    p.sec("gpsimd", init)

    esA = ExitStack()
    sqc = esA.enter_context(nc.sbuf_tensor("sqc", [128, SEQ], F32))
    srow = esA.enter_context(nc.sbuf_tensor("srow", [1, 3 * SEQ], F32))
    psSb = [esA.enter_context(nc.psum_tensor("psSb%d" % b, [1, 512], F32)) for b in range(4)]
    psS2b = [esA.enter_context(nc.psum_tensor("psS2b%d" % b, [1, 512], F32)) for b in range(4)]

    for c in range(16):
        def ldx(c=c):
            Y = p.Y
            Y.wait(p.ev.get(("st_s", c - 1)))
            Y.wait(p.ev.get(("st_t1", c - 1)))
            slX.dma(Y, xc[:, :], xT[c * 128:(c + 1) * 128, :])

        def st_t1(c=c):
            T = p.T
            T.wait(slX.done())
            T.wait(p.ev["init"])
            mm = None
            for b in range(4):
                mm = T.raw("matmul", psSb[b][0:1, :], ones128f[:, 0:1],
                           xc[:, b * 512:(b + 1) * 512], start=(c == 0), stop=(c == 15))
            p.ev[("st_t1", c)] = p.T.chan.ev(mm)

        def st_s(c=c):
            S = p.S
            S.wait(slX.done())
            S.wait(p.ev.get(("st_t2", c - 1)))
            S.op("activation", sqc[:, :], xc[:, :], AF.Square)
            p.ev[("st_s", c)] = S.chan.cur()

        def st_t2(c=c):
            T = p.T
            T.wait(p.ev[("st_s", c)])
            mm = None
            for b in range(4):
                mm = T.raw("matmul", psS2b[b][0:1, :], ones128f[:, 0:1],
                           sqc[:, b * 512:(b + 1) * 512], start=(c == 0), stop=(c == 15))
            p.ev[("st_t2", c)] = p.T.chan.ev(mm)

        p.sec("sync", ldx)
        p.sec("tensor", st_t1)
        p.sec("scalar", st_s)
        p.sec("tensor", st_t2)

    def fin_v():
        V = p.V
        V.wait(p.ev[("st_t1", 15)])
        V.wait(p.ev[("st_t2", 15)])
        for b in range(4):
            sl = slice(b * 512, (b + 1) * 512)
            V.op("tensor_scalar", srow[0:1, sl], psSb[b][0:1, :], 1.0 / D, None, ALU.mult)
            V.op("tensor_scalar", srow[0:1, 2048 + b * 512:2048 + (b + 1) * 512],
                 psS2b[b][0:1, :], 1.0 / D, None, ALU.mult)
        V.op("tensor_tensor", srow[0:1, 4096:6144], srow[0:1, 0:2048],
             srow[0:1, 0:2048], ALU.mult)
        V.op("tensor_sub", srow[0:1, 2048:4096], srow[0:1, 2048:4096],
             srow[0:1, 4096:6144])
        p.ev["fin_v"] = V.chan.cur()

    def fin_s():
        S = p.S
        S.wait(p.ev["fin_v"])
        S.wait(p.ev["init"])
        S.op("activation", srow[0:1, 2048:4096], srow[0:1, 2048:4096], AF.Ln,
             bias=epsln[0:1, 0:1])
        S.op("activation", srwb[0:1, 0, :], srow[0:1, 2048:4096], AF.Exp, scale=-0.5)
        S.op("activation", srow[0:1, 4096:6144], srow[0:1, 2048:4096], AF.Exp, scale=-0.5)
        p.ev["fin_s"] = S.chan.cur()

    def fin_v3():
        V = p.V
        V.wait(p.ev["fin_s"])
        V.op("tensor_tensor", srow[0:1, 2048:4096], srow[0:1, 0:2048],
             srow[0:1, 4096:6144], ALU.mult)
        V.op("tensor_scalar", srwb[0:1, 1, :], srow[0:1, 2048:4096], -1.0, None, ALU.mult)
        p.ev["stats"] = V.chan.cur()

    p.sec("vector", fin_v)
    p.sec("scalar", fin_s)
    p.sec("vector", fin_v3)
    esA.close()

    esB = ExitStack()
    xlnf = esB.enter_context(nc.sbuf_tensor("xlnf", [128, 16, SEQ], BF16))
    psBc = [esB.enter_context(nc.psum_tensor("psBc%d" % i, [128, 512], F32)) for i in range(2)]
    psPj = [esB.enter_context(nc.psum_tensor("psPj%d" % i, [128, 512], F32)) for i in range(4)]
    esB1 = ExitStack()
    rnb = esB1.enter_context(nc.sbuf_tensor("rnb", [128, 2, SEQ], BF16))
    wbuf = esB1.enter_context(nc.sbuf_tensor("wbuf", [128, 1, 16, 128], BF16))

    for b in range(4):
        def bc_t(b=b):
            T = p.T
            T.wait(p.ev["stats"])
            T.wait(p.ev.get(("bc_v", b - 1)))
            T.raw("matmul", psBc[0][:, :], ones1[0:1, 0:128],
                  srwb[0:1, 0, b * 512:(b + 1) * 512], start=True, stop=True)
            m2 = T.raw("matmul", psBc[1][:, :], ones1[0:1, 0:128],
                       srwb[0:1, 1, b * 512:(b + 1) * 512], start=True, stop=True)
            p.ev[("bc_t", b)] = p.T.chan.ev(m2)

        def bc_v(b=b):
            V = p.V
            V.wait(p.ev[("bc_t", b)])
            V.op("tensor_copy", rnb[:, 0, b * 512:(b + 1) * 512], psBc[0][:, :])
            V.op("tensor_copy", rnb[:, 1, b * 512:(b + 1) * 512], psBc[1][:, :])
            p.ev[("bc_v", b)] = V.chan.cur()

        p.sec("tensor", bc_t)
        p.sec("vector", bc_v)

    for c in range(16):
        def ldx2(c=c):
            Y = p.Y
            Y.wait(p.ev["stats"])
            Y.wait(p.ev.get(("ap_v", c - 1)))
            slX.dma(Y, xc[:, :], xT[c * 128:(c + 1) * 128, :])

        def ap_v(c=c):
            V = p.V
            V.wait(slX.done())
            V.wait(p.ev[("bc_v", 3)])
            V.op("tensor_tensor", xc[:, :], xc[:, :], rnb[:, 0, :], ALU.mult)
            V.op("tensor_tensor", xlnf[:, c, :], xc[:, :], rnb[:, 1, :], ALU.add)
            p.ev[("ap_v", c)] = V.chan.cur()

        p.sec("sync", ldx2)
        p.sec("vector", ap_v)

    xln_ev = lambda: p.ev[("ap_v", 15)]

    def emit_fm_proj(tag, wsrc, bias_idx, dstf, ncols, c0, prev=None):
        nb = ncols // 512
        for oc in range(16):
            par = 0

            def wload(oc=oc, par=par):
                G = p.G
                if oc == 0 and prev is not None:
                    G.wait(p.ev[(prev, "mm", 15)])
                G.wait(p.ev.get((tag, "mm", oc - 1)))
                slW[par].dma(G, wbuf[:, par, :, :],
                             wsrc[:, oc * 128:(oc + 1) * 128]
                             .rearrange("(kc p) c -> p kc c", p=128))

            def mmf(oc=oc, par=0):
                T = p.T
                T.wait(slW[0].done())
                T.wait(slB.done())
                T.wait(xln_ev())
                if oc == 0 and prev is not None:
                    T.wait(p.ev[(prev, "ev", 15)])
                T.wait(p.ev.get((tag, "ev", oc - 1)))
                mm = None
                for b in range(nb):
                    for kc in range(16):
                        T.raw("matmul", psPj[b][:, :], wbuf[:, par, kc, :],
                              xlnf[:, kc, c0 + b * 512:c0 + (b + 1) * 512],
                              start=(kc == 0), stop=False)
                    mm = T.raw("matmul", psPj[b][:, :],
                               brow[0:1, bias_idx, oc * 128:(oc + 1) * 128],
                               ones1[0:1, :], start=False, stop=True)
                p.ev[(tag, "mm", oc)] = p.T.chan.ev(mm)

            def evf(oc=oc):
                V = p.V
                V.wait(p.ev[(tag, "mm", oc)])
                for b in range(nb):
                    V.op("tensor_copy", dstf[:, oc, b * 512:(b + 1) * 512], psPj[b][:, :])
                p.ev[(tag, "ev", oc)] = V.chan.cur()

            p.sec("gpsimd", wload)
            p.sec("tensor", mmf)
            p.sec("vector", evf)

    emit_fm_proj("q", wqT, 0, qf, TLOC, Q0)
    emit_fm_proj("k", wkT, 1, kf, SEQ, 0, prev="q")

    esB1.close()
    esB2 = ExitStack()
    wvb = esB2.enter_context(nc.sbuf_tensor("wvb", [128, 16, 256], BF16))
    ev_t = esB2.enter_context(nc.sbuf_tensor("ev_t", [128, 256], BF16))

    for oblk in range(8):
        def wvload(oblk=oblk):
            G = p.G
            if oblk == 0:
                G.wait(p.ev[("k", "mm", 15)])
            G.wait(p.ev.get(("v_mm", oblk - 1, 15)))
            slW[0].dma(G, wvb[:, :, :],
                       wvT[:, oblk * 256:(oblk + 1) * 256]
                       .rearrange("(kc p) c -> p kc c", p=128))

        p.sec("gpsimd", wvload)
        for tc in range(16):
            def v_mm(oblk=oblk, tc=tc):
                T = p.T
                T.wait(slW[0].done())
                T.wait(slB.done())
                if oblk == 0 and tc < 2:
                    T.wait(p.ev[("k", "ev", 15)])
                    T.wait(p.ev[("q", "ev", 15)])
                if tc < 2:
                    T.wait(p.ev.get(("v_ev", oblk - 1, 14 + tc)))
                T.wait(p.ev.get(("v_ev", oblk, tc - 2)))
                pj = psPj[tc % 2]
                for kc in range(16):
                    T.raw("matmul", pj[:, 0:256], xlnf[:, kc, tc * 128:(tc + 1) * 128],
                          wvb[:, kc, :], start=(kc == 0), stop=False)
                mm = T.raw("matmul", pj[:, 0:256], ones1[0:1, 0:128],
                           brow[0:1, 2, oblk * 256:(oblk + 1) * 256],
                           start=False, stop=True)
                p.ev[("v_mm", oblk, tc)] = p.T.chan.ev(mm)

            def v_ev(oblk=oblk, tc=tc):
                S = p.S
                S.wait(p.ev[("v_mm", oblk, tc)])
                S.wait(slVs.done())
                S.op("activation", ev_t[:, :], psPj[tc % 2][:, 0:256], AF.Sigmoid)
                p.ev[("v_ev", oblk, tc)] = S.chan.cur()

            def v_st(oblk=oblk, tc=tc):
                Y = p.Y
                Y.wait(p.ev[("v_ev", oblk, tc)])
                slVs.dma(Y, v_d[tc * 128:(tc + 1) * 128, oblk * 256:(oblk + 1) * 256],
                         ev_t[:, :])

            p.sec("tensor", v_mm)
            p.sec("scalar", v_ev)
            p.sec("sync", v_st)

    p.ev["proj_done"] = p.ev[("v_ev", 7, 15)]
    esB2.close()
    esB.close()

    esC = ExitStack()
    cost = esC.enter_context(nc.sbuf_tensor("cost", [128, SEQ], BF16))
    sint = esC.enter_context(nc.sbuf_tensor("sint", [128, SEQ], BF16))
    bmsk = esC.enter_context(nc.sbuf_tensor("bmsk", [128, 1024], BF16))
    rtmp = esC.enter_context(nc.sbuf_tensor("rtmp", [128, SEQ], BF16))
    rtm2 = esC.enter_context(nc.sbuf_tensor("rtm2", [128, SEQ], BF16))
    sq2 = esC.enter_context(nc.sbuf_tensor("sq2", [128, SEQ], F32))
    acc = esC.enter_context(nc.sbuf_tensor("acc", [128, 2], F32))
    crow = esC.enter_context(nc.sbuf_tensor("crow", [1, 3, 64], F32))
    cb128 = esC.enter_context(nc.sbuf_tensor("cb128", [128, 32], F32))
    vbuf = [esC.enter_context(nc.sbuf_tensor("vbuf%d" % i, [128, 16, 128], BF16))
            for i in range(2)]
    atA = esC.enter_context(nc.sbuf_tensor("atA", [128, 2, 512], BF16))
    atB = esC.enter_context(nc.sbuf_tensor("atB", [128, 2, 512], BF16))
    rsb = esC.enter_context(nc.sbuf_tensor("rsb", [1, 2, 512], BF16))
    nrm = esC.enter_context(nc.sbuf_tensor("nrm", [128, 512], F32))
    obuf2 = esC.enter_context(nc.sbuf_tensor("obuf2", [64, 1024], BF16))
    esCa = ExitStack()
    psC1 = esCa.enter_context(nc.psum_tensor("psC1", [128, 64], F32))

    def ldrope():
        G = p.G
        G.wait(p.ev["proj_done"])
        slR.dma(G, cost[:, :], cosT[:, :])
        slR.dma(G, sint[:, :], sinT[:, :])
        slR.dma(G, bmsk[:, :], bmask[:, :])

    p.sec("gpsimd", ldrope)

    def rope_chunk(E, buf, c, n, c0):
        E.op("tensor_copy", rtmp[0:32, 0:n], buf[32:64, c, 0:n])
        E.op("tensor_copy", rtmp[32:64, 0:n], buf[0:32, c, 0:n])
        E.op("tensor_copy", rtmp[64:96, 0:n], buf[96:128, c, 0:n])
        E.op("tensor_copy", rtmp[96:128, 0:n], buf[64:96, c, 0:n])
        E.op("tensor_tensor", rtm2[:, 0:n], buf[:, c, 0:n], cost[:, c0:c0 + n], ALU.mult)
        E.op("tensor_tensor", rtmp[:, 0:n], rtmp[:, 0:n], sint[:, c0:c0 + n], ALU.mult)
        E.op("tensor_tensor", buf[:, c, 0:n], rtm2[:, 0:n], rtmp[:, 0:n], ALU.add)

    for c in range(16):
        def rq(c=c):
            V = p.V
            V.wait(slR.done())
            rope_chunk(V, qf, c, TLOC, Q0)
            p.ev[("rq", c)] = V.chan.cur()

        p.sec("vector", rq)
    for c in range(16):
        def rk(c=c):
            V = p.V
            V.wait(slR.done())
            rope_chunk(V, kf, c, SEQ, 0)
            p.ev[("rk", c)] = V.chan.cur()

        p.sec("vector", rk)

    for hp in range(16):
        def cb_s(hp=hp):
            S = p.S
            S.wait(p.ev[("rq", hp)])
            S.wait(p.ev[("rk", hp)])
            S.wait(p.ev.get(("cb_t", hp - 1)))
            S.op("activation", sq2[:, 0:TLOC], qf[:, hp, :], AF.Square,
                 accum_out=acc[:, 0:1])
            S.op("activation", sq2[:, 0:SEQ], kf[:, hp, :], AF.Square,
                 accum_out=acc[:, 1:2])
            p.ev[("cb_s", hp)] = S.chan.cur()

        def cb_t(hp=hp):
            T = p.T
            T.wait(p.ev[("cb_s", hp)])
            T.raw("matmul", psC1[0:1, hp * 2:hp * 2 + 1], oA[:, 0:1], acc[:, 0:1],
                  start=True, stop=True)
            T.raw("matmul", psC1[0:1, hp * 2 + 1:hp * 2 + 2], oB[:, 0:1], acc[:, 0:1],
                  start=True, stop=True)
            T.raw("matmul", psC1[0:1, 32 + hp * 2:32 + hp * 2 + 1], oA[:, 0:1],
                  acc[:, 1:2], start=True, stop=True)
            mm = T.raw("matmul", psC1[0:1, 32 + hp * 2 + 1:32 + hp * 2 + 2], oB[:, 0:1],
                       acc[:, 1:2], start=True, stop=True)
            p.ev[("cb_t", hp)] = p.T.chan.ev(mm)

        p.sec("scalar", cb_s)
        p.sec("tensor", cb_t)

    def cfin_v():
        V = p.V
        V.wait(p.ev[("cb_t", 15)])
        V.op("tensor_scalar", crow[0:1, 0, 0:32], psC1[0:1, 0:32], 1.0 / TLOC, None, ALU.mult)
        V.op("tensor_scalar", crow[0:1, 1, 0:32], psC1[0:1, 32:64], 1.0 / SEQ, None, ALU.mult)
        V.op("tensor_tensor", crow[0:1, 0, 0:32], crow[0:1, 0, 0:32],
             crow[0:1, 1, 0:32], ALU.mult)
        p.ev["cfin_v"] = V.chan.cur()

    def cfin_s():
        S = p.S
        S.wait(p.ev["cfin_v"])
        S.op("activation", crow[0:1, 1, 0:32], crow[0:1, 0, 0:32], AF.Ln)
        S.op("activation", crow[0:1, 2, 0:32], crow[0:1, 1, 0:32], AF.Exp, scale=0.5)
        p.ev["cfin_s"] = S.chan.cur()

    def cfin_v2():
        V = p.V
        V.wait(p.ev["cfin_s"])
        V.op("tensor_scalar", crow[0:1, 2, 0:32], crow[0:1, 2, 0:32], -0.5, None, ALU.mult)
        V.op("tensor_copy", rsb[0:1, 0, 0:32], crow[0:1, 2, 0:32])
        p.ev["cfin_v2"] = V.chan.cur()

    def cfin_t():
        T = p.T
        T.wait(p.ev["cfin_v2"])
        mm = T.raw("matmul", psC1[:, 0:32], ones1[0:1, 0:128], rsb[0:1, 0, 0:32],
                   start=True, stop=True)
        p.ev["cfin_t"] = p.T.chan.ev(mm)

    def cfin_v3():
        V = p.V
        V.wait(p.ev["cfin_t"])
        V.op("tensor_copy", cb128[:, :], psC1[:, 0:32])
        p.ev["cbias"] = V.chan.cur()

    p.sec("vector", cfin_v)
    p.sec("scalar", cfin_s)
    p.sec("vector", cfin_v2)
    p.sec("tensor", cfin_t)
    p.sec("vector", cfin_v3)
    esCa.close()
    psSc = [esC.enter_context(nc.psum_tensor("psSc%d" % i, [128, 512], F32))
            for i in range(4)]
    psOa = esC.enter_context(nc.psum_tensor("psOa", [64, 512], F32))
    psOb = esC.enter_context(nc.psum_tensor("psOb", [64, 512], F32))
    psRa = esC.enter_context(nc.psum_tensor("psRa", [1, 512], F32))
    psRb = esC.enter_context(nc.psum_tensor("psRb", [1, 512], F32))

    for hp in range(16):
        def vload(hp=hp):
            Y = p.Y
            Y.wait(slVs.done())
            Y.wait(p.ev.get(("av_done", hp - 2)))
            slV[hp % 2].dma(Y, vbuf[hp % 2][:, :, :],
                            v_d[:, hp * 128:(hp + 1) * 128]
                            .rearrange("(kc p) c -> p kc c", p=128))

        p.sec("sync", vload)

        for qb in range(2):
            qlo = Q0 + qb * 512
            kc_hi = (qlo + 512) // 128
            band_lo = qlo // 128
            last_kc = kc_hi - 1

            for kc in range(kc_hi):
                sp = kc % 2

                def sc_t(hp=hp, qb=qb, kc=kc, sp=sp):
                    T = p.T
                    T.wait(p.ev["cbias"])
                    T.wait(p.ev[("rq", hp)])
                    T.wait(p.ev[("rk", hp)])
                    if kc < 2:
                        T.wait(p.ev.get(("onrm", hp, qb - 1)) or
                               p.ev.get(("onrm", hp - 1, 1)))
                    T.wait(p.ev.get(("ex", hp, qb, kc - 2)))
                    m1 = T.raw("matmul", psSc[sp * 2][:, :],
                               kf[0:64, hp, kc * 128:(kc + 1) * 128],
                               qf[0:64, hp, qb * 512:(qb + 1) * 512],
                               start=True, stop=True, tile_position=(0, 0))
                    m2 = T.raw("matmul", psSc[sp * 2 + 1][:, :],
                               kf[64:128, hp, kc * 128:(kc + 1) * 128],
                               qf[64:128, hp, qb * 512:(qb + 1) * 512],
                               start=True, stop=True, tile_position=(64, 0))
                    p.ev[("sc", hp, qb, kc)] = p.T.chan.ev(m2)

                def ex_s(hp=hp, qb=qb, kc=kc, sp=sp):
                    S = p.S
                    S.wait(p.ev[("sc", hp, qb, kc)])
                    S.wait(p.ev["cbias"])
                    S.wait(p.ev.get(("rsav", hp, qb, kc - 2)))
                    S.op("activation", atA[:, sp, :], psSc[sp * 2][:, :], AF.Exp,
                         scale=0.125, bias=cb128[:, hp * 2:hp * 2 + 1])
                    S.op("activation", atB[:, sp, :], psSc[sp * 2 + 1][:, :], AF.Exp,
                         scale=0.125, bias=cb128[:, hp * 2 + 1:hp * 2 + 2])
                    p.ev[("ex", hp, qb, kc)] = S.chan.cur()

                p.sec("tensor", sc_t)
                p.sec("scalar", ex_s)

                if kc >= band_lo:
                    off = 384 - (kc * 128 - qlo)

                    def msk_g(hp=hp, qb=qb, kc=kc, sp=sp, off=off):
                        G = p.G
                        G.wait(p.ev[("ex", hp, qb, kc)])
                        G.wait(slR.done())
                        G.op("tensor_tensor", atA[:, sp, :], atA[:, sp, :],
                             bmsk[:, off:off + 512], ALU.mult)
                        G.op("tensor_tensor", atB[:, sp, :], atB[:, sp, :],
                             bmsk[:, off:off + 512], ALU.mult)
                        p.ev[("mk", hp, qb, kc)] = p.G.chan.cur()

                    p.sec("gpsimd", msk_g)

                def rs_av(hp=hp, qb=qb, kc=kc, sp=sp, band=(kc >= band_lo),
                          first=(kc == 0), last=(kc == last_kc)):
                    T = p.T
                    if band:
                        T.wait(p.ev[("mk", hp, qb, kc)])
                    else:
                        T.wait(p.ev[("ex", hp, qb, kc)])
                    T.wait(slV[hp % 2].done())
                    if first:
                        T.wait(p.ev.get(("onrm", hp, qb - 1)) or
                               p.ev.get(("onrm", hp - 1, 1)))
                    T.raw("matmul", psRa[0:1, :], ones128[:, 0:1], atA[:, sp, :],
                          start=first, stop=last)
                    T.raw("matmul", psRb[0:1, :], ones128[:, 0:1], atB[:, sp, :],
                          start=first, stop=last)
                    T.raw("matmul", psOa[:, :], vbuf[hp % 2][:, kc, 0:64],
                          atA[:, sp, :], start=first, stop=last)
                    mm = T.raw("matmul", psOb[:, :], vbuf[hp % 2][:, kc, 64:128],
                               atB[:, sp, :], start=first, stop=last)
                    p.ev[("rsav", hp, qb, kc)] = p.T.chan.ev(mm)

                p.sec("tensor", rs_av)

            def onrm_v(hp=hp, qb=qb):
                V = p.V
                V.wait(p.ev[("rsav", hp, qb, last_kc)])
                V.op("reciprocal", nrm[0:1, :], psRa[0:1, :])
                V.op("tensor_copy", rsb[0:1, 0, :], nrm[0:1, :])
                V.op("reciprocal", nrm[0:1, :], psRb[0:1, :])
                V.op("tensor_copy", rsb[0:1, 1, :], nrm[0:1, :])
                p.ev[("onrm_v", hp, qb)] = V.chan.cur()

            def onrm_t(hp=hp, qb=qb):
                T = p.T
                T.wait(p.ev[("onrm_v", hp, qb)])
                T.raw("matmul", psSc[0][0:64, :], ones1[0:1, 0:64], rsb[0:1, 0, :],
                      start=True, stop=True)
                mm = T.raw("matmul", psSc[1][0:64, :], ones1[0:1, 0:64], rsb[0:1, 1, :],
                           start=True, stop=True)
                p.ev[("onrm_t", hp, qb)] = p.T.chan.ev(mm)

            def onrm_v2(hp=hp, qb=qb):
                V = p.V
                V.wait(p.ev[("onrm_t", hp, qb)])
                V.wait(slOd.done())
                V.op("tensor_copy", nrm[0:64, :], psSc[0][0:64, :])
                V.op("tensor_tensor", obuf2[0:64, 0:512], psOa[:, :], nrm[0:64, :], ALU.mult)
                V.op("tensor_copy", nrm[0:64, :], psSc[1][0:64, :])
                V.op("tensor_tensor", obuf2[0:64, 512:1024], psOb[:, :], nrm[0:64, :],
                     ALU.mult)
                p.ev[("onrm", hp, qb)] = V.chan.cur()

            def ost(hp=hp, qb=qb):
                Y = p.Y
                Y.wait(p.ev[("onrm", hp, qb)])
                slOd.dma(Y, o_d[hp * 128:hp * 128 + 64, qb * 512:(qb + 1) * 512],
                         obuf2[0:64, 0:512])
                slOd.dma(Y, o_d[hp * 128 + 64:hp * 128 + 128, qb * 512:(qb + 1) * 512],
                         obuf2[0:64, 512:1024])
                if qb == 1:
                    p.ev[("av_done", hp)] = slOd.done()

            p.sec("vector", onrm_v)
            p.sec("tensor", onrm_t)
            p.sec("vector", onrm_v2)
            p.sec("sync", ost)

    esC.close()

    esD = ExitStack()
    obuf = [esD.enter_context(nc.sbuf_tensor("obuf%d" % i, [128, TLOC], BF16))
            for i in range(2)]
    wob = esD.enter_context(nc.sbuf_tensor("wob", [128, 16, 512], BF16))
    xhb = esD.enter_context(nc.sbuf_tensor("xhb", [128, 512], F32))
    xot = esD.enter_context(nc.sbuf_tensor("xot", [128, 512], F32))
    psJ = [esD.enter_context(nc.psum_tensor("psJ%d" % i, [128, 512], F32))
           for i in range(8)]

    for oblk in range(4):
        def woload(oblk=oblk):
            G = p.G
            G.wait(p.ev[("onrm", 15, 1)])
            G.wait(p.ev.get(("oj_mm", oblk - 1, 15)))
            slWo.dma(G, wob[:, :, :],
                     woT[:, oblk * 512:(oblk + 1) * 512]
                     .rearrange("(ec p) c -> p ec c", p=128))

        p.sec("gpsimd", woload)

        for ec in range(16):
            def oload(oblk=oblk, ec=ec):
                Y = p.Y
                Y.wait(p.ev[("onrm", 15, 1)])
                Y.wait(slOd.done())
                Y.wait(p.ev.get(("oj_mm", oblk, ec - 2)))
                slOb[ec % 2].dma(Y, obuf[ec % 2][:, :], o_d[ec * 128:(ec + 1) * 128, :])

            def oj_mm(oblk=oblk, ec=ec):
                T = p.T
                T.wait(p.ev[("onrm", 15, 1)])
                T.wait(slOb[ec % 2].done())
                T.wait(slWo.done())
                T.wait(slB.done())
                if ec == 0:
                    T.wait(p.ev.get(("oj_v", oblk - 1, 7)))
                mm = None
                for tc in range(8):
                    mm = T.raw("matmul", psJ[tc][:, :], obuf[ec % 2][:, tc * 128:(tc + 1) * 128],
                               wob[:, ec, :], start=(ec == 0), stop=False)
                if ec == 15:
                    for tc in range(8):
                        mm = T.raw("matmul", psJ[tc][:, :], ones1[0:1, 0:128],
                                   brow[0:1, 3, oblk * 512:(oblk + 1) * 512],
                                   start=False, stop=True)
                p.ev[("oj_mm", oblk, ec)] = p.T.chan.ev(mm)

            p.sec("sync", oload)
            p.sec("tensor", oj_mm)

        for tc in range(8):
            def xld(oblk=oblk, tc=tc):
                Y = p.Y
                Y.wait(p.ev[("onrm", 15, 1)])
                Y.wait(p.ev.get(("oj_st", oblk, tc - 1)))
                slH.dma(Y, xhb[:, :], xh[tc * 128:(tc + 1) * 128,
                                        oblk * 512:(oblk + 1) * 512])

            def oj_v(oblk=oblk, tc=tc):
                V = p.V
                V.wait(p.ev[("oj_mm", oblk, 15)])
                V.wait(slH.done())
                V.wait(slO.done())
                V.op("tensor_tensor", xot[:, :], psJ[tc][:, :], xhb[:, :], ALU.add)
                p.ev[("oj_v", oblk, tc)] = V.chan.cur()

            def oj_st(oblk=oblk, tc=tc):
                Y = p.Y
                Y.wait(p.ev[("oj_v", oblk, tc)])
                slO.dma(Y, xo[tc * 128:(tc + 1) * 128, oblk * 512:(oblk + 1) * 512],
                        xot[:, :])
                p.ev[("oj_st", oblk, tc)] = slO.done()

            p.sec("sync", xld)
            p.sec("vector", oj_v)
            p.sec("sync", oj_st)

    p.sec("sync", lambda: p.Y.wait(slO.done()))
    esD.close()
    return p.fin()


def build_attn_b0h():
    from contextlib import ExitStack
    p = Prog("attnb0h").start()
    nc = p.nc
    SEQ = 2048
    EL = 1024
    xT = nc.dram_tensor("xT", [D, SEQ], F32, kind="ExternalInput").ap()
    wqT = nc.dram_tensor("wqT", [D, EL], F32, kind="ExternalInput").ap()
    wkT = nc.dram_tensor("wkT", [D, EL], F32, kind="ExternalInput").ap()
    wvT = nc.dram_tensor("wvT", [D, EL], F32, kind="ExternalInput").ap()
    woT = nc.dram_tensor("woT", [EL, D], F32, kind="ExternalInput").ap()
    bq = nc.dram_tensor("bq", [1, EL], F32, kind="ExternalInput").ap()
    bk = nc.dram_tensor("bk", [1, EL], F32, kind="ExternalInput").ap()
    bv = nc.dram_tensor("bv", [1, EL], F32, kind="ExternalInput").ap()
    bo = nc.dram_tensor("bo", [1, D], F32, kind="ExternalInput").ap()
    cosT = nc.dram_tensor("cosT", [128, SEQ], F32, kind="ExternalInput").ap()
    sinT = nc.dram_tensor("sinT", [128, SEQ], F32, kind="ExternalInput").ap()
    bmask = nc.dram_tensor("bmask", [128, 1024], F32, kind="ExternalInput").ap()
    py = nc.dram_tensor("py", [SEQ, D], BF16, kind="ExternalOutput").ap()
    v_d = nc.dram_tensor("v_d", [SEQ, EL], BF16).ap()
    o_d = nc.dram_tensor("o_d", [EL, SEQ], BF16).ap()

    slX = p.slot("slX")
    slW = [p.slot("slW0"), p.slot("slW1")]
    slB = p.slot("slB")
    slR = p.slot("slR")
    slV = [p.slot("slV0"), p.slot("slV1")]
    slVs = p.slot("slVs")
    slOd = p.slot("slOd")
    slOb = [p.slot("slOb0"), p.slot("slOb1")]
    slWo = p.slot("slWo")
    slO = p.slot("slO")

    kf = p.sb("kf", [128, 8, SEQ], BF16)
    qf = p.sb("qf", [128, 8, SEQ], BF16)
    xc = p.sb("xc", [128, SEQ], F32)
    ones1 = p.sb("ones1", [1, 512], BF16)
    ones128 = p.sb("ones128", [128, 1], BF16)
    ones128f = p.sb("ones128f", [128, 1], F32)
    oA = p.sb("oA", [128, 1], F32)
    oB = p.sb("oB", [128, 1], F32)
    epsln = p.sb("epsln", [128, 1], F32)
    brow = p.sb("brow", [1, 3, EL], BF16)
    borow = p.sb("borow", [1, D], BF16)
    srwb = p.sb("srwb", [1, 2, SEQ], BF16)

    def init():
        G = p.G
        G.op("memset", ones1[:, :], 1.0)
        G.op("memset", ones128[:, :], 1.0)
        G.op("memset", ones128f[:, :], 1.0)
        G.op("memset", oA[:, :], 0.0)
        G.op("memset", oA[0:64, :], 1.0)
        G.op("memset", oB[:, :], 0.0)
        G.op("memset", oB[64:128, :], 1.0)
        G.op("memset", epsln[:, :], LN_EPS)
        slB.dma(G, brow[:, 0, :], bq[:, :])
        slB.dma(G, brow[:, 1, :], bk[:, :])
        slB.dma(G, brow[:, 2, :], bv[:, :])
        slB.dma(G, borow[:, :], bo[:, :])
        p.ev["init"] = p.G.chan.cur()

    p.sec("gpsimd", init)

    esA = ExitStack()
    sqc = esA.enter_context(nc.sbuf_tensor("sqc", [128, SEQ], F32))
    srow = esA.enter_context(nc.sbuf_tensor("srow", [1, 3 * SEQ], F32))
    psSb = [esA.enter_context(nc.psum_tensor("psSb%d" % b, [1, 512], F32)) for b in range(4)]
    psS2b = [esA.enter_context(nc.psum_tensor("psS2b%d" % b, [1, 512], F32)) for b in range(4)]

    for c in range(16):
        def ldx(c=c):
            Y = p.Y
            Y.wait(p.ev.get(("st_s", c - 1)))
            Y.wait(p.ev.get(("st_t1", c - 1)))
            slX.dma(Y, xc[:, :], xT[c * 128:(c + 1) * 128, :])

        def st_t1(c=c):
            T = p.T
            T.wait(slX.done())
            T.wait(p.ev["init"])
            mm = None
            for b in range(4):
                mm = T.raw("matmul", psSb[b][0:1, :], ones128f[:, 0:1],
                           xc[:, b * 512:(b + 1) * 512], start=(c == 0), stop=(c == 15))
            p.ev[("st_t1", c)] = p.T.chan.ev(mm)

        def st_s(c=c):
            S = p.S
            S.wait(slX.done())
            S.wait(p.ev.get(("st_t2", c - 1)))
            S.op("activation", sqc[:, :], xc[:, :], AF.Square)
            p.ev[("st_s", c)] = S.chan.cur()

        def st_t2(c=c):
            T = p.T
            T.wait(p.ev[("st_s", c)])
            mm = None
            for b in range(4):
                mm = T.raw("matmul", psS2b[b][0:1, :], ones128f[:, 0:1],
                           sqc[:, b * 512:(b + 1) * 512], start=(c == 0), stop=(c == 15))
            p.ev[("st_t2", c)] = p.T.chan.ev(mm)

        p.sec("sync", ldx)
        p.sec("tensor", st_t1)
        p.sec("scalar", st_s)
        p.sec("tensor", st_t2)

    def fin_v():
        V = p.V
        V.wait(p.ev[("st_t1", 15)])
        V.wait(p.ev[("st_t2", 15)])
        for b in range(4):
            sl = slice(b * 512, (b + 1) * 512)
            V.op("tensor_scalar", srow[0:1, sl], psSb[b][0:1, :], 1.0 / D, None, ALU.mult)
            V.op("tensor_scalar", srow[0:1, 2048 + b * 512:2048 + (b + 1) * 512],
                 psS2b[b][0:1, :], 1.0 / D, None, ALU.mult)
        V.op("tensor_tensor", srow[0:1, 4096:6144], srow[0:1, 0:2048],
             srow[0:1, 0:2048], ALU.mult)
        V.op("tensor_sub", srow[0:1, 2048:4096], srow[0:1, 2048:4096],
             srow[0:1, 4096:6144])
        p.ev["fin_v"] = V.chan.cur()

    def fin_s():
        S = p.S
        S.wait(p.ev["fin_v"])
        S.wait(p.ev["init"])
        S.op("activation", srow[0:1, 2048:4096], srow[0:1, 2048:4096], AF.Ln,
             bias=epsln[0:1, 0:1])
        S.op("activation", srwb[0:1, 0, :], srow[0:1, 2048:4096], AF.Exp, scale=-0.5)
        S.op("activation", srow[0:1, 4096:6144], srow[0:1, 2048:4096], AF.Exp, scale=-0.5)
        p.ev["fin_s"] = S.chan.cur()

    def fin_v3():
        V = p.V
        V.wait(p.ev["fin_s"])
        V.op("tensor_tensor", srow[0:1, 2048:4096], srow[0:1, 0:2048],
             srow[0:1, 4096:6144], ALU.mult)
        V.op("tensor_scalar", srwb[0:1, 1, :], srow[0:1, 2048:4096], -1.0, None, ALU.mult)
        p.ev["stats"] = V.chan.cur()

    p.sec("vector", fin_v)
    p.sec("scalar", fin_s)
    p.sec("vector", fin_v3)
    esA.close()

    esB = ExitStack()
    xlnf = esB.enter_context(nc.sbuf_tensor("xlnf", [128, 16, SEQ], BF16))
    psBc = [esB.enter_context(nc.psum_tensor("psBc%d" % i, [128, 512], F32)) for i in range(2)]
    psPj = [esB.enter_context(nc.psum_tensor("psPj%d" % i, [128, 512], F32)) for i in range(4)]
    esB1 = ExitStack()
    rnb = esB1.enter_context(nc.sbuf_tensor("rnb", [128, 2, SEQ], BF16))
    wbuf = esB1.enter_context(nc.sbuf_tensor("wbuf", [128, 1, 16, 128], BF16))

    for b in range(4):
        def bc_t(b=b):
            T = p.T
            T.wait(p.ev["stats"])
            T.wait(p.ev.get(("bc_v", b - 1)))
            T.raw("matmul", psBc[0][:, :], ones1[0:1, 0:128],
                  srwb[0:1, 0, b * 512:(b + 1) * 512], start=True, stop=True)
            m2 = T.raw("matmul", psBc[1][:, :], ones1[0:1, 0:128],
                       srwb[0:1, 1, b * 512:(b + 1) * 512], start=True, stop=True)
            p.ev[("bc_t", b)] = p.T.chan.ev(m2)

        def bc_v(b=b):
            V = p.V
            V.wait(p.ev[("bc_t", b)])
            V.op("tensor_copy", rnb[:, 0, b * 512:(b + 1) * 512], psBc[0][:, :])
            V.op("tensor_copy", rnb[:, 1, b * 512:(b + 1) * 512], psBc[1][:, :])
            p.ev[("bc_v", b)] = V.chan.cur()

        p.sec("tensor", bc_t)
        p.sec("vector", bc_v)

    for c in range(16):
        def ldx2(c=c):
            Y = p.Y
            Y.wait(p.ev["stats"])
            Y.wait(p.ev.get(("ap_v", c - 1)))
            slX.dma(Y, xc[:, :], xT[c * 128:(c + 1) * 128, :])

        def ap_v(c=c):
            V = p.V
            V.wait(slX.done())
            V.wait(p.ev[("bc_v", 3)])
            V.op("tensor_tensor", xc[:, :], xc[:, :], rnb[:, 0, :], ALU.mult)
            V.op("tensor_tensor", xlnf[:, c, :], xc[:, :], rnb[:, 1, :], ALU.add)
            p.ev[("ap_v", c)] = V.chan.cur()

        p.sec("sync", ldx2)
        p.sec("vector", ap_v)

    xln_ev = lambda: p.ev[("ap_v", 15)]

    def emit_fm_proj(tag, wsrc, bias_idx, dstf, prev=None):
        for oc in range(8):
            def wload(oc=oc):
                G = p.G
                if oc == 0 and prev is not None:
                    G.wait(p.ev[(prev, "mm", 7)])
                G.wait(p.ev.get((tag, "mm", oc - 1)))
                slW[0].dma(G, wbuf[:, 0, :, :],
                           wsrc[:, oc * 128:(oc + 1) * 128]
                           .rearrange("(kc p) c -> p kc c", p=128))

            def mmf(tag=tag, oc=oc, bias_idx=bias_idx, prev=prev):
                T = p.T
                T.wait(slW[0].done())
                T.wait(slB.done())
                T.wait(xln_ev())
                if oc == 0 and prev is not None:
                    T.wait(p.ev[(prev, "ev", 7)])
                T.wait(p.ev.get((tag, "ev", oc - 1)))
                mm = None
                for b in range(4):
                    for kc in range(16):
                        T.raw("matmul", psPj[b][:, :], wbuf[:, 0, kc, :],
                              xlnf[:, kc, b * 512:(b + 1) * 512],
                              start=(kc == 0), stop=False)
                    mm = T.raw("matmul", psPj[b][:, :],
                               brow[0:1, bias_idx, oc * 128:(oc + 1) * 128],
                               ones1[0:1, :], start=False, stop=True)
                p.ev[(tag, "mm", oc)] = p.T.chan.ev(mm)

            def evf(tag=tag, oc=oc):
                V = p.V
                V.wait(p.ev[(tag, "mm", oc)])
                for b in range(4):
                    V.op("tensor_copy", dstf[:, oc, b * 512:(b + 1) * 512], psPj[b][:, :])
                p.ev[(tag, "ev", oc)] = V.chan.cur()

            p.sec("gpsimd", wload)
            p.sec("tensor", mmf)
            p.sec("vector", evf)

    emit_fm_proj("q", wqT, 0, qf)
    emit_fm_proj("k", wkT, 1, kf, prev="q")

    esB1.close()
    esB2 = ExitStack()
    wvb = esB2.enter_context(nc.sbuf_tensor("wvb", [128, 16, 256], BF16))
    ev_t = esB2.enter_context(nc.sbuf_tensor("ev_t", [128, 256], BF16))

    for oblk in range(4):
        def wvload(oblk=oblk):
            G = p.G
            if oblk == 0:
                G.wait(p.ev[("k", "mm", 7)])
            G.wait(p.ev.get(("v_mm", oblk - 1, 15)))
            slW[0].dma(G, wvb[:, :, :],
                       wvT[:, oblk * 256:(oblk + 1) * 256]
                       .rearrange("(kc p) c -> p kc c", p=128))

        p.sec("gpsimd", wvload)
        for tc in range(16):
            def v_mm(oblk=oblk, tc=tc):
                T = p.T
                T.wait(slW[0].done())
                T.wait(slB.done())
                if oblk == 0 and tc < 2:
                    T.wait(p.ev[("k", "ev", 7)])
                    T.wait(p.ev[("q", "ev", 7)])
                if tc < 2:
                    T.wait(p.ev.get(("v_ev", oblk - 1, 14 + tc)))
                T.wait(p.ev.get(("v_ev", oblk, tc - 2)))
                pj = psPj[tc % 2]
                for kc in range(16):
                    T.raw("matmul", pj[:, 0:256], xlnf[:, kc, tc * 128:(tc + 1) * 128],
                          wvb[:, kc, :], start=(kc == 0), stop=False)
                mm = T.raw("matmul", pj[:, 0:256], ones1[0:1, 0:128],
                           brow[0:1, 2, oblk * 256:(oblk + 1) * 256],
                           start=False, stop=True)
                p.ev[("v_mm", oblk, tc)] = p.T.chan.ev(mm)

            def v_ev(oblk=oblk, tc=tc):
                S = p.S
                S.wait(p.ev[("v_mm", oblk, tc)])
                S.wait(slVs.done())
                S.op("activation", ev_t[:, :], psPj[tc % 2][:, 0:256], AF.Sigmoid)
                p.ev[("v_ev", oblk, tc)] = S.chan.cur()

            def v_st(oblk=oblk, tc=tc):
                Y = p.Y
                Y.wait(p.ev[("v_ev", oblk, tc)])
                slVs.dma(Y, v_d[tc * 128:(tc + 1) * 128, oblk * 256:(oblk + 1) * 256],
                         ev_t[:, :])

            p.sec("tensor", v_mm)
            p.sec("scalar", v_ev)
            p.sec("sync", v_st)

    p.ev["proj_done"] = p.ev[("v_ev", 3, 15)]
    esB2.close()
    esB.close()

    esC = ExitStack()
    cost = esC.enter_context(nc.sbuf_tensor("cost", [128, SEQ], BF16))
    sint = esC.enter_context(nc.sbuf_tensor("sint", [128, SEQ], BF16))
    bmsk = esC.enter_context(nc.sbuf_tensor("bmsk", [128, 1024], BF16))
    rtmp = esC.enter_context(nc.sbuf_tensor("rtmp", [128, SEQ], BF16))
    rtm2 = esC.enter_context(nc.sbuf_tensor("rtm2", [128, SEQ], BF16))
    sq2 = esC.enter_context(nc.sbuf_tensor("sq2", [128, SEQ], F32))
    acc = esC.enter_context(nc.sbuf_tensor("acc", [128, 2], F32))
    crow = esC.enter_context(nc.sbuf_tensor("crow", [1, 3, 32], F32))
    cb128 = esC.enter_context(nc.sbuf_tensor("cb128", [128, 16], F32))
    vbuf = [esC.enter_context(nc.sbuf_tensor("vbuf%d" % i, [128, 16, 128], BF16))
            for i in range(2)]
    atA = esC.enter_context(nc.sbuf_tensor("atA", [128, 2, 512], BF16))
    atB = esC.enter_context(nc.sbuf_tensor("atB", [128, 2, 512], BF16))
    rsb = esC.enter_context(nc.sbuf_tensor("rsb", [1, 2, 512], BF16))
    nrm = esC.enter_context(nc.sbuf_tensor("nrm", [128, 512], F32))
    obuf2 = esC.enter_context(nc.sbuf_tensor("obuf2", [64, 1024], BF16))
    esCa = ExitStack()
    psC1 = esCa.enter_context(nc.psum_tensor("psC1", [128, 32], F32))

    def ldrope():
        G = p.G
        G.wait(p.ev["proj_done"])
        slR.dma(G, cost[:, :], cosT[:, :])
        slR.dma(G, sint[:, :], sinT[:, :])
        slR.dma(G, bmsk[:, :], bmask[:, :])

    p.sec("gpsimd", ldrope)

    def rope_chunk(E, buf, c):
        E.op("tensor_copy", rtmp[0:32, :], buf[32:64, c, :])
        E.op("tensor_copy", rtmp[32:64, :], buf[0:32, c, :])
        E.op("tensor_copy", rtmp[64:96, :], buf[96:128, c, :])
        E.op("tensor_copy", rtmp[96:128, :], buf[64:96, c, :])
        E.op("tensor_tensor", rtm2[:, :], buf[:, c, :], cost[:, :], ALU.mult)
        E.op("tensor_tensor", rtmp[:, :], rtmp[:, :], sint[:, :], ALU.mult)
        E.op("tensor_tensor", buf[:, c, :], rtm2[:, :], rtmp[:, :], ALU.add)

    for c in range(8):
        def rq(c=c):
            V = p.V
            V.wait(slR.done())
            rope_chunk(V, qf, c)
            p.ev[("rq", c)] = V.chan.cur()

        p.sec("vector", rq)
    for c in range(8):
        def rk(c=c):
            V = p.V
            V.wait(slR.done())
            rope_chunk(V, kf, c)
            p.ev[("rk", c)] = V.chan.cur()

        p.sec("vector", rk)

    for hp in range(8):
        def cb_s(hp=hp):
            S = p.S
            S.wait(p.ev[("rq", hp)])
            S.wait(p.ev[("rk", hp)])
            S.wait(p.ev.get(("cb_t", hp - 1)))
            S.op("activation", sq2[:, :], qf[:, hp, :], AF.Square,
                 accum_out=acc[:, 0:1])
            S.op("activation", sq2[:, :], kf[:, hp, :], AF.Square,
                 accum_out=acc[:, 1:2])
            p.ev[("cb_s", hp)] = S.chan.cur()

        def cb_t(hp=hp):
            T = p.T
            T.wait(p.ev[("cb_s", hp)])
            T.raw("matmul", psC1[0:1, hp * 2:hp * 2 + 1], oA[:, 0:1], acc[:, 0:1],
                  start=True, stop=True)
            T.raw("matmul", psC1[0:1, hp * 2 + 1:hp * 2 + 2], oB[:, 0:1], acc[:, 0:1],
                  start=True, stop=True)
            T.raw("matmul", psC1[0:1, 16 + hp * 2:16 + hp * 2 + 1], oA[:, 0:1],
                  acc[:, 1:2], start=True, stop=True)
            mm = T.raw("matmul", psC1[0:1, 16 + hp * 2 + 1:16 + hp * 2 + 2], oB[:, 0:1],
                       acc[:, 1:2], start=True, stop=True)
            p.ev[("cb_t", hp)] = p.T.chan.ev(mm)

        p.sec("scalar", cb_s)
        p.sec("tensor", cb_t)

    def cfin_v():
        V = p.V
        V.wait(p.ev[("cb_t", 7)])
        V.op("tensor_scalar", crow[0:1, 0, 0:16], psC1[0:1, 0:16], 1.0 / SEQ, None, ALU.mult)
        V.op("tensor_scalar", crow[0:1, 1, 0:16], psC1[0:1, 16:32], 1.0 / SEQ, None, ALU.mult)
        V.op("tensor_tensor", crow[0:1, 0, 0:16], crow[0:1, 0, 0:16],
             crow[0:1, 1, 0:16], ALU.mult)
        p.ev["cfin_v"] = V.chan.cur()

    def cfin_s():
        S = p.S
        S.wait(p.ev["cfin_v"])
        S.op("activation", crow[0:1, 1, 0:16], crow[0:1, 0, 0:16], AF.Ln)
        S.op("activation", crow[0:1, 2, 0:16], crow[0:1, 1, 0:16], AF.Exp, scale=0.5)
        p.ev["cfin_s"] = S.chan.cur()

    def cfin_v2():
        V = p.V
        V.wait(p.ev["cfin_s"])
        V.op("tensor_scalar", crow[0:1, 2, 0:16], crow[0:1, 2, 0:16], -0.5, None, ALU.mult)
        V.op("tensor_copy", rsb[0:1, 0, 0:16], crow[0:1, 2, 0:16])
        p.ev["cfin_v2"] = V.chan.cur()

    def cfin_t():
        T = p.T
        T.wait(p.ev["cfin_v2"])
        mm = T.raw("matmul", psC1[:, 0:16], ones1[0:1, 0:128], rsb[0:1, 0, 0:16],
                   start=True, stop=True)
        p.ev["cfin_t"] = p.T.chan.ev(mm)

    def cfin_v3():
        V = p.V
        V.wait(p.ev["cfin_t"])
        V.op("tensor_copy", cb128[:, :], psC1[:, 0:16])
        p.ev["cbias"] = V.chan.cur()

    p.sec("vector", cfin_v)
    p.sec("scalar", cfin_s)
    p.sec("vector", cfin_v2)
    p.sec("tensor", cfin_t)
    p.sec("vector", cfin_v3)
    esCa.close()
    psSc = [esC.enter_context(nc.psum_tensor("psSc%d" % i, [128, 512], F32))
            for i in range(4)]
    psOa = esC.enter_context(nc.psum_tensor("psOa", [64, 512], F32))
    psOb = esC.enter_context(nc.psum_tensor("psOb", [64, 512], F32))
    psRa = esC.enter_context(nc.psum_tensor("psRa", [1, 512], F32))
    psRb = esC.enter_context(nc.psum_tensor("psRb", [1, 512], F32))

    for hp in range(8):
        def vload(hp=hp):
            Y = p.Y
            Y.wait(slVs.done())
            Y.wait(p.ev.get(("av_done", hp - 2)))
            slV[hp % 2].dma(Y, vbuf[hp % 2][:, :, :],
                            v_d[:, hp * 128:(hp + 1) * 128]
                            .rearrange("(kc p) c -> p kc c", p=128))

        p.sec("sync", vload)

        for qb in range(4):
            qlo = qb * 512
            kc_hi = qb * 4 + 4
            band_lo = qb * 4
            last_kc = kc_hi - 1

            for kc in range(kc_hi):
                sp = kc % 2

                def sc_t(hp=hp, qb=qb, kc=kc, sp=sp):
                    T = p.T
                    T.wait(p.ev["cbias"])
                    T.wait(p.ev[("rq", hp)])
                    T.wait(p.ev[("rk", hp)])
                    if kc < 2:
                        T.wait(p.ev.get(("onrm", hp, qb - 1)) or
                               p.ev.get(("onrm", hp - 1, 3)))
                    T.wait(p.ev.get(("ex", hp, qb, kc - 2)))
                    m1 = T.raw("matmul", psSc[sp * 2][:, :],
                               kf[0:64, hp, kc * 128:(kc + 1) * 128],
                               qf[0:64, hp, qb * 512:(qb + 1) * 512],
                               start=True, stop=True, tile_position=(0, 0))
                    m2 = T.raw("matmul", psSc[sp * 2 + 1][:, :],
                               kf[64:128, hp, kc * 128:(kc + 1) * 128],
                               qf[64:128, hp, qb * 512:(qb + 1) * 512],
                               start=True, stop=True, tile_position=(64, 0))
                    p.ev[("sc", hp, qb, kc)] = p.T.chan.ev(m2)

                def ex_s(hp=hp, qb=qb, kc=kc, sp=sp):
                    S = p.S
                    S.wait(p.ev[("sc", hp, qb, kc)])
                    S.wait(p.ev["cbias"])
                    S.wait(p.ev.get(("rsav", hp, qb, kc - 2)))
                    S.op("activation", atA[:, sp, :], psSc[sp * 2][:, :], AF.Exp,
                         scale=0.125, bias=cb128[:, hp * 2:hp * 2 + 1])
                    S.op("activation", atB[:, sp, :], psSc[sp * 2 + 1][:, :], AF.Exp,
                         scale=0.125, bias=cb128[:, hp * 2 + 1:hp * 2 + 2])
                    p.ev[("ex", hp, qb, kc)] = S.chan.cur()

                p.sec("tensor", sc_t)
                p.sec("scalar", ex_s)

                if kc >= band_lo:
                    off = 384 - (kc * 128 - qlo)

                    def msk_g(hp=hp, qb=qb, kc=kc, sp=sp, off=off):
                        G = p.G
                        G.wait(p.ev[("ex", hp, qb, kc)])
                        G.wait(slR.done())
                        G.op("tensor_tensor", atA[:, sp, :], atA[:, sp, :],
                             bmsk[:, off:off + 512], ALU.mult)
                        G.op("tensor_tensor", atB[:, sp, :], atB[:, sp, :],
                             bmsk[:, off:off + 512], ALU.mult)
                        p.ev[("mk", hp, qb, kc)] = p.G.chan.cur()

                    p.sec("gpsimd", msk_g)

                def rs_av(hp=hp, qb=qb, kc=kc, sp=sp, band=(kc >= band_lo),
                          first=(kc == 0), last=(kc == last_kc)):
                    T = p.T
                    if band:
                        T.wait(p.ev[("mk", hp, qb, kc)])
                    else:
                        T.wait(p.ev[("ex", hp, qb, kc)])
                    T.wait(slV[hp % 2].done())
                    if first:
                        T.wait(p.ev.get(("onrm", hp, qb - 1)) or
                               p.ev.get(("onrm", hp - 1, 3)))
                    T.raw("matmul", psRa[0:1, :], ones128[:, 0:1], atA[:, sp, :],
                          start=first, stop=last)
                    T.raw("matmul", psRb[0:1, :], ones128[:, 0:1], atB[:, sp, :],
                          start=first, stop=last)
                    T.raw("matmul", psOa[:, :], vbuf[hp % 2][:, kc, 0:64],
                          atA[:, sp, :], start=first, stop=last)
                    mm = T.raw("matmul", psOb[:, :], vbuf[hp % 2][:, kc, 64:128],
                               atB[:, sp, :], start=first, stop=last)
                    p.ev[("rsav", hp, qb, kc)] = p.T.chan.ev(mm)

                p.sec("tensor", rs_av)

            def onrm_v(hp=hp, qb=qb, last_kc=last_kc):
                V = p.V
                V.wait(p.ev[("rsav", hp, qb, last_kc)])
                V.op("reciprocal", nrm[0:1, :], psRa[0:1, :])
                V.op("tensor_copy", rsb[0:1, 0, :], nrm[0:1, :])
                V.op("reciprocal", nrm[0:1, :], psRb[0:1, :])
                V.op("tensor_copy", rsb[0:1, 1, :], nrm[0:1, :])
                p.ev[("onrm_v", hp, qb)] = V.chan.cur()

            def onrm_t(hp=hp, qb=qb):
                T = p.T
                T.wait(p.ev[("onrm_v", hp, qb)])
                T.raw("matmul", psSc[0][0:64, :], ones1[0:1, 0:64], rsb[0:1, 0, :],
                      start=True, stop=True)
                mm = T.raw("matmul", psSc[1][0:64, :], ones1[0:1, 0:64], rsb[0:1, 1, :],
                           start=True, stop=True)
                p.ev[("onrm_t", hp, qb)] = p.T.chan.ev(mm)

            def onrm_v2(hp=hp, qb=qb):
                V = p.V
                V.wait(p.ev[("onrm_t", hp, qb)])
                V.wait(slOd.done())
                V.op("tensor_copy", nrm[0:64, :], psSc[0][0:64, :])
                V.op("tensor_tensor", obuf2[0:64, 0:512], psOa[:, :], nrm[0:64, :], ALU.mult)
                V.op("tensor_copy", nrm[0:64, :], psSc[1][0:64, :])
                V.op("tensor_tensor", obuf2[0:64, 512:1024], psOb[:, :], nrm[0:64, :],
                     ALU.mult)
                p.ev[("onrm", hp, qb)] = V.chan.cur()

            def ost(hp=hp, qb=qb):
                Y = p.Y
                Y.wait(p.ev[("onrm", hp, qb)])
                slOd.dma(Y, o_d[hp * 128:hp * 128 + 64, qb * 512:(qb + 1) * 512],
                         obuf2[0:64, 0:512])
                slOd.dma(Y, o_d[hp * 128 + 64:hp * 128 + 128, qb * 512:(qb + 1) * 512],
                         obuf2[0:64, 512:1024])
                if qb == 3:
                    p.ev[("av_done", hp)] = slOd.done()

            p.sec("vector", onrm_v)
            p.sec("tensor", onrm_t)
            p.sec("vector", onrm_v2)
            p.sec("sync", ost)

    esC.close()

    esD = ExitStack()
    obuf = [esD.enter_context(nc.sbuf_tensor("obuf%d" % i, [128, 1024], BF16))
            for i in range(2)]
    wob = esD.enter_context(nc.sbuf_tensor("wob", [128, 8, 512], BF16))
    pyt = esD.enter_context(nc.sbuf_tensor("pyt", [128, 2, 512], BF16))
    psJ = [esD.enter_context(nc.psum_tensor("psJ%d" % i, [128, 512], F32))
           for i in range(8)]

    for oblk in range(4):
        def woload(oblk=oblk):
            G = p.G
            G.wait(p.ev[("onrm", 7, 3)])
            G.wait(p.ev.get(("oj_mm", oblk - 1, 1, 7)))
            slWo.dma(G, wob[:, :, :],
                     woT[:, oblk * 512:(oblk + 1) * 512]
                     .rearrange("(ec p) c -> p ec c", p=128))

        p.sec("gpsimd", woload)

        for tg in range(2):
            for ec in range(8):
                def oload(oblk=oblk, tg=tg, ec=ec):
                    Y = p.Y
                    Y.wait(p.ev[("onrm", 7, 3)])
                    Y.wait(slOd.done())
                    Y.wait(p.ev.get(("oj_mm", oblk, tg, ec - 2)) or
                           (p.ev.get(("oj_mm", oblk, tg - 1, 6 + ec)) if ec < 2 else None) or
                           (p.ev.get(("oj_mm", oblk - 1, 1, 6 + ec)) if ec < 2 else None))
                    slOb[ec % 2].dma(Y, obuf[ec % 2][:, :],
                                     o_d[ec * 128:(ec + 1) * 128,
                                         tg * 1024:(tg + 1) * 1024])

                def oj_mm(oblk=oblk, tg=tg, ec=ec):
                    T = p.T
                    T.wait(p.ev[("onrm", 7, 3)])
                    T.wait(slOb[ec % 2].done())
                    T.wait(slWo.done())
                    T.wait(slB.done())
                    if ec == 0:
                        T.wait(p.ev.get(("oj_ev", oblk, tg - 1, 7)) or
                               p.ev.get(("oj_ev", oblk - 1, 1, 7)))
                    mm = None
                    for tc in range(8):
                        mm = T.raw("matmul", psJ[tc][:, :],
                                   obuf[ec % 2][:, tc * 128:(tc + 1) * 128],
                                   wob[:, ec, :], start=(ec == 0), stop=False)
                    if ec == 7:
                        for tc in range(8):
                            mm = T.raw("matmul", psJ[tc][:, :], ones1[0:1, 0:128],
                                       borow[0:1, oblk * 512:(oblk + 1) * 512],
                                       start=False, stop=True)
                    p.ev[("oj_mm", oblk, tg, ec)] = p.T.chan.ev(mm)

                p.sec("sync", oload)
                p.sec("tensor", oj_mm)

            for tc in range(8):
                def oj_ev(oblk=oblk, tg=tg, tc=tc):
                    V = p.V
                    V.wait(p.ev[("oj_mm", oblk, tg, 7)])
                    V.wait(slO.done())
                    V.op("tensor_copy", pyt[:, tc % 2, :], psJ[tc][:, :])
                    if tc == 7:
                        p.ev[("oj_ev", oblk, tg, 7)] = V.chan.cur()
                    p.ev[("oj_pv", oblk, tg, tc)] = V.chan.cur()

                def oj_st(oblk=oblk, tg=tg, tc=tc):
                    Y = p.Y
                    Y.wait(p.ev[("oj_pv", oblk, tg, tc)])
                    slO.dma(Y, py[tg * 1024 + tc * 128:tg * 1024 + (tc + 1) * 128,
                                  oblk * 512:(oblk + 1) * 512], pyt[:, tc % 2, :])

                p.sec("vector", oj_ev)
                p.sec("sync", oj_st)

    p.sec("sync", lambda: p.Y.wait(slO.done()))
    esD.close()
    return p.fin()


def build_attn_b1(half):
    from contextlib import ExitStack
    p = Prog("attnb1").start()
    nc = p.nc
    SEQ = 2048
    xT = nc.dram_tensor("xT", [D, SEQ], F32, kind="ExternalInput").ap()
    xh = nc.dram_tensor("xh", [TLOC, D], F32, kind="ExternalInput").ap()
    wqT = nc.dram_tensor("wqT", [D, D], F32, kind="ExternalInput").ap()
    wsmT = nc.dram_tensor("wsmT", [D, 320], F32, kind="ExternalInput").ap()
    woT = nc.dram_tensor("woT", [D, D], F32, kind="ExternalInput").ap()
    cosT = nc.dram_tensor("cosT", [128, SEQ], F32, kind="ExternalInput").ap()
    sinT = nc.dram_tensor("sinT", [128, SEQ], F32, kind="ExternalInput").ap()
    bmask = nc.dram_tensor("bmask", [128, 1024], F32, kind="ExternalInput").ap()
    xo = nc.dram_tensor("xo", [TLOC, D], F32, kind="ExternalOutput").ap()
    sm_d = nc.dram_tensor("sm_d", [32, SEQ], F32).ap()
    v_d = nc.dram_tensor("v_d", [D, SEQ], BF16).ap()
    o_d = nc.dram_tensor("o_d", [D, TLOC], BF16).ap()

    Q0 = half * TLOC
    SCL = 0.08838834764831845
    CMUL = -0.04419417382415922

    slX = p.slot("slX")
    slW = [p.slot("slW0"), p.slot("slW1")]
    slR = p.slot("slR")
    slSm = p.slot("slSm")
    slKr = p.slot("slKr")
    slVs = p.slot("slVs")
    slV = [p.slot("slV0"), p.slot("slV1")]
    slOd = p.slot("slOd")
    slOb = [p.slot("slOb0"), p.slot("slOb1")]
    slWo = p.slot("slWo")
    slH = p.slot("slH")
    slO = p.slot("slO")

    qf = p.sb("qf", [128, 16, TLOC], BF16)
    kpf = p.sb("kpf", [128, SEQ], BF16)
    vpf = p.sb("vpf", [128, SEQ], BF16)
    hgf = p.sb("hgf", [64, SEQ], F32)
    xc = p.sb("xc", [128, SEQ], F32)
    ones1 = p.sb("ones1", [1, 512], BF16)
    ones128 = p.sb("ones128", [128, 1], BF16)
    ones128f = p.sb("ones128f", [128, 1], F32)
    epsln = p.sb("epsln", [128, 1], F32)
    onef = p.sb("onef", [128, 1], F32)
    srwb = p.sb("srwb", [1, 2, SEQ], BF16)

    def init():
        G = p.G
        G.op("memset", ones1[:, :], 1.0)
        G.op("memset", ones128[:, :], 1.0)
        G.op("memset", ones128f[:, :], 1.0)
        G.op("memset", epsln[:, :], LN_EPS)
        G.op("memset", onef[:, :], 1.0)
        p.ev["init"] = p.G.chan.cur()

    p.sec("gpsimd", init)

    esA = ExitStack()
    sqc = esA.enter_context(nc.sbuf_tensor("sqc", [128, SEQ], F32))
    srow = esA.enter_context(nc.sbuf_tensor("srow", [1, 3 * SEQ], F32))
    psSb = [esA.enter_context(nc.psum_tensor("psSb%d" % b, [1, 512], F32)) for b in range(4)]
    psS2b = [esA.enter_context(nc.psum_tensor("psS2b%d" % b, [1, 512], F32)) for b in range(4)]

    for c in range(16):
        def ldx(c=c):
            Y = p.Y
            Y.wait(p.ev.get(("st_s", c - 1)))
            Y.wait(p.ev.get(("st_t1", c - 1)))
            slX.dma(Y, xc[:, :], xT[c * 128:(c + 1) * 128, :])

        def st_t1(c=c):
            T = p.T
            T.wait(slX.done())
            T.wait(p.ev["init"])
            mm = None
            for b in range(4):
                mm = T.raw("matmul", psSb[b][0:1, :], ones128f[:, 0:1],
                           xc[:, b * 512:(b + 1) * 512], start=(c == 0), stop=(c == 15))
            p.ev[("st_t1", c)] = p.T.chan.ev(mm)

        def st_s(c=c):
            S = p.S
            S.wait(slX.done())
            S.wait(p.ev.get(("st_t2", c - 1)))
            S.op("activation", sqc[:, :], xc[:, :], AF.Square)
            p.ev[("st_s", c)] = S.chan.cur()

        def st_t2(c=c):
            T = p.T
            T.wait(p.ev[("st_s", c)])
            mm = None
            for b in range(4):
                mm = T.raw("matmul", psS2b[b][0:1, :], ones128f[:, 0:1],
                           sqc[:, b * 512:(b + 1) * 512], start=(c == 0), stop=(c == 15))
            p.ev[("st_t2", c)] = p.T.chan.ev(mm)

        p.sec("sync", ldx)
        p.sec("tensor", st_t1)
        p.sec("scalar", st_s)
        p.sec("tensor", st_t2)

    def fin_v():
        V = p.V
        V.wait(p.ev[("st_t1", 15)])
        V.wait(p.ev[("st_t2", 15)])
        for b in range(4):
            sl = slice(b * 512, (b + 1) * 512)
            V.op("tensor_scalar", srow[0:1, sl], psSb[b][0:1, :], 1.0 / D, None, ALU.mult)
            V.op("tensor_scalar", srow[0:1, 2048 + b * 512:2048 + (b + 1) * 512],
                 psS2b[b][0:1, :], 1.0 / D, None, ALU.mult)
        V.op("tensor_tensor", srow[0:1, 4096:6144], srow[0:1, 0:2048],
             srow[0:1, 0:2048], ALU.mult)
        V.op("tensor_sub", srow[0:1, 2048:4096], srow[0:1, 2048:4096],
             srow[0:1, 4096:6144])
        p.ev["fin_v"] = V.chan.cur()

    def fin_s():
        S = p.S
        S.wait(p.ev["fin_v"])
        S.wait(p.ev["init"])
        S.op("activation", srow[0:1, 2048:4096], srow[0:1, 2048:4096], AF.Ln,
             bias=epsln[0:1, 0:1])
        S.op("activation", srwb[0:1, 0, :], srow[0:1, 2048:4096], AF.Exp, scale=-0.5)
        S.op("activation", srow[0:1, 4096:6144], srow[0:1, 2048:4096], AF.Exp, scale=-0.5)
        p.ev["fin_s"] = S.chan.cur()

    def fin_v3():
        V = p.V
        V.wait(p.ev["fin_s"])
        V.op("tensor_tensor", srow[0:1, 2048:4096], srow[0:1, 0:2048],
             srow[0:1, 4096:6144], ALU.mult)
        V.op("tensor_scalar", srwb[0:1, 1, :], srow[0:1, 2048:4096], -1.0, None, ALU.mult)
        p.ev["stats"] = V.chan.cur()

    p.sec("vector", fin_v)
    p.sec("scalar", fin_s)
    p.sec("vector", fin_v3)
    esA.close()

    esB = ExitStack()
    xlnf = esB.enter_context(nc.sbuf_tensor("xlnf", [128, 16, SEQ], BF16))
    psBc = [esB.enter_context(nc.psum_tensor("psBc%d" % i, [128, 512], F32)) for i in range(2)]
    psPj = [esB.enter_context(nc.psum_tensor("psPj%d" % i, [128, 512], F32)) for i in range(4)]
    rnb = esB.enter_context(nc.sbuf_tensor("rnb", [128, 2, SEQ], BF16))
    wbuf = esB.enter_context(nc.sbuf_tensor("wbuf", [128, 1, 16, 128], BF16))

    for b in range(4):
        def bc_t(b=b):
            T = p.T
            T.wait(p.ev["stats"])
            T.wait(p.ev.get(("bc_v", b - 1)))
            T.raw("matmul", psBc[0][:, :], ones1[0:1, 0:128],
                  srwb[0:1, 0, b * 512:(b + 1) * 512], start=True, stop=True)
            m2 = T.raw("matmul", psBc[1][:, :], ones1[0:1, 0:128],
                       srwb[0:1, 1, b * 512:(b + 1) * 512], start=True, stop=True)
            p.ev[("bc_t", b)] = p.T.chan.ev(m2)

        def bc_v(b=b):
            V = p.V
            V.wait(p.ev[("bc_t", b)])
            V.op("tensor_copy", rnb[:, 0, b * 512:(b + 1) * 512], psBc[0][:, :])
            V.op("tensor_copy", rnb[:, 1, b * 512:(b + 1) * 512], psBc[1][:, :])
            p.ev[("bc_v", b)] = V.chan.cur()

        p.sec("tensor", bc_t)
        p.sec("vector", bc_v)

    for c in range(16):
        def ldx2(c=c):
            Y = p.Y
            Y.wait(p.ev["stats"])
            Y.wait(p.ev.get(("ap_v", c - 1)))
            slX.dma(Y, xc[:, :], xT[c * 128:(c + 1) * 128, :])

        def ap_v(c=c):
            V = p.V
            V.wait(slX.done())
            V.wait(p.ev[("bc_v", 3)])
            V.op("tensor_tensor", xc[:, :], xc[:, :], rnb[:, 0, :], ALU.mult)
            V.op("tensor_tensor", xlnf[:, c, :], xc[:, :], rnb[:, 1, :], ALU.add)
            p.ev[("ap_v", c)] = V.chan.cur()

        p.sec("sync", ldx2)
        p.sec("vector", ap_v)

    xln_ev = lambda: p.ev[("ap_v", 15)]

    for oc in range(16):
        def wload(oc=oc):
            G = p.G
            G.wait(p.ev.get(("q", "mm", oc - 1)))
            slW[0].dma(G, wbuf[:, 0, :, :],
                       wqT[:, oc * 128:(oc + 1) * 128]
                       .rearrange("(kc p) c -> p kc c", p=128))

        def mmf(oc=oc):
            T = p.T
            T.wait(slW[0].done())
            T.wait(xln_ev())
            T.wait(p.ev.get(("q", "ev", oc - 1)))
            mm = None
            for b in range(2):
                for kc in range(16):
                    mm = T.raw("matmul", psPj[b][:, :], wbuf[:, 0, kc, :],
                               xlnf[:, kc, Q0 + b * 512:Q0 + (b + 1) * 512],
                               start=(kc == 0), stop=(kc == 15))
            p.ev[("q", "mm", oc)] = p.T.chan.ev(mm)

        def evf(oc=oc):
            V = p.V
            V.wait(p.ev[("q", "mm", oc)])
            for b in range(2):
                V.op("tensor_copy", qf[:, oc, b * 512:(b + 1) * 512], psPj[b][:, :])
            p.ev[("q", "ev", oc)] = V.chan.cur()

        p.sec("gpsimd", wload)
        p.sec("tensor", mmf)
        p.sec("vector", evf)

    for rc, (c0w, Mc) in enumerate([(0, 128), (128, 128), (256, 64)]):
        def wsload(rc=rc, c0w=c0w, Mc=Mc):
            G = p.G
            if rc == 0:
                G.wait(p.ev[("q", "mm", 15)])
            G.wait(p.ev.get(("sm", "mm", rc - 1, 3)))
            slW[1].dma(G, wbuf[:, 0, :, 0:Mc],
                       wsmT[:, c0w:c0w + Mc].rearrange("(kc p) c -> p kc c", p=128))

        p.sec("gpsimd", wsload)
        for tb in range(4):
            def sm_mm(rc=rc, tb=tb, Mc=Mc):
                T = p.T
                T.wait(slW[1].done())
                if rc == 0 and tb < 2:
                    T.wait(p.ev[("q", "ev", 15)])
                if tb < 2:
                    T.wait(p.ev.get(("sm", "ev", rc - 1, 2 + tb)))
                T.wait(p.ev.get(("sm", "ev", rc, tb - 2)))
                mm = None
                for kc in range(16):
                    mm = T.raw("matmul", psPj[tb % 2][0:Mc, :], wbuf[:, 0, kc, 0:Mc],
                               xlnf[:, kc, tb * 512:(tb + 1) * 512],
                               start=(kc == 0), stop=(kc == 15))
                p.ev[("sm", "mm", rc, tb)] = p.T.chan.ev(mm)

            def sm_ev(rc=rc, tb=tb):
                V = p.V
                V.wait(p.ev[("sm", "mm", rc, tb)])
                sl = slice(tb * 512, (tb + 1) * 512)
                if rc == 0:
                    V.op("tensor_copy", kpf[:, sl], psPj[tb % 2][:, :])
                elif rc == 1:
                    V.op("tensor_copy", vpf[:, sl], psPj[tb % 2][:, :])
                else:
                    V.op("tensor_copy", hgf[0:64, sl], psPj[tb % 2][0:64, :])
                p.ev[("sm", "ev", rc, tb)] = V.chan.cur()

            p.sec("tensor", sm_mm)
            p.sec("vector", sm_ev)

    p.ev["smallp"] = p.ev[("sm", "ev", 2, 3)]
    esB.close()

    esC = ExitStack()
    kf = esC.enter_context(nc.sbuf_tensor("kf", [128, 16, SEQ], BF16))

    esC1 = ExitStack()
    gt1 = esC1.enter_context(nc.sbuf_tensor("gt1", [32, SEQ], F32))
    gt2 = esC1.enter_context(nc.sbuf_tensor("gt2", [32, SEQ], F32))
    gt3 = esC1.enter_context(nc.sbuf_tensor("gt3", [32, SEQ], F32))
    zer = esC1.enter_context(nc.sbuf_tensor("zer", [32, SEQ], F32))

    def scan_g():
        G = p.G
        G.wait(p.ev["smallp"])
        G.op("memset", zer[:, :], 0.0)
        G.op("memset", gt2[:, 0:1], 0.0)
        p.ev["scz"] = G.chan.cur()

    def scan_s1():
        S = p.S
        S.wait(p.ev["smallp"])
        S.op("activation", gt1[0:32, :], hgf[32:64, :], AF.Abs)
        S.op("activation", gt1[0:32, :], gt1[0:32, :], AF.Exp, scale=-1.0)
        S.op("activation", gt1[0:32, :], gt1[0:32, :], AF.Ln, bias=onef[0:32, 0:1])
        p.ev["sc1"] = S.chan.cur()

    def scan_v1():
        V = p.V
        V.wait(p.ev["sc1"])
        V.wait(p.ev["scz"])
        V.op("tensor_scalar", gt3[0:32, :], hgf[32:64, :], 0.0, None, ALU.min)
        V.op("tensor_sub", gt3[0:32, :], gt3[0:32, :], gt1[0:32, :])
        V.op("tensor_tensor_scan", gt2[0:32, 1:SEQ], gt3[0:32, 0:SEQ - 1],
             zer[0:32, 0:SEQ - 1], 0.0, ALU.add, ALU.add)
        p.ev["sc2"] = V.chan.cur()

    def scan_s2():
        S = p.S
        S.wait(p.ev["sc2"])
        S.op("activation", gt1[0:32, :], gt2[0:32, :], AF.Exp)
        p.ev["sc3"] = S.chan.cur()

    def scan_v2():
        V = p.V
        V.wait(p.ev["sc3"])
        V.op("tensor_tensor", gt1[0:32, :], hgf[0:32, :], gt1[0:32, :], ALU.mult)
        V.op("tensor_tensor_scan", gt3[0:32, :], gt1[0:32, :], zer[0:32, :],
             0.0, ALU.add, ALU.add)
        p.ev["sc4"] = V.chan.cur()

    def scan_st():
        Y = p.Y
        Y.wait(p.ev["sc4"])
        slSm.dma(Y, sm_d[:, :], gt3[0:32, :])

    p.sec("gpsimd", scan_g)
    p.sec("scalar", scan_s1)
    p.sec("vector", scan_v1)
    p.sec("scalar", scan_s2)
    p.sec("vector", scan_v2)
    p.sec("sync", scan_st)
    esC1.close()

    esC2 = ExitStack()
    cost = esC2.enter_context(nc.sbuf_tensor("cost", [128, SEQ], BF16))
    sint = esC2.enter_context(nc.sbuf_tensor("sint", [128, SEQ], BF16))
    bmsk = esC2.enter_context(nc.sbuf_tensor("bmsk", [128, 1024], BF16))
    khrb = esC2.enter_context(nc.sbuf_tensor("khrb", [1, SEQ], BF16))
    vhrb = esC2.enter_context(nc.sbuf_tensor("vhrb", [1, SEQ], BF16))
    khb = esC2.enter_context(nc.sbuf_tensor("khb", [128, SEQ], BF16))
    vhb = esC2.enter_context(nc.sbuf_tensor("vhb", [128, SEQ], BF16))
    vtmp = esC2.enter_context(nc.sbuf_tensor("vtmp", [128, SEQ], BF16))
    rtmp = esC2.enter_context(nc.sbuf_tensor("rtmp", [128, SEQ], BF16))
    rtm2 = esC2.enter_context(nc.sbuf_tensor("rtm2", [128, SEQ], BF16))
    sq2 = esC2.enter_context(nc.sbuf_tensor("sq2", [128, SEQ], F32))
    acc = esC2.enter_context(nc.sbuf_tensor("acc", [128, 2], F32))
    crow = esC2.enter_context(nc.sbuf_tensor("crow", [1, 3, 16], F32))
    cb128 = esC2.enter_context(nc.sbuf_tensor("cb128", [128, 16], F32))
    rsb = esC2.enter_context(nc.sbuf_tensor("rsb", [1, 2, 512], BF16))
    nrm = esC2.enter_context(nc.sbuf_tensor("nrm", [128, 512], F32))
    obuf = esC2.enter_context(nc.sbuf_tensor("obuf", [128, 512], BF16))
    vbuf = [esC2.enter_context(nc.sbuf_tensor("vbuf%d" % i, [128, 16, 128], BF16))
            for i in range(2)]
    atb = esC2.enter_context(nc.sbuf_tensor("atb", [128, 2, 512], BF16))
    esCa = ExitStack()
    psB1 = esCa.enter_context(nc.psum_tensor("psB1", [128, 512], F32))
    psB2 = esCa.enter_context(nc.psum_tensor("psB2", [128, 512], F32))
    psC1 = esCa.enter_context(nc.psum_tensor("psC1", [128, 32], F32))

    def ldrope():
        G = p.G
        G.wait(p.ev["sc4"])
        slR.dma(G, cost[:, :], cosT[:, :])
        slR.dma(G, sint[:, :], sinT[:, :])
        slR.dma(G, bmsk[:, :], bmask[:, :])

    p.sec("gpsimd", ldrope)

    for h in range(16):
        def krow(h=h):
            G = p.G
            G.wait(slSm.done())
            G.wait(p.ev.get(("bev", h - 1)))
            slKr.dma(G, khrb[0:1, :], sm_d[h:h + 1, :])
            slKr.dma(G, vhrb[0:1, :], sm_d[16 + h:17 + h, :])

        def bct(h=h, tb=0):
            T = p.T
            T.wait(slKr.done())
            T.wait(p.ev.get(("bcv", h, tb - 1)) or p.ev.get(("bcv", h - 1, 3)))
            T.raw("matmul", psB1[:, :], ones1[0:1, 0:128],
                  khrb[0:1, tb * 512:(tb + 1) * 512], start=True, stop=True)
            mm = T.raw("matmul", psB2[:, :], ones1[0:1, 0:128],
                       vhrb[0:1, tb * 512:(tb + 1) * 512], start=True, stop=True)
            p.ev[("bct", h, tb)] = p.T.chan.ev(mm)

        def bev(h=h, tb=0):
            V = p.V
            if tb == 0:
                V.wait(p.ev.get(("kv", h - 1)))
            V.wait(p.ev[("bct", h, tb)])
            sl = slice(tb * 512, (tb + 1) * 512)
            V.op("tensor_copy", khb[:, sl], psB1[:, :])
            V.op("tensor_copy", vhb[:, sl], psB2[:, :])
            p.ev[("bcv", h, tb)] = V.chan.cur()
            if tb == 3:
                p.ev[("bev", h)] = V.chan.cur()

        def kvbuild(h=h):
            G = p.G
            G.wait(p.ev[("bev", h)])
            G.wait(slVs.done())
            G.op("tensor_tensor", kf[:, h, :], kpf[:, :], khb[:, :], ALU.mult)
            G.op("tensor_tensor", vtmp[:, :], vpf[:, :], vhb[:, :], ALU.mult)
            p.ev[("kv", h)] = G.chan.cur()

        def vst(h=h):
            Y = p.Y
            Y.wait(p.ev[("kv", h)])
            slVs.dma(Y, v_d[h * 128:(h + 1) * 128, :], vtmp[:, :])

        def rk(h=h):
            V = p.V
            V.wait(slR.done())
            V.wait(p.ev[("kv", h)])
            V.op("tensor_copy", rtmp[0:64, :], kf[64:128, h, :])
            V.op("tensor_copy", rtmp[64:128, :], kf[0:64, h, :])
            V.op("tensor_tensor", rtm2[:, :], kf[:, h, :], cost[:, :], ALU.mult)
            V.op("tensor_tensor", rtmp[:, :], rtmp[:, :], sint[:, :], ALU.mult)
            V.op("tensor_tensor", kf[:, h, :], rtm2[:, :], rtmp[:, :], ALU.add)
            p.ev[("rk", h)] = V.chan.cur()

        p.sec("gpsimd", krow)
        for tb in range(4):
            p.sec("tensor", lambda h=h, tb=tb: bct(h, tb))
            p.sec("vector", lambda h=h, tb=tb: bev(h, tb))
        p.sec("gpsimd", kvbuild)
        p.sec("sync", vst)
        p.sec("vector", rk)

    for h in range(16):
        def rq(h=h):
            V = p.V
            V.wait(slR.done())
            V.wait(p.ev[("q", "ev", h)])
            V.op("tensor_copy", rtmp[0:64, 0:TLOC], qf[64:128, h, :])
            V.op("tensor_copy", rtmp[64:128, 0:TLOC], qf[0:64, h, :])
            V.op("tensor_tensor", rtm2[:, 0:TLOC], qf[:, h, :],
                 cost[:, Q0:Q0 + TLOC], ALU.mult)
            V.op("tensor_tensor", rtmp[:, 0:TLOC], rtmp[:, 0:TLOC],
                 sint[:, Q0:Q0 + TLOC], ALU.mult)
            V.op("tensor_tensor", qf[:, h, :], rtm2[:, 0:TLOC], rtmp[:, 0:TLOC], ALU.add)
            p.ev[("rq", h)] = V.chan.cur()

        p.sec("vector", rq)

    for h in range(16):
        def cb_s(h=h):
            S = p.S
            S.wait(p.ev[("rq", h)])
            S.wait(p.ev[("rk", h)])
            S.wait(p.ev.get(("cb_t", h - 1)))
            S.op("activation", sq2[:, 0:TLOC], qf[:, h, :], AF.Square,
                 accum_out=acc[:, 0:1])
            S.op("activation", sq2[:, 0:SEQ], kf[:, h, :], AF.Square,
                 accum_out=acc[:, 1:2])
            p.ev[("cb_s", h)] = S.chan.cur()

        def cb_t(h=h):
            T = p.T
            T.wait(p.ev[("cb_s", h)])
            T.raw("matmul", psC1[0:1, h:h + 1], ones128f[:, 0:1], acc[:, 0:1],
                  start=True, stop=True)
            mm = T.raw("matmul", psC1[0:1, 16 + h:17 + h], ones128f[:, 0:1],
                       acc[:, 1:2], start=True, stop=True)
            p.ev[("cb_t", h)] = p.T.chan.ev(mm)

        p.sec("scalar", cb_s)
        p.sec("tensor", cb_t)

    def cfin_v():
        V = p.V
        V.wait(p.ev[("cb_t", 15)])
        V.op("tensor_scalar", crow[0:1, 0, :], psC1[0:1, 0:16], 1.0 / TLOC, None, ALU.mult)
        V.op("tensor_scalar", crow[0:1, 1, :], psC1[0:1, 16:32], 1.0 / SEQ, None, ALU.mult)
        V.op("tensor_tensor", crow[0:1, 0, :], crow[0:1, 0, :], crow[0:1, 1, :], ALU.mult)
        p.ev["cfin_v"] = V.chan.cur()

    def cfin_s():
        S = p.S
        S.wait(p.ev["cfin_v"])
        S.op("activation", crow[0:1, 1, :], crow[0:1, 0, :], AF.Ln)
        S.op("activation", crow[0:1, 2, :], crow[0:1, 1, :], AF.Exp, scale=0.5)
        p.ev["cfin_s"] = S.chan.cur()

    def cfin_v2():
        V = p.V
        V.wait(p.ev["cfin_s"])
        V.op("tensor_scalar", crow[0:1, 2, :], crow[0:1, 2, :], CMUL, None, ALU.mult)
        V.op("tensor_copy", rsb[0:1, 0, 0:16], crow[0:1, 2, :])
        p.ev["cfin_v2"] = V.chan.cur()

    def cfin_t():
        T = p.T
        T.wait(p.ev["cfin_v2"])
        mm = T.raw("matmul", psC1[:, 0:16], ones1[0:1, 0:128], rsb[0:1, 0, 0:16],
                   start=True, stop=True)
        p.ev["cfin_t"] = p.T.chan.ev(mm)

    def cfin_v3():
        V = p.V
        V.wait(p.ev["cfin_t"])
        V.op("tensor_copy", cb128[:, :], psC1[:, 0:16])
        p.ev["cbias"] = V.chan.cur()

    p.sec("vector", cfin_v)
    p.sec("scalar", cfin_s)
    p.sec("vector", cfin_v2)
    p.sec("tensor", cfin_t)
    p.sec("vector", cfin_v3)
    esCa.close()
    psSc = [esC2.enter_context(nc.psum_tensor("psSc%d" % i, [128, 512], F32))
            for i in range(2)]
    psN = esC2.enter_context(nc.psum_tensor("psN", [128, 512], F32))
    psO = esC2.enter_context(nc.psum_tensor("psO", [128, 512], F32))
    psRa = esC2.enter_context(nc.psum_tensor("psRa", [1, 512], F32))

    for h in range(16):
        def vload(h=h):
            Y = p.Y
            Y.wait(slVs.done())
            Y.wait(p.ev.get(("av_done", h - 2)))
            for tc in range(16):
                slV[h % 2].dma_t(Y, vbuf[h % 2][:, tc, :],
                                 v_d[h * 128:(h + 1) * 128, tc * 128:(tc + 1) * 128])

        p.sec("sync", vload)

        for qb in range(2):
            qlo = Q0 + qb * 512
            kc_hi = (qlo + 512) // 128
            band_lo = qlo // 128
            last_kc = kc_hi - 1

            for kc in range(kc_hi):
                sp = kc % 2

                def sc_t(h=h, qb=qb, kc=kc, sp=sp):
                    T = p.T
                    T.wait(p.ev["cbias"])
                    T.wait(p.ev[("rq", h)])
                    T.wait(p.ev[("rk", h)])
                    if kc < 2:
                        T.wait(p.ev.get(("onrm", h, qb - 1)) or
                               p.ev.get(("onrm", h - 1, 1)))
                    T.wait(p.ev.get(("ex", h, qb, kc - 2)))
                    mm = T.raw("matmul", psSc[sp][:, :],
                               kf[:, h, kc * 128:(kc + 1) * 128],
                               qf[:, h, qb * 512:(qb + 1) * 512],
                               start=True, stop=True)
                    p.ev[("sc", h, qb, kc)] = p.T.chan.ev(mm)

                def ex_s(h=h, qb=qb, kc=kc, sp=sp):
                    S = p.S
                    S.wait(p.ev[("sc", h, qb, kc)])
                    S.wait(p.ev["cbias"])
                    S.wait(p.ev.get(("rsav", h, qb, kc - 2)))
                    S.op("activation", atb[:, sp, :], psSc[sp][:, :], AF.Exp,
                         scale=SCL, bias=cb128[:, h:h + 1])
                    p.ev[("ex", h, qb, kc)] = S.chan.cur()

                p.sec("tensor", sc_t)
                p.sec("scalar", ex_s)

                if kc >= band_lo:
                    off = 384 - (kc * 128 - qlo)

                    def msk_g(h=h, qb=qb, kc=kc, sp=sp, off=off):
                        G = p.G
                        G.wait(p.ev[("ex", h, qb, kc)])
                        G.wait(slR.done())
                        G.op("tensor_tensor", atb[:, sp, :], atb[:, sp, :],
                             bmsk[:, off:off + 512], ALU.mult)
                        p.ev[("mk", h, qb, kc)] = p.G.chan.cur()

                    p.sec("gpsimd", msk_g)

                def rs_av(h=h, qb=qb, kc=kc, sp=sp, band=(kc >= band_lo),
                          first=(kc == 0), last=(kc == last_kc)):
                    T = p.T
                    if band:
                        T.wait(p.ev[("mk", h, qb, kc)])
                    else:
                        T.wait(p.ev[("ex", h, qb, kc)])
                    T.wait(slV[h % 2].done())
                    if first:
                        T.wait(p.ev.get(("onrm", h, qb - 1)) or
                               p.ev.get(("onrm", h - 1, 1)))
                    T.raw("matmul", psRa[0:1, :], ones128[:, 0:1], atb[:, sp, :],
                          start=first, stop=last)
                    mm = T.raw("matmul", psO[:, :], vbuf[h % 2][:, kc, :],
                               atb[:, sp, :], start=first, stop=last)
                    p.ev[("rsav", h, qb, kc)] = p.T.chan.ev(mm)

                p.sec("tensor", rs_av)

            def onrm_v(h=h, qb=qb):
                V = p.V
                V.wait(p.ev[("rsav", h, qb, last_kc)])
                V.op("reciprocal", nrm[0:1, :], psRa[0:1, :])
                V.op("tensor_copy", rsb[0:1, 1, :], nrm[0:1, :])
                p.ev[("onrm_v", h, qb)] = V.chan.cur()

            def onrm_t(h=h, qb=qb):
                T = p.T
                T.wait(p.ev[("onrm_v", h, qb)])
                mm = T.raw("matmul", psN[:, :], ones1[0:1, 0:128], rsb[0:1, 1, :],
                           start=True, stop=True)
                p.ev[("onrm_t", h, qb)] = p.T.chan.ev(mm)

            def onrm_v2(h=h, qb=qb):
                V = p.V
                V.wait(p.ev[("onrm_t", h, qb)])
                V.wait(slOd.done())
                V.op("tensor_copy", nrm[:, :], psN[:, :])
                V.op("tensor_tensor", obuf[:, :], psO[:, :], nrm[:, :], ALU.mult)
                p.ev[("onrm", h, qb)] = V.chan.cur()

            def ost(h=h, qb=qb):
                Y = p.Y
                Y.wait(p.ev[("onrm", h, qb)])
                slOd.dma(Y, o_d[h * 128:(h + 1) * 128, qb * 512:(qb + 1) * 512],
                         obuf[:, :])
                if qb == 1:
                    p.ev[("av_done", h)] = slOd.done()

            p.sec("vector", onrm_v)
            p.sec("tensor", onrm_t)
            p.sec("vector", onrm_v2)
            p.sec("sync", ost)

    esC2.close()
    esC.close()

    esD = ExitStack()
    obf = [esD.enter_context(nc.sbuf_tensor("obf%d" % i, [128, TLOC], BF16))
           for i in range(2)]
    wob = esD.enter_context(nc.sbuf_tensor("wob", [128, 16, 512], BF16))
    xhb = esD.enter_context(nc.sbuf_tensor("xhb", [128, 512], F32))
    xot = esD.enter_context(nc.sbuf_tensor("xot", [128, 512], F32))
    psJ = [esD.enter_context(nc.psum_tensor("psJ%d" % i, [128, 512], F32))
           for i in range(8)]

    for oblk in range(4):
        def woload(oblk=oblk):
            G = p.G
            G.wait(p.ev[("onrm", 15, 1)])
            G.wait(p.ev.get(("oj_mm", oblk - 1, 15)))
            slWo.dma(G, wob[:, :, :],
                     woT[:, oblk * 512:(oblk + 1) * 512]
                     .rearrange("(ec p) c -> p ec c", p=128))

        p.sec("gpsimd", woload)

        for ec in range(16):
            def oload(oblk=oblk, ec=ec):
                Y = p.Y
                Y.wait(p.ev[("onrm", 15, 1)])
                Y.wait(slOd.done())
                Y.wait(p.ev.get(("oj_mm", oblk, ec - 2)))
                slOb[ec % 2].dma(Y, obf[ec % 2][:, :], o_d[ec * 128:(ec + 1) * 128, :])

            def oj_mm(oblk=oblk, ec=ec):
                T = p.T
                T.wait(p.ev[("onrm", 15, 1)])
                T.wait(slOb[ec % 2].done())
                T.wait(slWo.done())
                if ec == 0:
                    T.wait(p.ev.get(("oj_v", oblk - 1, 7)))
                mm = None
                for tc in range(8):
                    mm = T.raw("matmul", psJ[tc][:, :], obf[ec % 2][:, tc * 128:(tc + 1) * 128],
                               wob[:, ec, :], start=(ec == 0), stop=(ec == 15))
                p.ev[("oj_mm", oblk, ec)] = p.T.chan.ev(mm)

            p.sec("sync", oload)
            p.sec("tensor", oj_mm)

        for tc in range(8):
            def xld(oblk=oblk, tc=tc):
                Y = p.Y
                Y.wait(p.ev[("onrm", 15, 1)])
                Y.wait(p.ev.get(("oj_st", oblk, tc - 1)))
                slH.dma(Y, xhb[:, :], xh[tc * 128:(tc + 1) * 128,
                                        oblk * 512:(oblk + 1) * 512])

            def oj_v(oblk=oblk, tc=tc):
                V = p.V
                V.wait(p.ev[("oj_mm", oblk, 15)])
                V.wait(slH.done())
                V.wait(slO.done())
                V.op("tensor_tensor", xot[:, :], psJ[tc][:, :], xhb[:, :], ALU.add)
                p.ev[("oj_v", oblk, tc)] = V.chan.cur()

            def oj_st(oblk=oblk, tc=tc):
                Y = p.Y
                Y.wait(p.ev[("oj_v", oblk, tc)])
                slO.dma(Y, xo[tc * 128:(tc + 1) * 128, oblk * 512:(oblk + 1) * 512],
                        xot[:, :])
                p.ev[("oj_st", oblk, tc)] = slO.done()

            p.sec("sync", xld)
            p.sec("vector", oj_v)
            p.sec("sync", oj_st)

    p.sec("sync", lambda: p.Y.wait(slO.done()))
    esD.close()
    return p.fin()


def build_attn_b1h():
    from contextlib import ExitStack
    p = Prog("attnb1h").start()
    nc = p.nc
    SEQ = 2048
    EL = 1024
    SCL = 0.08838834764831845
    CMUL = -0.04419417382415922
    xT = nc.dram_tensor("xT", [D, SEQ], F32, kind="ExternalInput").ap()
    wqT = nc.dram_tensor("wqT", [D, EL], F32, kind="ExternalInput").ap()
    wsmT = nc.dram_tensor("wsmT", [D, 320], F32, kind="ExternalInput").ap()
    woT = nc.dram_tensor("woT", [EL, D], F32, kind="ExternalInput").ap()
    cosT = nc.dram_tensor("cosT", [128, SEQ], F32, kind="ExternalInput").ap()
    sinT = nc.dram_tensor("sinT", [128, SEQ], F32, kind="ExternalInput").ap()
    bmask = nc.dram_tensor("bmask", [128, 1024], F32, kind="ExternalInput").ap()
    py = nc.dram_tensor("py", [SEQ, D], BF16, kind="ExternalOutput").ap()
    sm_d = nc.dram_tensor("sm_d", [16, SEQ], F32).ap()
    v_d = nc.dram_tensor("v_d", [EL, SEQ], BF16).ap()
    o_d = nc.dram_tensor("o_d", [EL, SEQ], BF16).ap()

    slX = p.slot("slX")
    slW = [p.slot("slW0"), p.slot("slW1")]
    slR = p.slot("slR")
    slSm = p.slot("slSm")
    slKr = p.slot("slKr")
    slVs = p.slot("slVs")
    slV = [p.slot("slV0"), p.slot("slV1")]
    slOd = p.slot("slOd")
    slOb = [p.slot("slOb0"), p.slot("slOb1")]
    slWo = p.slot("slWo")
    slO = p.slot("slO")

    qf = p.sb("qf", [128, 8, SEQ], BF16)
    kpf = p.sb("kpf", [128, SEQ], BF16)
    vpf = p.sb("vpf", [128, SEQ], BF16)
    hgf = p.sb("hgf", [64, SEQ], F32)
    xc = p.sb("xc", [128, SEQ], F32)
    ones1 = p.sb("ones1", [1, 512], BF16)
    ones128 = p.sb("ones128", [128, 1], BF16)
    ones128f = p.sb("ones128f", [128, 1], F32)
    epsln = p.sb("epsln", [128, 1], F32)
    onef = p.sb("onef", [128, 1], F32)
    srwb = p.sb("srwb", [1, 2, SEQ], BF16)

    def init():
        G = p.G
        G.op("memset", ones1[:, :], 1.0)
        G.op("memset", ones128[:, :], 1.0)
        G.op("memset", ones128f[:, :], 1.0)
        G.op("memset", epsln[:, :], LN_EPS)
        G.op("memset", onef[:, :], 1.0)
        p.ev["init"] = p.G.chan.cur()

    p.sec("gpsimd", init)

    esA = ExitStack()
    sqc = esA.enter_context(nc.sbuf_tensor("sqc", [128, SEQ], F32))
    srow = esA.enter_context(nc.sbuf_tensor("srow", [1, 3 * SEQ], F32))
    psSb = [esA.enter_context(nc.psum_tensor("psSb%d" % b, [1, 512], F32)) for b in range(4)]
    psS2b = [esA.enter_context(nc.psum_tensor("psS2b%d" % b, [1, 512], F32)) for b in range(4)]

    for c in range(16):
        def ldx(c=c):
            Y = p.Y
            Y.wait(p.ev.get(("st_s", c - 1)))
            Y.wait(p.ev.get(("st_t1", c - 1)))
            slX.dma(Y, xc[:, :], xT[c * 128:(c + 1) * 128, :])

        def st_t1(c=c):
            T = p.T
            T.wait(slX.done())
            T.wait(p.ev["init"])
            mm = None
            for b in range(4):
                mm = T.raw("matmul", psSb[b][0:1, :], ones128f[:, 0:1],
                           xc[:, b * 512:(b + 1) * 512], start=(c == 0), stop=(c == 15))
            p.ev[("st_t1", c)] = p.T.chan.ev(mm)

        def st_s(c=c):
            S = p.S
            S.wait(slX.done())
            S.wait(p.ev.get(("st_t2", c - 1)))
            S.op("activation", sqc[:, :], xc[:, :], AF.Square)
            p.ev[("st_s", c)] = S.chan.cur()

        def st_t2(c=c):
            T = p.T
            T.wait(p.ev[("st_s", c)])
            mm = None
            for b in range(4):
                mm = T.raw("matmul", psS2b[b][0:1, :], ones128f[:, 0:1],
                           sqc[:, b * 512:(b + 1) * 512], start=(c == 0), stop=(c == 15))
            p.ev[("st_t2", c)] = p.T.chan.ev(mm)

        p.sec("sync", ldx)
        p.sec("tensor", st_t1)
        p.sec("scalar", st_s)
        p.sec("tensor", st_t2)

    def fin_v():
        V = p.V
        V.wait(p.ev[("st_t1", 15)])
        V.wait(p.ev[("st_t2", 15)])
        for b in range(4):
            sl = slice(b * 512, (b + 1) * 512)
            V.op("tensor_scalar", srow[0:1, sl], psSb[b][0:1, :], 1.0 / D, None, ALU.mult)
            V.op("tensor_scalar", srow[0:1, 2048 + b * 512:2048 + (b + 1) * 512],
                 psS2b[b][0:1, :], 1.0 / D, None, ALU.mult)
        V.op("tensor_tensor", srow[0:1, 4096:6144], srow[0:1, 0:2048],
             srow[0:1, 0:2048], ALU.mult)
        V.op("tensor_sub", srow[0:1, 2048:4096], srow[0:1, 2048:4096],
             srow[0:1, 4096:6144])
        p.ev["fin_v"] = V.chan.cur()

    def fin_s():
        S = p.S
        S.wait(p.ev["fin_v"])
        S.wait(p.ev["init"])
        S.op("activation", srow[0:1, 2048:4096], srow[0:1, 2048:4096], AF.Ln,
             bias=epsln[0:1, 0:1])
        S.op("activation", srwb[0:1, 0, :], srow[0:1, 2048:4096], AF.Exp, scale=-0.5)
        S.op("activation", srow[0:1, 4096:6144], srow[0:1, 2048:4096], AF.Exp, scale=-0.5)
        p.ev["fin_s"] = S.chan.cur()

    def fin_v3():
        V = p.V
        V.wait(p.ev["fin_s"])
        V.op("tensor_tensor", srow[0:1, 2048:4096], srow[0:1, 0:2048],
             srow[0:1, 4096:6144], ALU.mult)
        V.op("tensor_scalar", srwb[0:1, 1, :], srow[0:1, 2048:4096], -1.0, None, ALU.mult)
        p.ev["stats"] = V.chan.cur()

    p.sec("vector", fin_v)
    p.sec("scalar", fin_s)
    p.sec("vector", fin_v3)
    esA.close()

    esB = ExitStack()
    xlnf = esB.enter_context(nc.sbuf_tensor("xlnf", [128, 16, SEQ], BF16))
    psBc = [esB.enter_context(nc.psum_tensor("psBc%d" % i, [128, 512], F32)) for i in range(2)]
    psPj = [esB.enter_context(nc.psum_tensor("psPj%d" % i, [128, 512], F32)) for i in range(4)]
    rnb = esB.enter_context(nc.sbuf_tensor("rnb", [128, 2, SEQ], BF16))
    wbuf = esB.enter_context(nc.sbuf_tensor("wbuf", [128, 1, 16, 128], BF16))

    for b in range(4):
        def bc_t(b=b):
            T = p.T
            T.wait(p.ev["stats"])
            T.wait(p.ev.get(("bc_v", b - 1)))
            T.raw("matmul", psBc[0][:, :], ones1[0:1, 0:128],
                  srwb[0:1, 0, b * 512:(b + 1) * 512], start=True, stop=True)
            m2 = T.raw("matmul", psBc[1][:, :], ones1[0:1, 0:128],
                       srwb[0:1, 1, b * 512:(b + 1) * 512], start=True, stop=True)
            p.ev[("bc_t", b)] = p.T.chan.ev(m2)

        def bc_v(b=b):
            V = p.V
            V.wait(p.ev[("bc_t", b)])
            V.op("tensor_copy", rnb[:, 0, b * 512:(b + 1) * 512], psBc[0][:, :])
            V.op("tensor_copy", rnb[:, 1, b * 512:(b + 1) * 512], psBc[1][:, :])
            p.ev[("bc_v", b)] = V.chan.cur()

        p.sec("tensor", bc_t)
        p.sec("vector", bc_v)

    for c in range(16):
        def ldx2(c=c):
            Y = p.Y
            Y.wait(p.ev["stats"])
            Y.wait(p.ev.get(("ap_v", c - 1)))
            slX.dma(Y, xc[:, :], xT[c * 128:(c + 1) * 128, :])

        def ap_v(c=c):
            V = p.V
            V.wait(slX.done())
            V.wait(p.ev[("bc_v", 3)])
            V.op("tensor_tensor", xc[:, :], xc[:, :], rnb[:, 0, :], ALU.mult)
            V.op("tensor_tensor", xlnf[:, c, :], xc[:, :], rnb[:, 1, :], ALU.add)
            p.ev[("ap_v", c)] = V.chan.cur()

        p.sec("sync", ldx2)
        p.sec("vector", ap_v)

    xln_ev = lambda: p.ev[("ap_v", 15)]

    for oc in range(8):
        def wload(oc=oc):
            G = p.G
            G.wait(p.ev.get(("q", "mm", oc - 1)))
            slW[0].dma(G, wbuf[:, 0, :, :],
                       wqT[:, oc * 128:(oc + 1) * 128]
                       .rearrange("(kc p) c -> p kc c", p=128))

        def mmf(oc=oc):
            T = p.T
            T.wait(slW[0].done())
            T.wait(xln_ev())
            T.wait(p.ev.get(("q", "ev", oc - 1)))
            mm = None
            for b in range(4):
                for kc in range(16):
                    mm = T.raw("matmul", psPj[b][:, :], wbuf[:, 0, kc, :],
                               xlnf[:, kc, b * 512:(b + 1) * 512],
                               start=(kc == 0), stop=(kc == 15))
            p.ev[("q", "mm", oc)] = p.T.chan.ev(mm)

        def evf(oc=oc):
            V = p.V
            V.wait(p.ev[("q", "mm", oc)])
            for b in range(4):
                V.op("tensor_copy", qf[:, oc, b * 512:(b + 1) * 512], psPj[b][:, :])
            p.ev[("q", "ev", oc)] = V.chan.cur()

        p.sec("gpsimd", wload)
        p.sec("tensor", mmf)
        p.sec("vector", evf)

    for rc, (c0w, Mc) in enumerate([(0, 128), (128, 128), (256, 64)]):
        def wsload(rc=rc, c0w=c0w, Mc=Mc):
            G = p.G
            if rc == 0:
                G.wait(p.ev[("q", "mm", 7)])
            G.wait(p.ev.get(("sm", "mm", rc - 1, 3)))
            slW[1].dma(G, wbuf[:, 0, :, 0:Mc],
                       wsmT[:, c0w:c0w + Mc].rearrange("(kc p) c -> p kc c", p=128))

        p.sec("gpsimd", wsload)
        for tb in range(4):
            def sm_mm(rc=rc, tb=tb, Mc=Mc):
                T = p.T
                T.wait(slW[1].done())
                if rc == 0 and tb < 2:
                    T.wait(p.ev[("q", "ev", 7)])
                if tb < 2:
                    T.wait(p.ev.get(("sm", "ev", rc - 1, 2 + tb)))
                T.wait(p.ev.get(("sm", "ev", rc, tb - 2)))
                mm = None
                for kc in range(16):
                    mm = T.raw("matmul", psPj[tb % 2][0:Mc, :], wbuf[:, 0, kc, 0:Mc],
                               xlnf[:, kc, tb * 512:(tb + 1) * 512],
                               start=(kc == 0), stop=(kc == 15))
                p.ev[("sm", "mm", rc, tb)] = p.T.chan.ev(mm)

            def sm_ev(rc=rc, tb=tb):
                V = p.V
                V.wait(p.ev[("sm", "mm", rc, tb)])
                sl = slice(tb * 512, (tb + 1) * 512)
                if rc == 0:
                    V.op("tensor_copy", kpf[:, sl], psPj[tb % 2][:, :])
                elif rc == 1:
                    V.op("tensor_copy", vpf[:, sl], psPj[tb % 2][:, :])
                else:
                    V.op("tensor_copy", hgf[0:64, sl], psPj[tb % 2][0:64, :])
                p.ev[("sm", "ev", rc, tb)] = V.chan.cur()

            p.sec("tensor", sm_mm)
            p.sec("vector", sm_ev)

    p.ev["smallp"] = p.ev[("sm", "ev", 2, 3)]
    esB.close()

    esC = ExitStack()
    kf = esC.enter_context(nc.sbuf_tensor("kf", [128, 8, SEQ], BF16))

    esC1 = ExitStack()
    gt1 = esC1.enter_context(nc.sbuf_tensor("gt1", [16, SEQ], F32))
    gt2 = esC1.enter_context(nc.sbuf_tensor("gt2", [16, SEQ], F32))
    gt3 = esC1.enter_context(nc.sbuf_tensor("gt3", [16, SEQ], F32))
    zer = esC1.enter_context(nc.sbuf_tensor("zer", [16, SEQ], F32))

    def scan_g():
        G = p.G
        G.wait(p.ev["smallp"])
        G.op("memset", zer[:, :], 0.0)
        G.op("memset", gt2[:, 0:1], 0.0)
        p.ev["scz"] = G.chan.cur()

    def scan_s1():
        S = p.S
        S.wait(p.ev["smallp"])
        S.op("activation", gt1[0:16, :], hgf[32:48, :], AF.Abs)
        S.op("activation", gt1[0:16, :], gt1[0:16, :], AF.Exp, scale=-1.0)
        S.op("activation", gt1[0:16, :], gt1[0:16, :], AF.Ln, bias=onef[0:16, 0:1])
        p.ev["sc1"] = S.chan.cur()

    def scan_v1():
        V = p.V
        V.wait(p.ev["sc1"])
        V.wait(p.ev["scz"])
        V.op("tensor_scalar", gt3[0:16, :], hgf[32:48, :], 0.0, None, ALU.min)
        V.op("tensor_sub", gt3[0:16, :], gt3[0:16, :], gt1[0:16, :])
        V.op("tensor_tensor_scan", gt2[0:16, 1:SEQ], gt3[0:16, 0:SEQ - 1],
             zer[0:16, 0:SEQ - 1], 0.0, ALU.add, ALU.add)
        p.ev["sc2"] = V.chan.cur()

    def scan_s2():
        S = p.S
        S.wait(p.ev["sc2"])
        S.op("activation", gt1[0:16, :], gt2[0:16, :], AF.Exp)
        p.ev["sc3"] = S.chan.cur()

    def scan_v2():
        V = p.V
        V.wait(p.ev["sc3"])
        V.op("tensor_tensor", gt1[0:16, :], hgf[0:16, :], gt1[0:16, :], ALU.mult)
        V.op("tensor_tensor_scan", gt3[0:16, :], gt1[0:16, :], zer[0:16, :],
             0.0, ALU.add, ALU.add)
        p.ev["sc4"] = V.chan.cur()

    def scan_st():
        Y = p.Y
        Y.wait(p.ev["sc4"])
        slSm.dma(Y, sm_d[:, :], gt3[0:16, :])

    p.sec("gpsimd", scan_g)
    p.sec("scalar", scan_s1)
    p.sec("vector", scan_v1)
    p.sec("scalar", scan_s2)
    p.sec("vector", scan_v2)
    p.sec("sync", scan_st)
    esC1.close()

    esC2 = ExitStack()
    cost = esC2.enter_context(nc.sbuf_tensor("cost", [128, SEQ], BF16))
    sint = esC2.enter_context(nc.sbuf_tensor("sint", [128, SEQ], BF16))
    bmsk = esC2.enter_context(nc.sbuf_tensor("bmsk", [128, 1024], BF16))
    khrb = esC2.enter_context(nc.sbuf_tensor("khrb", [1, SEQ], BF16))
    vhrb = esC2.enter_context(nc.sbuf_tensor("vhrb", [1, SEQ], BF16))
    khb = esC2.enter_context(nc.sbuf_tensor("khb", [128, SEQ], BF16))
    vhb = esC2.enter_context(nc.sbuf_tensor("vhb", [128, SEQ], BF16))
    vtmp = esC2.enter_context(nc.sbuf_tensor("vtmp", [128, SEQ], BF16))
    rtmp = esC2.enter_context(nc.sbuf_tensor("rtmp", [128, SEQ], BF16))
    rtm2 = esC2.enter_context(nc.sbuf_tensor("rtm2", [128, SEQ], BF16))
    sq2 = esC2.enter_context(nc.sbuf_tensor("sq2", [128, SEQ], F32))
    acc = esC2.enter_context(nc.sbuf_tensor("acc", [128, 2], F32))
    crow = esC2.enter_context(nc.sbuf_tensor("crow", [1, 3, 8], F32))
    cb128 = esC2.enter_context(nc.sbuf_tensor("cb128", [128, 8], F32))
    rsb = esC2.enter_context(nc.sbuf_tensor("rsb", [1, 2, 512], BF16))
    nrm = esC2.enter_context(nc.sbuf_tensor("nrm", [128, 512], F32))
    obuf = esC2.enter_context(nc.sbuf_tensor("obuf", [128, 512], BF16))
    vbuf = [esC2.enter_context(nc.sbuf_tensor("vbuf%d" % i, [128, 16, 128], BF16))
            for i in range(2)]
    atb = esC2.enter_context(nc.sbuf_tensor("atb", [128, 2, 512], BF16))
    esCa = ExitStack()
    psB1 = esCa.enter_context(nc.psum_tensor("psB1", [128, 512], F32))
    psB2 = esCa.enter_context(nc.psum_tensor("psB2", [128, 512], F32))
    psC1 = esCa.enter_context(nc.psum_tensor("psC1", [128, 16], F32))

    def ldrope():
        G = p.G
        G.wait(p.ev["sc4"])
        slR.dma(G, cost[:, :], cosT[:, :])
        slR.dma(G, sint[:, :], sinT[:, :])
        slR.dma(G, bmsk[:, :], bmask[:, :])

    p.sec("gpsimd", ldrope)

    for h in range(8):
        def krow(h=h):
            G = p.G
            G.wait(slSm.done())
            G.wait(p.ev.get(("bev", h - 1)))
            slKr.dma(G, khrb[0:1, :], sm_d[h:h + 1, :])
            slKr.dma(G, vhrb[0:1, :], sm_d[8 + h:9 + h, :])

        def bct(h=h, tb=0):
            T = p.T
            T.wait(slKr.done())
            T.wait(p.ev.get(("bcv", h, tb - 1)) or p.ev.get(("bcv", h - 1, 3)))
            T.raw("matmul", psB1[:, :], ones1[0:1, 0:128],
                  khrb[0:1, tb * 512:(tb + 1) * 512], start=True, stop=True)
            mm = T.raw("matmul", psB2[:, :], ones1[0:1, 0:128],
                       vhrb[0:1, tb * 512:(tb + 1) * 512], start=True, stop=True)
            p.ev[("bct", h, tb)] = p.T.chan.ev(mm)

        def bev(h=h, tb=0):
            V = p.V
            if tb == 0:
                V.wait(p.ev.get(("kv", h - 1)))
            V.wait(p.ev[("bct", h, tb)])
            sl = slice(tb * 512, (tb + 1) * 512)
            V.op("tensor_copy", khb[:, sl], psB1[:, :])
            V.op("tensor_copy", vhb[:, sl], psB2[:, :])
            p.ev[("bcv", h, tb)] = V.chan.cur()
            if tb == 3:
                p.ev[("bev", h)] = V.chan.cur()

        def kvbuild(h=h):
            G = p.G
            G.wait(p.ev[("bev", h)])
            G.wait(slVs.done())
            G.op("tensor_tensor", kf[:, h, :], kpf[:, :], khb[:, :], ALU.mult)
            G.op("tensor_tensor", vtmp[:, :], vpf[:, :], vhb[:, :], ALU.mult)
            p.ev[("kv", h)] = G.chan.cur()

        def vst(h=h):
            Y = p.Y
            Y.wait(p.ev[("kv", h)])
            slVs.dma(Y, v_d[h * 128:(h + 1) * 128, :], vtmp[:, :])

        def rk(h=h):
            V = p.V
            V.wait(slR.done())
            V.wait(p.ev[("kv", h)])
            V.op("tensor_copy", rtmp[0:64, :], kf[64:128, h, :])
            V.op("tensor_copy", rtmp[64:128, :], kf[0:64, h, :])
            V.op("tensor_tensor", rtm2[:, :], kf[:, h, :], cost[:, :], ALU.mult)
            V.op("tensor_tensor", rtmp[:, :], rtmp[:, :], sint[:, :], ALU.mult)
            V.op("tensor_tensor", kf[:, h, :], rtm2[:, :], rtmp[:, :], ALU.add)
            p.ev[("rk", h)] = V.chan.cur()

        p.sec("gpsimd", krow)
        for tb in range(4):
            p.sec("tensor", lambda h=h, tb=tb: bct(h, tb))
            p.sec("vector", lambda h=h, tb=tb: bev(h, tb))
        p.sec("gpsimd", kvbuild)
        p.sec("sync", vst)
        p.sec("vector", rk)

    for h in range(8):
        def rq(h=h):
            V = p.V
            V.wait(slR.done())
            V.wait(p.ev[("q", "ev", h)])
            V.op("tensor_copy", rtmp[0:64, :], qf[64:128, h, :])
            V.op("tensor_copy", rtmp[64:128, :], qf[0:64, h, :])
            V.op("tensor_tensor", rtm2[:, :], qf[:, h, :], cost[:, :], ALU.mult)
            V.op("tensor_tensor", rtmp[:, :], rtmp[:, :], sint[:, :], ALU.mult)
            V.op("tensor_tensor", qf[:, h, :], rtm2[:, :], rtmp[:, :], ALU.add)
            p.ev[("rq", h)] = V.chan.cur()

        p.sec("vector", rq)

    for h in range(8):
        def cb_s(h=h):
            S = p.S
            S.wait(p.ev[("rq", h)])
            S.wait(p.ev[("rk", h)])
            S.wait(p.ev.get(("cb_t", h - 1)))
            S.op("activation", sq2[:, :], qf[:, h, :], AF.Square,
                 accum_out=acc[:, 0:1])
            S.op("activation", sq2[:, :], kf[:, h, :], AF.Square,
                 accum_out=acc[:, 1:2])
            p.ev[("cb_s", h)] = S.chan.cur()

        def cb_t(h=h):
            T = p.T
            T.wait(p.ev[("cb_s", h)])
            T.raw("matmul", psC1[0:1, h:h + 1], ones128f[:, 0:1], acc[:, 0:1],
                  start=True, stop=True)
            mm = T.raw("matmul", psC1[0:1, 8 + h:9 + h], ones128f[:, 0:1],
                       acc[:, 1:2], start=True, stop=True)
            p.ev[("cb_t", h)] = p.T.chan.ev(mm)

        p.sec("scalar", cb_s)
        p.sec("tensor", cb_t)

    def cfin_v():
        V = p.V
        V.wait(p.ev[("cb_t", 7)])
        V.op("tensor_scalar", crow[0:1, 0, :], psC1[0:1, 0:8], 1.0 / SEQ, None, ALU.mult)
        V.op("tensor_scalar", crow[0:1, 1, :], psC1[0:1, 8:16], 1.0 / SEQ, None, ALU.mult)
        V.op("tensor_tensor", crow[0:1, 0, :], crow[0:1, 0, :], crow[0:1, 1, :], ALU.mult)
        p.ev["cfin_v"] = V.chan.cur()

    def cfin_s():
        S = p.S
        S.wait(p.ev["cfin_v"])
        S.op("activation", crow[0:1, 1, :], crow[0:1, 0, :], AF.Ln)
        S.op("activation", crow[0:1, 2, :], crow[0:1, 1, :], AF.Exp, scale=0.5)
        p.ev["cfin_s"] = S.chan.cur()

    def cfin_v2():
        V = p.V
        V.wait(p.ev["cfin_s"])
        V.op("tensor_scalar", crow[0:1, 2, :], crow[0:1, 2, :], CMUL, None, ALU.mult)
        V.op("tensor_copy", rsb[0:1, 0, 0:8], crow[0:1, 2, :])
        p.ev["cfin_v2"] = V.chan.cur()

    def cfin_t():
        T = p.T
        T.wait(p.ev["cfin_v2"])
        mm = T.raw("matmul", psC1[:, 0:8], ones1[0:1, 0:128], rsb[0:1, 0, 0:8],
                   start=True, stop=True)
        p.ev["cfin_t"] = p.T.chan.ev(mm)

    def cfin_v3():
        V = p.V
        V.wait(p.ev["cfin_t"])
        V.op("tensor_copy", cb128[:, :], psC1[:, 0:8])
        p.ev["cbias"] = V.chan.cur()

    p.sec("vector", cfin_v)
    p.sec("scalar", cfin_s)
    p.sec("vector", cfin_v2)
    p.sec("tensor", cfin_t)
    p.sec("vector", cfin_v3)
    esCa.close()
    psSc = [esC2.enter_context(nc.psum_tensor("psSc%d" % i, [128, 512], F32))
            for i in range(2)]
    psN = esC2.enter_context(nc.psum_tensor("psN", [128, 512], F32))
    psO = esC2.enter_context(nc.psum_tensor("psO", [128, 512], F32))
    psRa = esC2.enter_context(nc.psum_tensor("psRa", [1, 512], F32))

    for h in range(8):
        def vload(h=h):
            Y = p.Y
            Y.wait(slVs.done())
            Y.wait(p.ev.get(("av_done", h - 2)))
            for tc in range(16):
                slV[h % 2].dma_t(Y, vbuf[h % 2][:, tc, :],
                                 v_d[h * 128:(h + 1) * 128, tc * 128:(tc + 1) * 128])

        p.sec("sync", vload)

        for qb in range(4):
            qlo = qb * 512
            kc_hi = qb * 4 + 4
            band_lo = qb * 4
            last_kc = kc_hi - 1

            for kc in range(kc_hi):
                sp = kc % 2

                def sc_t(h=h, qb=qb, kc=kc, sp=sp):
                    T = p.T
                    T.wait(p.ev["cbias"])
                    T.wait(p.ev[("rq", h)])
                    T.wait(p.ev[("rk", h)])
                    if kc < 2:
                        T.wait(p.ev.get(("onrm", h, qb - 1)) or
                               p.ev.get(("onrm", h - 1, 3)))
                    T.wait(p.ev.get(("ex", h, qb, kc - 2)))
                    mm = T.raw("matmul", psSc[sp][:, :],
                               kf[:, h, kc * 128:(kc + 1) * 128],
                               qf[:, h, qb * 512:(qb + 1) * 512],
                               start=True, stop=True)
                    p.ev[("sc", h, qb, kc)] = p.T.chan.ev(mm)

                def ex_s(h=h, qb=qb, kc=kc, sp=sp):
                    S = p.S
                    S.wait(p.ev[("sc", h, qb, kc)])
                    S.wait(p.ev["cbias"])
                    S.wait(p.ev.get(("rsav", h, qb, kc - 2)))
                    S.op("activation", atb[:, sp, :], psSc[sp][:, :], AF.Exp,
                         scale=SCL, bias=cb128[:, h:h + 1])
                    p.ev[("ex", h, qb, kc)] = S.chan.cur()

                p.sec("tensor", sc_t)
                p.sec("scalar", ex_s)

                if kc >= band_lo:
                    off = 384 - (kc * 128 - qlo)

                    def msk_g(h=h, qb=qb, kc=kc, sp=sp, off=off):
                        G = p.G
                        G.wait(p.ev[("ex", h, qb, kc)])
                        G.wait(slR.done())
                        G.op("tensor_tensor", atb[:, sp, :], atb[:, sp, :],
                             bmsk[:, off:off + 512], ALU.mult)
                        p.ev[("mk", h, qb, kc)] = p.G.chan.cur()

                    p.sec("gpsimd", msk_g)

                def rs_av(h=h, qb=qb, kc=kc, sp=sp, band=(kc >= band_lo),
                          first=(kc == 0), last=(kc == last_kc)):
                    T = p.T
                    if band:
                        T.wait(p.ev[("mk", h, qb, kc)])
                    else:
                        T.wait(p.ev[("ex", h, qb, kc)])
                    T.wait(slV[h % 2].done())
                    if first:
                        T.wait(p.ev.get(("onrm", h, qb - 1)) or
                               p.ev.get(("onrm", h - 1, 3)))
                    T.raw("matmul", psRa[0:1, :], ones128[:, 0:1], atb[:, sp, :],
                          start=first, stop=last)
                    mm = T.raw("matmul", psO[:, :], vbuf[h % 2][:, kc, :],
                               atb[:, sp, :], start=first, stop=last)
                    p.ev[("rsav", h, qb, kc)] = p.T.chan.ev(mm)

                p.sec("tensor", rs_av)

            def onrm_v(h=h, qb=qb, last_kc=last_kc):
                V = p.V
                V.wait(p.ev[("rsav", h, qb, last_kc)])
                V.op("reciprocal", nrm[0:1, :], psRa[0:1, :])
                V.op("tensor_copy", rsb[0:1, 1, :], nrm[0:1, :])
                p.ev[("onrm_v", h, qb)] = V.chan.cur()

            def onrm_t(h=h, qb=qb):
                T = p.T
                T.wait(p.ev[("onrm_v", h, qb)])
                mm = T.raw("matmul", psN[:, :], ones1[0:1, 0:128], rsb[0:1, 1, :],
                           start=True, stop=True)
                p.ev[("onrm_t", h, qb)] = p.T.chan.ev(mm)

            def onrm_v2(h=h, qb=qb):
                V = p.V
                V.wait(p.ev[("onrm_t", h, qb)])
                V.wait(slOd.done())
                V.op("tensor_copy", nrm[:, :], psN[:, :])
                V.op("tensor_tensor", obuf[:, :], psO[:, :], nrm[:, :], ALU.mult)
                p.ev[("onrm", h, qb)] = V.chan.cur()

            def ost(h=h, qb=qb):
                Y = p.Y
                Y.wait(p.ev[("onrm", h, qb)])
                slOd.dma(Y, o_d[h * 128:(h + 1) * 128, qb * 512:(qb + 1) * 512],
                         obuf[:, :])
                if qb == 3:
                    p.ev[("av_done", h)] = slOd.done()

            p.sec("vector", onrm_v)
            p.sec("tensor", onrm_t)
            p.sec("vector", onrm_v2)
            p.sec("sync", ost)

    esC2.close()
    esC.close()

    esD = ExitStack()
    obf = [esD.enter_context(nc.sbuf_tensor("obf%d" % i, [128, 1024], BF16))
           for i in range(2)]
    wob = esD.enter_context(nc.sbuf_tensor("wob", [128, 8, 512], BF16))
    pyt = esD.enter_context(nc.sbuf_tensor("pyt", [128, 2, 512], BF16))
    psJ = [esD.enter_context(nc.psum_tensor("psJ%d" % i, [128, 512], F32))
           for i in range(8)]

    for oblk in range(4):
        def woload(oblk=oblk):
            G = p.G
            G.wait(p.ev[("onrm", 7, 3)])
            G.wait(p.ev.get(("oj_mm", oblk - 1, 1, 7)))
            slWo.dma(G, wob[:, :, :],
                     woT[:, oblk * 512:(oblk + 1) * 512]
                     .rearrange("(ec p) c -> p ec c", p=128))

        p.sec("gpsimd", woload)

        for tg in range(2):
            for ec in range(8):
                def oload(oblk=oblk, tg=tg, ec=ec):
                    Y = p.Y
                    Y.wait(p.ev[("onrm", 7, 3)])
                    Y.wait(slOd.done())
                    Y.wait(p.ev.get(("oj_mm", oblk, tg, ec - 2)) or
                           (p.ev.get(("oj_mm", oblk, tg - 1, 6 + ec)) if ec < 2 else None) or
                           (p.ev.get(("oj_mm", oblk - 1, 1, 6 + ec)) if ec < 2 else None))
                    slOb[ec % 2].dma(Y, obf[ec % 2][:, :],
                                     o_d[ec * 128:(ec + 1) * 128,
                                         tg * 1024:(tg + 1) * 1024])

                def oj_mm(oblk=oblk, tg=tg, ec=ec):
                    T = p.T
                    T.wait(p.ev[("onrm", 7, 3)])
                    T.wait(slOb[ec % 2].done())
                    T.wait(slWo.done())
                    if ec == 0:
                        T.wait(p.ev.get(("oj_ev", oblk, tg - 1, 7)) or
                               p.ev.get(("oj_ev", oblk - 1, 1, 7)))
                    mm = None
                    for tc in range(8):
                        mm = T.raw("matmul", psJ[tc][:, :],
                                   obf[ec % 2][:, tc * 128:(tc + 1) * 128],
                                   wob[:, ec, :], start=(ec == 0), stop=(ec == 7))
                    p.ev[("oj_mm", oblk, tg, ec)] = p.T.chan.ev(mm)

                p.sec("sync", oload)
                p.sec("tensor", oj_mm)

            for tc in range(8):
                def oj_ev(oblk=oblk, tg=tg, tc=tc):
                    V = p.V
                    V.wait(p.ev[("oj_mm", oblk, tg, 7)])
                    V.wait(slO.done())
                    V.op("tensor_copy", pyt[:, tc % 2, :], psJ[tc][:, :])
                    if tc == 7:
                        p.ev[("oj_ev", oblk, tg, 7)] = V.chan.cur()
                    p.ev[("oj_pv", oblk, tg, tc)] = V.chan.cur()

                def oj_st(oblk=oblk, tg=tg, tc=tc):
                    Y = p.Y
                    Y.wait(p.ev[("oj_pv", oblk, tg, tc)])
                    slO.dma(Y, py[tg * 1024 + tc * 128:tg * 1024 + (tc + 1) * 128,
                                  oblk * 512:(oblk + 1) * 512], pyt[:, tc % 2, :])

                p.sec("vector", oj_ev)
                p.sec("sync", oj_st)

    p.sec("sync", lambda: p.Y.wait(slO.done()))
    esD.close()
    return p.fin()


def build_attn_a0():
    from contextlib import ExitStack
    p = Prog("attna0").start()
    nc = p.nc
    SEQ = 2048
    EL = 1024
    xT = nc.dram_tensor("xT", [D, SEQ], F32, kind="ExternalInput").ap()
    wq = nc.dram_tensor("wq", [D, EL], F32, kind="ExternalInput").ap()
    wk = nc.dram_tensor("wk", [D, D], F32, kind="ExternalInput").ap()
    wv = nc.dram_tensor("wv", [D, D], F32, kind="ExternalInput").ap()
    wo = nc.dram_tensor("wo", [EL, D], F32, kind="ExternalInput").ap()
    py = nc.dram_tensor("py", [SEQ, D], BF16, kind="ExternalOutput").ap()
    q_d = nc.dram_tensor("q_d", [SEQ, EL], BF16).ap()
    k_d = nc.dram_tensor("k_d", [SEQ, D], BF16).ap()
    v_d = nc.dram_tensor("v_d", [D, SEQ], BF16).ap()
    a_d = nc.dram_tensor("a_d", [EL, SEQ], BF16).ap()

    slX = p.slot("slX")
    slW = p.slot("slW")
    slQs = p.slot("slQs")
    slKs = p.slot("slKs")
    slVs = p.slot("slVs")
    slA = p.slot("slA")
    slKr = p.slot("slKr")
    slVr = p.slot("slVr")
    slQc = p.slot("slQc")
    slT = p.slot("slT")
    slWo = p.slot("slWo")
    slP = p.slot("slP")

    xc = p.sb("xc", [128, SEQ], F32)
    ones1 = p.sb("ones1", [1, 512], BF16)
    ones128f = p.sb("ones128f", [128, 1], F32)
    epsln = p.sb("epsln", [128, 1], F32)
    srwb = p.sb("srwb", [1, 2, SEQ], BF16)

    def init():
        G = p.G
        G.op("memset", ones1[:, :], 1.0)
        G.op("memset", ones128f[:, :], 1.0)
        G.op("memset", epsln[:, :], LN_EPS)
        p.ev["init"] = p.G.chan.cur()

    p.sec("gpsimd", init)

    esA = ExitStack()
    sqc = esA.enter_context(nc.sbuf_tensor("sqc", [128, SEQ], F32))
    srow = esA.enter_context(nc.sbuf_tensor("srow", [1, 3 * SEQ], F32))
    psSb = [esA.enter_context(nc.psum_tensor("psSb%d" % b, [1, 512], F32)) for b in range(4)]
    psS2b = [esA.enter_context(nc.psum_tensor("psS2b%d" % b, [1, 512], F32)) for b in range(4)]

    for c in range(16):
        def ldx(c=c):
            Y = p.Y
            Y.wait(p.ev.get(("st_s", c - 1)))
            Y.wait(p.ev.get(("st_t1", c - 1)))
            slX.dma(Y, xc[:, :], xT[c * 128:(c + 1) * 128, :])

        def st_t1(c=c):
            T = p.T
            T.wait(slX.done())
            T.wait(p.ev["init"])
            mm = None
            for b in range(4):
                mm = T.raw("matmul", psSb[b][0:1, :], ones128f[:, 0:1],
                           xc[:, b * 512:(b + 1) * 512], start=(c == 0), stop=(c == 15))
            p.ev[("st_t1", c)] = p.T.chan.ev(mm)

        def st_s(c=c):
            S = p.S
            S.wait(slX.done())
            S.wait(p.ev.get(("st_t2", c - 1)))
            S.op("activation", sqc[:, :], xc[:, :], AF.Square)
            p.ev[("st_s", c)] = S.chan.cur()

        def st_t2(c=c):
            T = p.T
            T.wait(p.ev[("st_s", c)])
            mm = None
            for b in range(4):
                mm = T.raw("matmul", psS2b[b][0:1, :], ones128f[:, 0:1],
                           sqc[:, b * 512:(b + 1) * 512], start=(c == 0), stop=(c == 15))
            p.ev[("st_t2", c)] = p.T.chan.ev(mm)

        p.sec("sync", ldx)
        p.sec("tensor", st_t1)
        p.sec("scalar", st_s)
        p.sec("tensor", st_t2)

    def fin_v():
        V = p.V
        V.wait(p.ev[("st_t1", 15)])
        V.wait(p.ev[("st_t2", 15)])
        for b in range(4):
            sl = slice(b * 512, (b + 1) * 512)
            V.op("tensor_scalar", srow[0:1, sl], psSb[b][0:1, :], 1.0 / D, None, ALU.mult)
            V.op("tensor_scalar", srow[0:1, 2048 + b * 512:2048 + (b + 1) * 512],
                 psS2b[b][0:1, :], 1.0 / D, None, ALU.mult)
        V.op("tensor_tensor", srow[0:1, 4096:6144], srow[0:1, 0:2048],
             srow[0:1, 0:2048], ALU.mult)
        V.op("tensor_sub", srow[0:1, 2048:4096], srow[0:1, 2048:4096],
             srow[0:1, 4096:6144])
        p.ev["fin_v"] = V.chan.cur()

    def fin_s():
        S = p.S
        S.wait(p.ev["fin_v"])
        S.wait(p.ev["init"])
        S.op("activation", srow[0:1, 2048:4096], srow[0:1, 2048:4096], AF.Ln,
             bias=epsln[0:1, 0:1])
        S.op("activation", srwb[0:1, 0, :], srow[0:1, 2048:4096], AF.Exp, scale=-0.5)
        S.op("activation", srow[0:1, 4096:6144], srow[0:1, 2048:4096], AF.Exp, scale=-0.5)
        p.ev["fin_s"] = S.chan.cur()

    def fin_v3():
        V = p.V
        V.wait(p.ev["fin_s"])
        V.op("tensor_tensor", srow[0:1, 2048:4096], srow[0:1, 0:2048],
             srow[0:1, 4096:6144], ALU.mult)
        V.op("tensor_scalar", srwb[0:1, 1, :], srow[0:1, 2048:4096], -1.0, None, ALU.mult)
        p.ev["stats"] = V.chan.cur()

    p.sec("vector", fin_v)
    p.sec("scalar", fin_s)
    p.sec("vector", fin_v3)
    esA.close()

    esB = ExitStack()
    xlnf = esB.enter_context(nc.sbuf_tensor("xlnf", [128, 16, SEQ], BF16))
    psBc = [esB.enter_context(nc.psum_tensor("psBc%d" % i, [128, 512], F32)) for i in range(2)]
    psPj = [esB.enter_context(nc.psum_tensor("psPj%d" % i, [128, 512], F32)) for i in range(2)]
    rnb = esB.enter_context(nc.sbuf_tensor("rnb", [128, 2, SEQ], BF16))
    wpb = esB.enter_context(nc.sbuf_tensor("wpb", [128, 16, 512], BF16))
    wvb = esB.enter_context(nc.sbuf_tensor("wvb", [128, 16, 128], BF16))
    etb = esB.enter_context(nc.sbuf_tensor("etb", [128, 2, 512], BF16))

    for b in range(4):
        def bc_t(b=b):
            T = p.T
            T.wait(p.ev["stats"])
            T.wait(p.ev.get(("bc_v", b - 1)))
            T.raw("matmul", psBc[0][:, :], ones1[0:1, 0:128],
                  srwb[0:1, 0, b * 512:(b + 1) * 512], start=True, stop=True)
            m2 = T.raw("matmul", psBc[1][:, :], ones1[0:1, 0:128],
                       srwb[0:1, 1, b * 512:(b + 1) * 512], start=True, stop=True)
            p.ev[("bc_t", b)] = p.T.chan.ev(m2)

        def bc_v(b=b):
            V = p.V
            V.wait(p.ev[("bc_t", b)])
            V.op("tensor_copy", rnb[:, 0, b * 512:(b + 1) * 512], psBc[0][:, :])
            V.op("tensor_copy", rnb[:, 1, b * 512:(b + 1) * 512], psBc[1][:, :])
            p.ev[("bc_v", b)] = V.chan.cur()

        p.sec("tensor", bc_t)
        p.sec("vector", bc_v)

    for c in range(16):
        def ldx2(c=c):
            Y = p.Y
            Y.wait(p.ev["stats"])
            Y.wait(p.ev.get(("ap_v", c - 1)))
            slX.dma(Y, xc[:, :], xT[c * 128:(c + 1) * 128, :])

        def ap_v(c=c):
            V = p.V
            V.wait(slX.done())
            V.wait(p.ev[("bc_v", 3)])
            V.op("tensor_tensor", xc[:, :], xc[:, :], rnb[:, 0, :], ALU.mult)
            V.op("tensor_tensor", xlnf[:, c, :], xc[:, :], rnb[:, 1, :], ALU.add)
            p.ev[("ap_v", c)] = V.chan.cur()

        p.sec("sync", ldx2)
        p.sec("vector", ap_v)

    xln_ev = lambda: p.ev[("ap_v", 15)]

    def emit_tm_proj(tag, wsrc, dst, nblk, slSt, prevtag, prevslot=None):
        for eb in range(nblk):
            def wload(eb=eb):
                G = p.G
                if eb == 0 and prevtag is not None:
                    G.wait(p.ev[(prevtag + "mm", None, 15)])
                G.wait(p.ev.get((tag + "mm", eb - 1, 15)))
                slW.dma(G, wpb[:, :, :],
                        wsrc[:, eb * 512:(eb + 1) * 512]
                        .rearrange("(kc p) c -> p kc c", p=128))

            p.sec("gpsimd", wload)
            for tc in range(16):
                def mmf(tag=tag, eb=eb, tc=tc):
                    T = p.T
                    T.wait(slW.done())
                    T.wait(xln_ev())
                    T.wait(p.ev.get((tag + "ev", eb, tc - 2)) or
                           (p.ev.get((tag + "ev", eb - 1, 14 + tc)) if tc < 2 else None))
                    if eb == 0 and tc < 2 and prevtag is not None:
                        T.wait(p.ev[(prevtag + "ev", None, 14 + tc)])
                    mm = None
                    for kc in range(16):
                        mm = T.raw("matmul", psPj[tc % 2][:, :],
                                   xlnf[:, kc, tc * 128:(tc + 1) * 128],
                                   wpb[:, kc, :], start=(kc == 0), stop=(kc == 15))
                    p.ev[(tag + "mm", eb, tc)] = p.T.chan.ev(mm)
                    if tc == 15:
                        p.ev[(tag + "mm", None, 15)] = p.ev[(tag + "mm", eb, tc)]

                def evf(tag=tag, eb=eb, tc=tc):
                    V = p.V
                    V.wait(p.ev[(tag + "mm", eb, tc)])
                    V.wait(slSt.done())
                    if eb == 0 and tc < 2 and prevslot is not None:
                        V.wait(prevslot.done())
                    V.op("tensor_copy", etb[:, tc % 2, :], psPj[tc % 2][:, :])
                    p.ev[(tag + "ev", eb, tc)] = V.chan.cur()
                    if eb == nblk - 1 and tc >= 14:
                        p.ev[(tag + "ev", None, tc)] = p.ev[(tag + "ev", eb, tc)]

                def stf(tag=tag, eb=eb, tc=tc):
                    Y = p.Y
                    Y.wait(p.ev[(tag + "ev", eb, tc)])
                    slSt.dma(Y, dst[tc * 128:(tc + 1) * 128, eb * 512:(eb + 1) * 512],
                             etb[:, tc % 2, :])

                p.sec("tensor", mmf)
                p.sec("vector", evf)
                p.sec("sync", stf)

    emit_tm_proj("q", wq, q_d, 2, slQs, None)
    emit_tm_proj("k", wk, k_d, 4, slKs, "q", slQs)

    for fc in range(16):
        def wvload(fc=fc):
            G = p.G
            if fc == 0:
                G.wait(p.ev[("kmm", None, 15)])
            G.wait(p.ev.get(("vmm", fc - 1, 3)))
            slW.dma(G, wvb[:, :, :],
                    wv[:, fc * 128:(fc + 1) * 128].rearrange("(kc p) c -> p kc c", p=128))

        p.sec("gpsimd", wvload)
        for sb in range(4):
            def vmm(fc=fc, sb=sb):
                T = p.T
                T.wait(slW.done())
                T.wait(xln_ev())
                if fc == 0 and sb < 2:
                    T.wait(p.ev[("kev", None, 14 + sb)])
                T.wait(p.ev.get(("vev", fc, sb - 2)) or
                       (p.ev.get(("vev", fc - 1, 2 + sb)) if sb < 2 else None))
                mm = None
                for kc in range(16):
                    mm = T.raw("matmul", psPj[sb % 2][:, :], wvb[:, kc, :],
                               xlnf[:, kc, sb * 512:(sb + 1) * 512],
                               start=(kc == 0), stop=(kc == 15))
                p.ev[("vmm", fc, sb)] = p.T.chan.ev(mm)

            def vev(fc=fc, sb=sb):
                V = p.V
                V.wait(p.ev[("vmm", fc, sb)])
                V.wait(slVs.done())
                if fc == 0 and sb < 2:
                    V.wait(slKs.done())
                V.op("tensor_copy", etb[:, sb % 2, :], psPj[sb % 2][:, :])
                p.ev[("vev", fc, sb)] = V.chan.cur()

            def vst(fc=fc, sb=sb):
                Y = p.Y
                Y.wait(p.ev[("vev", fc, sb)])
                slVs.dma(Y, v_d[fc * 128:(fc + 1) * 128, sb * 512:(sb + 1) * 512],
                         etb[:, sb % 2, :])

            p.sec("tensor", vmm)
            p.sec("vector", vev)
            p.sec("sync", vst)

    p.ev["proj_mm_last"] = p.ev[("vmm", 15, 3)]
    esB.close()

    esC = ExitStack()
    vres = esC.enter_context(nc.sbuf_tensor("vres", [128, 16, SEQ], BF16))
    esC1 = ExitStack()
    kres = esC1.enter_context(nc.sbuf_tensor("kres", [128, 16, SEQ], BF16))
    qcb = esC1.enter_context(nc.sbuf_tensor("qcb", [128, 16, 128], BF16))
    af = esC1.enter_context(nc.sbuf_tensor("af", [128, SEQ], F32))
    ab = esC1.enter_context(nc.sbuf_tensor("ab", [128, SEQ], BF16))
    mx4 = esC1.enter_context(nc.sbuf_tensor("mx4", [128, 8], F32))
    racc = esC1.enter_context(nc.sbuf_tensor("racc", [128, 6], F32))
    psS = [esC1.enter_context(nc.psum_tensor("psS%d" % i, [128, 512], F32))
           for i in range(4)]

    def ldkv():
        Y = p.Y
        Y.wait(p.ev["proj_mm_last"])
        Y.wait(p.ev[("vev", 15, 3)])
        Y.wait(slQs.done())
        Y.wait(slKs.done())
        Y.wait(slVs.done())
        slKr.dma(Y, kres[:, :, :], k_d.rearrange("(sc p) f -> p sc f", p=128))
        slVr.dma(Y, vres[:, :, :], v_d.rearrange("(fc p) s -> p fc s", p=128))

    p.sec("sync", ldkv)

    for ec in range(8):
        def qld(ec=ec):
            Y = p.Y
            Y.wait(slQs.done())
            Y.wait(p.ev.get(("samm", ec - 1, 3)))
            slQc.dma(Y, qcb[:, :, :],
                     q_d[:, ec * 128:(ec + 1) * 128].rearrange("(sc p) e -> p sc e", p=128))

        p.sec("sync", qld)

        for fb in range(4):
            def samm(ec=ec, fb=fb):
                T = p.T
                T.wait(slQc.done())
                T.wait(slKr.done())
                T.wait(p.ev.get(("ex", ec - 1, fb)))
                mm = None
                for sc in range(16):
                    mm = T.raw("matmul", psS[fb][:, :], qcb[:, sc, :],
                               kres[:, sc, fb * 512:(fb + 1) * 512],
                               start=(sc == 0), stop=(sc == 15))
                p.ev[("samm", ec, fb)] = p.T.chan.ev(mm)

            p.sec("tensor", samm)

        def stat_v(ec=ec):
            V = p.V
            V.wait(p.ev[("samm", ec, 3)])
            for fb in range(4):
                V.op("tensor_reduce", mx4[:, fb:fb + 1], psS[fb][:, :],
                     axis=AX.X, op=ALU.max)
            V.op("tensor_tensor", mx4[:, 4:5], mx4[:, 0:1], mx4[:, 1:2], ALU.max)
            V.op("tensor_tensor", mx4[:, 5:6], mx4[:, 2:3], mx4[:, 3:4], ALU.max)
            V.op("tensor_tensor", mx4[:, 6:7], mx4[:, 4:5], mx4[:, 5:6], ALU.max)
            V.op("tensor_scalar", mx4[:, 7:8], mx4[:, 6:7], -0.125, None, ALU.mult)
            p.ev[("mx", ec)] = V.chan.cur()

        def ex_s(ec=ec):
            S = p.S
            S.wait(p.ev[("mx", ec)])
            for fb in range(4):
                S.op("activation", af[:, fb * 512:(fb + 1) * 512], psS[fb][:, :],
                     AF.Exp, scale=0.125, bias=mx4[:, 7:8],
                     accum_out=racc[:, fb:fb + 1])
                p.ev[("ex", ec, fb)] = S.chan.cur()
            p.ev[("exd", ec)] = S.chan.cur()

        def ab_v(ec=ec):
            V = p.V
            V.wait(p.ev[("exd", ec)])
            V.wait(slA.done())
            V.op("tensor_tensor", racc[:, 4:5], racc[:, 0:1], racc[:, 1:2], ALU.add)
            V.op("tensor_tensor", racc[:, 5:6], racc[:, 2:3], racc[:, 3:4], ALU.add)
            V.op("tensor_tensor", racc[:, 4:5], racc[:, 4:5], racc[:, 5:6], ALU.add)
            V.op("reciprocal", racc[:, 5:6], racc[:, 4:5])
            V.op("tensor_scalar", ab[:, :], af[:, :], racc[:, 5:6], None, ALU.mult)
            p.ev[("ab", ec)] = V.chan.cur()

        def ast(ec=ec):
            Y = p.Y
            Y.wait(p.ev[("ab", ec)])
            slA.dma(Y, a_d[ec * 128:(ec + 1) * 128, :], ab[:, :])

        p.sec("vector", stat_v)
        p.sec("scalar", ex_s)
        p.sec("vector", ab_v)
        p.sec("sync", ast)

    esC1.close()

    esC2 = ExitStack()
    atb = esC2.enter_context(nc.sbuf_tensor("atb", [128, 16, EL], BF16))
    ofm = esC2.enter_context(nc.sbuf_tensor("ofm", [128, 8, SEQ], BF16))
    wob = esC2.enter_context(nc.sbuf_tensor("wob", [128, 8, 512], BF16))
    pyt = esC2.enter_context(nc.sbuf_tensor("pyt", [128, 2, 512], BF16))
    psO = [esC2.enter_context(nc.psum_tensor("psO%d" % i, [128, 512], F32))
           for i in range(2)]
    psJ = [esC2.enter_context(nc.psum_tensor("psJ%d" % i, [128, 512], F32))
           for i in range(2)]

    def atl():
        Y = p.Y
        Y.wait(slA.done())
        for fc in range(16):
            slT.dma_t(Y, atb[:, fc, :], a_d[:, fc * 128:(fc + 1) * 128])

    p.sec("sync", atl)

    for ec in range(8):
        for sb in range(4):
            def avmm(ec=ec, sb=sb):
                T = p.T
                T.wait(slT.done())
                T.wait(slVr.done())
                T.wait(p.ev.get(("ofme", ec, sb - 2)) or
                       (p.ev.get(("ofme", ec - 1, 2 + sb)) if sb < 2 else None))
                mm = None
                for fc in range(16):
                    mm = T.raw("matmul", psO[sb % 2][:, :],
                               atb[:, fc, ec * 128:(ec + 1) * 128],
                               vres[:, fc, sb * 512:(sb + 1) * 512],
                               start=(fc == 0), stop=(fc == 15))
                p.ev[("avmm", ec, sb)] = p.T.chan.ev(mm)

            def ofme(ec=ec, sb=sb):
                V = p.V
                V.wait(p.ev[("avmm", ec, sb)])
                V.op("tensor_copy", ofm[:, ec, sb * 512:(sb + 1) * 512], psO[sb % 2][:, :])
                p.ev[("ofme", ec, sb)] = V.chan.cur()

            p.sec("tensor", avmm)
            p.sec("vector", ofme)

    for oc in range(4):
        def pwload(oc=oc):
            G = p.G
            G.wait(slA.done())
            G.wait(p.ev.get(("pjmm", oc - 1, 15)))
            slWo.dma(G, wob[:, :, :],
                     wo[:, oc * 512:(oc + 1) * 512].rearrange("(ec p) c -> p ec c", p=128))

        p.sec("gpsimd", pwload)
        for tc in range(16):
            def pjmm(oc=oc, tc=tc):
                T = p.T
                T.wait(slWo.done())
                T.wait(p.ev[("ofme", 7, 3)])
                T.wait(p.ev.get(("pjev", oc, tc - 2)) or
                       (p.ev.get(("pjev", oc - 1, 14 + tc)) if tc < 2 else None))
                mm = None
                for ec in range(8):
                    mm = T.raw("matmul", psJ[tc % 2][:, :],
                               ofm[:, ec, tc * 128:(tc + 1) * 128],
                               wob[:, ec, :], start=(ec == 0), stop=(ec == 7))
                p.ev[("pjmm", oc, tc)] = p.T.chan.ev(mm)

            def pjev(oc=oc, tc=tc):
                V = p.V
                V.wait(p.ev[("pjmm", oc, tc)])
                V.wait(slP.done())
                V.op("tensor_copy", pyt[:, tc % 2, :], psJ[tc % 2][:, :])
                p.ev[("pjev", oc, tc)] = V.chan.cur()

            def pjst(oc=oc, tc=tc):
                Y = p.Y
                Y.wait(p.ev[("pjev", oc, tc)])
                slP.dma(Y, py[tc * 128:(tc + 1) * 128, oc * 512:(oc + 1) * 512],
                        pyt[:, tc % 2, :])

            p.sec("tensor", pjmm)
            p.sec("vector", pjev)
            p.sec("sync", pjst)

    p.sec("sync", lambda: p.Y.wait(slP.done()))
    esC2.close()
    esC.close()
    return p.fin()


def build_attn_a1():
    from contextlib import ExitStack
    p = Prog("attna1").start()
    nc = p.nc
    SEQ = 2048
    SCL2 = 0.08838834764831845
    xT = nc.dram_tensor("xT", [D, SEQ], F32, kind="ExternalInput").ap()
    xh = nc.dram_tensor("xh", [TLOC, D], F32, kind="ExternalInput").ap()
    wq = nc.dram_tensor("wq", [D, 1024], F32, kind="ExternalInput").ap()
    wk = nc.dram_tensor("wk", [D, D], F32, kind="ExternalInput").ap()
    wv = nc.dram_tensor("wv", [D, D], F32, kind="ExternalInput").ap()
    wp = nc.dram_tensor("wp", [D, D], F32, kind="ExternalInput").ap()
    bq = nc.dram_tensor("bq", [1, 1024], F32, kind="ExternalInput").ap()
    bk = nc.dram_tensor("bk", [1, D], F32, kind="ExternalInput").ap()
    bv = nc.dram_tensor("bv", [1, D], F32, kind="ExternalInput").ap()
    bp = nc.dram_tensor("bp", [1, D], F32, kind="ExternalInput").ap()
    xo = nc.dram_tensor("xo", [TLOC, D], F32, kind="ExternalOutput").ap()
    q_d = nc.dram_tensor("q_d", [SEQ, 1024], BF16).ap()
    k_d = nc.dram_tensor("k_d", [SEQ, D], BF16).ap()
    v_d = nc.dram_tensor("v_d", [SEQ, D], BF16).ap()
    ctx_d = nc.dram_tensor("ctx_d", [SEQ, 1024], BF16).ap()

    slX = p.slot("slX")
    slW = p.slot("slW")
    slB = p.slot("slB")
    slQs = p.slot("slQs")
    slKs = p.slot("slKs")
    slVs = p.slot("slVs")
    slQc = p.slot("slQc")
    slKc = p.slot("slKc")
    slVc = p.slot("slVc")
    slCs = p.slot("slCs")
    slT = p.slot("slT")
    slWo = p.slot("slWo")
    slH = p.slot("slH")
    slO = p.slot("slO")

    xc = p.sb("xc", [128, SEQ], F32)
    ones1 = p.sb("ones1", [1, 512], BF16)
    ones128f = p.sb("ones128f", [128, 1], F32)
    epsln = p.sb("epsln", [128, 1], F32)
    srwb = p.sb("srwb", [1, 2, SEQ], BF16)
    brow = p.sb("brow", [1, 4, D], BF16)

    def init():
        G = p.G
        G.op("memset", ones1[:, :], 1.0)
        G.op("memset", ones128f[:, :], 1.0)
        G.op("memset", epsln[:, :], LN_EPS)
        slB.dma(G, brow[:, 0, 0:1024], bq[:, :])
        slB.dma(G, brow[:, 1, :], bk[:, :])
        slB.dma(G, brow[:, 2, :], bv[:, :])
        slB.dma(G, brow[:, 3, :], bp[:, :])
        p.ev["init"] = p.G.chan.cur()

    p.sec("gpsimd", init)

    esA = ExitStack()
    sqc = esA.enter_context(nc.sbuf_tensor("sqc", [128, SEQ], F32))
    srow = esA.enter_context(nc.sbuf_tensor("srow", [1, 3 * SEQ], F32))
    psSb = [esA.enter_context(nc.psum_tensor("psSb%d" % b, [1, 512], F32)) for b in range(4)]
    psS2b = [esA.enter_context(nc.psum_tensor("psS2b%d" % b, [1, 512], F32)) for b in range(4)]

    for c in range(16):
        def ldx(c=c):
            Y = p.Y
            Y.wait(p.ev.get(("st_s", c - 1)))
            Y.wait(p.ev.get(("st_t1", c - 1)))
            slX.dma(Y, xc[:, :], xT[c * 128:(c + 1) * 128, :])

        def st_t1(c=c):
            T = p.T
            T.wait(slX.done())
            T.wait(p.ev["init"])
            mm = None
            for b in range(4):
                mm = T.raw("matmul", psSb[b][0:1, :], ones128f[:, 0:1],
                           xc[:, b * 512:(b + 1) * 512], start=(c == 0), stop=(c == 15))
            p.ev[("st_t1", c)] = p.T.chan.ev(mm)

        def st_s(c=c):
            S = p.S
            S.wait(slX.done())
            S.wait(p.ev.get(("st_t2", c - 1)))
            S.op("activation", sqc[:, :], xc[:, :], AF.Square)
            p.ev[("st_s", c)] = S.chan.cur()

        def st_t2(c=c):
            T = p.T
            T.wait(p.ev[("st_s", c)])
            mm = None
            for b in range(4):
                mm = T.raw("matmul", psS2b[b][0:1, :], ones128f[:, 0:1],
                           sqc[:, b * 512:(b + 1) * 512], start=(c == 0), stop=(c == 15))
            p.ev[("st_t2", c)] = p.T.chan.ev(mm)

        p.sec("sync", ldx)
        p.sec("tensor", st_t1)
        p.sec("scalar", st_s)
        p.sec("tensor", st_t2)

    def fin_v():
        V = p.V
        V.wait(p.ev[("st_t1", 15)])
        V.wait(p.ev[("st_t2", 15)])
        for b in range(4):
            sl = slice(b * 512, (b + 1) * 512)
            V.op("tensor_scalar", srow[0:1, sl], psSb[b][0:1, :], 1.0 / D, None, ALU.mult)
            V.op("tensor_scalar", srow[0:1, 2048 + b * 512:2048 + (b + 1) * 512],
                 psS2b[b][0:1, :], 1.0 / D, None, ALU.mult)
        V.op("tensor_tensor", srow[0:1, 4096:6144], srow[0:1, 0:2048],
             srow[0:1, 0:2048], ALU.mult)
        V.op("tensor_sub", srow[0:1, 2048:4096], srow[0:1, 2048:4096],
             srow[0:1, 4096:6144])
        p.ev["fin_v"] = V.chan.cur()

    def fin_s():
        S = p.S
        S.wait(p.ev["fin_v"])
        S.wait(p.ev["init"])
        S.op("activation", srow[0:1, 2048:4096], srow[0:1, 2048:4096], AF.Ln,
             bias=epsln[0:1, 0:1])
        S.op("activation", srwb[0:1, 0, :], srow[0:1, 2048:4096], AF.Exp, scale=-0.5)
        S.op("activation", srow[0:1, 4096:6144], srow[0:1, 2048:4096], AF.Exp, scale=-0.5)
        p.ev["fin_s"] = S.chan.cur()

    def fin_v3():
        V = p.V
        V.wait(p.ev["fin_s"])
        V.op("tensor_tensor", srow[0:1, 2048:4096], srow[0:1, 0:2048],
             srow[0:1, 4096:6144], ALU.mult)
        V.op("tensor_scalar", srwb[0:1, 1, :], srow[0:1, 2048:4096], -1.0, None, ALU.mult)
        p.ev["stats"] = V.chan.cur()

    p.sec("vector", fin_v)
    p.sec("scalar", fin_s)
    p.sec("vector", fin_v3)
    esA.close()

    esB = ExitStack()
    xlnf = esB.enter_context(nc.sbuf_tensor("xlnf", [128, 16, SEQ], BF16))
    psBc = [esB.enter_context(nc.psum_tensor("psBc%d" % i, [128, 512], F32)) for i in range(2)]
    psPj = [esB.enter_context(nc.psum_tensor("psPj%d" % i, [128, 512], F32)) for i in range(2)]
    rnb = esB.enter_context(nc.sbuf_tensor("rnb", [128, 2, SEQ], BF16))
    wpb = esB.enter_context(nc.sbuf_tensor("wpb", [128, 16, 512], BF16))
    etb = esB.enter_context(nc.sbuf_tensor("etb", [128, 2, 512], BF16))

    for b in range(4):
        def bc_t(b=b):
            T = p.T
            T.wait(p.ev["stats"])
            T.wait(p.ev.get(("bc_v", b - 1)))
            T.raw("matmul", psBc[0][:, :], ones1[0:1, 0:128],
                  srwb[0:1, 0, b * 512:(b + 1) * 512], start=True, stop=True)
            m2 = T.raw("matmul", psBc[1][:, :], ones1[0:1, 0:128],
                       srwb[0:1, 1, b * 512:(b + 1) * 512], start=True, stop=True)
            p.ev[("bc_t", b)] = p.T.chan.ev(m2)

        def bc_v(b=b):
            V = p.V
            V.wait(p.ev[("bc_t", b)])
            V.op("tensor_copy", rnb[:, 0, b * 512:(b + 1) * 512], psBc[0][:, :])
            V.op("tensor_copy", rnb[:, 1, b * 512:(b + 1) * 512], psBc[1][:, :])
            p.ev[("bc_v", b)] = V.chan.cur()

        p.sec("tensor", bc_t)
        p.sec("vector", bc_v)

    for c in range(16):
        def ldx2(c=c):
            Y = p.Y
            Y.wait(p.ev["stats"])
            Y.wait(p.ev.get(("ap_v", c - 1)))
            slX.dma(Y, xc[:, :], xT[c * 128:(c + 1) * 128, :])

        def ap_v(c=c):
            V = p.V
            V.wait(slX.done())
            V.wait(p.ev[("bc_v", 3)])
            V.op("tensor_tensor", xc[:, :], xc[:, :], rnb[:, 0, :], ALU.mult)
            V.op("tensor_tensor", xlnf[:, c, :], xc[:, :], rnb[:, 1, :], ALU.add)
            p.ev[("ap_v", c)] = V.chan.cur()

        p.sec("sync", ldx2)
        p.sec("vector", ap_v)

    xln_ev = lambda: p.ev[("ap_v", 15)]

    projs = [("q", wq, q_d, 2, slQs, 0, None, None),
             ("k", wk, k_d, 4, slKs, 1, "q", slQs),
             ("v", wv, v_d, 4, slVs, 2, "k", slKs)]
    for tag, wsrc, dst, nblk, slSt, bidx, prevtag, prevslot in projs:
        for eb in range(nblk):
            def wload(tag=tag, wsrc=wsrc, eb=eb, prevtag=prevtag):
                G = p.G
                if eb == 0 and prevtag is not None:
                    G.wait(p.ev[(prevtag + "mm", None, 15)])
                G.wait(p.ev.get((tag + "mm", eb - 1, 15)))
                slW.dma(G, wpb[:, :, :],
                        wsrc[:, eb * 512:(eb + 1) * 512]
                        .rearrange("(kc p) c -> p kc c", p=128))

            p.sec("gpsimd", wload)
            for tc in range(16):
                def mmf(tag=tag, eb=eb, tc=tc, bidx=bidx, prevtag=prevtag, nblk=nblk):
                    T = p.T
                    T.wait(slW.done())
                    T.wait(slB.done())
                    T.wait(xln_ev())
                    T.wait(p.ev.get((tag + "ev", eb, tc - 2)) or
                           (p.ev.get((tag + "ev", eb - 1, 14 + tc)) if tc < 2 else None))
                    if eb == 0 and tc < 2 and prevtag is not None:
                        T.wait(p.ev[(prevtag + "ev", None, 14 + tc)])
                    for kc in range(16):
                        T.raw("matmul", psPj[tc % 2][:, :],
                              xlnf[:, kc, tc * 128:(tc + 1) * 128],
                              wpb[:, kc, :], start=(kc == 0), stop=False)
                    mm = T.raw("matmul", psPj[tc % 2][:, :], ones1[0:1, 0:128],
                               brow[0:1, bidx, eb * 512:(eb + 1) * 512],
                               start=False, stop=True)
                    p.ev[(tag + "mm", eb, tc)] = p.T.chan.ev(mm)
                    if tc == 15:
                        p.ev[(tag + "mm", None, 15)] = p.ev[(tag + "mm", eb, tc)]

                def evf(tag=tag, eb=eb, tc=tc, slSt=slSt, prevslot=prevslot, nblk=nblk):
                    V = p.V
                    V.wait(p.ev[(tag + "mm", eb, tc)])
                    V.wait(slSt.done())
                    if eb == 0 and tc < 2 and prevslot is not None:
                        V.wait(prevslot.done())
                    V.op("tensor_copy", etb[:, tc % 2, :], psPj[tc % 2][:, :])
                    p.ev[(tag + "ev", eb, tc)] = V.chan.cur()
                    if eb == nblk - 1 and tc >= 14:
                        p.ev[(tag + "ev", None, tc)] = p.ev[(tag + "ev", eb, tc)]

                def stf(tag=tag, eb=eb, tc=tc, dst=dst, slSt=slSt):
                    Y = p.Y
                    Y.wait(p.ev[(tag + "ev", eb, tc)])
                    slSt.dma(Y, dst[tc * 128:(tc + 1) * 128, eb * 512:(eb + 1) * 512],
                             etb[:, tc % 2, :])

                p.sec("tensor", mmf)
                p.sec("vector", evf)
                p.sec("sync", stf)

    p.ev["proj_mm_last"] = p.ev[("vmm", None, 15)]
    esB.close()

    esC = ExitStack()
    qc = esC.enter_context(nc.sbuf_tensor("qc", [128, 1024], BF16))
    kcb = esC.enter_context(nc.sbuf_tensor("kcb", [128, SEQ], BF16))
    vcb = esC.enter_context(nc.sbuf_tensor("vcb", [128, 2, SEQ], BF16))
    s16 = esC.enter_context(nc.sbuf_tensor("s16", [128, 128], F32))
    e16 = esC.enter_context(nc.sbuf_tensor("e16", [128, 128], F32))
    w16 = esC.enter_context(nc.sbuf_tensor("w16", [128, 2, 128], F32))
    prW = esC.enter_context(nc.sbuf_tensor("prW", [128, 16, 128], F32))
    prT = esC.enter_context(nc.sbuf_tensor("prT", [128, 2, 128, 16], F32))
    mx8 = esC.enter_context(nc.sbuf_tensor("mx8", [128, 8], F32))
    rs8 = esC.enter_context(nc.sbuf_tensor("rs8", [128, 8], F32))
    ctx_t = esC.enter_context(nc.sbuf_tensor("ctx_t", [128, 2, 8, 128], F32))
    ctxb = esC.enter_context(nc.sbuf_tensor("ctxb", [128, 2, 1024], BF16))

    s16g = s16.rearrange("p (h t) -> p h t", t=16)
    e16g = e16.rearrange("p (h t) -> p h t", t=16)

    for tc in range(16):
        pr = tc % 2

        def hld(tc=tc, pr=pr):
            Y = p.Y
            if tc == 0:
                Y.wait(p.ev["proj_mm_last"])
                Y.wait(p.ev[("vev", None, 15)])
                Y.wait(slQs.done())
                Y.wait(slKs.done())
                Y.wait(slVs.done())
            Y.wait(p.ev.get(("scv", tc - 1)))
            Y.wait(p.ev.get(("ctg0", tc - 2)))
            slQc.dma(Y, qc[:, :], q_d[tc * 128:(tc + 1) * 128, :])
            slKc.dma(Y, kcb[:, :], k_d[tc * 128:(tc + 1) * 128, :])
            slVc.dma(Y, vcb[:, pr, :], v_d[tc * 128:(tc + 1) * 128, :])

        def sc_v(tc=tc):
            V = p.V
            V.wait(slQc.done())
            V.wait(slKc.done())
            V.wait(p.ev.get(("exp", tc - 1)))
            kg = kcb[:, :].rearrange("p (t d) -> p t d", d=128)
            for hh in range(8):
                V.op("tensor_tensor", prW[:, :, :],
                     qc[:, hh * 128:(hh + 1) * 128].unsqueeze(1)
                     .broadcast_to([128, 16, 128]), kg, ALU.mult)
                V.op("tensor_reduce", s16[:, hh * 16:(hh + 1) * 16],
                     prW[:, :, :], axis=AX.X, op=ALU.add)
            p.ev[("scv", tc)] = V.chan.cur()

        def sm_v(tc=tc):
            V = p.V
            V.op("tensor_reduce", mx8[:, :], s16g, axis=AX.X, op=ALU.max)
            V.op("tensor_tensor", s16g, s16g,
                 mx8[:, :].unsqueeze(2).broadcast_to([128, 8, 16]), ALU.subtract)
            p.ev[("smv", tc)] = V.chan.cur()

        def sm_s(tc=tc):
            S = p.S
            S.wait(p.ev[("smv", tc)])
            S.op("activation", e16[:, :], s16[:, :], AF.Exp, scale=SCL2)
            p.ev[("exp", tc)] = S.chan.cur()

        def sm_v2(tc=tc, pr=pr):
            V = p.V
            V.wait(p.ev[("exp", tc)])
            V.wait(p.ev.get(("ctg0", tc - 2)))
            V.op("tensor_reduce", rs8[:, :], e16g, axis=AX.X, op=ALU.add)
            V.op("reciprocal", rs8[:, :], rs8[:, :])
            V.op("tensor_tensor", w16[:, pr, :].rearrange("p (h t) -> p h t", t=16),
                 e16g, rs8[:, :].unsqueeze(2).broadcast_to([128, 8, 16]), ALU.mult)
            p.ev[("wv", tc)] = V.chan.cur()

        def ct_g(tc=tc, pr=pr, hh=0):
            G = p.G
            if hh == 0:
                G.wait(slVc.done())
                G.wait(p.ev[("wv", tc)])
            vg = vcb[:, pr, :].rearrange("p (t d) -> p t d", d=128)
            G.wait(p.ev.get(("ctr", tc, hh - 2)) or
                   (p.ev.get(("ctr", tc - 1, 6 + hh)) if hh < 2 else None))
            G.op("tensor_tensor",
                 prT[:, hh % 2, :, :].rearrange("p d t -> p t d"),
                 w16[:, pr, hh * 16:(hh + 1) * 16].unsqueeze(2)
                 .broadcast_to([128, 16, 128]), vg, ALU.mult)
            p.ev[("ctp", tc, hh)] = G.chan.cur()
            if hh == 7:
                p.ev[("ctg0", tc)] = G.chan.cur()

        def ct_r(tc=tc, pr=pr, hh=0):
            V = p.V
            V.wait(p.ev[("ctp", tc, hh)])
            V.op("tensor_reduce", ctx_t[:, pr, hh, :], prT[:, hh % 2, :, :],
                 axis=AX.X, op=ALU.add)
            p.ev[("ctr", tc, hh)] = V.chan.cur()
            if hh == 7:
                p.ev[("ctg", tc)] = V.chan.cur()

        def ct_e(tc=tc, pr=pr):
            V = p.V
            V.wait(p.ev[("ctg", tc)])
            V.wait(slCs.done())
            V.op("tensor_copy", ctxb[:, pr, :], ctx_t[:, pr, :, :])
            p.ev[("cte", tc)] = V.chan.cur()

        def ct_s(tc=tc, pr=pr):
            Y = p.Y
            Y.wait(p.ev[("cte", tc)])
            slCs.dma(Y, ctx_d[tc * 128:(tc + 1) * 128, :], ctxb[:, pr, :])

        p.sec("sync", hld)
        p.sec("vector", sc_v)
        p.sec("vector", sm_v)
        p.sec("scalar", sm_s)
        p.sec("vector", sm_v2)
        for hh in range(8):
            p.sec("gpsimd", lambda tc=tc, pr=pr, hh=hh: ct_g(tc, pr, hh))
            p.sec("vector", lambda tc=tc, pr=pr, hh=hh: ct_r(tc, pr, hh))
        p.sec("vector", ct_e)
        p.sec("sync", ct_s)

    esC.close()

    esD = ExitStack()
    ctxT = esD.enter_context(nc.sbuf_tensor("ctxT", [128, 8, SEQ], BF16))
    ofb = esD.enter_context(nc.sbuf_tensor("ofb", [128, 8, 16, 128], BF16))
    wob = esD.enter_context(nc.sbuf_tensor("wob", [128, 16, 512], BF16))
    xhb = esD.enter_context(nc.sbuf_tensor("xhb", [128, 512], F32))
    xot = esD.enter_context(nc.sbuf_tensor("xot", [128, 512], F32))
    psJ = [esD.enter_context(nc.psum_tensor("psJ%d" % i, [128, 512], F32))
           for i in range(2)]

    def ctl():
        Y = p.Y
        Y.wait(slCs.done())
        for hh in range(8):
            slT.dma_t(Y, ctxT[:, hh, :], ctx_d[:, hh * 128:(hh + 1) * 128])

    p.sec("sync", ctl)

    def scr_v():
        V = p.V
        V.wait(slT.done())
        for hh in range(8):
            V.op("tensor_copy", ofb[:, hh, :, :],
                 ctxT[:, hh, :].rearrange("p (u r) -> p r u", r=16))
        p.ev["scr"] = V.chan.cur()

    p.sec("vector", scr_v)

    for oc in range(4):
        def pwload(oc=oc):
            G = p.G
            G.wait(slCs.done())
            G.wait(p.ev.get(("pjmm", oc - 1, 7)))
            slWo.dma(G, wob[:, :, :],
                     wp[:, oc * 512:(oc + 1) * 512].rearrange("(kc p) c -> p kc c", p=128))

        p.sec("gpsimd", pwload)
        for sc in range(8):
            def pjmm(oc=oc, sc=sc):
                T = p.T
                T.wait(slWo.done())
                T.wait(slB.done())
                T.wait(p.ev["scr"])
                T.wait(p.ev.get(("pje", oc, sc - 2)) or
                       (p.ev.get(("pje", oc - 1, 6 + sc)) if sc < 2 else None))
                for kc in range(16):
                    T.raw("matmul", psJ[sc % 2][:, :], ofb[:, sc, kc, :],
                          wob[:, kc, :], start=(kc == 0), stop=False)
                mm = T.raw("matmul", psJ[sc % 2][:, :], ones1[0:1, 0:128],
                           brow[0:1, 3, oc * 512:(oc + 1) * 512],
                           start=False, stop=True)
                p.ev[("pjmm", oc, sc)] = p.T.chan.ev(mm)

            def xld(oc=oc, sc=sc):
                Y = p.Y
                Y.wait(p.ev.get(("pje", oc, sc - 1)))
                slH.dma(Y, xhb[:, :], xh[sc * 128:(sc + 1) * 128,
                                        oc * 512:(oc + 1) * 512])

            def pje(oc=oc, sc=sc):
                V = p.V
                V.wait(p.ev[("pjmm", oc, sc)])
                V.wait(slH.done())
                V.wait(slO.done())
                V.op("tensor_tensor", xot[:, :], psJ[sc % 2][:, :], xhb[:, :], ALU.add)
                p.ev[("pje", oc, sc)] = V.chan.cur()

            def pst(oc=oc, sc=sc):
                Y = p.Y
                Y.wait(p.ev[("pje", oc, sc)])
                slO.dma(Y, xo[sc * 128:(sc + 1) * 128, oc * 512:(oc + 1) * 512],
                        xot[:, :])

            p.sec("tensor", pjmm)
            p.sec("sync", xld)
            p.sec("vector", pje)
            p.sec("sync", pst)

    p.sec("sync", lambda: p.Y.wait(slO.done()))
    esD.close()
    return p.fin()



_NC_CACHE = {}


def _get(name, builder):
    if name not in _NC_CACHE:
        _NC_CACHE[name] = builder()
    return _NC_CACHE[name]


def run_quant(wgu_full, wpr_full):
    nc = _get("quant", build_quant)
    ins = []
    for c in range(NCORE):
        ins.append({
            "wgu": np.ascontiguousarray(wgu_full[c * 2048:(c + 1) * 2048, :]),
            "wpr": np.ascontiguousarray(wpr_full[c * 256:(c + 1) * 256, :]),
        })
    res = run_bass_kernel_spmd(nc, ins, core_ids=list(range(NCORE)))
    qgu = np.concatenate([r["qgu"] for r in res.results], 0)
    qpr = np.concatenate([r["qpr"] for r in res.results], 0)
    return qgu, qpr


def run_ffn(x_full, wguT, wprT, parts=None):
    nc = _get("ffn%d" % (parts is not None), lambda: build_ffn(parts is not None))
    xf = x_full.reshape(-1, D)
    ins = []
    for c in range(NCORE):
        d = {"x": np.ascontiguousarray(xf[c * TLOC:(c + 1) * TLOC]),
             "wguT": wguT, "wprT": wprT}
        if parts is not None:
            p1f, p2f = parts
            d["p1"] = np.ascontiguousarray(p1f.reshape(-1, D)[c * TLOC:(c + 1) * TLOC])
            d["p2"] = np.ascontiguousarray(p2f.reshape(-1, D)[c * TLOC:(c + 1) * TLOC])
        ins.append(d)
    res = run_bass_kernel_spmd(nc, ins, core_ids=list(range(NCORE)))
    return np.concatenate([r["xo"] for r in res.results], 0).reshape(x_full.shape)


def _asc(a):
    return np.ascontiguousarray(a)


def _rope_tables_l0():
    half = 32
    inv = 1.0 / 10000.0 ** (np.arange(0, half, dtype=np.float32) / half)
    pos = np.arange(2048, dtype=np.float32)
    ang = np.einsum("i,j->ij", pos, inv)
    cos64 = np.cos(np.concatenate([ang, ang], -1))
    sin64 = np.sin(np.concatenate([ang, ang], -1))
    cosT = np.tile(cos64.T, (2, 1)).astype(np.float32)
    sgn = np.concatenate([-sin64.T[:32], sin64.T[32:]], 0)
    sinT = np.tile(sgn, (2, 1)).astype(np.float32)
    return cosT, sinT


def _rope_tables_l1():
    half = 64
    inv = 1.0 / 10000.0 ** (np.arange(0, half, dtype=np.float32) / half)
    pos = np.arange(2048, dtype=np.float32)
    ang = np.einsum("i,j->ij", pos, inv)
    cosT = np.tile(np.cos(ang).T, (2, 1)).astype(np.float32)
    sinT = np.concatenate([-np.sin(ang).T, np.sin(ang).T], 0).astype(np.float32)
    return cosT, sinT


def prep_weights(ins):
    W = {}
    W["bmask"] = (np.arange(1024)[None, :] >=
                  (np.arange(128)[:, None] + 384)).astype(np.float32)
    W["cosT0"], W["sinT0"] = _rope_tables_l0()
    W["cosT1"], W["sinT1"] = _rope_tables_l1()
    wqT0 = t2(ins["L0_attn_b_Wq"])
    wkT0 = t2(ins["L0_attn_b_Wk"])
    wvT0 = t2(ins["L0_attn_b_Wv"])
    woT0 = t2(ins["L0_attn_b_Wo"])
    W["b0_wqT"] = [_asc(wqT0[:, h * 1024:(h + 1) * 1024]) for h in range(2)]
    W["b0_wkT"] = [_asc(wkT0[:, h * 1024:(h + 1) * 1024]) for h in range(2)]
    W["b0_wvT"] = [_asc(wvT0[:, h * 1024:(h + 1) * 1024]) for h in range(2)]
    W["b0_woT"] = [_asc(woT0[h * 1024:(h + 1) * 1024]) for h in range(2)]
    W["b0_bq"] = [_asc(ins["L0_attn_b_bq"][None, h * 1024:(h + 1) * 1024])
                  for h in range(2)]
    W["b0_bk"] = [_asc(ins["L0_attn_b_bk"][None, h * 1024:(h + 1) * 1024])
                  for h in range(2)]
    W["b0_bv"] = [_asc(ins["L0_attn_b_bv"][None, h * 1024:(h + 1) * 1024])
                  for h in range(2)]
    W["b0_bo"] = [_asc(ins["L0_attn_b_bo"][None].astype(np.float32)),
                  np.zeros((1, D), np.float32)]
    wqT1 = t2(ins["L1_attn_b_Wq"])
    wkvhT = t2(ins["L1_attn_b_Wkvh"])
    wgT = t2(ins["L1_attn_b_Wg"])
    wkvT = t2(ins["L1_attn_b_Wkv"])
    woT1 = t2(ins["L1_attn_b_Wout"])
    W["b1_wqT"] = [_asc(wqT1[:, h * 1024:(h + 1) * 1024]) for h in range(2)]
    W["b1_woT"] = [_asc(woT1[h * 1024:(h + 1) * 1024]) for h in range(2)]
    wsm = []
    for h in range(2):
        z8 = np.zeros((D, 8), np.float32)
        z16 = np.zeros((D, 16), np.float32)
        wsm.append(_asc(np.concatenate(
            [wkvT,
             wkvhT[:, h * 8:(h + 1) * 8], wkvhT[:, 16 + h * 8:16 + (h + 1) * 8],
             z16,
             wgT[:, h * 8:(h + 1) * 8], wgT[:, 16 + h * 8:16 + (h + 1) * 8],
             z16], axis=1)))
    W["b1_wsmT"] = wsm
    Wqkv = ins["L1_attn_a_Wqkv"]
    bqkv = ins["L1_attn_a_bqkv"]
    W["a1_wq"] = [_asc(Wqkv[:, th * 1024:(th + 1) * 1024]) for th in range(2)]
    W["a1_wk"] = _asc(Wqkv[:, 2048:4096])
    W["a1_wv"] = _asc(Wqkv[:, 4096:6144])
    W["a1_wp"] = ins["L1_attn_a_Wproj"]
    W["a1_bq"] = [_asc(bqkv[None, th * 1024:(th + 1) * 1024]) for th in range(2)]
    W["a1_bk"] = _asc(bqkv[None, 2048:4096])
    W["a1_bv"] = _asc(bqkv[None, 4096:6144])
    W["a1_bp"] = _asc(ins["L1_attn_a_bproj"][None])
    W["a0_wq"] = [_asc(ins["L0_attn_a_Wq"][:, eh * 1024:(eh + 1) * 1024])
                  for eh in range(2)]
    W["a0_wk"] = ins["L0_attn_a_Wk"]
    W["a0_wv"] = ins["L0_attn_a_Wv"]
    W["a0_wo"] = [_asc(ins["L0_attn_a_Wo"][eh * 1024:(eh + 1) * 1024])
                  for eh in range(2)]
    W["gwT"] = t2(ins["L1_ffn_gate_weight"])
    W["guT"] = t2(ins["L1_ffn_gate_up_weights"])
    W["dnT"] = t2(ins["L1_ffn_down_weights"])
    W["sguT"] = t2(ins["L1_ffn_shared_gate_up"])
    W["sdnT"] = t2(ins["L1_ffn_shared_down"])
    sel8 = np.zeros((8, 8 * 128), np.float32)
    for e in range(8):
        sel8[e, e * 128:(e + 1) * 128] = 1.0
    W["sel8"] = sel8
    W["ident"] = np.eye(128, dtype=np.float32)
    return W


def im_quant(ins):
    wgu, wpr = ins["L0_ffn_w_gate_up"], ins["L0_ffn_w_proj"]
    return [{"wgu": _asc(wgu[c * 2048:(c + 1) * 2048]),
             "wpr": _asc(wpr[c * 256:(c + 1) * 256])} for c in range(NCORE)]


def im_attn_a0(W, x):
    im = []
    for c in range(NCORE):
        b, eh = c // 2, c % 2
        im.append({"xT": t2(x[b]), "wq": W["a0_wq"][eh], "wk": W["a0_wk"],
                   "wv": W["a0_wv"], "wo": W["a0_wo"][eh]})
    return im


def im_ffn(x, wguT, wprT, parts=None):
    xf = x.reshape(-1, D)
    im = []
    for c in range(NCORE):
        d = {"x": _asc(xf[c * TLOC:(c + 1) * TLOC]), "wguT": wguT, "wprT": wprT}
        if parts is not None:
            p1f, p2f = parts
            d["p1"] = _asc(p1f.reshape(-1, D)[c * TLOC:(c + 1) * TLOC])
            d["p2"] = _asc(p2f.reshape(-1, D)[c * TLOC:(c + 1) * TLOC])
        im.append(d)
    return im


def im_attn_b0h(W, x):
    im = []
    for c in range(NCORE):
        b, hh = c // 2, c % 2
        im.append({"xT": t2(x[b]),
                   "wqT": W["b0_wqT"][hh], "wkT": W["b0_wkT"][hh],
                   "wvT": W["b0_wvT"][hh], "woT": W["b0_woT"][hh],
                   "bq": W["b0_bq"][hh], "bk": W["b0_bk"][hh],
                   "bv": W["b0_bv"][hh], "bo": W["b0_bo"][hh],
                   "cosT": W["cosT0"], "sinT": W["sinT0"], "bmask": W["bmask"]})
    return im


def im_attn_a1(W, x):
    im = []
    for c in range(NCORE):
        b, th = c // 2, c % 2
        im.append({"xT": t2(x[b]), "xh": _asc(x[b, th * 1024:(th + 1) * 1024]),
                   "wq": W["a1_wq"][th], "wk": W["a1_wk"], "wv": W["a1_wv"],
                   "wp": W["a1_wp"], "bq": W["a1_bq"][th], "bk": W["a1_bk"],
                   "bv": W["a1_bv"], "bp": W["a1_bp"]})
    return im


def im_moe(W, x, parts=None):
    xf = x.reshape(-1, D)
    im = []
    for c in range(NCORE):
        d = {"x": _asc(xf[c * TLOC:(c + 1) * TLOC]), "gwT": W["gwT"],
             "guT": W["guT"], "dnT": W["dnT"], "sguT": W["sguT"],
             "sdnT": W["sdnT"], "sel8": W["sel8"], "ident": W["ident"]}
        if parts is not None:
            p1f, p2f = parts
            d["p1"] = _asc(p1f.reshape(-1, D)[c * TLOC:(c + 1) * TLOC])
            d["p2"] = _asc(p2f.reshape(-1, D)[c * TLOC:(c + 1) * TLOC])
        im.append(d)
    return im


def im_attn_b1h(W, x):
    im = []
    for c in range(NCORE):
        b, hh = c // 2, c % 2
        im.append({"xT": t2(x[b]),
                   "wqT": W["b1_wqT"][hh], "wsmT": W["b1_wsmT"][hh],
                   "woT": W["b1_woT"][hh],
                   "cosT": W["cosT1"], "sinT": W["sinT1"], "bmask": W["bmask"]})
    return im


def _launch(name, builder, im, keys):
    nc = _get(name, builder)
    res = run_bass_kernel_spmd(nc, im, core_ids=list(range(NCORE)))
    return [{k: np.asarray(r[k]) for k in keys} for r in res.results]


def _halves_to_x(outs0, outs1, shape):
    x = np.empty(shape, np.float32)
    for b in range(4):
        x[b, 0:1024] = outs0[b]["xo"]
        x[b, 1024:2048] = outs1[b]["xo"]
    return x


def kernel(**inputs):
    ins = inputs
    x0 = np.asarray(ins["L0_attn_a_x"], np.float32)
    shape = x0.shape
    W = prep_weights(ins)

    qres = _launch("quant", build_quant, im_quant(ins), ["qgu", "qpr"])
    qgu = np.concatenate([r["qgu"] for r in qres], 0)
    qpr = np.concatenate([r["qpr"] for r in qres], 0)
    wguT, wprT = t2(qgu), t2(qpr)

    a0 = _launch("attna0", build_attn_a0, im_attn_a0(W, x0), ["py"])
    p1f = np.stack([a0[2 * b]["py"] for b in range(4)])
    p2f = np.stack([a0[2 * b + 1]["py"] for b in range(4)])

    f1 = _launch("ffn1", lambda: build_ffn(True),
                 im_ffn(x0, wguT, wprT, (p1f, p2f)), ["xo"])
    x2 = np.concatenate([r["xo"] for r in f1], 0).reshape(shape)

    b0 = _launch("attnb0h", build_attn_b0h, im_attn_b0h(W, x2), ["py"])
    q1f = np.stack([b0[2 * b]["py"] for b in range(4)])
    q2f = np.stack([b0[2 * b + 1]["py"] for b in range(4)])

    f2 = _launch("ffn1", lambda: build_ffn(True),
                 im_ffn(x2, wguT, wprT, (q1f, q2f)), ["xo"])
    x4 = np.concatenate([r["xo"] for r in f2], 0).reshape(shape)

    a1 = _launch("attna1", build_attn_a1, im_attn_a1(W, x4), ["xo"])
    x5 = np.empty(shape, np.float32)
    for c in range(NCORE):
        b, th = c // 2, c % 2
        x5[b, th * 1024:(th + 1) * 1024] = a1[c]["xo"]

    m1 = _launch("moe", build_moe, im_moe(W, x5), ["xo"])
    x6 = np.concatenate([r["xo"] for r in m1], 0).reshape(shape)

    b1 = _launch("attnb1h", build_attn_b1h, im_attn_b1h(W, x6), ["py"])
    r1f = np.stack([b1[2 * b]["py"] for b in range(4)])
    r2f = np.stack([b1[2 * b + 1]["py"] for b in range(4)])

    m2 = _launch("moe1", lambda: build_moe(True),
                 im_moe(W, x6, (r1f, r2f)), ["xo"])
    x8 = np.concatenate([r["xo"] for r in m2], 0).reshape(shape)
    return x8.astype(np.float32)
